# version 03
# baseline (speedup 1.0000x reference)
import contextlib
import numpy as np
import ml_dtypes
import concourse.bass as bass
import concourse.mybir as mybir
from concourse.bass_utils import run_bass_kernel_spmd

F32 = mybir.dt.float32
BF16 = mybir.dt.bfloat16
AF = mybir.ActivationFunctionType
ALU = mybir.AluOpType
AX = mybir.AxisListType

B_CORE = 2
N = 2048
D = 1024
T = B_CORE * N
H = 16
HD = 64
CHUNK = 64
NCH = N // CHUNK
NPAIR = 8
DCH = D // 128

ENGINES = ("sync", "tensor", "vector", "scalar", "gpsimd")


ARENA_BASE = 20 * 1024
SBUF_BYTES = 229312


class K:

    def __init__(self, nc):
        self.nc = nc
        self.q = {e: [] for e in ENGINES}
        self.sem_names = []
        self.cnt = {}
        self._uid = 0
        self.mm_base = None
        self.p_off = ARENA_BASE
        self.s_base = None
        self.s_off = None
        self.s_peak = 0

    def _alloc_at(self, shape, dtype, off):
        self._uid += 1
        nbytes = int(np.prod(shape[1:])) * (2 if dtype == BF16 else 4)
        t = self.nc.alloc_sbuf_tensor_at(f"sb{self._uid}", list(shape), dtype,
                                         offset=off)
        return t, off + ((nbytes + 31) // 32) * 32

    def palloc(self, shape, dtype):
        t, self.p_off = self._alloc_at(shape, dtype, self.p_off)
        assert self.s_base is None, "persistent alloc after freeze"
        return t

    def freeze_persistent(self):
        self.s_base = self.p_off
        self.s_off = self.p_off

    def sreset(self):
        self.s_peak = max(self.s_peak, self.s_off)
        self.s_off = self.s_base

    def salloc(self, shape, dtype):
        t, self.s_off = self._alloc_at(shape, dtype, self.s_off)
        assert self.s_off <= SBUF_BYTES, f"sbuf over budget: {self.s_off}"
        return t

    def smark(self):
        return self.s_off

    def srewind(self, mark):
        self.s_peak = max(self.s_peak, self.s_off)
        self.s_off = mark

    def sem(self, name):
        self.sem_names.append(name)
        self.cnt[name] = 0
        return name

    def vop_s(self, fn):
        self.cnt["dv"] += 1
        self.q["vector"].append(("op", fn, "dv", 1))
        self.q["vector"].append(("wait", "dv", self.cnt["dv"]))
        return self.cnt["dv"]

    def op(self, eng, fn, inc=None, amt=1, mmb=None):
        if mmb is not None and eng == "tensor":
            if self.mm_base is not None and self.mm_base != mmb:
                self.q[eng].append(("op", lambda e: e.nop(), None, 1))
                self.q[eng].append(("op", lambda e: e.nop(), None, 1))
            self.mm_base = mmb
        if inc is not None:
            self.cnt[inc] += amt
        self.q[eng].append(("op", fn, inc, amt))
        return self.cnt[inc] if inc is not None else None

    def dma(self, eng, out, in_, sem="ld", **kw):
        self.cnt[sem] += 16
        tgt = self.cnt[sem]
        self.q[eng].append(
            ("op", lambda e, o=out, i=in_, kw=kw: e.dma_start(out=o, in_=i, **kw),
             sem, 16))
        return tgt

    def wait(self, eng, sem, val):
        self.q[eng].append(("wait", sem, val))

    def fence(self, eng, sem):
        self.cnt[sem] += 1
        if eng in ("vector", "gpsimd"):
            self.q[eng].append(("op", lambda e: e.engine_nop(), sem, 1))
        else:
            self.q[eng].append(("op", lambda e: e.nop(), sem, 1))
        return self.cnt[sem]

    def build(self):
        nc = self.nc
        with contextlib.ExitStack() as stack:
            sems = {n: stack.enter_context(nc.semaphore(n)) for n in self.sem_names}
            with nc.Block() as block:
                def mk(engname):
                    def emit(eng):
                        for item in self.q[engname]:
                            if item[0] == "wait":
                                eng.wait_ge(sems[item[1]], item[2])
                            else:
                                _, fn, inc, amt = item
                                ins = fn(eng)
                                if inc is not None:
                                    ins.then_inc(sems[inc], amt)
                    return emit
                block.sync(mk("sync"))
                block.tensor(mk("tensor"))
                block.vector(mk("vector"))
                block.scalar(mk("scalar"))
                block.gpsimd(mk("gpsimd"))
        return nc



def host_constants():
    c = {}
    t = np.arange(N, dtype=np.float64)[None, :]
    j = np.arange(HD, dtype=np.float64)[:, None]
    theta = t * (10000.0 ** (-2.0 * j / HD))
    cos1 = np.cos(theta).astype(np.float32)
    sin1 = np.sin(theta).astype(np.float32)
    c["cstab"] = np.concatenate([cos1, sin1], 0).astype(ml_dtypes.bfloat16)
    for hh in range(2):
        selT = np.zeros((128, 128), np.float32)
        for pcol in range(128):
            selT[64 * hh + (pcol % 64), pcol] = 1.0
        c[f"selT{hh}"] = selT
    c["ident_f32"] = np.eye(128, dtype=np.float32)
    c["ident_bf"] = np.eye(128, dtype=np.float32).astype(ml_dtypes.bfloat16)
    s = np.arange(CHUNK)[:, None]
    cc = np.arange(CHUNK)[None, :]
    c["triu1"] = (cc >= s).astype(np.float32)
    c["ones_col_bf"] = np.ones((128, 1), np.float32).astype(ml_dtypes.bfloat16)
    c["ones_row_f32"] = np.ones((1, 128), np.float32)
    c["ones_row_bf"] = np.ones((1, 128), np.float32).astype(ml_dtypes.bfloat16)
    c["k09c"] = np.full((16, N), 0.9, np.float32)
    c["ones_col_f32"] = np.ones((128, 1), np.float32)
    return c


CONST_SPECS = {
    "cstab": ([128, N], BF16),
    "selT0": ([128, 128], F32), "selT1": ([128, 128], F32),
    "ident_f32": ([128, 128], F32), "ident_bf": ([128, 128], BF16),
    "triu1": ([64, 64], F32),
    "ones_col_bf": ([128, 1], BF16), "ones_row_f32": ([1, 128], F32),
    "ones_row_bf": ([1, 128], BF16), "k09c": ([16, N], F32),
    "ones_col_f32": ([128, 1], F32),
}


def stage_consts(nc, k):
    sb = {}
    for name, (shape, dt) in CONST_SPECS.items():
        dram = nc.dram_tensor(name, shape, dt, kind="ExternalInput").ap()
        if name == "k09c":
            sb["k09c_dram"] = dram
            continue
        sb[name] = k.palloc(shape, dt)
        k.dma("sync", sb[name][:], dram, "ld")
    return sb



def emit_phase_a(nc, k, cst, x_in, wq_d, wk_d, wv_d, wo_d, wg1_d, wg2_d,
                 normw_d, xr0, xr1, o_s, dbg_bp=99, dbg_a3=True):
    k.sreset()
    ln0 = k.salloc([128, DCH * T], BF16)
    xt = k.salloc([128, 2 * D], F32)
    lnt = k.salloc([128, 2 * D], BF16)
    xsq = k.salloc([128, D], F32)
    stat = k.salloc([128, 8], F32)
    xcol = k.salloc([128, 2 * D], F32)
    m_gla = k.smark()

    wk = k.salloc([128, DCH * 128], BF16)
    wqd = [k.salloc([128, DCH * 128], BF16) for _ in range(2)]
    wvh = [k.salloc([128, DCH * 64], BF16) for _ in range(2)]
    kr = k.salloc([128, N], F32)
    qs = k.salloc([128, N], F32)
    KG = k.salloc([128, N], F32)
    CS = k.salloc([128, N], F32)
    EQ = k.salloc([128, N], F32)
    tmpf = k.salloc([128, N], F32)
    sc = k.salloc([128, 512], F32)
    QG = k.salloc([128, N], BF16)
    KGC = k.salloc([128, N], BF16)
    K2 = k.salloc([128, N], BF16)
    EQd = k.salloc([128, 32], F32)
    vfm_h = k.salloc([64, N], BF16)
    v_cm_h = k.salloc([64, NCH * 64], BF16)
    Atile = k.salloc([64, 64], BF16)
    k2tok = k.salloc([64, 128], BF16)
    S_h = k.salloc([128, 64], F32)
    S_hb = k.salloc([128, 64], BF16)
    O_h = k.salloc([64, N], BF16)

    psT = k.pbank[0][:]
    psT2 = k.pbank[1][:]
    psP = k.pbank[2][:]
    psP2 = k.pbank[3][:]
    psA = k.pbank[4][:][:, 0:128]
    psO = k.pbank[5][:][:, 0:64]
    psS = k.pbank[5][:][:, 64:192]
    psK = k.pbank[6][:].bitcast(BF16)[0:64, 0:384]

    ntile = T // 128
    xt_chk = [0] * 2
    lnt_chk = [0] * 2
    xcol_chk = [0] * 2
    for i in range(ntile):
        buf = i % 2
        xt_i = xt[:, buf * D:(buf + 1) * D]
        lnt_i = lnt[:, buf * D:(buf + 1) * D]
        xcol_i = xcol[:, buf * D:(buf + 1) * D]
        if i >= 2:
            k.wait("sync", "pe", xt_chk[buf])
        ldt = k.dma("sync", xt_i, x_in[i * 128:(i + 1) * 128, :], "ld")
        k.wait("vector", "ld", ldt)
        k.op("vector", lambda e, o=stat[:, 0:1], i_=xt_i: e.reduce_sum(o, i_, AX.X))
        k.wait("scalar", "ld", ldt)
        k.op("scalar", lambda e, o=xsq[:], i_=xt_i:
             e.activation(o, i_, AF.Square, bias=0.0, scale=1.0), inc="ac")
        k.wait("vector", "ac", k.cnt["ac"])
        k.op("vector", lambda e, o=stat[:, 1:2], i_=xsq[:]: e.reduce_sum(o, i_, AX.X),
             inc="dv")
        k.wait("vector", "dv", k.cnt["dv"])
        k.op("vector", lambda e, o=stat[:, 2:4], i_=stat[:, 0:2]:
             e.tensor_scalar(o, i_, 1.0 / D, None, ALU.mult), inc="dv")
        k.wait("vector", "dv", k.cnt["dv"])
        k.op("vector", lambda e, o=stat[:, 4:5], a=stat[:, 2:3], b=stat[:, 2:3]:
             e.tensor_mul(o, a, b), inc="dv")
        k.wait("vector", "dv", k.cnt["dv"])
        k.op("vector", lambda e, o=stat[:, 5:6], a=stat[:, 3:4], b=stat[:, 4:5]:
             e.tensor_sub(o, a, b), inc="dv")
        k.wait("vector", "dv", k.cnt["dv"])
        k.op("vector", lambda e, o=stat[:, 7:8], i_=stat[:, 5:6]:
             e.tensor_scalar(o, i_, 1e-5, None, ALU.add), inc="dv")
        k.wait("scalar", "dv", k.cnt["dv"])
        k.op("scalar", lambda e, o=stat[:, 5:6], i_=stat[:, 7:8]:
             e.activation(o, i_, AF.Sqrt, bias=0.0, scale=1.0), inc="ac")
        k.wait("vector", "ac", k.cnt["ac"])
        k.op("vector", lambda e, o=stat[:, 6:7], i_=stat[:, 5:6]:
             e.reciprocal(o, i_), inc="dv")
        k.wait("vector", "dv", k.cnt["dv"])
        if i >= 2:
            k.wait("vector", "pe", lnt_chk[buf])
        k.op("vector", lambda e, o=lnt_i, i_=xt_i, m=stat[:, 2:3], r=stat[:, 6:7]:
             e.tensor_scalar(o, i_, m, r, ALU.subtract, ALU.mult), inc="dv")
        dv_ln = k.cnt["dv"]

        k.wait("tensor", "ld", ldt)
        if i >= 2:
            k.wait("tensor", "st", xcol_chk[buf])
        for j in range(DCH):
            bank = psT if j % 2 == 0 else psT2
            pet = k.op("tensor", lambda e, o=bank[:, 0:128],
                       i_=xt_i[:, j * 128:(j + 1) * 128], idn=cst["ident_f32"][:]:
                       e.transpose(o, i_, idn), inc="pe")
            k.wait("vector", "pe", pet)
            dvt = k.op("vector", lambda e, o=xcol_i[:, j * 128:(j + 1) * 128],
                       i_=bank[:, 0:128]: e.tensor_copy(o, i_), inc="dv")
            k.wait("tensor", "dv", dvt)
        k.wait("sync", "dv", k.cnt["dv"])
        dst = xr0[:, i * 128:(i + 1) * 128].rearrange("(c p) t -> p c t", p=128)
        xcol_chk[buf] = k.dma("sync", dst, xcol_i.rearrange("p (c t) -> p c t", t=128),
                              "st")
        k.wait("tensor", "dv", dv_ln)
        for j in range(DCH):
            bank = (psT if j % 2 == 0 else psT2)[:, 0:128].bitcast(BF16)
            pet = k.op("tensor", lambda e, o=bank[:, 0:128],
                       i_=lnt_i[:, j * 128:(j + 1) * 128], idn=cst["ident_bf"][:]:
                       e.transpose(o, i_, idn), inc="pe")
            k.wait("vector", "pe", pet)
            dvt = k.op("vector", lambda e, o=ln0[:, j * T + i * 128: j * T + (i + 1) * 128],
                       i_=bank[:, 0:128]: e.tensor_copy(o, i_), inc="dv")
            k.wait("tensor", "dv", dvt)
        lnt_chk[buf] = k.cnt["pe"]
        xt_chk[buf] = k.cnt["pe"]
    a1_dv = k.fence("vector", "dv")
    for eng in ("tensor", "sync", "gpsimd"):
        k.wait(eng, "dv", a1_dv)

    NSL = N // 512

    def proj_fm(wsb, out_sb, bslice, out_dtype, act=None):
        for ns in range(NSL):
            bank = psP if ns % 2 == 0 else psP2
            for a in range(DCH):
                pet = k.op("tensor", lambda e, o=bank[:],
                           l=wsb[:, a * 128:(a + 1) * 128],
                           r=ln0[:, a * T + bslice * N + ns * 512:
                                 a * T + bslice * N + ns * 512 + 512],
                           s=(a == 0), p=(a == DCH - 1):
                           e.matmul(o, l, r, start=s, stop=p), inc="pe")
            k.wait("vector" if act is None else "scalar", "pe", pet)
            o_sl = out_sb[:, ns * 512:(ns + 1) * 512]
            if act is None:
                dvt = k.op("vector", lambda e, o=o_sl, i_=bank[:]:
                           e.tensor_copy(o, i_), inc="dv")
                k.wait("tensor", "dv", dvt)
            else:
                sct = k.op("scalar", lambda e, o=o_sl, i_=bank[:], f=act:
                           e.activation(o, i_, f, bias=0.0, scale=1.0), inc="ac")
                k.wait("tensor", "ac", sct)

    import os
    a2stop = int(os.environ.get("DBG_A2STOP", "99"))
    for bp in range(min(B_CORE * NPAIR, dbg_bp)):
        p, b = divmod(bp, B_CORE)
        c0 = p * 128
        if b == 0:
            if p > 0:
                k.wait("gpsimd", "pe", k.cnt["pe"])
            src = wk_d[:, c0:c0 + 128].rearrange("(a p) c -> p a c", p=128)
            k.dma("gpsimd", wk[:].rearrange("p (a c) -> p a c", c=128), src, "wld")
            for hh in range(2):
                hc = c0 + 64 * hh
                qsrc = wq_d[:, hc:hc + 64].rearrange("(a p) c -> p a c", p=128)
                wqv = wqd[hh][:].rearrange("p (a c) -> p a c", c=128)
                k.dma("gpsimd", wqv[:, :, 0:64], qsrc, "wld")
                k.dma("gpsimd", wqv[:, :, 64:128], qsrc, "wld")
                vsrc = wv_d[:, hc:hc + 64].rearrange("(a p) c -> p a c", p=128)
                k.dma("gpsimd", wvh[hh][:].rearrange("p (a c) -> p a c", c=64),
                      vsrc, "wld")
            k.wait("tensor", "wld", k.cnt["wld"])
        proj_fm(wk, kr, b, F32)
        if a2stop <= 1:
            continue

        kr3 = kr[:].rearrange("p (c k) -> p c k", k=CHUNK)
        m32 = sc[:, 0:32]
        T32 = sc[:, 32:64]
        La = sc[:, 96:128]
        Lb = sc[:, 128:160]
        Zc = sc[:, 160:192]
        t32a = sc[:, 192:224]
        t32b = sc[:, 224:256]
        EQend = sc[:, 256:288]
        k.vop_s(lambda e, o=m32, i_=kr3: e.reduce_max(o, i_, AX.X))
        k.op("vector", lambda e, o=tmpf[:].rearrange("p (c k) -> p c k", k=CHUNK),
             a=kr3, m=m32: e.tensor_tensor(
                 o, a, m.unsqueeze(2).to_broadcast((128, NCH, CHUNK)), ALU.subtract),
             inc="dv")
        k.wait("scalar", "dv", k.cnt["dv"])
        k.op("scalar", lambda e, o=tmpf[:], i_=tmpf[:]:
             e.activation(o, i_, AF.Exp, bias=0.0, scale=1.0), inc="ac")
        k.wait("vector", "ac", k.cnt["ac"])
        k.op("vector", lambda e, o=T32,
             i_=tmpf[:].rearrange("p (c k) -> p c k", k=CHUNK):
             e.reduce_sum(o, i_, AX.X), inc="dv")
        k.wait("scalar", "dv", k.cnt["dv"])
        k.op("scalar", lambda e, o=T32, i_=T32:
             e.activation(o, i_, AF.Ln, bias=0.0, scale=1.0), inc="ac")
        k.wait("vector", "ac", k.cnt["ac"])
        k.vop_s(lambda e, o=La, a=m32, b_=T32: e.tensor_add(o, a, b_))
        cur, nxt = La, Lb
        for d in (1, 2, 4, 8, 16):
            k.vop_s(lambda e, o=nxt[:, 0:d], i_=cur[:, 0:d], d=d:
                    e.tensor_copy(o, i_))
            k.vop_s(lambda e, o=nxt[:, d:32], a=cur[:, d:32], b_=cur[:, 0:32 - d],
                    d=d: e.tensor_max(o, a, b_))
            k.vop_s(lambda e, o=t32a[:, 0:32 - d], a=cur[:, d:32],
                    b_=cur[:, 0:32 - d], d=d: e.tensor_tensor(o, a, b_, ALU.min))
            k.vop_s(lambda e, o=t32a[:, 0:32 - d], a=t32a[:, 0:32 - d],
                    b_=nxt[:, d:32], d=d: e.tensor_sub(o, a, b_))
            k.wait("scalar", "dv", k.cnt["dv"])
            sct = k.op("scalar", lambda e, o=t32b[:, 0:32 - d], i_=t32a[:, 0:32 - d],
                       d=d: e.activation(o, i_, AF.Exp, bias=0.0, scale=1.0),
                       inc="ac")
            k.wait("scalar", "ac", sct)
            k.op("scalar", lambda e, o=t32b[:, 0:32 - d], i_=t32b[:, 0:32 - d], d=d:
                 e.activation(o, i_, AF.Ln, bias=1.0, scale=1.0), inc="ac")
            k.wait("vector", "ac", k.cnt["ac"])
            k.vop_s(lambda e, o=nxt[:, d:32], a=nxt[:, d:32],
                    b_=t32b[:, 0:32 - d], d=d: e.tensor_add(o, a, b_))
            cur, nxt = nxt, cur
        k.vop_s(lambda e, o=Zc[:, 0:1]: e.memset(o, 0.0))
        k.wait("scalar", "dv", k.cnt["dv"])
        sct = k.op("scalar", lambda e, o=t32a[:, 0:31], i_=cur[:, 0:31]:
                   e.activation(o, i_, AF.Abs, bias=0.0, scale=1.0), inc="ac")
        k.wait("scalar", "ac", sct)
        sct = k.op("scalar", lambda e, o=t32a[:, 0:31], i_=t32a[:, 0:31]:
                   e.activation(o, i_, AF.Exp, bias=0.0, scale=-1.0), inc="ac")
        k.wait("scalar", "ac", sct)
        k.op("scalar", lambda e, o=t32a[:, 0:31], i_=t32a[:, 0:31]:
             e.activation(o, i_, AF.Ln, bias=1.0, scale=1.0), inc="ac")
        k.wait("vector", "ac", k.cnt["ac"])
        k.vop_s(lambda e, o=t32b[:, 0:31], i_=cur[:, 0:31]:
                e.tensor_scalar(o, i_, 0.0, None, ALU.max))
        k.op("vector", lambda e, o=Zc[:, 1:32], a=t32b[:, 0:31], b_=t32a[:, 0:31]:
             e.tensor_add(o, a, b_), inc="dv")
        k.op("vector", lambda e, o=KG[:].rearrange("p (c k) -> p c k", k=CHUNK),
             a=kr3, z=Zc: e.tensor_tensor(
                 o, a, z.unsqueeze(2).to_broadcast((128, NCH, CHUNK)), ALU.subtract),
             inc="dv")
        k.wait("scalar", "dv", k.cnt["dv"])
        k.op("scalar", lambda e, o=KG[:], i_=KG[:]:
             e.activation(o, i_, AF.Exp, bias=0.0, scale=1.0), inc="ac")
        ac_kg = k.cnt["ac"]
        k.wait("vector", "ac", ac_kg)
        for c in range(NCH):
            k.op("vector", lambda e, o=CS[:, c * 64:(c + 1) * 64],
                 d0=KG[:, c * 64:(c + 1) * 64]:
                 e.tensor_tensor_scan(o, d0, d0, 1.0, ALU.add, ALU.bypass))
        k.op("vector", lambda e: e.engine_nop(), inc="dv")
        k.wait("scalar", "dv", k.cnt["dv"])
        k.op("scalar", lambda e, o=EQ[:], i_=CS[:]:
             e.activation(o, i_, AF.Ln, bias=0.0, scale=1.0))
        k.op("scalar", lambda e, o=EQ[:], i_=EQ[:]:
             e.activation(o, i_, AF.Exp, bias=0.0, scale=-1.0), inc="ac")
        k.wait("vector", "ac", k.cnt["ac"])
        k.vop_s(lambda e, o=EQend.unsqueeze(2),
                i_=EQ[:].rearrange("p (c k) -> p c k", k=CHUNK)[:, :, 63:64]:
                e.tensor_copy(o, i_))
        k.op("vector", lambda e: e.engine_nop(), inc="dv")
        zfull_dv = k.cnt["dv"]
        zfull_ac = k.cnt["ac"]
        if os.environ.get("DBG_DUMP") and bp == 0:
            dmp = nc.dram_tensor("dump", [6, 128, N], F32, kind="ExternalOutput").ap()
            k.wait("sync", "dv", k.cnt["dv"])
            k.wait("sync", "ac", k.cnt["ac"])
            for di, buf in enumerate((kr, KG, CS, EQ)):
                k.dma("sync", dmp[di], buf[:], "st")
            k.dma("gpsimd", dmp[4], ln0[:, 0:N], "wld")
            k.wait("sync", "wld", k.cnt["wld"])
            k.dma("sync", dmp[5, 0:128, 0:64], sc[:, 0:64], "st")
            k.dma("sync", dmp[5, 0:128, 96:192], sc[:, 96:192], "st")
            k.dma("sync", dmp[5, 0:128, 256:288], sc[:, 256:288], "st")
            k.wait("vector", "st", k.cnt["st"])
        if a2stop <= 2:
            continue

        for hh in range(2):
            proj_fm(wqd[hh], qs, b, F32, act=AF.Silu)
            for ns in range(NSL):
                bank = (psP if ns % 2 == 0 else psP2)[0:64, :]
                for a in range(DCH):
                    pet = k.op("tensor", lambda e, o=bank,
                               l=wvh[hh][:, a * 64:(a + 1) * 64],
                               r=ln0[:, a * T + b * N + ns * 512:
                                     a * T + b * N + ns * 512 + 512],
                               s=(a == 0), q=(a == DCH - 1):
                               e.matmul(o, l, r, start=s, stop=q), inc="pe",
                               mmb=0)
                k.wait("vector", "pe", pet)
                dvt = k.op("vector", lambda e, o=vfm_h[:, ns * 512:(ns + 1) * 512],
                           i_=bank: e.tensor_copy(o, i_), inc="dv")
                k.wait("tensor", "dv", dvt)
            k.wait("tensor", "dv", zfull_dv)
            k.wait("tensor", "ac", zfull_ac)
            for ns in range(NSL):
                nsl = slice(ns * 512, (ns + 1) * 512)
                pet = k.op("tensor", lambda e, o=psT[:], l=cst[f"selT{hh}"][:],
                           r=EQ[:, nsl]: e.matmul(o, l, r, start=True, stop=True),
                           mmb=0)
                pet = k.op("tensor", lambda e, o=psT2[:], l=cst[f"selT{hh}"][:],
                           r=KG[:, nsl]: e.matmul(o, l, r, start=True, stop=True),
                           inc="pe", mmb=0)
                k.wait("vector", "pe", pet)
                k.op("vector", lambda e, o=QG[:, nsl], a=qs[:, nsl],
                     b_=cst["cstab"][:, nsl]: e.tensor_mul(o, a, b_))
                k.op("vector", lambda e, o=QG[:, nsl], a=QG[:, nsl], b_=psT[:]:
                     e.tensor_mul(o, a, b_))
                dvt = k.op("vector", lambda e, o=KGC[:, nsl],
                           a=cst["cstab"][:, nsl], b_=psT2[:]:
                           e.tensor_mul(o, a, b_), inc="dv")
                k.wait("tensor", "dv", dvt)
            pet = k.op("tensor", lambda e, o=psT[:, 0:32], l=cst[f"selT{hh}"][:],
                       r=EQend: e.matmul(o, l, r, start=True, stop=True),
                       inc="pe", mmb=0)
            k.wait("vector", "pe", pet)
            k.vop_s(lambda e, o=EQd[:], i_=psT[:, 0:32]: e.tensor_copy(o, i_))
            k.op("vector", lambda e,
                 o=K2[:].rearrange("p (c k) -> p c k", k=CHUNK),
                 a=KGC[:].rearrange("p (c k) -> p c k", k=CHUNK), z=EQd[:]:
                 e.tensor_tensor(o, a, z.unsqueeze(2).to_broadcast(
                     (128, NCH, CHUNK)), ALU.mult))
            dvt = k.op("vector", lambda e: e.engine_nop(), inc="dv")
            k.wait("tensor", "dv", dvt)
            for c in range(NCH):
                bank = psK[:, 256 + (c % 2) * 64: 256 + (c % 2) * 64 + 64]
                pet = k.op("tensor", lambda e, o=bank,
                           i_=vfm_h[:, c * 64:(c + 1) * 64],
                           idn=cst["ident_bf"][0:64, 0:64]:
                           e.transpose(o, i_, idn), inc="pe", mmb=0)
                k.wait("vector", "pe", pet)
                dvt = k.op("vector", lambda e, o=v_cm_h[:, c * 64:(c + 1) * 64],
                           i_=bank: e.tensor_copy(o, i_), inc="dv")
                k.wait("tensor", "dv", dvt)
            k.op("vector", lambda e, o=S_h[:]: e.memset(o, 0.0))
            k.op("vector", lambda e, o=S_hb[:]: e.memset(o, 0.0), inc="dv")
            k.wait("tensor", "dv", k.cnt["dv"])
            if a2stop <= 3:
                continue

            cmask = int(os.environ.get("DBG_CHUNK", "15"))
            for c in range(NCH):
                csl = slice(c * 64, (c + 1) * 64)
                vsl = slice(c * 64, (c + 1) * 64)
                if cmask & 8:
                    k.op("tensor", lambda e, o=psK[:, 0:128], i_=K2[:, csl],
                         idn=cst["ident_bf"][:]: e.transpose(o, i_, idn), mmb=0)
                pet = None
                if cmask & 1:
                    pet = k.op("tensor", lambda e, o=psA[0:64, 0:64], l=KGC[:, csl],
                               r=QG[:, csl]: e.matmul(o, l, r, start=True, stop=True),
                               inc="pe", mmb=0)
                if pet is not None:
                    k.wait("vector", "pe", pet)
                k.op("vector", lambda e, o=k2tok[:], i_=psK[0:64, 0:128]:
                     e.tensor_copy(o, i_))
                dvt = k.op("vector", lambda e, o=Atile[:], a=psA[0:64, 0:64],
                           m=cst["triu1"][:]: e.tensor_mul(o, a, m), inc="dv")
                k.wait("tensor", "dv", dvt)
                pet = None
                if cmask & 2:
                    k.op("tensor", lambda e, o=psO[0:64, 0:64], l=S_hb[:],
                         r=QG[:, csl]: e.matmul(o, l, r, start=True, stop=False),
                         mmb=0)
                    pet = k.op("tensor", lambda e, o=psO[0:64, 0:64], l=v_cm_h[:, vsl],
                               r=Atile[:]: e.matmul(o, l, r, start=False, stop=True),
                               mmb=0, inc="pe")
                if cmask & 4:
                    pet = k.op("tensor", lambda e, o=psS[:, 0:64], l=k2tok[:],
                               r=v_cm_h[:, vsl]:
                               e.matmul(o, l, r, start=True, stop=True),
                               inc="pe", mmb=0)
                if pet is not None:
                    k.wait("vector", "pe", pet)
                k.op("vector", lambda e, o=O_h[:, csl], i_=psO[0:64, 0:64]:
                     e.tensor_copy(o, i_))
                k.op("vector", lambda e, o=S_h[:], i_=S_h[:], s=EQd[:, c:c + 1],
                     p_=psS[:, 0:64]:
                     e.scalar_tensor_tensor(o, i_, s, p_, ALU.mult, ALU.add))
                dvt = k.op("vector", lambda e, o=S_hb[:], i_=S_h[:]:
                           e.tensor_copy(o, i_), inc="dv")
                k.wait("tensor", "dv", dvt)
            k.op("vector", lambda e: e.engine_nop(), inc="dv")
            k.wait("sync", "dv", k.cnt["dv"])
            r0 = (2 * p + hh) * 64
            ot = k.dma("sync", o_s[r0:r0 + 64, b * N:(b + 1) * N], O_h[:], "st")
            k.wait("vector", "st", ot)

    if not dbg_a3:
        fin = k.fence("vector", "dv")
        k.wait("sync", "dv", fin)
        k.wait("sync", "st", k.cnt["st"])
        return
    k.srewind(m_gla)
    wg1 = k.salloc([128, DCH * 16], BF16)
    wg2 = k.salloc([16, D], BF16)
    wo = k.salloc([128, DCH * D], BF16)
    nrm = k.salloc([128, DCH], F32)
    t1 = k.salloc([16, T], BF16)
    gt = k.salloc([128, 512], BF16)
    og = k.salloc([128, DCH * 512], BF16)
    osl = k.salloc([128, 512], BF16)
    sq = k.salloc([128, 512], BF16)
    ssq = k.salloc([1, 512], F32)
    rr = k.salloc([1, 512], F32)
    xold = k.salloc([128, 2 * 512], F32)
    xnew = k.salloc([128, 2 * 512], F32)

    k.wait("gpsimd", "pe", k.cnt["pe"])
    k.dma("gpsimd", wg1[:].rearrange("p (a c) -> p a c", c=16),
          wg1_d.rearrange("(a p) c -> p a c", p=128), "wld")
    k.dma("gpsimd", wg2[:], wg2_d, "wld")
    k.dma("gpsimd", wo[:].rearrange("p (a c) -> p a c", c=D),
          wo_d.rearrange("(a p) c -> p a c", p=128), "wld")
    for j in range(DCH):
        k.dma("gpsimd", nrm[:, j:j + 1],
              normw_d[j * 128:(j + 1) * 128].unsqueeze(1), "wld")
    wld3 = k.cnt["wld"]
    k.wait("tensor", "wld", wld3)
    k.wait("vector", "wld", wld3)

    for ns in range(T // 512):
        bank = psP if ns % 2 == 0 else psP2
        for a in range(DCH):
            pet = k.op("tensor", lambda e, o=bank[0:16, :],
                       l=wg1[:, a * 16:(a + 1) * 16],
                       r=ln0[:, a * T + ns * 512: a * T + ns * 512 + 512],
                       s=(a == 0), p=(a == DCH - 1):
                       e.matmul(o, l, r, start=s, stop=p), inc="pe")
        k.wait("vector", "pe", pet)
        dvt = k.op("vector", lambda e, o=t1[:, ns * 512:(ns + 1) * 512],
                   i_=bank[0:16, :]: e.tensor_copy(o, i_), inc="dv")
        k.wait("tensor", "dv", dvt)

    og_prev = 0
    sqmm_prev = 0
    for ts in range(T // 512):
        tsl = slice(ts * 512, (ts + 1) * 512)
        for j in range(DCH):
            bank = psP
            pet = k.op("tensor", lambda e, o=bank[:], l=wg2[:, j * 128:(j + 1) * 128],
                       r=t1[:, tsl]: e.matmul(o, l, r, start=True, stop=True),
                       inc="pe")
            k.wait("scalar", "pe", pet)
            k.wait("scalar", "dv", og_prev)
            act = k.op("scalar", lambda e, o=gt[:], i_=bank[:]:
                       e.activation(o, i_, AF.Sigmoid, bias=0.0, scale=1.0),
                       inc="ac")
            k.wait("tensor", "ac", act)
            k.wait("sync", "dv", og_prev)
            lo = k.dma("sync", osl[:], o_s[j * 128:(j + 1) * 128, tsl], "ld")
            k.wait("vector", "ld", lo)
            k.wait("vector", "ac", act)
            og_prev = k.op("vector", lambda e, o=og[:, j * 512:(j + 1) * 512],
                           a=osl[:], b_=gt[:]: e.tensor_mul(o, a, b_), inc="dv")
            k.wait("scalar", "dv", og_prev)
            k.wait("scalar", "pe", sqmm_prev)
            act2 = k.op("scalar", lambda e, o=sq[:], i_=og[:, j * 512:(j + 1) * 512]:
                        e.activation(o, i_, AF.Square, bias=0.0, scale=1.0),
                        inc="ac")
            k.wait("tensor", "ac", act2)
            sqmm_prev = k.op("tensor", lambda e, o=psP2[0:1, :],
                             l=cst["ones_col_bf"][:], r=sq[:],
                             s=(j == 0), p=(j == DCH - 1):
                             e.matmul(o, l, r, start=s, stop=p), inc="pe")
        k.wait("vector", "pe", k.cnt["pe"])
        k.op("vector", lambda e, o=ssq[:], i_=psP2[0:1, :]: e.tensor_copy(o, i_))
        k.op("vector", lambda e, o=ssq[:], i_=ssq[:]:
             e.tensor_scalar(o, i_, 1.0 / D, 1e-6, ALU.mult, ALU.add), inc="dv")
        k.wait("scalar", "dv", k.cnt["dv"])
        k.op("scalar", lambda e, o=rr[:], i_=ssq[:]:
             e.activation(o, i_, AF.Sqrt, bias=0.0, scale=1.0), inc="ac")
        k.wait("vector", "ac", k.cnt["ac"])
        k.op("vector", lambda e, o=rr[:], i_=rr[:]: e.reciprocal(o, i_), inc="dv")
        k.wait("tensor", "dv", k.cnt["dv"])
        pet = k.op("tensor", lambda e, o=psP2[:], l=cst["ones_row_f32"][:], r=rr[:]:
                   e.matmul(o, l, r, start=True, stop=True), inc="pe")
        k.wait("vector", "pe", pet)
        for j in range(DCH):
            k.op("vector", lambda e, o=og[:, j * 512:(j + 1) * 512],
                 i_=og[:, j * 512:(j + 1) * 512], s=nrm[:, j:j + 1], p_=psP2[:]:
                 e.scalar_tensor_tensor(o, i_, s, p_, ALU.mult, ALU.mult))
        dvt = k.op("vector", lambda e: e.engine_nop(), inc="dv")
        k.wait("tensor", "dv", dvt)
        for j2 in range(DCH):
            bank = psP
            for a in range(DCH):
                pet = k.op("tensor", lambda e, o=bank[:],
                           l=wo[:, a * D + j2 * 128: a * D + j2 * 128 + 128],
                           r=og[:, a * 512:(a + 1) * 512],
                           s=(a == 0), p=(a == DCH - 1):
                           e.matmul(o, l, r, start=s, stop=p), inc="pe")
            buf = j2 % 2
            lo = k.dma("sync", xold[:, buf * 512:(buf + 1) * 512],
                       xr0[j2 * 128:(j2 + 1) * 128, tsl], "ld")
            k.wait("vector", "pe", pet)
            k.wait("vector", "ld", lo)
            k.op("vector", lambda e, o=xnew[:, buf * 512:(buf + 1) * 512],
                 a=bank[:], b_=xold[:, buf * 512:(buf + 1) * 512]:
                 e.tensor_add(o, a, b_), inc="dv")
            k.wait("tensor", "dv", k.cnt["dv"])
            k.wait("sync", "dv", k.cnt["dv"])
            k.dma("sync", xr1[j2 * 128:(j2 + 1) * 128, tsl],
                  xnew[:, buf * 512:(buf + 1) * 512], "st")
    fin = k.fence("vector", "dv")
    k.wait("sync", "dv", fin)
    k.wait("sync", "st", k.cnt["st"])



def new_program():
    nc = bass.Bass("TRN2", target_bir_lowering=False,
                   detect_race_conditions=False)
    k = K(nc)
    for s in ("ld", "st", "wld", "pe", "dv", "ac", "gp"):
        k.sem(s)
    k.pb2k = [nc.alloc_psum_tensor(f"pb2k{i}", [128, 1024], F32) for i in range(4)]

    class _V:
        def __init__(self, ap):
            self._ap = ap

        def __getitem__(self, key):
            return self._ap[key]

    k.pbank = [_V(k.pb2k[i // 2][:][:, (i % 2) * 512:(i % 2 + 1) * 512])
               for i in range(8)]
    return nc, k


def build_phase_a_program():
    nc, k = new_program()
    x_in = nc.dram_tensor("x", [T, D], F32, kind="ExternalInput").ap()
    wq_d = nc.dram_tensor("Wq", [D, D], F32, kind="ExternalInput").ap()
    wk_d = nc.dram_tensor("Wk", [D, D], F32, kind="ExternalInput").ap()
    wv_d = nc.dram_tensor("Wv", [D, D], F32, kind="ExternalInput").ap()
    wo_d = nc.dram_tensor("Wo", [D, D], F32, kind="ExternalInput").ap()
    wg1_d = nc.dram_tensor("Wg1", [D, 16], F32, kind="ExternalInput").ap()
    wg2_d = nc.dram_tensor("Wg2", [16, D], F32, kind="ExternalInput").ap()
    normw_d = nc.dram_tensor("norm_w", [D], F32, kind="ExternalInput").ap()
    xr0 = nc.dram_tensor("xr0", [D, T], F32, kind="Internal").ap()
    xr1 = nc.dram_tensor("xr1", [D, T], F32, kind="ExternalOutput").ap()
    import os as _os
    o_s = nc.dram_tensor("o_s", [D, T], BF16,
                         kind="ExternalOutput" if _os.environ.get("DBG_OS")
                         else "Internal").ap()
    cst = stage_consts(nc, k)
    k.freeze_persistent()
    import os
    dbg_bp = int(os.environ.get("DBG_BP", "99"))
    dbg_a3 = os.environ.get("DBG_A3", "1") == "1"
    emit_phase_a(nc, k, cst, x_in, wq_d, wk_d, wv_d, wo_d, wg1_d, wg2_d,
                 normw_d, xr0, xr1, o_s, dbg_bp=dbg_bp, dbg_a3=dbg_a3)
    k.build()
    return nc


_PROGS = {}


def _prog(name, builder):
    if name not in _PROGS:
        _PROGS[name] = builder()
    return _PROGS[name]


def kernel(**inputs):
    consts = host_constants()
    NCORES = 8
    x_full = inputs["L0_attn_x"]
    x_slices = [np.ascontiguousarray(
        x_full[c * B_CORE:(c + 1) * B_CORE].reshape(T, D)) for c in range(NCORES)]

    ncA = _prog("a", build_phase_a_program)
    maps = []
    for c in range(NCORES):
        m = dict(consts)
        m["x"] = x_slices[c]
        for nm, src in (("Wq", "L0_attn_Wq"), ("Wk", "L0_attn_Wk"),
                        ("Wv", "L0_attn_Wv"), ("Wo", "L0_attn_Wo"),
                        ("Wg1", "L0_attn_Wg1"), ("Wg2", "L0_attn_Wg2"),
                        ("norm_w", "L0_attn_norm_w")):
            m[nm] = inputs[src]
        maps.append(m)
    resA = run_bass_kernel_spmd(ncA, maps, core_ids=list(range(NCORES)))
    xr1s = [r["xr1"] for r in resA.results]

    ncB = _prog("b", build_phase_b_program)
    maps = []
    for c in range(NCORES):
        m = dict(consts)
        m["xr1"] = xr1s[c]
        for nm, src in (("router", "L0_ffn_router_DE"), ("w1", "L0_ffn_w1_eDF"),
                        ("w3", "L0_ffn_w3_eDF"), ("w2", "L0_ffn_w2_eFD"),
                        ("ws1", "L0_ffn_ws1_DF"), ("ws3", "L0_ffn_ws3_DF"),
                        ("ws2", "L0_ffn_ws2_FD")):
            m[nm] = inputs[src]
        maps.append(m)
    resB = run_bass_kernel_spmd(ncB, maps, core_ids=list(range(NCORES)))
    xr2s = [r["xr2"] for r in resB.results]

    ncC = _prog("c", build_phase_c_program)
    maps = []
    for c in range(NCORES):
        m = dict(consts)
        m["xr2"] = xr2s[c]
        for nm, src in (("w_in", "L1_attn_w_in"), ("conv_w", "L1_attn_conv_w"),
                        ("conv_b", "L1_attn_conv_b"), ("w_B", "L1_attn_w_B"),
                        ("w_C", "L1_attn_w_C"), ("w_out", "L1_attn_w_out")):
            m[nm] = inputs[src]
        maps.append(m)
    resC = run_bass_kernel_spmd(ncC, maps, core_ids=list(range(NCORES)))
    x3ts = [r["x3t"] for r in resC.results]

    ncD = _prog("d", build_phase_d_program)
    maps = []
    for c in range(NCORES):
        m = dict(consts)
        m["x3t"] = x3ts[c]
        for nm, src in (("w_gate", "L1_ffn_w_gate"), ("w_up", "L1_ffn_w_up"),
                        ("w_down", "L1_ffn_w_down")):
            m[nm] = inputs[src]
        maps.append(m)
    resD = run_bass_kernel_spmd(ncD, maps, core_ids=list(range(NCORES)))
    outs = [np.asarray(r["out"]) for r in resD.results]
    return np.concatenate(outs, axis=0).reshape(16, N, D).astype(np.float32)



F_HALF = 1408
FC_HALF = 11
NS_B = 4


def emit_phase_b(nc, k, cst, xr1, router_d, w1_d, w3_d, w2_d, ws1_d, ws3_d,
                 ws2_d, xr2, lnS, alpha_d):
    k.sreset()
    m_b1 = k.smark()
    xsl = k.salloc([128, 512], F32)
    lnf = [k.salloc([128, 512], F32) for _ in range(DCH)]
    xb = k.salloc([128, 512], BF16)
    xsqb = k.salloc([128, 512], BF16)
    row = k.salloc([1, 512 * 4], F32)
    Rb = k.salloc([128, 512], F32)
    lnbf = k.salloc([128, 512], BF16)
    alpha = k.palloc_dummy if False else None
    alpha = k.salloc([8, T], F32)
    router = k.salloc([128, DCH * 8], F32)
    scst = k.salloc([128, 19], F32)

    psA_ = k.pbank[0][:]
    psB_ = k.pbank[1][:]
    psM = k.pbank[2][:]
    psR = k.pbank[3][:]
    psS_ = k.pbank[4][:]

    k.dma("sync", router[:].rearrange("p (a c) -> p a c", c=8),
          router_d.rearrange("(a p) c -> p a c", p=128), "ld")
    b1_ld = k.cnt["ld"]
    k.wait("tensor", "ld", b1_ld)

    prev_copy_dv = 0
    prev_mm_pe = 0
    for ts in range(T // 512):
        tsl = slice(ts * 512, (ts + 1) * 512)
        for j in range(DCH):
            k.wait("sync", "dv", prev_copy_dv)
            ldt = k.dma("sync", xsl[:], xr1[j * 128:(j + 1) * 128, tsl], "ld")
            k.wait("vector", "ld", ldt)
            k.wait("vector", "pe", prev_mm_pe)
            dvt = k.op("vector", lambda e, o=xb[:], i_=xsl[:]:
                       e.tensor_copy(o, i_), inc="dv")
            k.wait("scalar", "dv", dvt)
            k.wait("scalar", "pe", prev_mm_pe)
            act = k.op("scalar", lambda e, o=xsqb[:], i_=xb[:]:
                       e.activation(o, i_, AF.Square, bias=0.0, scale=1.0),
                       inc="ac")
            k.wait("tensor", "dv", dvt)
            k.wait("tensor", "ac", act)
            k.op("tensor", lambda e, o=psA_[0:1, :], l=cst["ones_col_bf"][:],
                 r=xb[:], s=(j == 0), q=(j == DCH - 1):
                 e.matmul(o, l, r, start=s, stop=q), mmb=0)
            prev_mm_pe = k.op("tensor", lambda e, o=psB_[0:1, :],
                              l=cst["ones_col_bf"][:],
                              r=xsqb[:], s=(j == 0), q=(j == DCH - 1):
                              e.matmul(o, l, r, start=s, stop=q), inc="pe", mmb=0)
            prev_copy_dv = k.op("vector", lambda e, o=lnf[j][:], i_=xsl[:]:
                                e.tensor_copy(o, i_), inc="dv")
        k.wait("vector", "pe", k.cnt["pe"])
        r_m = row[:, 0:512]
        r_msq = row[:, 512:1024]
        r_var = row[:, 1024:1536]
        r_rstd = row[:, 1536:2048]
        k.vop_s(lambda e, o=r_m, i_=psA_[0:1, :]:
                e.tensor_scalar(o, i_, 1.0 / D, None, ALU.mult))
        k.vop_s(lambda e, o=r_msq, i_=psB_[0:1, :]:
                e.tensor_scalar(o, i_, 1.0 / D, None, ALU.mult))
        k.vop_s(lambda e, o=r_var, a=r_m, b_=r_m: e.tensor_mul(o, a, b_))
        k.vop_s(lambda e, o=r_rstd, a=r_msq, b_=r_var: e.tensor_sub(o, a, b_))
        k.vop_s(lambda e, o=r_var, i_=r_rstd:
                e.tensor_scalar(o, i_, 1e-5, None, ALU.add))
        k.wait("scalar", "dv", k.cnt["dv"])
        sct = k.op("scalar", lambda e, o=r_rstd, i_=r_var:
                   e.activation(o, i_, AF.Sqrt, bias=0.0, scale=1.0), inc="ac")
        k.wait("vector", "ac", sct)
        k.vop_s(lambda e, o=r_var, i_=r_rstd: e.reciprocal(o, i_))
        k.wait("tensor", "dv", k.cnt["dv"])
        k.op("tensor", lambda e, o=psM[:], l=cst["ones_row_f32"][:], r=r_m:
             e.matmul(o, l, r, start=True, stop=True), mmb=0)
        pet = k.op("tensor", lambda e, o=psR[:], l=cst["ones_row_f32"][:],
                   r=r_var: e.matmul(o, l, r, start=True, stop=True),
                   inc="pe", mmb=0)
        k.wait("vector", "pe", pet)
        k.op("vector", lambda e, o=Rb[:], i_=psR[:]: e.tensor_copy(o, i_))
        for j in range(DCH):
            k.op("vector", lambda e, o=lnf[j][:], a=lnf[j][:], b_=psM[:]:
                 e.tensor_sub(o, a, b_))
            k.op("vector", lambda e, o=lnf[j][:], a=lnf[j][:], b_=Rb[:]:
                 e.tensor_mul(o, a, b_))
            dvt = k.op("vector", lambda e, o=lnbf[:], i_=lnf[j][:]:
                       e.tensor_copy(o, i_), inc="dv")
            k.wait("sync", "dv", dvt)
            stt = k.dma("sync", lnS[j * 128:(j + 1) * 128, tsl], lnbf[:], "st")
            k.wait("vector", "st", stt)
        k.op("vector", lambda e: e.engine_nop(), inc="dv")
        k.wait("tensor", "dv", k.cnt["dv"])
        for tt in range(4):
            ttl = slice(tt * 128, (tt + 1) * 128)
            for j in range(DCH):
                pet = k.op("tensor", lambda e, o=psS_[:, 0:8], l=lnf[j][:, ttl],
                           r=router[:, j * 8:(j + 1) * 8], s=(j == 0),
                           q=(j == DCH - 1): e.matmul(o, l, r, start=s, stop=q),
                           inc="pe", mmb=0)
            prev_mm_pe = pet
            k.wait("vector", "pe", pet)
            k.vop_s(lambda e, o=scst[:, 0:1], i_=psS_[:, 0:8]:
                    e.reduce_max(o, i_, AX.X))
            k.wait("scalar", "dv", k.cnt["dv"])
            sct = k.op("scalar", lambda e, o=scst[:, 1:2], i_=scst[:, 0:1]:
                       e.activation(o, i_, AF.Sigmoid, bias=0.0, scale=1.0),
                       inc="ac")
            k.vop_s(lambda e, o=scst[:, 2:3], i_=scst[:, 0:1]:
                    e.tensor_copy(o, i_))
            k.op("vector", lambda e, o=scst[:, 3:11], i_=psS_[:, 0:8],
                 m=scst[:, 0:1]:
                 e.tensor_scalar(o, i_, m, None, ALU.is_equal))
            k.wait("vector", "ac", sct)
            k.vop_s(lambda e, o=scst[:, 11:19], i_=scst[:, 3:11], g=scst[:, 1:2]:
                    e.tensor_scalar(o, i_, g, None, ALU.mult))
            k.wait("tensor", "dv", k.cnt["dv"])
            pet = k.op("tensor", lambda e, o=psS_[0:8, 8:136], i_=scst[:, 11:19],
                       idn=cst["ident_f32"][:]: e.transpose(o, i_, idn),
                       inc="pe", mmb=0)
            k.wait("vector", "pe", pet)
            dvt = k.op("vector", lambda e, o=alpha[:, ts * 512 + tt * 128:
                                               ts * 512 + (tt + 1) * 128],
                       i_=psS_[0:8, 8:136]: e.tensor_copy(o, i_), inc="dv")
            k.wait("tensor", "dv", dvt)
            prev_mm_pe = pet
    b1_done = k.fence("vector", "dv")
    k.wait("gpsimd", "dv", b1_done)
    k.wait("sync", "dv", b1_done)
    k.dma("gpsimd", alpha_d, alpha[:], "wld")
    k.wait("sync", "wld", k.cnt["wld"])
    k.wait("sync", "st", k.cnt["st"])

    k.srewind(m_b1)
    alpha2 = alpha
    out_sb = [k.salloc([128, T], BF16) for _ in range(DCH)]
    m_b2 = k.smark()
    w1h = k.salloc([128, DCH * F_HALF], BF16)
    w3h = k.salloc([128, DCH * F_HALF], BF16)
    w2h = k.salloc([128, FC_HALF * 1024], BF16)
    midh = k.salloc([128, FC_HALF * 1024], BF16)
    xe = k.salloc([128, DCH * 1024], BF16)
    arow = k.salloc([1, T], BF16)
    Ae = k.salloc([128, T], BF16)
    lnt2 = k.salloc([128, 2 * 1024], BF16)
    sil = k.salloc([128, 1024], BF16)

    psU = k.pb2k[1][:]
    psW = k.pb2k[2][:]
    psD = k.pb2k[3][:]

    xe_dv = [0, 0]
    xe_pe = 0
    for e_i in range(9):
        shared = (e_i == 8)
        if not shared:
            al = k.dma("sync", arow[:], alpha_d[e_i:e_i + 1, :], "ld")
            k.wait("tensor", "ld", al)
            for ns2 in range(DCH):
                n2 = slice(ns2 * 512, (ns2 + 1) * 512)
                pet = k.op("tensor", lambda e, o=psM[:], l=cst["ones_row_bf"][:],
                           r=arow[0:1, n2]:
                           e.matmul(o, l, r, start=True, stop=True),
                           inc="pe", mmb=0)
                k.wait("vector", "pe", pet)
                dvt = k.op("vector", lambda e, o=Ae[:, n2], i_=psM[:]:
                           e.tensor_copy(o, i_), inc="dv")
                k.wait("tensor", "dv", dvt)
        for fh in range(2):
            f0 = fh * F_HALF
            k.wait("gpsimd", "pe", k.cnt["pe"])
            for kc in range(DCH):
                if shared:
                    srcw1 = ws1_d[kc * 128:(kc + 1) * 128, f0:f0 + F_HALF]
                    srcw3 = ws3_d[kc * 128:(kc + 1) * 128, f0:f0 + F_HALF]
                else:
                    srcw1 = w1_d[e_i, kc * 128:(kc + 1) * 128, f0:f0 + F_HALF]
                    srcw3 = w3_d[e_i, kc * 128:(kc + 1) * 128, f0:f0 + F_HALF]
                k.dma("gpsimd", w1h[:, kc * F_HALF:(kc + 1) * F_HALF], srcw1, "wld")
                k.dma("gpsimd", w3h[:, kc * F_HALF:(kc + 1) * F_HALF], srcw3, "wld")
            for kc in range(FC_HALF):
                if shared:
                    srcw2 = ws2_d[f0 + kc * 128:f0 + (kc + 1) * 128, :]
                else:
                    srcw2 = w2_d[e_i, f0 + kc * 128:f0 + (kc + 1) * 128, :]
                k.dma("gpsimd", w2h[:, kc * 1024:(kc + 1) * 1024], srcw2, "wld")
            wld_t = k.cnt["wld"]
            k.wait("tensor", "wld", wld_t)
            for ns in range(NS_B):
                nsl = slice(ns * 1024, (ns + 1) * 1024)
                for j in range(DCH):
                    buf = j % 2
                    lslice = lnt2[:, buf * 1024:(buf + 1) * 1024]
                    k.wait("sync", "dv", xe_dv[buf])
                    ldt = k.dma("sync", lslice, lnS[j * 128:(j + 1) * 128, nsl],
                                "ld")
                    k.wait("vector", "ld", ldt)
                    k.wait("vector", "pe", xe_pe)
                    if shared:
                        dvt = k.op("vector", lambda e,
                                   o=xe[:, j * 1024:(j + 1) * 1024], i_=lslice:
                                   e.tensor_copy(o, i_), inc="dv")
                    else:
                        dvt = k.op("vector", lambda e,
                                   o=xe[:, j * 1024:(j + 1) * 1024], a=lslice,
                                   b_=Ae[:, nsl]: e.tensor_mul(o, a, b_),
                                   inc="dv")
                    xe_dv[buf] = dvt
                k.wait("tensor", "dv", k.cnt["dv"])
                for f in range(FC_HALF):
                    for half in range(2):
                        hsl = slice(half * 512, (half + 1) * 512)
                        for kc in range(DCH):
                            k.op("tensor", lambda e, o=psU[:, hsl],
                                 l=w1h[:, kc * F_HALF + f * 128:
                                       kc * F_HALF + (f + 1) * 128],
                                 r=xe[:, kc * 1024 + half * 512:
                                       kc * 1024 + (half + 1) * 512],
                                 s=(kc == 0), q=(kc == DCH - 1):
                                 e.matmul(o, l, r, start=s, stop=q), mmb=0)
                        for kc in range(DCH):
                            pet = k.op("tensor", lambda e, o=psW[:, hsl],
                                       l=w3h[:, kc * F_HALF + f * 128:
                                             kc * F_HALF + (f + 1) * 128],
                                       r=xe[:, kc * 1024 + half * 512:
                                             kc * 1024 + (half + 1) * 512],
                                       s=(kc == 0), q=(kc == DCH - 1):
                                       e.matmul(o, l, r, start=s, stop=q),
                                       inc="pe", mmb=0)
                    xe_pe = pet
                    k.wait("scalar", "pe", pet)
                    act = k.op("scalar", lambda e, o=sil[:], i_=psU[:]:
                               e.activation(o, i_, AF.Silu, bias=0.0, scale=1.0),
                               inc="ac")
                    k.wait("vector", "ac", act)
                    dvt = k.op("vector", lambda e,
                               o=midh[:, f * 1024:(f + 1) * 1024], a=sil[:],
                               b_=psW[:]: e.tensor_mul(o, a, b_), inc="dv")
                    k.wait("tensor", "dv", dvt)
                for j in range(DCH):
                    for half in range(2):
                        hsl = slice(half * 512, (half + 1) * 512)
                        for kc in range(FC_HALF):
                            pet = k.op("tensor", lambda e, o=psD[:, hsl],
                                       l=w2h[:, kc * 1024 + j * 128:
                                             kc * 1024 + (j + 1) * 128],
                                       r=midh[:, kc * 1024 + half * 512:
                                              kc * 1024 + (half + 1) * 512],
                                       s=(kc == 0), q=(kc == FC_HALF - 1):
                                       e.matmul(o, l, r, start=s, stop=q),
                                       inc="pe", mmb=0)
                    k.wait("vector", "pe", pet)
                    if e_i == 0 and fh == 0:
                        dvt = k.op("vector", lambda e, o=out_sb[j][:, nsl],
                                   i_=psD[:]: e.tensor_copy(o, i_), inc="dv")
                    else:
                        dvt = k.op("vector", lambda e, o=out_sb[j][:, nsl],
                                   a=out_sb[j][:, nsl], b_=psD[:]:
                                   e.tensor_add(o, a, b_), inc="dv")
                    k.wait("tensor", "dv", dvt)

    k.srewind(m_b2)
    xo = k.salloc([128, 2 * 512], F32)
    xn = k.salloc([128, 2 * 512], F32)
    b2_done = k.fence("vector", "dv")
    k.wait("sync", "dv", b2_done)
    for ts in range(T // 512):
        tsl = slice(ts * 512, (ts + 1) * 512)
        for j in range(DCH):
            buf = j % 2
            ldt = k.dma("sync", xo[:, buf * 512:(buf + 1) * 512],
                        xr1[j * 128:(j + 1) * 128, tsl], "ld")
            k.wait("vector", "ld", ldt)
            dvt = k.op("vector", lambda e, o=xn[:, buf * 512:(buf + 1) * 512],
                       a=xo[:, buf * 512:(buf + 1) * 512],
                       b_=out_sb[j][:, tsl]: e.tensor_add(o, a, b_), inc="dv")
            k.wait("sync", "dv", dvt)
            k.dma("sync", xr2[j * 128:(j + 1) * 128, tsl],
                  xn[:, buf * 512:(buf + 1) * 512], "st")
    fin = k.fence("vector", "dv")
    k.wait("sync", "dv", fin)
    k.wait("sync", "st", k.cnt["st"])


def build_phase_b_program():
    nc, k = new_program()
    xr1 = nc.dram_tensor("xr1", [D, T], F32, kind="ExternalInput").ap()
    router_d = nc.dram_tensor("router", [D, 8], F32, kind="ExternalInput").ap()
    w1_d = nc.dram_tensor("w1", [8, D, 2816], F32, kind="ExternalInput").ap()
    w3_d = nc.dram_tensor("w3", [8, D, 2816], F32, kind="ExternalInput").ap()
    w2_d = nc.dram_tensor("w2", [8, 2816, D], F32, kind="ExternalInput").ap()
    ws1_d = nc.dram_tensor("ws1", [D, 2816], F32, kind="ExternalInput").ap()
    ws3_d = nc.dram_tensor("ws3", [D, 2816], F32, kind="ExternalInput").ap()
    ws2_d = nc.dram_tensor("ws2", [2816, D], F32, kind="ExternalInput").ap()
    xr2 = nc.dram_tensor("xr2", [D, T], F32, kind="ExternalOutput").ap()
    lnS = nc.dram_tensor("lnS", [D, T], BF16, kind="Internal").ap()
    alpha_d = nc.dram_tensor("alpha_d", [8, T], BF16, kind="Internal").ap()
    cst = stage_consts(nc, k)
    k.freeze_persistent()
    emit_phase_b(nc, k, cst, xr1, router_d, w1_d, w3_d, w2_d, ws1_d, ws3_d,
                 ws2_d, xr2, lnS, alpha_d)
    k.build()
    return nc



DI = 2048
DIC = DI // 128


def emit_ln_resident(nc, k, cst, src, dst, psA_, psB_, psM, psR, scratch):
    xsl, xb, xsqb, row, Rb = scratch
    prev_copy_dv = 0
    prev_mm_pe = 0
    for ts in range(T // 512):
        tsl = slice(ts * 512, (ts + 1) * 512)
        lnf_dv = 0
        for j in range(DCH):
            k.wait("sync", "dv", prev_copy_dv)
            ldt = k.dma("sync", xsl[:, j * 512:(j + 1) * 512],
                        src[j * 128:(j + 1) * 128, tsl], "ld")
            k.wait("vector", "ld", ldt)
            k.wait("vector", "pe", prev_mm_pe)
            dvt = k.op("vector", lambda e, o=xb[:], i_=xsl[:, j * 512:(j + 1) * 512]:
                       e.tensor_copy(o, i_), inc="dv")
            k.wait("scalar", "dv", dvt)
            k.wait("scalar", "pe", prev_mm_pe)
            act = k.op("scalar", lambda e, o=xsqb[:], i_=xb[:]:
                       e.activation(o, i_, AF.Square, bias=0.0, scale=1.0),
                       inc="ac")
            k.wait("tensor", "dv", dvt)
            k.wait("tensor", "ac", act)
            k.op("tensor", lambda e, o=psA_[0:1, :], l=cst["ones_col_bf"][:],
                 r=xb[:], s=(j == 0), q=(j == DCH - 1):
                 e.matmul(o, l, r, start=s, stop=q), mmb=0)
            prev_mm_pe = k.op("tensor", lambda e, o=psB_[0:1, :],
                              l=cst["ones_col_bf"][:],
                              r=xsqb[:], s=(j == 0), q=(j == DCH - 1):
                              e.matmul(o, l, r, start=s, stop=q), inc="pe",
                              mmb=0)
            prev_copy_dv = k.cnt["dv"]
        k.wait("vector", "pe", k.cnt["pe"])
        r_m = row[:, 0:512]
        r_msq = row[:, 512:1024]
        r_var = row[:, 1024:1536]
        r_rstd = row[:, 1536:2048]
        k.vop_s(lambda e, o=r_m, i_=psA_[0:1, :]:
                e.tensor_scalar(o, i_, 1.0 / D, None, ALU.mult))
        k.vop_s(lambda e, o=r_msq, i_=psB_[0:1, :]:
                e.tensor_scalar(o, i_, 1.0 / D, None, ALU.mult))
        k.vop_s(lambda e, o=r_var, a=r_m, b_=r_m: e.tensor_mul(o, a, b_))
        k.vop_s(lambda e, o=r_rstd, a=r_msq, b_=r_var: e.tensor_sub(o, a, b_))
        k.vop_s(lambda e, o=r_var, i_=r_rstd:
                e.tensor_scalar(o, i_, 1e-5, None, ALU.add))
        k.wait("scalar", "dv", k.cnt["dv"])
        sct = k.op("scalar", lambda e, o=r_rstd, i_=r_var:
                   e.activation(o, i_, AF.Sqrt, bias=0.0, scale=1.0), inc="ac")
        k.wait("vector", "ac", sct)
        k.vop_s(lambda e, o=r_var, i_=r_rstd: e.reciprocal(o, i_))
        k.wait("tensor", "dv", k.cnt["dv"])
        k.op("tensor", lambda e, o=psM[:], l=cst["ones_row_f32"][:], r=r_m:
             e.matmul(o, l, r, start=True, stop=True), mmb=0)
        pet = k.op("tensor", lambda e, o=psR[:], l=cst["ones_row_f32"][:],
                   r=r_var: e.matmul(o, l, r, start=True, stop=True),
                   inc="pe", mmb=0)
        prev_mm_pe = pet
        k.wait("vector", "pe", pet)
        k.op("vector", lambda e, o=Rb[:], i_=psR[:]: e.tensor_copy(o, i_))
        for j in range(DCH):
            k.op("vector", lambda e, o=xsl[:, j * 512:(j + 1) * 512],
                 a=xsl[:, j * 512:(j + 1) * 512], b_=psM[:]:
                 e.tensor_sub(o, a, b_))
            dvt = k.op("vector", lambda e, o=dst[:, j * T + ts * 512:
                                              j * T + (ts + 1) * 512],
                       a=xsl[:, j * 512:(j + 1) * 512], b_=Rb[:]:
                       e.tensor_mul(o, a, b_), inc="dv")
        prev_copy_dv = k.cnt["dv"]
    done = k.fence("vector", "dv")
    for eng in ("tensor", "sync", "gpsimd", "scalar"):
        k.wait(eng, "dv", done)


def emit_phase_c(nc, k, cst, xr2, win_d, convw_d, convb_d, wB_d, wC_d, wout_d,
                 x3t):
    k.sreset()
    ln2 = k.salloc([128, DCH * T], BF16)
    xsl = k.salloc([128, DCH * 512], F32)
    xb = k.salloc([128, 512], BF16)
    xsqb = k.salloc([128, 512], BF16)
    row = k.salloc([1, 2048], F32)
    Rb = k.salloc([128, 512], F32)

    k09c = k.salloc([16, N], F32)
    k.dma("sync", k09c[:], cst["k09c_dram"], "ld")
    cst = dict(cst)
    cst["k09c"] = k09c

    psA_ = k.pbank[0][:]
    psB_ = k.pbank[1][:]
    psM = k.pbank[2][:]
    psR = k.pbank[3][:]
    emit_ln_resident(nc, k, cst, xr2, ln2, psA_, psB_, psM, psR,
                     (xsl, xb, xsqb, row, Rb))

    wsl = k.salloc([128, DCH * 128], BF16)
    convw = k.salloc([128, DIC * 4], F32)
    convb = k.salloc([128, DIC], F32)
    wB = k.salloc([128, DIC * 16], BF16)
    wC = k.salloc([16, DI], BF16)
    woutb = k.salloc([128, 2 * DIC * 128], BF16)
    xpb = k.salloc([128, 4 + N], F32)
    xcc = k.salloc([128, N], BF16)
    Bsb = k.salloc([16, N], F32)
    Hb = k.salloc([16, N], BF16)
    zsil = k.salloc([128, 512], BF16)
    ygns = k.salloc([128, DIC * 512], BF16)
    xns = [k.salloc([128, 512], F32) for _ in range(DCH)]
    xold = k.salloc([128, 2 * 512], F32)
    xtok = k.salloc([128, D], F32)

    for kc in range(DIC):
        k.dma("gpsimd", convw[:, kc * 4:(kc + 1) * 4],
              convw_d[kc * 128:(kc + 1) * 128, 0, :], "wld")
        k.dma("gpsimd", convb[:, kc:kc + 1],
              convb_d[kc * 128:(kc + 1) * 128].unsqueeze(1), "wld")
    k.dma("gpsimd", wB[:].rearrange("p (a c) -> p a c", c=16),
          wB_d.rearrange("(a p) c -> p a c", p=128), "wld")
    k.dma("gpsimd", wC[:], wC_d, "wld")
    wld0 = k.cnt["wld"]
    k.wait("tensor", "wld", wld0)
    k.wait("vector", "wld", wld0)
    k.wait("scalar", "wld", wld0)

    psX = k.pb2k[0][:]
    psY = k.pb2k[1][:]
    psBm = k.pb2k[2][:]
    psBm2 = k.pb2k[3][:]
    psD2 = k.pb2k[1][:]
    psT_ = k.pb2k[2][:]

    import os as _os
    if _os.environ.get("DBG_DUMPC"):
        dmpc = nc.dram_tensor("dumpc", [5, 128, N + 4], F32,
                              kind="ExternalOutput").ap()
    wout_pe = [0, 0]
    xold_dv = [0, 0]
    for b in range(B_CORE):
        bsl = slice(b * N, (b + 1) * N)
        for kc in range(DIC):
            k.wait("gpsimd", "pe", k.cnt["pe"])
            wldt = k.dma("gpsimd", wsl[:].rearrange("p (a c) -> p a c", c=128),
                         win_d[:, kc * 128:(kc + 1) * 128].rearrange(
                             "(a p) c -> p a c", p=128), "wld")
            k.wait("tensor", "wld", wldt)
            k.vop_s(lambda e, o=xpb[:, 0:4]: e.memset(o, 0.0))
            for ns in range(4):
                bank = psX[:, (ns % 2) * 512:(ns % 2) * 512 + 512]
                for a in range(DCH):
                    pet = k.op("tensor", lambda e, o=bank,
                               l=wsl[:, a * 128:(a + 1) * 128],
                               r=ln2[:, a * T + b * N + ns * 512:
                                     a * T + b * N + ns * 512 + 512],
                               s=(a == 0), q=(a == DCH - 1):
                               e.matmul(o, l, r, start=s, stop=q), inc="pe",
                               mmb=0)
                k.wait("vector", "pe", pet)
                dvt = k.op("vector", lambda e, o=xpb[:, 4 + ns * 512:
                                                  4 + (ns + 1) * 512],
                           i_=bank: e.tensor_copy(o, i_), inc="dv")
                k.wait("tensor", "dv", dvt)
            V = lambda d: xpb[:, 4 - d: 4 - d + N]
            acc = [xsl[:, 0:N], xsl[:, N:2 * N]]
            k.op("vector", lambda e, o=acc[0], i_=V(3),
                 w=convw[:, kc * 4:kc * 4 + 1]:
                 e.tensor_scalar(o, i_, w, None, ALU.mult))
            for jj in range(1, 4):
                k.op("vector", lambda e, o=acc[jj % 2], i_=V(3 - jj),
                     w=convw[:, kc * 4 + jj:kc * 4 + jj + 1],
                     a=acc[(jj + 1) % 2]:
                     e.scalar_tensor_tensor(o, i_, w, a, ALU.mult, ALU.add),
                     inc="dv")
            k.wait("scalar", "dv", k.cnt["dv"])
            k.wait("scalar", "pe", k.cnt["pe"])
            act = k.op("scalar", lambda e, o=xcc[:], i_=acc[1],
                       bb=convb[:, kc:kc + 1]:
                       e.activation(o, i_, AF.Silu, bias=bb, scale=1.0),
                       inc="ac")
            k.wait("tensor", "ac", act)
            import os as _os
            if _os.environ.get("DBG_DUMPC") and b == 0 and kc == 0:
                k.wait("sync", "ac", act)
                k.wait("gpsimd", "ac", act)
                k.dma("gpsimd", dmpc[2][:, 0:N], xcc[:], "wld")
                k.dma("sync", dmpc[4][:, 0:N + 4], xpb[:], "st")
            for ns in range(4):
                bank = (psBm if ns < 2 else psBm2)[0:16, (ns % 2) * 512:
                                                   (ns % 2) * 512 + 512]
                pet = k.op("tensor", lambda e, o=bank,
                           l=wB[:, kc * 16:(kc + 1) * 16],
                           r=xcc[:, ns * 512:(ns + 1) * 512],
                           s=(kc == 0), q=(kc == DIC - 1):
                           e.matmul(o, l, r, start=s, stop=q), inc="pe", mmb=0)
        k.wait("vector", "pe", k.cnt["pe"])
        for ns in range(4):
            bank = (psBm if ns < 2 else psBm2)[0:16, (ns % 2) * 512:
                                               (ns % 2) * 512 + 512]
            k.op("vector", lambda e, o=Bsb[:, ns * 512:(ns + 1) * 512], i_=bank:
                 e.tensor_copy(o, i_))
        k.vop_s(lambda e, o=Bsb[:], i_=Bsb[:]:
                e.tensor_scalar(o, i_, 0.1, None, ALU.mult))
        k.vop_s(lambda e, o=Hb[:], d0=cst["k09c"][:], d1=Bsb[:]:
                e.tensor_tensor_scan(o, d0, d1, 0.0, ALU.mult, ALU.add))
        import os as _os
        if _os.environ.get("DBG_DUMPC") and b == 0:
            k.wait("sync", "dv", k.cnt["dv"])
            k.wait("gpsimd", "dv", k.cnt["dv"])
            k.dma("sync", dmpc[0][0:16, 0:N], Bsb[:], "st")
            k.dma("gpsimd", dmpc[1][0:16, 0:N], Hb[:], "wld")
        dvt = k.op("vector", lambda e: e.engine_nop(), inc="dv")
        k.wait("tensor", "dv", dvt)
        for ns in range(4):
            nsl = slice(b * N + ns * 512, b * N + (ns + 1) * 512)
            for kc in range(DIC):
                wldt = k.dma("gpsimd", wsl[:].rearrange("p (a c) -> p a c", c=128),
                             win_d[:, DI + kc * 128: DI + (kc + 1) * 128].rearrange(
                                 "(a p) c -> p a c", p=128), "wld")
                k.wait("tensor", "wld", wldt)
                bankZ = psX[:, (kc % 2) * 512:(kc % 2) * 512 + 512]
                bankY = psY[:, (kc % 2) * 512:(kc % 2) * 512 + 512]
                for a in range(DCH):
                    pet = k.op("tensor", lambda e, o=bankZ,
                               l=wsl[:, a * 128:(a + 1) * 128],
                               r=ln2[:, a * T + b * N + ns * 512:
                                     a * T + b * N + ns * 512 + 512],
                               s=(a == 0), q=(a == DCH - 1):
                               e.matmul(o, l, r, start=s, stop=q), inc="pe",
                               mmb=0)
                pety = k.op("tensor", lambda e, o=bankY,
                            l=wC[:, kc * 128:(kc + 1) * 128],
                            r=Hb[:, ns * 512:(ns + 1) * 512]:
                            e.matmul(o, l, r, start=True, stop=True), inc="pe",
                            mmb=0)
                k.wait("scalar", "pe", pet)
                act = k.op("scalar", lambda e, o=zsil[:], i_=bankZ:
                           e.activation(o, i_, AF.Silu, bias=0.0, scale=1.0),
                           inc="ac")
                k.wait("vector", "ac", act)
                k.wait("vector", "pe", pety)
                dvt = k.op("vector", lambda e,
                           o=ygns[:, kc * 512:(kc + 1) * 512],
                           a=zsil[:], b_=bankY:
                           e.tensor_mul(o, a, b_), inc="dv")
                k.wait("tensor", "dv", dvt)
                k.wait("scalar", "dv", dvt)
                k.wait("gpsimd", "pe", pet)
                import os as _os
                if _os.environ.get("DBG_DUMPC") and b == 0 and ns == 0 and kc == 0:
                    k.wait("gpsimd", "dv", dvt)
                    k.dma("gpsimd", dmpc[3][:, 0:512], ygns[:, 0:512], "wld")
            for j in range(DCH):
                wbuf = woutb[:, (j % 2) * DIC * 128:(j % 2 + 1) * DIC * 128]
                k.wait("gpsimd", "pe", wout_pe[j % 2])
                wldt = k.dma("gpsimd",
                             wbuf.rearrange("p (a c) -> p a c", c=128),
                             wout_d[:, j * 128:(j + 1) * 128].rearrange(
                                 "(a p) c -> p a c", p=128), "wld")
                k.wait("tensor", "wld", wldt)
                bank = psD2[:, (j % 2) * 512:(j % 2) * 512 + 512]
                for kc in range(DIC):
                    pet = k.op("tensor", lambda e, o=bank,
                               l=wbuf[:, kc * 128:(kc + 1) * 128],
                               r=ygns[:, kc * 512:(kc + 1) * 512],
                               s=(kc == 0), q=(kc == DIC - 1):
                               e.matmul(o, l, r, start=s, stop=q), inc="pe",
                               mmb=0)
                wout_pe[j % 2] = pet
                buf = j % 2
                k.wait("sync", "dv", xold_dv[buf])
                ldt = k.dma("sync", xold[:, buf * 512:(buf + 1) * 512],
                            xr2[j * 128:(j + 1) * 128, nsl], "ld")
                k.wait("vector", "pe", pet)
                k.wait("vector", "ld", ldt)
                dvt = k.op("vector", lambda e, o=xns[j][:], a=bank,
                           b_=xold[:, buf * 512:(buf + 1) * 512]:
                           e.tensor_add(o, a, b_), inc="dv")
                xold_dv[buf] = dvt
                k.wait("tensor", "dv", dvt)
            for tt in range(4):
                for j in range(DCH):
                    bankT = psT_[:, (j % 2) * 512:(j % 2) * 512 + 128]
                    pet = k.op("tensor", lambda e, o=bankT,
                               i_=xns[j][:, tt * 128:(tt + 1) * 128],
                               idn=cst["ident_f32"][:]:
                               e.transpose(o, i_, idn), inc="pe", mmb=0)
                    k.wait("vector", "pe", pet)
                    dvt = k.op("vector", lambda e, o=xtok[:, j * 128:(j + 1) * 128],
                               i_=bankT: e.tensor_copy(o, i_), inc="dv")
                    k.wait("tensor", "dv", dvt)
                k.wait("sync", "dv", k.cnt["dv"])
                t0 = b * N + ns * 512 + tt * 128
                stt = k.dma("sync", x3t[t0:t0 + 128, :], xtok[:], "st")
                k.wait("vector", "st", stt)
    fin = k.fence("vector", "dv")
    k.wait("sync", "dv", fin)
    k.wait("sync", "st", k.cnt["st"])


def build_phase_c_program():
    nc, k = new_program()
    xr2 = nc.dram_tensor("xr2", [D, T], F32, kind="ExternalInput").ap()
    win_d = nc.dram_tensor("w_in", [D, 2 * DI], F32, kind="ExternalInput").ap()
    convw_d = nc.dram_tensor("conv_w", [DI, 1, 4], F32, kind="ExternalInput").ap()
    convb_d = nc.dram_tensor("conv_b", [DI], F32, kind="ExternalInput").ap()
    wB_d = nc.dram_tensor("w_B", [DI, 16], F32, kind="ExternalInput").ap()
    wC_d = nc.dram_tensor("w_C", [16, DI], F32, kind="ExternalInput").ap()
    wout_d = nc.dram_tensor("w_out", [DI, D], F32, kind="ExternalInput").ap()
    x3t = nc.dram_tensor("x3t", [T, D], F32, kind="ExternalOutput").ap()
    cst = stage_consts(nc, k)
    k.freeze_persistent()
    emit_phase_c(nc, k, cst, xr2, win_d, convw_d, convb_d, wB_d, wC_d, wout_d,
                 x3t)
    k.build()
    return nc



FCH = 22


def emit_quant(k, x_ap, out_ap, W, scr, scalar_dep=None):
    nb = W // 16
    mr_m = scr[0][:, 0:nb]
    mr_r = scr[0][:, nb:2 * nb]
    ab = scr[1][:, 0:W]
    qa = scr[2][:, 0:W]
    qb = scr[3][:, 0:W]
    qc = scr[4][:, 0:W]
    x3 = x_ap.rearrange("p (b s) -> p b s", s=16)
    k.vop_s(lambda e, o=mr_m, i_=x3:
            e.reduce_max(o, i_, AX.X, apply_absolute_value=True))
    k.vop_s(lambda e, o=mr_m, i_=mr_m:
            e.tensor_scalar(o, i_, 1.0 / 6.0, None, ALU.mult))
    k.vop_s(lambda e, o=mr_m, i_=mr_m:
            e.tensor_scalar(o, i_, 1e-12, None, ALU.max))
    k.vop_s(lambda e, o=mr_r, i_=mr_m: e.reciprocal(o, i_))
    act = k.op("scalar", lambda e, o=ab, i_=x_ap:
               e.activation(o, i_, AF.Abs, bias=0.0, scale=1.0), inc="ac")
    k.wait("vector", "ac", act)
    rb = mr_r.rearrange("p (b one) -> p b one", one=1).to_broadcast((128, nb, 16))
    k.op("vector", lambda e, o=ab.rearrange("p (b s) -> p b s", s=16),
         a=ab.rearrange("p (b s) -> p b s", s=16), b_=rb:
         e.tensor_tensor(o, a, b_, ALU.mult))
    k.op("vector", lambda e, o=qa, i_=ab:
         e.tensor_scalar(o, i_, 0.25, None, ALU.is_gt))
    k.op("vector", lambda e, o=qb, i_=ab, a=qa:
         e.scalar_tensor_tensor(o, i_, 0.75, a, ALU.is_gt, ALU.add))
    k.op("vector", lambda e, o=qa, i_=ab, a=qb:
         e.scalar_tensor_tensor(o, i_, 1.25, a, ALU.is_gt, ALU.add))
    k.op("vector", lambda e, o=qb, i_=ab, a=qa:
         e.scalar_tensor_tensor(o, i_, 1.75, a, ALU.is_gt, ALU.add))
    k.op("vector", lambda e, o=qa, i_=ab:
         e.tensor_scalar(o, i_, 2.5, None, ALU.is_gt))
    k.op("vector", lambda e, o=qc, i_=ab, a=qa:
         e.scalar_tensor_tensor(o, i_, 3.5, a, ALU.is_gt, ALU.add))
    k.op("vector", lambda e, o=qa, i_=ab:
         e.tensor_scalar(o, i_, 5.0, None, ALU.is_gt))
    k.op("vector", lambda e, o=qb, i_=qb, a=qc:
         e.scalar_tensor_tensor(o, i_, 0.5, a, ALU.mult, ALU.add))
    k.op("vector", lambda e, o=qa, i_=qa, a=qb:
         e.scalar_tensor_tensor(o, i_, 2.0, a, ALU.mult, ALU.add), inc="dv")
    k.wait("scalar", "dv", k.cnt["dv"])
    act = k.op("scalar", lambda e, o=qc, i_=x_ap:
               e.activation(o, i_, AF.Sign, bias=0.0, scale=1.0), inc="ac")
    k.wait("vector", "ac", act)
    k.op("vector", lambda e, o=qa, a=qa, b_=qc: e.tensor_mul(o, a, b_))
    mb = mr_m.rearrange("p (b one) -> p b one", one=1).to_broadcast((128, nb, 16))
    dvt = k.op("vector", lambda e, o=out_ap.rearrange("p (b s) -> p b s", s=16),
               a=qa.rearrange("p (b s) -> p b s", s=16), b_=mb:
               e.tensor_tensor(o, a, b_, ALU.mult), inc="dv")
    return dvt


def emit_phase_d(nc, k, cst, x3t, wg_d, wu_d, wd_d, out_d, wgq_d, wuq_d, wdq_d):
    k.sreset()
    xq = k.salloc([128, DCH * T], BF16)
    mr = k.salloc([128, 128], F32)
    ab = k.salloc([128, 512], F32)
    qa = k.salloc([128, 512], F32)
    qb = k.salloc([128, 512], F32)
    qc = k.salloc([128, 512], F32)
    scr = (mr, ab, qa, qb, qc)

    def quant_sliced(x_ap, out_ap, W):
        for cs in range(0, W, 512):
            cw = min(512, W - cs)
            emit_quant(k, x_ap[:, cs:cs + cw], out_ap[:, cs:cs + cw], cw, scr)
    xt = k.salloc([128, D], F32)
    lnt = k.salloc([128, D], F32)
    stat = k.salloc([128, 8], F32)
    qt = k.salloc([128, D], BF16)
    wgq = k.salloc([128, 2 * 1024], BF16)
    wuq = k.salloc([128, 2 * 1024], BF16)
    wdj = k.salloc([128, 2 * 2816], BF16)
    sg = k.salloc([128, 1024], F32)
    hf = k.salloc([128, 1024], F32)
    htok = k.salloc([128, 1024], F32)
    hqt = k.salloc([128, 1024], BF16)
    outfm = k.salloc([128, DCH * 1024], BF16)
    outtok = k.salloc([128, D], F32)
    xold = k.salloc([128, D], F32)
    xnew = k.salloc([128, D], F32)
    m_pre = k.smark()
    wtile = k.salloc([128, 2816], F32)
    wqt = k.salloc([128, 2816], BF16)
    wT = k.salloc([128, 1024], BF16)

    psT = k.pb2k[0][:]
    psU = k.pb2k[1][:]
    psW = k.pb2k[2][:]
    psD = k.pb2k[3][:]

    ntile = T // 128
    prev_pe = 0
    for i in range(ntile):
        xt_i = xt[:]
        k.wait("sync", "dv", k.cnt["dv"])
        ldt = k.dma("sync", xt_i, x3t[i * 128:(i + 1) * 128, :], "ld")
        k.wait("vector", "ld", ldt)
        k.wait("scalar", "ld", ldt)
        k.vop_s(lambda e, o=stat[:, 0:1], i_=xt_i: e.reduce_sum(o, i_, AX.X))
        k.vop_s(lambda e, o=hf[:], a=xt_i, b_=xt_i: e.tensor_mul(o, a, b_))
        k.vop_s(lambda e, o=stat[:, 1:2], i_=hf[:]:
                e.reduce_sum(o, i_, AX.X))
        k.vop_s(lambda e, o=stat[:, 2:4], i_=stat[:, 0:2]:
                e.tensor_scalar(o, i_, 1.0 / D, None, ALU.mult))
        k.vop_s(lambda e, o=stat[:, 4:5], a=stat[:, 2:3], b_=stat[:, 2:3]:
                e.tensor_mul(o, a, b_))
        k.vop_s(lambda e, o=stat[:, 5:6], a=stat[:, 3:4], b_=stat[:, 4:5]:
                e.tensor_sub(o, a, b_))
        k.vop_s(lambda e, o=stat[:, 6:7], i_=stat[:, 5:6]:
                e.tensor_scalar(o, i_, 1e-5, None, ALU.add))
        k.wait("scalar", "dv", k.cnt["dv"])
        sct = k.op("scalar", lambda e, o=stat[:, 5:6], i_=stat[:, 6:7]:
                   e.activation(o, i_, AF.Sqrt, bias=0.0, scale=1.0), inc="ac")
        k.wait("vector", "ac", sct)
        k.vop_s(lambda e, o=stat[:, 6:7], i_=stat[:, 5:6]: e.reciprocal(o, i_))
        k.op("vector", lambda e, o=lnt[:], i_=xt_i, m=stat[:, 2:3], r=stat[:, 6:7]:
             e.tensor_scalar(o, i_, m, r, ALU.subtract, ALU.mult), inc="dv")
        k.wait("scalar", "dv", k.cnt["dv"])
        quant_sliced(lnt[:], qt[:], D)
        k.wait("tensor", "dv", k.cnt["dv"])
        k.wait("tensor", "pe", prev_pe)
        for a in range(DCH):
            bank = psT[:, (a % 2) * 512:(a % 2) * 512 + 128].bitcast(BF16)
            pet = k.op("tensor", lambda e, o=bank[:, 0:128],
                       i_=qt[:, a * 128:(a + 1) * 128], idn=cst["ident_bf"][:]:
                       e.transpose(o, i_, idn), inc="pe", mmb=0)
            k.wait("vector", "pe", pet)
            dvt = k.op("vector", lambda e, o=xq[:, a * T + i * 128:
                                             a * T + (i + 1) * 128],
                       i_=bank[:, 0:128]: e.tensor_copy(o, i_), inc="dv")
            k.wait("tensor", "dv", dvt)
        prev_pe = k.cnt["pe"]

    def wprep(src_d, dst_d, nrow_chunks, wcols):
        for f in range(nrow_chunks):
            k.wait("sync", "dv", k.cnt["dv"])
            k.wait("sync", "wld", k.cnt["wld"])
            ldt = k.dma("sync", wtile[:, 0:wcols],
                        src_d[f * 128:(f + 1) * 128, :], "ld")
            k.wait("vector", "ld", ldt)
            k.wait("scalar", "ld", ldt)
            quant_sliced(wtile[:, 0:wcols], wqt[:, 0:wcols], wcols)
            k.wait("tensor", "dv", k.cnt["dv"])
            nblk = wcols // 128
            for a in range(nblk):
                bank = psT[:, (a % 2) * 512:(a % 2) * 512 + 128].bitcast(BF16)
                pet = k.op("tensor", lambda e, o=bank[:, 0:128],
                           i_=wqt[:, a * 128:(a + 1) * 128],
                           idn=cst["ident_bf"][:]:
                           e.transpose(o, i_, idn), inc="pe", mmb=0)
                k.wait("vector", "pe", pet)
                if a % 8 == 0:
                    k.wait("vector", "wld", k.cnt["wld"])
                col = (a % 8) * 128
                dvt = k.op("vector", lambda e, o=wT[:, col:col + 128],
                           i_=bank[:, 0:128]: e.tensor_copy(o, i_), inc="dv")
                k.wait("tensor", "dv", dvt)
                if a % 8 == 7 or a == nblk - 1:
                    k.wait("gpsimd", "dv", dvt)
                    k.dma("gpsimd", dst_d[f, :, (a // 8) * 1024:
                                          (a // 8) * 1024 + col + 128],
                          wT[:, 0:col + 128], "wld")
    wprep(wg_d, wgq_d, FCH, D)
    wprep(wu_d, wuq_d, FCH, D)
    wprep(wd_d, wdq_d, DCH, 2816)
    wp_done = k.fence("vector", "dv")
    k.wait("sync", "dv", wp_done)
    k.wait("sync", "wld", k.cnt["wld"])
    k.srewind(m_pre)
    hq = k.salloc([128, FCH * 1024], BF16)

    wld_guard = [0, 0]
    for ns in range(4):
        nsl = slice(ns * 1024, (ns + 1) * 1024)
        for f in range(FCH):
            buf = f % 2
            k.wait("sync", "pe", wld_guard[buf])
            lg = k.dma("sync", wgq[:, buf * 1024:(buf + 1) * 1024],
                       wgq_d[f, :, :], "ld")
            lu = k.dma("sync", wuq[:, buf * 1024:(buf + 1) * 1024],
                       wuq_d[f, :, :], "ld")
            k.wait("tensor", "ld", lu)
            for half in range(2):
                hsl = slice(half * 512, (half + 1) * 512)
                for a in range(DCH):
                    k.op("tensor", lambda e, o=psU[:, hsl],
                         l=wgq[:, buf * 1024 + a * 128:buf * 1024 + (a + 1) * 128],
                         r=xq[:, a * T + ns * 1024 + half * 512:
                               a * T + ns * 1024 + half * 512 + 512],
                         s=(a == 0), q=(a == DCH - 1):
                         e.matmul(o, l, r, start=s, stop=q), mmb=0)
                for a in range(DCH):
                    pet = k.op("tensor", lambda e, o=psW[:, hsl],
                               l=wuq[:, buf * 1024 + a * 128:
                                     buf * 1024 + (a + 1) * 128],
                               r=xq[:, a * T + ns * 1024 + half * 512:
                                     a * T + ns * 1024 + half * 512 + 512],
                               s=(a == 0), q=(a == DCH - 1):
                               e.matmul(o, l, r, start=s, stop=q), inc="pe",
                               mmb=0)
            wld_guard[buf] = pet
            k.wait("scalar", "pe", pet)
            act = k.op("scalar", lambda e, o=sg[:], i_=psU[:]:
                       e.activation(o, i_, AF.Silu, bias=0.0, scale=1.0),
                       inc="ac")
            k.wait("vector", "ac", act)
            dvt = k.op("vector", lambda e, o=hf[:], a=sg[:], b_=psW[:]:
                       e.tensor_mul(o, a, b_), inc="dv")
            k.wait("tensor", "dv", dvt)
            for a in range(DCH):
                bank = psT[:, (a % 2) * 512:(a % 2) * 512 + 128]
                pet = k.op("tensor", lambda e, o=bank,
                           i_=hf[:, a * 128:(a + 1) * 128],
                           idn=cst["ident_f32"][:]:
                           e.transpose(o, i_, idn), inc="pe", mmb=0)
                k.wait("vector", "pe", pet)
                dvt = k.op("vector", lambda e, o=htok[:, a * 128:(a + 1) * 128],
                           i_=bank: e.tensor_copy(o, i_), inc="dv")
                k.wait("tensor", "dv", dvt)
            k.wait("scalar", "dv", k.cnt["dv"])
            quant_sliced(htok[:], hqt[:], D)
            k.wait("tensor", "dv", k.cnt["dv"])
            for a in range(DCH):
                bank = psT[:, (a % 2) * 512:(a % 2) * 512 + 128].bitcast(BF16)
                pet = k.op("tensor", lambda e, o=bank[:, 0:128],
                           i_=hqt[:, a * 128:(a + 1) * 128],
                           idn=cst["ident_bf"][:]:
                           e.transpose(o, i_, idn), inc="pe", mmb=0)
                k.wait("vector", "pe", pet)
                dvt = k.op("vector", lambda e,
                           o=hq[:, f * 1024 + a * 128:f * 1024 + (a + 1) * 128],
                           i_=bank[:, 0:128]: e.tensor_copy(o, i_), inc="dv")
                k.wait("tensor", "dv", dvt)
        for j in range(DCH):
            buf = j % 2
            k.wait("sync", "pe", wld_guard[buf])
            ld = k.dma("sync", wdj[:, buf * 2816:(buf + 1) * 2816],
                       wdq_d[j, :, :], "ld")
            k.wait("tensor", "ld", ld)
            for half in range(2):
                bank = psD[:, half * 512:(half + 1) * 512]
                for f in range(FCH):
                    pet = k.op("tensor", lambda e, o=bank,
                               l=wdj[:, buf * 2816 + f * 128:
                                     buf * 2816 + (f + 1) * 128],
                               r=hq[:, f * 1024 + half * 512:
                                     f * 1024 + half * 512 + 512],
                               s=(f == 0), q=(f == FCH - 1):
                               e.matmul(o, l, r, start=s, stop=q), inc="pe",
                               mmb=0)
            wld_guard[buf] = pet
            k.wait("vector", "pe", pet)
            dvt = k.op("vector", lambda e, o=outfm[:, j * 1024:(j + 1) * 1024],
                       i_=psD[:]: e.tensor_copy(o, i_), inc="dv")
            k.wait("tensor", "dv", dvt)
        for tt in range(8):
            t0 = ns * 1024 + tt * 128
            for j in range(DCH):
                bank = psT[:, (j % 2) * 512:(j % 2) * 512 + 128].bitcast(BF16)
                pet = k.op("tensor", lambda e, o=bank[:, 0:128],
                           i_=outfm[:, j * 1024 + tt * 128:
                                    j * 1024 + (tt + 1) * 128],
                           idn=cst["ident_bf"][:]:
                           e.transpose(o, i_, idn), inc="pe", mmb=0)
                k.wait("vector", "pe", pet)
                dvt = k.op("vector", lambda e, o=outtok[:, j * 128:(j + 1) * 128],
                           i_=bank[:, 0:128]: e.tensor_copy(o, i_), inc="dv")
                k.wait("tensor", "dv", dvt)
            k.wait("sync", "dv", k.cnt["dv"])
            ldt = k.dma("sync", xold[:], x3t[t0:t0 + 128, :], "ld")
            k.wait("vector", "ld", ldt)
            dvt = k.op("vector", lambda e, o=xnew[:], a=outtok[:], b_=xold[:]:
                       e.tensor_add(o, a, b_), inc="dv")
            k.wait("sync", "dv", dvt)
            stt = k.dma("sync", out_d[t0:t0 + 128, :], xnew[:], "st")
            k.wait("vector", "st", stt)
    fin = k.fence("vector", "dv")
    k.wait("sync", "dv", fin)
    k.wait("sync", "st", k.cnt["st"])


def build_phase_d_program():
    nc, k = new_program()
    x3t = nc.dram_tensor("x3t", [T, D], F32, kind="ExternalInput").ap()
    wg_d = nc.dram_tensor("w_gate", [2816, D], F32, kind="ExternalInput").ap()
    wu_d = nc.dram_tensor("w_up", [2816, D], F32, kind="ExternalInput").ap()
    wd_d = nc.dram_tensor("w_down", [D, 2816], F32, kind="ExternalInput").ap()
    out_d = nc.dram_tensor("out", [T, D], F32, kind="ExternalOutput").ap()
    wgq_d = nc.dram_tensor("wgq", [FCH, 128, D], BF16, kind="Internal").ap()
    wuq_d = nc.dram_tensor("wuq", [FCH, 128, D], BF16, kind="Internal").ap()
    wdq_d = nc.dram_tensor("wdq", [DCH, 128, 2816], BF16, kind="Internal").ap()
    cst = stage_consts(nc, k)
    k.freeze_persistent()
    emit_phase_d(nc, k, cst, x3t, wg_d, wu_d, wd_d, out_d, wgq_d, wuq_d, wdq_d)
    k.build()
    return nc


# version 06
# speedup vs baseline: 1.0005x; 1.0005x over previous
import contextlib
import numpy as np
import ml_dtypes
import concourse.bass as bass
import concourse.mybir as mybir
from concourse.bass_utils import run_bass_kernel_spmd

F32 = mybir.dt.float32
BF16 = mybir.dt.bfloat16
AF = mybir.ActivationFunctionType
ALU = mybir.AluOpType
AX = mybir.AxisListType

B_CORE = 2
N = 2048
D = 1024
T = B_CORE * N
H = 16
HD = 64
CHUNK = 64
NCH = N // CHUNK
NPAIR = 8
DCH = D // 128

ENGINES = ("sync", "tensor", "vector", "scalar", "gpsimd")


ARENA_BASE = 20 * 1024
SBUF_BYTES = 229312


BASE_SEMS = ("ld", "st", "wld", "pe", "dv", "ac", "gp", "sy")


class _CntView:

    def __init__(self, k):
        self._k = k

    def __getitem__(self, base):
        return self._k._cnt[self._k._n(base)]

    def __setitem__(self, base, val):
        self._k._cnt[self._k._n(base)] = val


class K:

    def __init__(self, nc):
        self.nc = nc
        self.q = {e: [] for e in ENGINES}
        self.sem_names = []
        self._cnt = {}
        self.sfx = ""
        self.cnt = _CntView(self)
        self._uid = 0
        self.mm_base = None
        self.p_off = ARENA_BASE
        self.s_base = None
        self.s_off = None
        self.s_peak = 0

    def _n(self, base):
        return base + self.sfx

    def set_phase(self, tag):
        self.sfx = "_" + tag
        for b in BASE_SEMS:
            name = b + self.sfx
            if name not in self._cnt:
                self.sem_names.append(name)
                self._cnt[name] = 0


    def _alloc_at(self, shape, dtype, off):
        self._uid += 1
        nbytes = int(np.prod(shape[1:])) * (2 if dtype == BF16 else 4)
        t = self.nc.alloc_sbuf_tensor_at(f"sb{self._uid}", list(shape), dtype,
                                         offset=off)
        return t, off + ((nbytes + 31) // 32) * 32

    def palloc(self, shape, dtype):
        t, self.p_off = self._alloc_at(shape, dtype, self.p_off)
        assert self.s_base is None, "persistent alloc after freeze"
        return t

    def freeze_persistent(self):
        self.s_base = self.p_off
        self.s_off = self.p_off

    def sreset(self):
        self.s_peak = max(self.s_peak, self.s_off)
        self.s_off = self.s_base

    def salloc(self, shape, dtype):
        t, self.s_off = self._alloc_at(shape, dtype, self.s_off)
        assert self.s_off <= SBUF_BYTES, f"sbuf over budget: {self.s_off}"
        return t

    def smark(self):
        return self.s_off

    def srewind(self, mark):
        self.s_peak = max(self.s_peak, self.s_off)
        self.s_off = mark

    def sem(self, name):
        self.sem_names.append(name)
        self.cnt[name] = 0
        return name

    def vop_s(self, fn):
        dv = self._n("dv")
        self._cnt[dv] += 1
        self.q["vector"].append(("op", fn, dv, 1))
        self.q["vector"].append(("wait", dv, self._cnt[dv]))
        return self._cnt[dv]

    def op(self, eng, fn, inc=None, amt=1, mmb=None):
        if mmb is not None and eng == "tensor":
            if self.mm_base is not None and self.mm_base != mmb:
                self.q[eng].append(("op", lambda e: e.nop(), None, 1))
                self.q[eng].append(("op", lambda e: e.nop(), None, 1))
            self.mm_base = mmb
        if inc is not None:
            inc = self._n(inc)
            self._cnt[inc] += amt
        self.q[eng].append(("op", fn, inc, amt))
        return self._cnt[inc] if inc is not None else None

    def dma(self, eng, out, in_, sem="ld", **kw):
        sem = self._n(sem)
        self._cnt[sem] += 16
        tgt = self._cnt[sem]
        self.q[eng].append(
            ("op", lambda e, o=out, i=in_, kw=kw: e.dma_start(out=o, in_=i, **kw),
             sem, 16))
        return tgt

    def wait(self, eng, sem, val):
        self.q[eng].append(("wait", self._n(sem), val))

    def fence(self, eng, sem):
        sem = self._n(sem)
        self._cnt[sem] += 1
        if eng in ("vector", "gpsimd"):
            self.q[eng].append(("op", lambda e: e.engine_nop(), sem, 1))
        else:
            self.q[eng].append(("op", lambda e: e.nop(), sem, 1))
        return self._cnt[sem]

    def barrier_all(self):
        for _round in range(2):
            tpe = self.fence("tensor", "pe")
            tdv = self.fence("vector", "dv")
            self._cnt[self._n("ac")] += 1
            tac = self._cnt[self._n("ac")]
            bar = self.bar_tile
            self.q["scalar"].append(
                ("op", lambda e, o=bar: e.activation(
                    o, o, AF.Identity, bias=0.0, scale=1.0), self._n("ac"), 1))
            tgp = self.fence("gpsimd", "gp")
            tsy = self.fence("sync", "sy")
            for eng in ENGINES:
                for sem, tgt in (("pe", tpe), ("dv", tdv), ("ac", tac),
                                 ("gp", tgp), ("sy", tsy)):
                    self.wait(eng, sem, tgt)

    def build(self):
        nc = self.nc
        with contextlib.ExitStack() as stack:
            sems = {n: stack.enter_context(nc.semaphore(n)) for n in self.sem_names}
            with nc.Block() as block:
                def mk(engname):
                    def emit(eng):
                        for item in self.q[engname]:
                            if item[0] == "wait":
                                eng.wait_ge(sems[item[1]], item[2])
                            else:
                                _, fn, inc, amt = item
                                ins = fn(eng)
                                if inc is not None:
                                    ins.then_inc(sems[inc], amt)
                    return emit
                block.sync(mk("sync"))
                block.tensor(mk("tensor"))
                block.vector(mk("vector"))
                block.scalar(mk("scalar"))
                block.gpsimd(mk("gpsimd"))
        return nc



def host_constants():
    c = {}
    t = np.arange(N, dtype=np.float64)[None, :]
    j = np.arange(HD, dtype=np.float64)[:, None]
    theta = t * (10000.0 ** (-2.0 * j / HD))
    cos1 = np.cos(theta).astype(np.float32)
    sin1 = np.sin(theta).astype(np.float32)
    c["cstab"] = np.concatenate([cos1, sin1], 0).astype(ml_dtypes.bfloat16)
    for hh in range(2):
        selT = np.zeros((128, 128), np.float32)
        for pcol in range(128):
            selT[64 * hh + (pcol % 64), pcol] = 1.0
        c[f"selT{hh}"] = selT
    c["ident_f32"] = np.eye(128, dtype=np.float32)
    c["ident_bf"] = np.eye(128, dtype=np.float32).astype(ml_dtypes.bfloat16)
    s = np.arange(CHUNK)[:, None]
    cc = np.arange(CHUNK)[None, :]
    c["triu1"] = (cc >= s).astype(np.float32)
    c["ones_col_bf"] = np.ones((128, 1), np.float32).astype(ml_dtypes.bfloat16)
    c["ones_row_f32"] = np.ones((1, 128), np.float32)
    c["ones_row_bf"] = np.ones((1, 128), np.float32).astype(ml_dtypes.bfloat16)
    c["k09c"] = np.full((16, N), 0.9, np.float32)
    c["ones_col_f32"] = np.ones((128, 1), np.float32)
    return c


CONST_SPECS = {
    "cstab": ([128, N], BF16),
    "selT0": ([128, 128], F32), "selT1": ([128, 128], F32),
    "ident_f32": ([128, 128], F32), "ident_bf": ([128, 128], BF16),
    "triu1": ([64, 64], F32),
    "ones_col_bf": ([128, 1], BF16), "ones_row_f32": ([1, 128], F32),
    "ones_row_bf": ([1, 128], BF16), "k09c": ([16, N], F32),
    "ones_col_f32": ([128, 1], F32),
}


def stage_consts(nc, k):
    sb = {}
    for name, (shape, dt) in CONST_SPECS.items():
        dram = nc.dram_tensor(name, shape, dt, kind="ExternalInput").ap()
        if name == "k09c":
            sb["k09c_dram"] = dram
            continue
        sb[name] = k.palloc(shape, dt)
        k.dma("sync", sb[name][:], dram, "ld")
    return sb



def emit_phase_a(nc, k, cst, x_in, wq_d, wk_d, wv_d, wo_d, wg1_d, wg2_d,
                 normw_d, xr0, xr1, o_s, dbg_bp=99, dbg_a3=True):
    k.sreset()
    ln0 = k.salloc([128, DCH * T], BF16)
    xt = k.salloc([128, 2 * D], F32)
    lnt = k.salloc([128, 2 * D], BF16)
    xsq = k.salloc([128, D], F32)
    stat = k.salloc([128, 8], F32)
    xcol = k.salloc([128, 2 * D], F32)
    m_gla = k.smark()

    wk = k.salloc([128, DCH * 128], BF16)
    wqd = [k.salloc([128, DCH * 128], BF16) for _ in range(2)]
    wvh = [k.salloc([128, DCH * 64], BF16) for _ in range(2)]
    kr = k.salloc([128, N], F32)
    qs = k.salloc([128, N], F32)
    KG = k.salloc([128, N], F32)
    CS = k.salloc([128, N], F32)
    EQ = k.salloc([128, N], F32)
    tmpf = k.salloc([128, N], F32)
    sc = k.salloc([128, 512], F32)
    QG = k.salloc([128, N], BF16)
    KGC = k.salloc([128, N], BF16)
    K2 = k.salloc([128, N], BF16)
    EQd = k.salloc([128, 32], F32)
    vfm_h = k.salloc([64, N], BF16)
    v_cm_h = k.salloc([64, NCH * 64], BF16)
    Atile = k.salloc([64, 64], BF16)
    k2tok = k.salloc([64, 128], BF16)
    S_h = k.salloc([128, 64], F32)
    S_hb = k.salloc([128, 64], BF16)
    O_h = k.salloc([64, N], BF16)

    psT = k.pbank[0][:]
    psT2 = k.pbank[1][:]
    psP = k.pbank[2][:]
    psP2 = k.pbank[3][:]
    psA = k.pbank[4][:][:, 0:128]
    psO = k.pbank[5][:][:, 0:64]
    psS = k.pbank[5][:][:, 64:192]
    psK = k.pbank[6][:].bitcast(BF16)[0:64, 0:384]

    ntile = T // 128
    xt_chk = [0] * 2
    lnt_chk = [0] * 2
    xcol_chk = [0] * 2
    for i in range(ntile):
        buf = i % 2
        xt_i = xt[:, buf * D:(buf + 1) * D]
        lnt_i = lnt[:, buf * D:(buf + 1) * D]
        xcol_i = xcol[:, buf * D:(buf + 1) * D]
        if i >= 2:
            k.wait("sync", "pe", xt_chk[buf])
        ldt = k.dma("sync", xt_i, x_in[i * 128:(i + 1) * 128, :], "ld")
        k.wait("vector", "ld", ldt)
        k.op("vector", lambda e, o=stat[:, 0:1], i_=xt_i: e.reduce_sum(o, i_, AX.X))
        k.wait("scalar", "ld", ldt)
        k.op("scalar", lambda e, o=xsq[:], i_=xt_i:
             e.activation(o, i_, AF.Square, bias=0.0, scale=1.0), inc="ac")
        k.wait("vector", "ac", k.cnt["ac"])
        k.op("vector", lambda e, o=stat[:, 1:2], i_=xsq[:]: e.reduce_sum(o, i_, AX.X),
             inc="dv")
        k.wait("vector", "dv", k.cnt["dv"])
        k.op("vector", lambda e, o=stat[:, 2:4], i_=stat[:, 0:2]:
             e.tensor_scalar(o, i_, 1.0 / D, None, ALU.mult), inc="dv")
        k.wait("vector", "dv", k.cnt["dv"])
        k.op("vector", lambda e, o=stat[:, 4:5], a=stat[:, 2:3], b=stat[:, 2:3]:
             e.tensor_mul(o, a, b), inc="dv")
        k.wait("vector", "dv", k.cnt["dv"])
        k.op("vector", lambda e, o=stat[:, 5:6], a=stat[:, 3:4], b=stat[:, 4:5]:
             e.tensor_sub(o, a, b), inc="dv")
        k.wait("vector", "dv", k.cnt["dv"])
        k.op("vector", lambda e, o=stat[:, 7:8], i_=stat[:, 5:6]:
             e.tensor_scalar(o, i_, 1e-5, None, ALU.add), inc="dv")
        k.wait("scalar", "dv", k.cnt["dv"])
        k.op("scalar", lambda e, o=stat[:, 5:6], i_=stat[:, 7:8]:
             e.activation(o, i_, AF.Sqrt, bias=0.0, scale=1.0), inc="ac")
        k.wait("vector", "ac", k.cnt["ac"])
        k.op("vector", lambda e, o=stat[:, 6:7], i_=stat[:, 5:6]:
             e.reciprocal(o, i_), inc="dv")
        k.wait("vector", "dv", k.cnt["dv"])
        if i >= 2:
            k.wait("vector", "pe", lnt_chk[buf])
        k.op("vector", lambda e, o=lnt_i, i_=xt_i, m=stat[:, 2:3], r=stat[:, 6:7]:
             e.tensor_scalar(o, i_, m, r, ALU.subtract, ALU.mult), inc="dv")
        dv_ln = k.cnt["dv"]

        k.wait("tensor", "ld", ldt)
        if i >= 2:
            k.wait("tensor", "st", xcol_chk[buf])
        for j in range(DCH):
            bank = psT if j % 2 == 0 else psT2
            pet = k.op("tensor", lambda e, o=bank[:, 0:128],
                       i_=xt_i[:, j * 128:(j + 1) * 128], idn=cst["ident_f32"][:]:
                       e.transpose(o, i_, idn), inc="pe")
            k.wait("vector", "pe", pet)
            dvt = k.op("vector", lambda e, o=xcol_i[:, j * 128:(j + 1) * 128],
                       i_=bank[:, 0:128]: e.tensor_copy(o, i_), inc="dv")
            k.wait("tensor", "dv", dvt)
        k.wait("sync", "dv", k.cnt["dv"])
        dst = xr0[:, i * 128:(i + 1) * 128].rearrange("(c p) t -> p c t", p=128)
        xcol_chk[buf] = k.dma("sync", dst, xcol_i.rearrange("p (c t) -> p c t", t=128),
                              "st")
        k.wait("tensor", "dv", dv_ln)
        for j in range(DCH):
            bank = (psT if j % 2 == 0 else psT2)[:, 0:128].bitcast(BF16)
            pet = k.op("tensor", lambda e, o=bank[:, 0:128],
                       i_=lnt_i[:, j * 128:(j + 1) * 128], idn=cst["ident_bf"][:]:
                       e.transpose(o, i_, idn), inc="pe")
            k.wait("vector", "pe", pet)
            dvt = k.op("vector", lambda e, o=ln0[:, j * T + i * 128: j * T + (i + 1) * 128],
                       i_=bank[:, 0:128]: e.tensor_copy(o, i_), inc="dv")
            k.wait("tensor", "dv", dvt)
        lnt_chk[buf] = k.cnt["pe"]
        xt_chk[buf] = k.cnt["pe"]
    a1_dv = k.fence("vector", "dv")
    for eng in ("tensor", "sync", "gpsimd"):
        k.wait(eng, "dv", a1_dv)

    NSL = N // 512

    def proj_fm(wsb, out_sb, bslice, out_dtype, act=None):
        for ns in range(NSL):
            bank = psP if ns % 2 == 0 else psP2
            for a in range(DCH):
                pet = k.op("tensor", lambda e, o=bank[:],
                           l=wsb[:, a * 128:(a + 1) * 128],
                           r=ln0[:, a * T + bslice * N + ns * 512:
                                 a * T + bslice * N + ns * 512 + 512],
                           s=(a == 0), p=(a == DCH - 1):
                           e.matmul(o, l, r, start=s, stop=p), inc="pe")
            k.wait("vector" if act is None else "scalar", "pe", pet)
            o_sl = out_sb[:, ns * 512:(ns + 1) * 512]
            if act is None:
                dvt = k.op("vector", lambda e, o=o_sl, i_=bank[:]:
                           e.tensor_copy(o, i_), inc="dv")
                k.wait("tensor", "dv", dvt)
            else:
                sct = k.op("scalar", lambda e, o=o_sl, i_=bank[:], f=act:
                           e.activation(o, i_, f, bias=0.0, scale=1.0), inc="ac")
                k.wait("tensor", "ac", sct)

    import os
    a2stop = int(os.environ.get("DBG_A2STOP", "99"))
    for bp in range(min(B_CORE * NPAIR, dbg_bp)):
        p, b = divmod(bp, B_CORE)
        c0 = p * 128
        if b == 0:
            if p > 0:
                k.wait("gpsimd", "pe", k.cnt["pe"])
            src = wk_d[:, c0:c0 + 128].rearrange("(a p) c -> p a c", p=128)
            k.dma("gpsimd", wk[:].rearrange("p (a c) -> p a c", c=128), src, "wld")
            for hh in range(2):
                hc = c0 + 64 * hh
                qsrc = wq_d[:, hc:hc + 64].rearrange("(a p) c -> p a c", p=128)
                wqv = wqd[hh][:].rearrange("p (a c) -> p a c", c=128)
                k.dma("gpsimd", wqv[:, :, 0:64], qsrc, "wld")
                k.dma("gpsimd", wqv[:, :, 64:128], qsrc, "wld")
                vsrc = wv_d[:, hc:hc + 64].rearrange("(a p) c -> p a c", p=128)
                k.dma("gpsimd", wvh[hh][:].rearrange("p (a c) -> p a c", c=64),
                      vsrc, "wld")
            k.wait("tensor", "wld", k.cnt["wld"])
        proj_fm(wk, kr, b, F32)
        if a2stop <= 1:
            continue

        kr3 = kr[:].rearrange("p (c k) -> p c k", k=CHUNK)
        m32 = sc[:, 0:32]
        T32 = sc[:, 32:64]
        La = sc[:, 96:128]
        Lb = sc[:, 128:160]
        Zc = sc[:, 160:192]
        t32a = sc[:, 192:224]
        t32b = sc[:, 224:256]
        EQend = sc[:, 256:288]
        k.vop_s(lambda e, o=m32, i_=kr3: e.reduce_max(o, i_, AX.X))
        k.op("vector", lambda e, o=tmpf[:].rearrange("p (c k) -> p c k", k=CHUNK),
             a=kr3, m=m32: e.tensor_tensor(
                 o, a, m.unsqueeze(2).to_broadcast((128, NCH, CHUNK)), ALU.subtract),
             inc="dv")
        k.wait("scalar", "dv", k.cnt["dv"])
        k.op("scalar", lambda e, o=tmpf[:], i_=tmpf[:]:
             e.activation(o, i_, AF.Exp, bias=0.0, scale=1.0), inc="ac")
        k.wait("vector", "ac", k.cnt["ac"])
        k.op("vector", lambda e, o=T32,
             i_=tmpf[:].rearrange("p (c k) -> p c k", k=CHUNK):
             e.reduce_sum(o, i_, AX.X), inc="dv")
        k.wait("scalar", "dv", k.cnt["dv"])
        k.op("scalar", lambda e, o=T32, i_=T32:
             e.activation(o, i_, AF.Ln, bias=0.0, scale=1.0), inc="ac")
        k.wait("vector", "ac", k.cnt["ac"])
        k.vop_s(lambda e, o=La, a=m32, b_=T32: e.tensor_add(o, a, b_))
        cur, nxt = La, Lb
        for d in (1, 2, 4, 8, 16):
            k.vop_s(lambda e, o=nxt[:, 0:d], i_=cur[:, 0:d], d=d:
                    e.tensor_copy(o, i_))
            k.vop_s(lambda e, o=nxt[:, d:32], a=cur[:, d:32], b_=cur[:, 0:32 - d],
                    d=d: e.tensor_max(o, a, b_))
            k.vop_s(lambda e, o=t32a[:, 0:32 - d], a=cur[:, d:32],
                    b_=cur[:, 0:32 - d], d=d: e.tensor_tensor(o, a, b_, ALU.min))
            k.vop_s(lambda e, o=t32a[:, 0:32 - d], a=t32a[:, 0:32 - d],
                    b_=nxt[:, d:32], d=d: e.tensor_sub(o, a, b_))
            k.wait("scalar", "dv", k.cnt["dv"])
            sct = k.op("scalar", lambda e, o=t32b[:, 0:32 - d], i_=t32a[:, 0:32 - d],
                       d=d: e.activation(o, i_, AF.Exp, bias=0.0, scale=1.0),
                       inc="ac")
            k.wait("scalar", "ac", sct)
            k.op("scalar", lambda e, o=t32b[:, 0:32 - d], i_=t32b[:, 0:32 - d], d=d:
                 e.activation(o, i_, AF.Ln, bias=1.0, scale=1.0), inc="ac")
            k.wait("vector", "ac", k.cnt["ac"])
            k.vop_s(lambda e, o=nxt[:, d:32], a=nxt[:, d:32],
                    b_=t32b[:, 0:32 - d], d=d: e.tensor_add(o, a, b_))
            cur, nxt = nxt, cur
        k.vop_s(lambda e, o=Zc[:, 0:1]: e.memset(o, 0.0))
        k.wait("scalar", "dv", k.cnt["dv"])
        sct = k.op("scalar", lambda e, o=t32a[:, 0:31], i_=cur[:, 0:31]:
                   e.activation(o, i_, AF.Abs, bias=0.0, scale=1.0), inc="ac")
        k.wait("scalar", "ac", sct)
        sct = k.op("scalar", lambda e, o=t32a[:, 0:31], i_=t32a[:, 0:31]:
                   e.activation(o, i_, AF.Exp, bias=0.0, scale=-1.0), inc="ac")
        k.wait("scalar", "ac", sct)
        k.op("scalar", lambda e, o=t32a[:, 0:31], i_=t32a[:, 0:31]:
             e.activation(o, i_, AF.Ln, bias=1.0, scale=1.0), inc="ac")
        k.wait("vector", "ac", k.cnt["ac"])
        k.vop_s(lambda e, o=t32b[:, 0:31], i_=cur[:, 0:31]:
                e.tensor_scalar(o, i_, 0.0, None, ALU.max))
        k.op("vector", lambda e, o=Zc[:, 1:32], a=t32b[:, 0:31], b_=t32a[:, 0:31]:
             e.tensor_add(o, a, b_), inc="dv")
        k.op("vector", lambda e, o=KG[:].rearrange("p (c k) -> p c k", k=CHUNK),
             a=kr3, z=Zc: e.tensor_tensor(
                 o, a, z.unsqueeze(2).to_broadcast((128, NCH, CHUNK)), ALU.subtract),
             inc="dv")
        k.wait("scalar", "dv", k.cnt["dv"])
        k.op("scalar", lambda e, o=KG[:], i_=KG[:]:
             e.activation(o, i_, AF.Exp, bias=0.0, scale=1.0), inc="ac")
        ac_kg = k.cnt["ac"]
        k.wait("vector", "ac", ac_kg)
        for c in range(NCH):
            k.op("vector", lambda e, o=CS[:, c * 64:(c + 1) * 64],
                 d0=KG[:, c * 64:(c + 1) * 64]:
                 e.tensor_tensor_scan(o, d0, d0, 1.0, ALU.add, ALU.bypass))
        k.op("vector", lambda e: e.engine_nop(), inc="dv")
        k.wait("scalar", "dv", k.cnt["dv"])
        k.op("scalar", lambda e, o=EQ[:], i_=CS[:]:
             e.activation(o, i_, AF.Ln, bias=0.0, scale=1.0))
        k.op("scalar", lambda e, o=EQ[:], i_=EQ[:]:
             e.activation(o, i_, AF.Exp, bias=0.0, scale=-1.0), inc="ac")
        k.wait("vector", "ac", k.cnt["ac"])
        k.vop_s(lambda e, o=EQend.unsqueeze(2),
                i_=EQ[:].rearrange("p (c k) -> p c k", k=CHUNK)[:, :, 63:64]:
                e.tensor_copy(o, i_))
        k.op("vector", lambda e: e.engine_nop(), inc="dv")
        zfull_dv = k.cnt["dv"]
        zfull_ac = k.cnt["ac"]
        if os.environ.get("DBG_DUMP") and bp == 0:
            dmp = nc.dram_tensor("dump", [6, 128, N], F32, kind="ExternalOutput").ap()
            k.wait("sync", "dv", k.cnt["dv"])
            k.wait("sync", "ac", k.cnt["ac"])
            for di, buf in enumerate((kr, KG, CS, EQ)):
                k.dma("sync", dmp[di], buf[:], "st")
            k.dma("gpsimd", dmp[4], ln0[:, 0:N], "wld")
            k.wait("sync", "wld", k.cnt["wld"])
            k.dma("sync", dmp[5, 0:128, 0:64], sc[:, 0:64], "st")
            k.dma("sync", dmp[5, 0:128, 96:192], sc[:, 96:192], "st")
            k.dma("sync", dmp[5, 0:128, 256:288], sc[:, 256:288], "st")
            k.wait("vector", "st", k.cnt["st"])
        if a2stop <= 2:
            continue

        for hh in range(2):
            proj_fm(wqd[hh], qs, b, F32, act=AF.Silu)
            for ns in range(NSL):
                bank = (psP if ns % 2 == 0 else psP2)[0:64, :]
                for a in range(DCH):
                    pet = k.op("tensor", lambda e, o=bank,
                               l=wvh[hh][:, a * 64:(a + 1) * 64],
                               r=ln0[:, a * T + b * N + ns * 512:
                                     a * T + b * N + ns * 512 + 512],
                               s=(a == 0), q=(a == DCH - 1):
                               e.matmul(o, l, r, start=s, stop=q), inc="pe",
                               mmb=0)
                k.wait("vector", "pe", pet)
                dvt = k.op("vector", lambda e, o=vfm_h[:, ns * 512:(ns + 1) * 512],
                           i_=bank: e.tensor_copy(o, i_), inc="dv")
                k.wait("tensor", "dv", dvt)
            k.wait("tensor", "dv", zfull_dv)
            k.wait("tensor", "ac", zfull_ac)
            for ns in range(NSL):
                nsl = slice(ns * 512, (ns + 1) * 512)
                pet = k.op("tensor", lambda e, o=psT[:], l=cst[f"selT{hh}"][:],
                           r=EQ[:, nsl]: e.matmul(o, l, r, start=True, stop=True),
                           mmb=0)
                pet = k.op("tensor", lambda e, o=psT2[:], l=cst[f"selT{hh}"][:],
                           r=KG[:, nsl]: e.matmul(o, l, r, start=True, stop=True),
                           inc="pe", mmb=0)
                k.wait("vector", "pe", pet)
                k.op("vector", lambda e, o=QG[:, nsl], a=qs[:, nsl],
                     b_=cst["cstab"][:, nsl]: e.tensor_mul(o, a, b_))
                k.op("vector", lambda e, o=QG[:, nsl], a=QG[:, nsl], b_=psT[:]:
                     e.tensor_mul(o, a, b_))
                dvt = k.op("vector", lambda e, o=KGC[:, nsl],
                           a=cst["cstab"][:, nsl], b_=psT2[:]:
                           e.tensor_mul(o, a, b_), inc="dv")
                k.wait("tensor", "dv", dvt)
            pet = k.op("tensor", lambda e, o=psT[:, 0:32], l=cst[f"selT{hh}"][:],
                       r=EQend: e.matmul(o, l, r, start=True, stop=True),
                       inc="pe", mmb=0)
            k.wait("vector", "pe", pet)
            k.vop_s(lambda e, o=EQd[:], i_=psT[:, 0:32]: e.tensor_copy(o, i_))
            k.op("vector", lambda e,
                 o=K2[:].rearrange("p (c k) -> p c k", k=CHUNK),
                 a=KGC[:].rearrange("p (c k) -> p c k", k=CHUNK), z=EQd[:]:
                 e.tensor_tensor(o, a, z.unsqueeze(2).to_broadcast(
                     (128, NCH, CHUNK)), ALU.mult))
            dvt = k.op("vector", lambda e: e.engine_nop(), inc="dv")
            k.wait("tensor", "dv", dvt)
            for c in range(NCH):
                bank = psK[:, 256 + (c % 2) * 64: 256 + (c % 2) * 64 + 64]
                pet = k.op("tensor", lambda e, o=bank,
                           i_=vfm_h[:, c * 64:(c + 1) * 64],
                           idn=cst["ident_bf"][0:64, 0:64]:
                           e.transpose(o, i_, idn), inc="pe", mmb=0)
                k.wait("vector", "pe", pet)
                dvt = k.op("vector", lambda e, o=v_cm_h[:, c * 64:(c + 1) * 64],
                           i_=bank: e.tensor_copy(o, i_), inc="dv")
                k.wait("tensor", "dv", dvt)
            k.op("vector", lambda e, o=S_h[:]: e.memset(o, 0.0))
            k.op("vector", lambda e, o=S_hb[:]: e.memset(o, 0.0), inc="dv")
            k.wait("tensor", "dv", k.cnt["dv"])
            if a2stop <= 3:
                continue

            cmask = int(os.environ.get("DBG_CHUNK", "15"))
            for c in range(NCH):
                csl = slice(c * 64, (c + 1) * 64)
                vsl = slice(c * 64, (c + 1) * 64)
                if cmask & 8:
                    k.op("tensor", lambda e, o=psK[:, 0:128], i_=K2[:, csl],
                         idn=cst["ident_bf"][:]: e.transpose(o, i_, idn), mmb=0)
                pet = None
                if cmask & 1:
                    pet = k.op("tensor", lambda e, o=psA[0:64, 0:64], l=KGC[:, csl],
                               r=QG[:, csl]: e.matmul(o, l, r, start=True, stop=True),
                               inc="pe", mmb=0)
                if pet is not None:
                    k.wait("vector", "pe", pet)
                k.op("vector", lambda e, o=k2tok[:], i_=psK[0:64, 0:128]:
                     e.tensor_copy(o, i_))
                dvt = k.op("vector", lambda e, o=Atile[:], a=psA[0:64, 0:64],
                           m=cst["triu1"][:]: e.tensor_mul(o, a, m), inc="dv")
                k.wait("tensor", "dv", dvt)
                pet = None
                if cmask & 2:
                    k.op("tensor", lambda e, o=psO[0:64, 0:64], l=S_hb[:],
                         r=QG[:, csl]: e.matmul(o, l, r, start=True, stop=False),
                         mmb=0)
                    pet = k.op("tensor", lambda e, o=psO[0:64, 0:64], l=v_cm_h[:, vsl],
                               r=Atile[:]: e.matmul(o, l, r, start=False, stop=True),
                               mmb=0, inc="pe")
                if cmask & 4:
                    pet = k.op("tensor", lambda e, o=psS[:, 0:64], l=k2tok[:],
                               r=v_cm_h[:, vsl]:
                               e.matmul(o, l, r, start=True, stop=True),
                               inc="pe", mmb=0)
                if pet is not None:
                    k.wait("vector", "pe", pet)
                k.op("vector", lambda e, o=O_h[:, csl], i_=psO[0:64, 0:64]:
                     e.tensor_copy(o, i_))
                k.op("vector", lambda e, o=S_h[:], i_=S_h[:], s=EQd[:, c:c + 1],
                     p_=psS[:, 0:64]:
                     e.scalar_tensor_tensor(o, i_, s, p_, ALU.mult, ALU.add))
                dvt = k.op("vector", lambda e, o=S_hb[:], i_=S_h[:]:
                           e.tensor_copy(o, i_), inc="dv")
                k.wait("tensor", "dv", dvt)
            k.op("vector", lambda e: e.engine_nop(), inc="dv")
            k.wait("sync", "dv", k.cnt["dv"])
            r0 = (2 * p + hh) * 64
            ot = k.dma("sync", o_s[r0:r0 + 64, b * N:(b + 1) * N], O_h[:], "st")
            k.wait("vector", "st", ot)

    if not dbg_a3:
        fin = k.fence("vector", "dv")
        k.wait("sync", "dv", fin)
        k.wait("sync", "st", k.cnt["st"])
        return
    k.srewind(m_gla)
    wg1 = k.salloc([128, DCH * 16], BF16)
    wg2 = k.salloc([16, D], BF16)
    wo = k.salloc([128, DCH * D], BF16)
    nrm = k.salloc([128, DCH], F32)
    t1 = k.salloc([16, T], BF16)
    gt = k.salloc([128, 512], BF16)
    og = k.salloc([128, DCH * 512], BF16)
    osl = k.salloc([128, 512], BF16)
    sq = k.salloc([128, 512], BF16)
    ssq = k.salloc([1, 512], F32)
    rr = k.salloc([1, 512], F32)
    xold = k.salloc([128, 2 * 512], F32)
    xnew = k.salloc([128, 2 * 512], F32)

    k.wait("gpsimd", "pe", k.cnt["pe"])
    k.dma("gpsimd", wg1[:].rearrange("p (a c) -> p a c", c=16),
          wg1_d.rearrange("(a p) c -> p a c", p=128), "wld")
    k.dma("gpsimd", wg2[:], wg2_d, "wld")
    k.dma("gpsimd", wo[:].rearrange("p (a c) -> p a c", c=D),
          wo_d.rearrange("(a p) c -> p a c", p=128), "wld")
    for j in range(DCH):
        k.dma("gpsimd", nrm[:, j:j + 1],
              normw_d[j * 128:(j + 1) * 128].unsqueeze(1), "wld")
    wld3 = k.cnt["wld"]
    k.wait("tensor", "wld", wld3)
    k.wait("vector", "wld", wld3)

    for ns in range(T // 512):
        bank = psP if ns % 2 == 0 else psP2
        for a in range(DCH):
            pet = k.op("tensor", lambda e, o=bank[0:16, :],
                       l=wg1[:, a * 16:(a + 1) * 16],
                       r=ln0[:, a * T + ns * 512: a * T + ns * 512 + 512],
                       s=(a == 0), p=(a == DCH - 1):
                       e.matmul(o, l, r, start=s, stop=p), inc="pe")
        k.wait("vector", "pe", pet)
        dvt = k.op("vector", lambda e, o=t1[:, ns * 512:(ns + 1) * 512],
                   i_=bank[0:16, :]: e.tensor_copy(o, i_), inc="dv")
        k.wait("tensor", "dv", dvt)

    og_prev = 0
    sqmm_prev = 0
    for ts in range(T // 512):
        tsl = slice(ts * 512, (ts + 1) * 512)
        for j in range(DCH):
            bank = psP
            pet = k.op("tensor", lambda e, o=bank[:], l=wg2[:, j * 128:(j + 1) * 128],
                       r=t1[:, tsl]: e.matmul(o, l, r, start=True, stop=True),
                       inc="pe")
            k.wait("scalar", "pe", pet)
            k.wait("scalar", "dv", og_prev)
            act = k.op("scalar", lambda e, o=gt[:], i_=bank[:]:
                       e.activation(o, i_, AF.Sigmoid, bias=0.0, scale=1.0),
                       inc="ac")
            k.wait("tensor", "ac", act)
            k.wait("sync", "dv", og_prev)
            lo = k.dma("sync", osl[:], o_s[j * 128:(j + 1) * 128, tsl], "ld")
            k.wait("vector", "ld", lo)
            k.wait("vector", "ac", act)
            og_prev = k.op("vector", lambda e, o=og[:, j * 512:(j + 1) * 512],
                           a=osl[:], b_=gt[:]: e.tensor_mul(o, a, b_), inc="dv")
            k.wait("scalar", "dv", og_prev)
            k.wait("scalar", "pe", sqmm_prev)
            act2 = k.op("scalar", lambda e, o=sq[:], i_=og[:, j * 512:(j + 1) * 512]:
                        e.activation(o, i_, AF.Square, bias=0.0, scale=1.0),
                        inc="ac")
            k.wait("tensor", "ac", act2)
            sqmm_prev = k.op("tensor", lambda e, o=psP2[0:1, :],
                             l=cst["ones_col_bf"][:], r=sq[:],
                             s=(j == 0), p=(j == DCH - 1):
                             e.matmul(o, l, r, start=s, stop=p), inc="pe")
        k.wait("vector", "pe", k.cnt["pe"])
        k.op("vector", lambda e, o=ssq[:], i_=psP2[0:1, :]: e.tensor_copy(o, i_))
        k.op("vector", lambda e, o=ssq[:], i_=ssq[:]:
             e.tensor_scalar(o, i_, 1.0 / D, 1e-6, ALU.mult, ALU.add), inc="dv")
        k.wait("scalar", "dv", k.cnt["dv"])
        k.op("scalar", lambda e, o=rr[:], i_=ssq[:]:
             e.activation(o, i_, AF.Sqrt, bias=0.0, scale=1.0), inc="ac")
        k.wait("vector", "ac", k.cnt["ac"])
        k.op("vector", lambda e, o=rr[:], i_=rr[:]: e.reciprocal(o, i_), inc="dv")
        k.wait("tensor", "dv", k.cnt["dv"])
        pet = k.op("tensor", lambda e, o=psP2[:], l=cst["ones_row_f32"][:], r=rr[:]:
                   e.matmul(o, l, r, start=True, stop=True), inc="pe")
        k.wait("vector", "pe", pet)
        for j in range(DCH):
            k.op("vector", lambda e, o=og[:, j * 512:(j + 1) * 512],
                 i_=og[:, j * 512:(j + 1) * 512], s=nrm[:, j:j + 1], p_=psP2[:]:
                 e.scalar_tensor_tensor(o, i_, s, p_, ALU.mult, ALU.mult))
        dvt = k.op("vector", lambda e: e.engine_nop(), inc="dv")
        k.wait("tensor", "dv", dvt)
        for j2 in range(DCH):
            bank = psP
            for a in range(DCH):
                pet = k.op("tensor", lambda e, o=bank[:],
                           l=wo[:, a * D + j2 * 128: a * D + j2 * 128 + 128],
                           r=og[:, a * 512:(a + 1) * 512],
                           s=(a == 0), p=(a == DCH - 1):
                           e.matmul(o, l, r, start=s, stop=p), inc="pe")
            buf = j2 % 2
            lo = k.dma("sync", xold[:, buf * 512:(buf + 1) * 512],
                       xr0[j2 * 128:(j2 + 1) * 128, tsl], "ld")
            k.wait("vector", "pe", pet)
            k.wait("vector", "ld", lo)
            k.op("vector", lambda e, o=xnew[:, buf * 512:(buf + 1) * 512],
                 a=bank[:], b_=xold[:, buf * 512:(buf + 1) * 512]:
                 e.tensor_add(o, a, b_), inc="dv")
            k.wait("tensor", "dv", k.cnt["dv"])
            k.wait("sync", "dv", k.cnt["dv"])
            k.dma("sync", xr1[j2 * 128:(j2 + 1) * 128, tsl],
                  xnew[:, buf * 512:(buf + 1) * 512], "st")
    if os.environ.get("DBG_A3D"):
        dmp3 = nc.dram_tensor("dumpa3", [4, 128, 4096], F32,
                              kind="ExternalOutput").ap()
        f2 = k.fence("vector", "dv")
        k.wait("sync", "dv", f2)
        k.wait("gpsimd", "dv", f2)
        k.dma("gpsimd", dmp3[0], og[:], "wld")
        k.dma("gpsimd", dmp3[1][:, 0:512], gt[:], "wld")
        k.dma("sync", dmp3[2][0:1, 0:512], rr[:], "st")
        k.dma("sync", dmp3[3][:, 0:1024], xold[:], "st")
        k.wait("sync", "wld", k.cnt["wld"])
    fin = k.fence("vector", "dv")
    k.wait("sync", "dv", fin)
    k.wait("sync", "st", k.cnt["st"])



def new_program():
    nc = bass.Bass("TRN2", target_bir_lowering=False,
                   detect_race_conditions=False)
    k = K(nc)
    for s in BASE_SEMS:
        k.sem(s)
    bt = nc.alloc_sbuf_tensor_at("bar_tile", [1, 8], F32, offset=ARENA_BASE - 64)
    k.bar_tile = bt[0:1, 0:1]
    k.q["scalar"].append(("op", lambda e, o=k.bar_tile: e.memzero(o), None, 1))
    k.pb2k = [nc.alloc_psum_tensor(f"pb2k{i}", [128, 1024], F32) for i in range(4)]

    class _V:
        def __init__(self, ap):
            self._ap = ap

        def __getitem__(self, key):
            return self._ap[key]

    k.pbank = [_V(k.pb2k[i // 2][:][:, (i % 2) * 512:(i % 2 + 1) * 512])
               for i in range(8)]
    return nc, k


def build_phase_a_program():
    nc, k = new_program()
    x_in = nc.dram_tensor("x", [T, D], F32, kind="ExternalInput").ap()
    wq_d = nc.dram_tensor("Wq", [D, D], F32, kind="ExternalInput").ap()
    wk_d = nc.dram_tensor("Wk", [D, D], F32, kind="ExternalInput").ap()
    wv_d = nc.dram_tensor("Wv", [D, D], F32, kind="ExternalInput").ap()
    wo_d = nc.dram_tensor("Wo", [D, D], F32, kind="ExternalInput").ap()
    wg1_d = nc.dram_tensor("Wg1", [D, 16], F32, kind="ExternalInput").ap()
    wg2_d = nc.dram_tensor("Wg2", [16, D], F32, kind="ExternalInput").ap()
    normw_d = nc.dram_tensor("norm_w", [D], F32, kind="ExternalInput").ap()
    xr0 = nc.dram_tensor("xr0", [D, T], F32, kind="Internal").ap()
    xr1 = nc.dram_tensor("xr1", [D, T], F32, kind="ExternalOutput").ap()
    import os as _os
    o_s = nc.dram_tensor("o_s", [D, T], BF16,
                         kind="ExternalOutput" if _os.environ.get("DBG_OS")
                         else "Internal").ap()
    cst = stage_consts(nc, k)
    for eng in ENGINES:
        k.wait(eng, "ld", k.cnt["ld"])
    k.freeze_persistent()
    k.set_phase("pa")
    import os
    dbg_bp = int(os.environ.get("DBG_BP", "99"))
    dbg_a3 = os.environ.get("DBG_A3", "1") == "1"
    emit_phase_a(nc, k, cst, x_in, wq_d, wk_d, wv_d, wo_d, wg1_d, wg2_d,
                 normw_d, xr0, xr1, o_s, dbg_bp=dbg_bp, dbg_a3=dbg_a3)
    k.build()
    return nc


_PROGS = {}


def _prog(name, builder):
    if name not in _PROGS:
        _PROGS[name] = builder()
    return _PROGS[name]


def kernel(**inputs):
    import os as _os
    if _os.environ.get("UNFUSED"):
        return kernel_unfused(**inputs)
    return kernel_fused(**inputs)


def kernel_unfused(**inputs):
    consts = host_constants()
    NCORES = 8
    x_full = inputs["L0_attn_x"]
    x_slices = [np.ascontiguousarray(
        x_full[c * B_CORE:(c + 1) * B_CORE].reshape(T, D)) for c in range(NCORES)]

    ncA = _prog("a", build_phase_a_program)
    maps = []
    for c in range(NCORES):
        m = dict(consts)
        m["x"] = x_slices[c]
        for nm, src in (("Wq", "L0_attn_Wq"), ("Wk", "L0_attn_Wk"),
                        ("Wv", "L0_attn_Wv"), ("Wo", "L0_attn_Wo"),
                        ("Wg1", "L0_attn_Wg1"), ("Wg2", "L0_attn_Wg2"),
                        ("norm_w", "L0_attn_norm_w")):
            m[nm] = inputs[src]
        maps.append(m)
    resA = run_bass_kernel_spmd(ncA, maps, core_ids=list(range(NCORES)))
    xr1s = [r["xr1"] for r in resA.results]

    ncB = _prog("b", build_phase_b_program)
    maps = []
    for c in range(NCORES):
        m = dict(consts)
        m["xr1"] = xr1s[c]
        for nm, src in (("router", "L0_ffn_router_DE"), ("w1", "L0_ffn_w1_eDF"),
                        ("w3", "L0_ffn_w3_eDF"), ("w2", "L0_ffn_w2_eFD"),
                        ("ws1", "L0_ffn_ws1_DF"), ("ws3", "L0_ffn_ws3_DF"),
                        ("ws2", "L0_ffn_ws2_FD")):
            m[nm] = inputs[src]
        maps.append(m)
    resB = run_bass_kernel_spmd(ncB, maps, core_ids=list(range(NCORES)))
    xr2s = [r["xr2"] for r in resB.results]

    ncC = _prog("c", build_phase_c_program)
    maps = []
    for c in range(NCORES):
        m = dict(consts)
        m["xr2"] = xr2s[c]
        for nm, src in (("w_in", "L1_attn_w_in"), ("conv_w", "L1_attn_conv_w"),
                        ("conv_b", "L1_attn_conv_b"), ("w_B", "L1_attn_w_B"),
                        ("w_C", "L1_attn_w_C"), ("w_out", "L1_attn_w_out")):
            m[nm] = inputs[src]
        maps.append(m)
    resC = run_bass_kernel_spmd(ncC, maps, core_ids=list(range(NCORES)))
    x3ts = [r["x3t"] for r in resC.results]

    ncD = _prog("d", build_phase_d_program)
    maps = []
    for c in range(NCORES):
        m = dict(consts)
        m["x3t"] = x3ts[c]
        for nm, src in (("w_gate", "L1_ffn_w_gate"), ("w_up", "L1_ffn_w_up"),
                        ("w_down", "L1_ffn_w_down")):
            m[nm] = inputs[src]
        maps.append(m)
    resD = run_bass_kernel_spmd(ncD, maps, core_ids=list(range(NCORES)))
    outs = [np.asarray(r["out"]) for r in resD.results]
    return np.concatenate(outs, axis=0).reshape(16, N, D).astype(np.float32)



F_HALF = 1408
FC_HALF = 11
NS_B = 4


def emit_phase_b(nc, k, cst, xr1, router_d, w1_d, w3_d, w2_d, ws1_d, ws3_d,
                 ws2_d, xr2, lnS, alpha_d):
    k.sreset()
    m_b1 = k.smark()
    xsl = k.salloc([128, 512], F32)
    lnf = [k.salloc([128, 512], F32) for _ in range(DCH)]
    xb = k.salloc([128, 512], BF16)
    xsqb = k.salloc([128, 512], BF16)
    row = k.salloc([1, 512 * 4], F32)
    Rb = k.salloc([128, 512], F32)
    lnbf = k.salloc([128, 512], BF16)
    alpha = k.palloc_dummy if False else None
    alpha = k.salloc([8, T], F32)
    router = k.salloc([128, DCH * 8], F32)
    scst = k.salloc([128, 19], F32)

    psA_ = k.pbank[0][:]
    psB_ = k.pbank[1][:]
    psM = k.pbank[2][:]
    psR = k.pbank[3][:]
    psS_ = k.pbank[4][:]

    k.dma("sync", router[:].rearrange("p (a c) -> p a c", c=8),
          router_d.rearrange("(a p) c -> p a c", p=128), "ld")
    b1_ld = k.cnt["ld"]
    k.wait("tensor", "ld", b1_ld)

    prev_copy_dv = 0
    prev_mm_pe = 0
    for ts in range(T // 512):
        tsl = slice(ts * 512, (ts + 1) * 512)
        for j in range(DCH):
            k.wait("sync", "dv", prev_copy_dv)
            ldt = k.dma("sync", xsl[:], xr1[j * 128:(j + 1) * 128, tsl], "ld")
            k.wait("vector", "ld", ldt)
            k.wait("vector", "pe", prev_mm_pe)
            dvt = k.op("vector", lambda e, o=xb[:], i_=xsl[:]:
                       e.tensor_copy(o, i_), inc="dv")
            k.wait("scalar", "dv", dvt)
            k.wait("scalar", "pe", prev_mm_pe)
            act = k.op("scalar", lambda e, o=xsqb[:], i_=xb[:]:
                       e.activation(o, i_, AF.Square, bias=0.0, scale=1.0),
                       inc="ac")
            k.wait("tensor", "dv", dvt)
            k.wait("tensor", "ac", act)
            k.op("tensor", lambda e, o=psA_[0:1, :], l=cst["ones_col_bf"][:],
                 r=xb[:], s=(j == 0), q=(j == DCH - 1):
                 e.matmul(o, l, r, start=s, stop=q), mmb=0)
            prev_mm_pe = k.op("tensor", lambda e, o=psB_[0:1, :],
                              l=cst["ones_col_bf"][:],
                              r=xsqb[:], s=(j == 0), q=(j == DCH - 1):
                              e.matmul(o, l, r, start=s, stop=q), inc="pe", mmb=0)
            prev_copy_dv = k.op("vector", lambda e, o=lnf[j][:], i_=xsl[:]:
                                e.tensor_copy(o, i_), inc="dv")
        k.wait("vector", "pe", k.cnt["pe"])
        r_m = row[:, 0:512]
        r_msq = row[:, 512:1024]
        r_var = row[:, 1024:1536]
        r_rstd = row[:, 1536:2048]
        k.vop_s(lambda e, o=r_m, i_=psA_[0:1, :]:
                e.tensor_scalar(o, i_, 1.0 / D, None, ALU.mult))
        k.vop_s(lambda e, o=r_msq, i_=psB_[0:1, :]:
                e.tensor_scalar(o, i_, 1.0 / D, None, ALU.mult))
        k.vop_s(lambda e, o=r_var, a=r_m, b_=r_m: e.tensor_mul(o, a, b_))
        k.vop_s(lambda e, o=r_rstd, a=r_msq, b_=r_var: e.tensor_sub(o, a, b_))
        k.vop_s(lambda e, o=r_var, i_=r_rstd:
                e.tensor_scalar(o, i_, 1e-5, None, ALU.add))
        k.wait("scalar", "dv", k.cnt["dv"])
        sct = k.op("scalar", lambda e, o=r_rstd, i_=r_var:
                   e.activation(o, i_, AF.Sqrt, bias=0.0, scale=1.0), inc="ac")
        k.wait("vector", "ac", sct)
        k.vop_s(lambda e, o=r_var, i_=r_rstd: e.reciprocal(o, i_))
        k.wait("tensor", "dv", k.cnt["dv"])
        k.op("tensor", lambda e, o=psM[:], l=cst["ones_row_f32"][:], r=r_m:
             e.matmul(o, l, r, start=True, stop=True), mmb=0)
        pet = k.op("tensor", lambda e, o=psR[:], l=cst["ones_row_f32"][:],
                   r=r_var: e.matmul(o, l, r, start=True, stop=True),
                   inc="pe", mmb=0)
        k.wait("vector", "pe", pet)
        k.op("vector", lambda e, o=Rb[:], i_=psR[:]: e.tensor_copy(o, i_))
        for j in range(DCH):
            k.op("vector", lambda e, o=lnf[j][:], a=lnf[j][:], b_=psM[:]:
                 e.tensor_sub(o, a, b_))
            k.op("vector", lambda e, o=lnf[j][:], a=lnf[j][:], b_=Rb[:]:
                 e.tensor_mul(o, a, b_))
            dvt = k.op("vector", lambda e, o=lnbf[:], i_=lnf[j][:]:
                       e.tensor_copy(o, i_), inc="dv")
            k.wait("sync", "dv", dvt)
            stt = k.dma("sync", lnS[j * 128:(j + 1) * 128, tsl], lnbf[:], "st")
            k.wait("vector", "st", stt)
        k.op("vector", lambda e: e.engine_nop(), inc="dv")
        k.wait("tensor", "dv", k.cnt["dv"])
        for tt in range(4):
            ttl = slice(tt * 128, (tt + 1) * 128)
            for j in range(DCH):
                pet = k.op("tensor", lambda e, o=psS_[:, 0:8], l=lnf[j][:, ttl],
                           r=router[:, j * 8:(j + 1) * 8], s=(j == 0),
                           q=(j == DCH - 1): e.matmul(o, l, r, start=s, stop=q),
                           inc="pe", mmb=0)
            prev_mm_pe = pet
            k.wait("vector", "pe", pet)
            k.vop_s(lambda e, o=scst[:, 0:1], i_=psS_[:, 0:8]:
                    e.reduce_max(o, i_, AX.X))
            k.wait("scalar", "dv", k.cnt["dv"])
            sct = k.op("scalar", lambda e, o=scst[:, 1:2], i_=scst[:, 0:1]:
                       e.activation(o, i_, AF.Sigmoid, bias=0.0, scale=1.0),
                       inc="ac")
            k.vop_s(lambda e, o=scst[:, 2:3], i_=scst[:, 0:1]:
                    e.tensor_copy(o, i_))
            k.op("vector", lambda e, o=scst[:, 3:11], i_=psS_[:, 0:8],
                 m=scst[:, 0:1]:
                 e.tensor_scalar(o, i_, m, None, ALU.is_equal))
            k.wait("vector", "ac", sct)
            k.vop_s(lambda e, o=scst[:, 11:19], i_=scst[:, 3:11], g=scst[:, 1:2]:
                    e.tensor_scalar(o, i_, g, None, ALU.mult))
            k.wait("tensor", "dv", k.cnt["dv"])
            pet = k.op("tensor", lambda e, o=psS_[0:8, 8:136], i_=scst[:, 11:19],
                       idn=cst["ident_f32"][:]: e.transpose(o, i_, idn),
                       inc="pe", mmb=0)
            k.wait("vector", "pe", pet)
            dvt = k.op("vector", lambda e, o=alpha[:, ts * 512 + tt * 128:
                                               ts * 512 + (tt + 1) * 128],
                       i_=psS_[0:8, 8:136]: e.tensor_copy(o, i_), inc="dv")
            k.wait("tensor", "dv", dvt)
            prev_mm_pe = pet
    b1_done = k.fence("vector", "dv")
    k.wait("gpsimd", "dv", b1_done)
    k.wait("sync", "dv", b1_done)
    k.dma("gpsimd", alpha_d, alpha[:], "wld")
    k.wait("sync", "wld", k.cnt["wld"])
    k.wait("sync", "st", k.cnt["st"])

    k.srewind(m_b1)
    alpha2 = alpha
    out_sb = [k.salloc([128, T], BF16) for _ in range(DCH)]
    m_b2 = k.smark()
    w1h = k.salloc([128, DCH * F_HALF], BF16)
    w3h = k.salloc([128, DCH * F_HALF], BF16)
    w2h = k.salloc([128, FC_HALF * 1024], BF16)
    midh = k.salloc([128, FC_HALF * 1024], BF16)
    xe = k.salloc([128, DCH * 1024], BF16)
    arow = k.salloc([1, T], BF16)
    Ae = k.salloc([128, T], BF16)
    lnt2 = k.salloc([128, 2 * 1024], BF16)
    sil = k.salloc([128, 1024], BF16)

    psU = k.pb2k[1][:]
    psW = k.pb2k[2][:]
    psD = k.pb2k[3][:]

    xe_dv = [0, 0]
    xe_pe = 0
    for e_i in range(9):
        shared = (e_i == 8)
        if not shared:
            al = k.dma("sync", arow[:], alpha_d[e_i:e_i + 1, :], "ld")
            k.wait("tensor", "ld", al)
            for ns2 in range(DCH):
                n2 = slice(ns2 * 512, (ns2 + 1) * 512)
                pet = k.op("tensor", lambda e, o=psM[:], l=cst["ones_row_bf"][:],
                           r=arow[0:1, n2]:
                           e.matmul(o, l, r, start=True, stop=True),
                           inc="pe", mmb=0)
                k.wait("vector", "pe", pet)
                dvt = k.op("vector", lambda e, o=Ae[:, n2], i_=psM[:]:
                           e.tensor_copy(o, i_), inc="dv")
                k.wait("tensor", "dv", dvt)
        for fh in range(2):
            f0 = fh * F_HALF
            k.wait("gpsimd", "pe", k.cnt["pe"])
            for kc in range(DCH):
                if shared:
                    srcw1 = ws1_d[kc * 128:(kc + 1) * 128, f0:f0 + F_HALF]
                    srcw3 = ws3_d[kc * 128:(kc + 1) * 128, f0:f0 + F_HALF]
                else:
                    srcw1 = w1_d[e_i, kc * 128:(kc + 1) * 128, f0:f0 + F_HALF]
                    srcw3 = w3_d[e_i, kc * 128:(kc + 1) * 128, f0:f0 + F_HALF]
                k.dma("gpsimd", w1h[:, kc * F_HALF:(kc + 1) * F_HALF], srcw1, "wld")
                k.dma("gpsimd", w3h[:, kc * F_HALF:(kc + 1) * F_HALF], srcw3, "wld")
            for kc in range(FC_HALF):
                if shared:
                    srcw2 = ws2_d[f0 + kc * 128:f0 + (kc + 1) * 128, :]
                else:
                    srcw2 = w2_d[e_i, f0 + kc * 128:f0 + (kc + 1) * 128, :]
                k.dma("gpsimd", w2h[:, kc * 1024:(kc + 1) * 1024], srcw2, "wld")
            wld_t = k.cnt["wld"]
            k.wait("tensor", "wld", wld_t)
            for ns in range(NS_B):
                nsl = slice(ns * 1024, (ns + 1) * 1024)
                for j in range(DCH):
                    buf = j % 2
                    lslice = lnt2[:, buf * 1024:(buf + 1) * 1024]
                    k.wait("sync", "dv", xe_dv[buf])
                    ldt = k.dma("sync", lslice, lnS[j * 128:(j + 1) * 128, nsl],
                                "ld")
                    k.wait("vector", "ld", ldt)
                    k.wait("vector", "pe", xe_pe)
                    if shared:
                        dvt = k.op("vector", lambda e,
                                   o=xe[:, j * 1024:(j + 1) * 1024], i_=lslice:
                                   e.tensor_copy(o, i_), inc="dv")
                    else:
                        dvt = k.op("vector", lambda e,
                                   o=xe[:, j * 1024:(j + 1) * 1024], a=lslice,
                                   b_=Ae[:, nsl]: e.tensor_mul(o, a, b_),
                                   inc="dv")
                    xe_dv[buf] = dvt
                k.wait("tensor", "dv", k.cnt["dv"])
                for f in range(FC_HALF):
                    for half in range(2):
                        hsl = slice(half * 512, (half + 1) * 512)
                        for kc in range(DCH):
                            k.op("tensor", lambda e, o=psU[:, hsl],
                                 l=w1h[:, kc * F_HALF + f * 128:
                                       kc * F_HALF + (f + 1) * 128],
                                 r=xe[:, kc * 1024 + half * 512:
                                       kc * 1024 + (half + 1) * 512],
                                 s=(kc == 0), q=(kc == DCH - 1):
                                 e.matmul(o, l, r, start=s, stop=q), mmb=0)
                        for kc in range(DCH):
                            pet = k.op("tensor", lambda e, o=psW[:, hsl],
                                       l=w3h[:, kc * F_HALF + f * 128:
                                             kc * F_HALF + (f + 1) * 128],
                                       r=xe[:, kc * 1024 + half * 512:
                                             kc * 1024 + (half + 1) * 512],
                                       s=(kc == 0), q=(kc == DCH - 1):
                                       e.matmul(o, l, r, start=s, stop=q),
                                       inc="pe", mmb=0)
                    xe_pe = pet
                    k.wait("scalar", "pe", pet)
                    act = k.op("scalar", lambda e, o=sil[:], i_=psU[:]:
                               e.activation(o, i_, AF.Silu, bias=0.0, scale=1.0),
                               inc="ac")
                    k.wait("vector", "ac", act)
                    dvt = k.op("vector", lambda e,
                               o=midh[:, f * 1024:(f + 1) * 1024], a=sil[:],
                               b_=psW[:]: e.tensor_mul(o, a, b_), inc="dv")
                    k.wait("tensor", "dv", dvt)
                for j in range(DCH):
                    for half in range(2):
                        hsl = slice(half * 512, (half + 1) * 512)
                        for kc in range(FC_HALF):
                            pet = k.op("tensor", lambda e, o=psD[:, hsl],
                                       l=w2h[:, kc * 1024 + j * 128:
                                             kc * 1024 + (j + 1) * 128],
                                       r=midh[:, kc * 1024 + half * 512:
                                              kc * 1024 + (half + 1) * 512],
                                       s=(kc == 0), q=(kc == FC_HALF - 1):
                                       e.matmul(o, l, r, start=s, stop=q),
                                       inc="pe", mmb=0)
                    k.wait("vector", "pe", pet)
                    if e_i == 0 and fh == 0:
                        dvt = k.op("vector", lambda e, o=out_sb[j][:, nsl],
                                   i_=psD[:]: e.tensor_copy(o, i_), inc="dv")
                    else:
                        dvt = k.op("vector", lambda e, o=out_sb[j][:, nsl],
                                   a=out_sb[j][:, nsl], b_=psD[:]:
                                   e.tensor_add(o, a, b_), inc="dv")
                    k.wait("tensor", "dv", dvt)

    k.srewind(m_b2)
    xo = k.salloc([128, 2 * 512], F32)
    xn = k.salloc([128, 2 * 512], F32)
    b2_done = k.fence("vector", "dv")
    k.wait("sync", "dv", b2_done)
    for ts in range(T // 512):
        tsl = slice(ts * 512, (ts + 1) * 512)
        for j in range(DCH):
            buf = j % 2
            ldt = k.dma("sync", xo[:, buf * 512:(buf + 1) * 512],
                        xr1[j * 128:(j + 1) * 128, tsl], "ld")
            k.wait("vector", "ld", ldt)
            dvt = k.op("vector", lambda e, o=xn[:, buf * 512:(buf + 1) * 512],
                       a=xo[:, buf * 512:(buf + 1) * 512],
                       b_=out_sb[j][:, tsl]: e.tensor_add(o, a, b_), inc="dv")
            k.wait("sync", "dv", dvt)
            k.dma("sync", xr2[j * 128:(j + 1) * 128, tsl],
                  xn[:, buf * 512:(buf + 1) * 512], "st")
    fin = k.fence("vector", "dv")
    k.wait("sync", "dv", fin)
    k.wait("sync", "st", k.cnt["st"])


def build_phase_b_program():
    nc, k = new_program()
    xr1 = nc.dram_tensor("xr1", [D, T], F32, kind="ExternalInput").ap()
    router_d = nc.dram_tensor("router", [D, 8], F32, kind="ExternalInput").ap()
    w1_d = nc.dram_tensor("w1", [8, D, 2816], F32, kind="ExternalInput").ap()
    w3_d = nc.dram_tensor("w3", [8, D, 2816], F32, kind="ExternalInput").ap()
    w2_d = nc.dram_tensor("w2", [8, 2816, D], F32, kind="ExternalInput").ap()
    ws1_d = nc.dram_tensor("ws1", [D, 2816], F32, kind="ExternalInput").ap()
    ws3_d = nc.dram_tensor("ws3", [D, 2816], F32, kind="ExternalInput").ap()
    ws2_d = nc.dram_tensor("ws2", [2816, D], F32, kind="ExternalInput").ap()
    xr2 = nc.dram_tensor("xr2", [D, T], F32, kind="ExternalOutput").ap()
    lnS = nc.dram_tensor("lnS", [D, T], BF16, kind="Internal").ap()
    alpha_d = nc.dram_tensor("alpha_d", [8, T], BF16, kind="Internal").ap()
    cst = stage_consts(nc, k)
    for eng in ENGINES:
        k.wait(eng, "ld", k.cnt["ld"])
    k.freeze_persistent()
    k.set_phase("pb")
    emit_phase_b(nc, k, cst, xr1, router_d, w1_d, w3_d, w2_d, ws1_d, ws3_d,
                 ws2_d, xr2, lnS, alpha_d)
    k.build()
    return nc



DI = 2048
DIC = DI // 128


def emit_ln_resident(nc, k, cst, src, dst, psA_, psB_, psM, psR, scratch):
    xsl, xb, xsqb, row, Rb = scratch
    prev_copy_dv = 0
    prev_mm_pe = 0
    for ts in range(T // 512):
        tsl = slice(ts * 512, (ts + 1) * 512)
        lnf_dv = 0
        for j in range(DCH):
            k.wait("sync", "dv", prev_copy_dv)
            ldt = k.dma("sync", xsl[:, j * 512:(j + 1) * 512],
                        src[j * 128:(j + 1) * 128, tsl], "ld")
            k.wait("vector", "ld", ldt)
            k.wait("vector", "pe", prev_mm_pe)
            dvt = k.op("vector", lambda e, o=xb[:], i_=xsl[:, j * 512:(j + 1) * 512]:
                       e.tensor_copy(o, i_), inc="dv")
            k.wait("scalar", "dv", dvt)
            k.wait("scalar", "pe", prev_mm_pe)
            act = k.op("scalar", lambda e, o=xsqb[:], i_=xb[:]:
                       e.activation(o, i_, AF.Square, bias=0.0, scale=1.0),
                       inc="ac")
            k.wait("tensor", "dv", dvt)
            k.wait("tensor", "ac", act)
            k.op("tensor", lambda e, o=psA_[0:1, :], l=cst["ones_col_bf"][:],
                 r=xb[:], s=(j == 0), q=(j == DCH - 1):
                 e.matmul(o, l, r, start=s, stop=q), mmb=0)
            prev_mm_pe = k.op("tensor", lambda e, o=psB_[0:1, :],
                              l=cst["ones_col_bf"][:],
                              r=xsqb[:], s=(j == 0), q=(j == DCH - 1):
                              e.matmul(o, l, r, start=s, stop=q), inc="pe",
                              mmb=0)
            prev_copy_dv = k.cnt["dv"]
        k.wait("vector", "pe", k.cnt["pe"])
        r_m = row[:, 0:512]
        r_msq = row[:, 512:1024]
        r_var = row[:, 1024:1536]
        r_rstd = row[:, 1536:2048]
        k.vop_s(lambda e, o=r_m, i_=psA_[0:1, :]:
                e.tensor_scalar(o, i_, 1.0 / D, None, ALU.mult))
        k.vop_s(lambda e, o=r_msq, i_=psB_[0:1, :]:
                e.tensor_scalar(o, i_, 1.0 / D, None, ALU.mult))
        k.vop_s(lambda e, o=r_var, a=r_m, b_=r_m: e.tensor_mul(o, a, b_))
        k.vop_s(lambda e, o=r_rstd, a=r_msq, b_=r_var: e.tensor_sub(o, a, b_))
        k.vop_s(lambda e, o=r_var, i_=r_rstd:
                e.tensor_scalar(o, i_, 1e-5, None, ALU.add))
        k.wait("scalar", "dv", k.cnt["dv"])
        sct = k.op("scalar", lambda e, o=r_rstd, i_=r_var:
                   e.activation(o, i_, AF.Sqrt, bias=0.0, scale=1.0), inc="ac")
        k.wait("vector", "ac", sct)
        k.vop_s(lambda e, o=r_var, i_=r_rstd: e.reciprocal(o, i_))
        k.wait("tensor", "dv", k.cnt["dv"])
        k.op("tensor", lambda e, o=psM[:], l=cst["ones_row_f32"][:], r=r_m:
             e.matmul(o, l, r, start=True, stop=True), mmb=0)
        pet = k.op("tensor", lambda e, o=psR[:], l=cst["ones_row_f32"][:],
                   r=r_var: e.matmul(o, l, r, start=True, stop=True),
                   inc="pe", mmb=0)
        prev_mm_pe = pet
        k.wait("vector", "pe", pet)
        k.op("vector", lambda e, o=Rb[:], i_=psR[:]: e.tensor_copy(o, i_))
        for j in range(DCH):
            k.op("vector", lambda e, o=xsl[:, j * 512:(j + 1) * 512],
                 a=xsl[:, j * 512:(j + 1) * 512], b_=psM[:]:
                 e.tensor_sub(o, a, b_))
            dvt = k.op("vector", lambda e, o=dst[:, j * T + ts * 512:
                                              j * T + (ts + 1) * 512],
                       a=xsl[:, j * 512:(j + 1) * 512], b_=Rb[:]:
                       e.tensor_mul(o, a, b_), inc="dv")
        prev_copy_dv = k.cnt["dv"]
    done = k.fence("vector", "dv")
    for eng in ("tensor", "sync", "gpsimd", "scalar"):
        k.wait(eng, "dv", done)


def emit_phase_c(nc, k, cst, xr2, win_d, convw_d, convb_d, wB_d, wC_d, wout_d,
                 x3t):
    k.sreset()
    ln2 = k.salloc([128, DCH * T], BF16)
    xsl = k.salloc([128, DCH * 512], F32)
    xb = k.salloc([128, 512], BF16)
    xsqb = k.salloc([128, 512], BF16)
    row = k.salloc([1, 2048], F32)
    Rb = k.salloc([128, 512], F32)

    k09c = k.salloc([16, N], F32)
    _k09t = k.dma("sync", k09c[:], cst["k09c_dram"], "ld")
    k.wait("vector", "ld", _k09t)
    cst = dict(cst)
    cst["k09c"] = k09c

    psA_ = k.pbank[0][:]
    psB_ = k.pbank[1][:]
    psM = k.pbank[2][:]
    psR = k.pbank[3][:]
    emit_ln_resident(nc, k, cst, xr2, ln2, psA_, psB_, psM, psR,
                     (xsl, xb, xsqb, row, Rb))

    wsl = k.salloc([128, DCH * 128], BF16)
    convw = k.salloc([128, DIC * 4], F32)
    convb = k.salloc([128, DIC], F32)
    wB = k.salloc([128, DIC * 16], BF16)
    wC = k.salloc([16, DI], BF16)
    woutb = k.salloc([128, 2 * DIC * 128], BF16)
    xpb = k.salloc([128, 4 + N], F32)
    xcc = k.salloc([128, N], BF16)
    Bsb = k.salloc([16, N], F32)
    Hb = k.salloc([16, N], BF16)
    zsil = k.salloc([128, 512], BF16)
    ygns = k.salloc([128, DIC * 512], BF16)
    xns = [k.salloc([128, 512], F32) for _ in range(DCH)]
    xold = k.salloc([128, 2 * 512], F32)
    xtok = k.salloc([128, D], F32)

    for kc in range(DIC):
        k.dma("gpsimd", convw[:, kc * 4:(kc + 1) * 4],
              convw_d[kc * 128:(kc + 1) * 128, 0, :], "wld")
        k.dma("gpsimd", convb[:, kc:kc + 1],
              convb_d[kc * 128:(kc + 1) * 128].unsqueeze(1), "wld")
    k.dma("gpsimd", wB[:].rearrange("p (a c) -> p a c", c=16),
          wB_d.rearrange("(a p) c -> p a c", p=128), "wld")
    k.dma("gpsimd", wC[:], wC_d, "wld")
    wld0 = k.cnt["wld"]
    k.wait("tensor", "wld", wld0)
    k.wait("vector", "wld", wld0)
    k.wait("scalar", "wld", wld0)

    psX = k.pb2k[0][:]
    psY = k.pb2k[1][:]
    psBm = k.pb2k[2][:]
    psBm2 = k.pb2k[3][:]
    psD2 = k.pb2k[1][:]
    psT_ = k.pb2k[2][:]

    import os as _os
    if _os.environ.get("DBG_DUMPC"):
        dmpc = nc.dram_tensor("dumpc", [5, 128, N + 4], F32,
                              kind="ExternalOutput").ap()
    wout_pe = [0, 0]
    xold_dv = [0, 0]
    for b in range(B_CORE):
        bsl = slice(b * N, (b + 1) * N)
        for kc in range(DIC):
            k.wait("gpsimd", "pe", k.cnt["pe"])
            wldt = k.dma("gpsimd", wsl[:].rearrange("p (a c) -> p a c", c=128),
                         win_d[:, kc * 128:(kc + 1) * 128].rearrange(
                             "(a p) c -> p a c", p=128), "wld")
            k.wait("tensor", "wld", wldt)
            k.vop_s(lambda e, o=xpb[:, 0:4]: e.memset(o, 0.0))
            for ns in range(4):
                bank = psX[:, (ns % 2) * 512:(ns % 2) * 512 + 512]
                for a in range(DCH):
                    pet = k.op("tensor", lambda e, o=bank,
                               l=wsl[:, a * 128:(a + 1) * 128],
                               r=ln2[:, a * T + b * N + ns * 512:
                                     a * T + b * N + ns * 512 + 512],
                               s=(a == 0), q=(a == DCH - 1):
                               e.matmul(o, l, r, start=s, stop=q), inc="pe",
                               mmb=0)
                k.wait("vector", "pe", pet)
                dvt = k.op("vector", lambda e, o=xpb[:, 4 + ns * 512:
                                                  4 + (ns + 1) * 512],
                           i_=bank: e.tensor_copy(o, i_), inc="dv")
                k.wait("tensor", "dv", dvt)
            V = lambda d: xpb[:, 4 - d: 4 - d + N]
            acc = [xsl[:, 0:N], xsl[:, N:2 * N]]
            k.op("vector", lambda e, o=acc[0], i_=V(3),
                 w=convw[:, kc * 4:kc * 4 + 1]:
                 e.tensor_scalar(o, i_, w, None, ALU.mult))
            for jj in range(1, 4):
                k.op("vector", lambda e, o=acc[jj % 2], i_=V(3 - jj),
                     w=convw[:, kc * 4 + jj:kc * 4 + jj + 1],
                     a=acc[(jj + 1) % 2]:
                     e.scalar_tensor_tensor(o, i_, w, a, ALU.mult, ALU.add),
                     inc="dv")
            k.wait("scalar", "dv", k.cnt["dv"])
            k.wait("scalar", "pe", k.cnt["pe"])
            act = k.op("scalar", lambda e, o=xcc[:], i_=acc[1],
                       bb=convb[:, kc:kc + 1]:
                       e.activation(o, i_, AF.Silu, bias=bb, scale=1.0),
                       inc="ac")
            k.wait("tensor", "ac", act)
            import os as _os
            if _os.environ.get("DBG_DUMPC") and b == 0 and kc == 0:
                k.wait("sync", "ac", act)
                k.wait("gpsimd", "ac", act)
                k.dma("gpsimd", dmpc[2][:, 0:N], xcc[:], "wld")
                k.dma("sync", dmpc[4][:, 0:N + 4], xpb[:], "st")
            for ns in range(4):
                bank = (psBm if ns < 2 else psBm2)[0:16, (ns % 2) * 512:
                                                   (ns % 2) * 512 + 512]
                pet = k.op("tensor", lambda e, o=bank,
                           l=wB[:, kc * 16:(kc + 1) * 16],
                           r=xcc[:, ns * 512:(ns + 1) * 512],
                           s=(kc == 0), q=(kc == DIC - 1):
                           e.matmul(o, l, r, start=s, stop=q), inc="pe", mmb=0)
        k.wait("vector", "pe", k.cnt["pe"])
        for ns in range(4):
            bank = (psBm if ns < 2 else psBm2)[0:16, (ns % 2) * 512:
                                               (ns % 2) * 512 + 512]
            k.op("vector", lambda e, o=Bsb[:, ns * 512:(ns + 1) * 512], i_=bank:
                 e.tensor_copy(o, i_))
        k.vop_s(lambda e, o=Bsb[:], i_=Bsb[:]:
                e.tensor_scalar(o, i_, 0.1, None, ALU.mult))
        k.vop_s(lambda e, o=Hb[:], d0=cst["k09c"][:], d1=Bsb[:]:
                e.tensor_tensor_scan(o, d0, d1, 0.0, ALU.mult, ALU.add))
        import os as _os
        if _os.environ.get("DBG_DUMPC") and b == 0:
            k.wait("sync", "dv", k.cnt["dv"])
            k.wait("gpsimd", "dv", k.cnt["dv"])
            k.dma("sync", dmpc[0][0:16, 0:N], Bsb[:], "st")
            k.dma("gpsimd", dmpc[1][0:16, 0:N], Hb[:], "wld")
        dvt = k.op("vector", lambda e: e.engine_nop(), inc="dv")
        k.wait("tensor", "dv", dvt)
        for ns in range(4):
            nsl = slice(b * N + ns * 512, b * N + (ns + 1) * 512)
            for kc in range(DIC):
                wldt = k.dma("gpsimd", wsl[:].rearrange("p (a c) -> p a c", c=128),
                             win_d[:, DI + kc * 128: DI + (kc + 1) * 128].rearrange(
                                 "(a p) c -> p a c", p=128), "wld")
                k.wait("tensor", "wld", wldt)
                bankZ = psX[:, (kc % 2) * 512:(kc % 2) * 512 + 512]
                bankY = psY[:, (kc % 2) * 512:(kc % 2) * 512 + 512]
                for a in range(DCH):
                    pet = k.op("tensor", lambda e, o=bankZ,
                               l=wsl[:, a * 128:(a + 1) * 128],
                               r=ln2[:, a * T + b * N + ns * 512:
                                     a * T + b * N + ns * 512 + 512],
                               s=(a == 0), q=(a == DCH - 1):
                               e.matmul(o, l, r, start=s, stop=q), inc="pe",
                               mmb=0)
                pety = k.op("tensor", lambda e, o=bankY,
                            l=wC[:, kc * 128:(kc + 1) * 128],
                            r=Hb[:, ns * 512:(ns + 1) * 512]:
                            e.matmul(o, l, r, start=True, stop=True), inc="pe",
                            mmb=0)
                k.wait("scalar", "pe", pet)
                act = k.op("scalar", lambda e, o=zsil[:], i_=bankZ:
                           e.activation(o, i_, AF.Silu, bias=0.0, scale=1.0),
                           inc="ac")
                k.wait("vector", "ac", act)
                k.wait("vector", "pe", pety)
                dvt = k.op("vector", lambda e,
                           o=ygns[:, kc * 512:(kc + 1) * 512],
                           a=zsil[:], b_=bankY:
                           e.tensor_mul(o, a, b_), inc="dv")
                k.wait("tensor", "dv", dvt)
                k.wait("scalar", "dv", dvt)
                k.wait("gpsimd", "pe", pet)
                import os as _os
                if _os.environ.get("DBG_DUMPC") and b == 0 and ns == 0 and kc == 0:
                    k.wait("gpsimd", "dv", dvt)
                    k.dma("gpsimd", dmpc[3][:, 0:512], ygns[:, 0:512], "wld")
            for j in range(DCH):
                wbuf = woutb[:, (j % 2) * DIC * 128:(j % 2 + 1) * DIC * 128]
                k.wait("gpsimd", "pe", wout_pe[j % 2])
                wldt = k.dma("gpsimd",
                             wbuf.rearrange("p (a c) -> p a c", c=128),
                             wout_d[:, j * 128:(j + 1) * 128].rearrange(
                                 "(a p) c -> p a c", p=128), "wld")
                k.wait("tensor", "wld", wldt)
                bank = psD2[:, (j % 2) * 512:(j % 2) * 512 + 512]
                for kc in range(DIC):
                    pet = k.op("tensor", lambda e, o=bank,
                               l=wbuf[:, kc * 128:(kc + 1) * 128],
                               r=ygns[:, kc * 512:(kc + 1) * 512],
                               s=(kc == 0), q=(kc == DIC - 1):
                               e.matmul(o, l, r, start=s, stop=q), inc="pe",
                               mmb=0)
                wout_pe[j % 2] = pet
                buf = j % 2
                k.wait("sync", "dv", xold_dv[buf])
                ldt = k.dma("sync", xold[:, buf * 512:(buf + 1) * 512],
                            xr2[j * 128:(j + 1) * 128, nsl], "ld")
                k.wait("vector", "pe", pet)
                k.wait("vector", "ld", ldt)
                dvt = k.op("vector", lambda e, o=xns[j][:], a=bank,
                           b_=xold[:, buf * 512:(buf + 1) * 512]:
                           e.tensor_add(o, a, b_), inc="dv")
                xold_dv[buf] = dvt
                k.wait("tensor", "dv", dvt)
            for tt in range(4):
                for j in range(DCH):
                    bankT = psT_[:, (j % 2) * 512:(j % 2) * 512 + 128]
                    pet = k.op("tensor", lambda e, o=bankT,
                               i_=xns[j][:, tt * 128:(tt + 1) * 128],
                               idn=cst["ident_f32"][:]:
                               e.transpose(o, i_, idn), inc="pe", mmb=0)
                    k.wait("vector", "pe", pet)
                    dvt = k.op("vector", lambda e, o=xtok[:, j * 128:(j + 1) * 128],
                               i_=bankT: e.tensor_copy(o, i_), inc="dv")
                    k.wait("tensor", "dv", dvt)
                k.wait("sync", "dv", k.cnt["dv"])
                t0 = b * N + ns * 512 + tt * 128
                stt = k.dma("sync", x3t[t0:t0 + 128, :], xtok[:], "st")
                k.wait("vector", "st", stt)
    fin = k.fence("vector", "dv")
    k.wait("sync", "dv", fin)
    k.wait("sync", "st", k.cnt["st"])


def build_phase_c_program():
    nc, k = new_program()
    xr2 = nc.dram_tensor("xr2", [D, T], F32, kind="ExternalInput").ap()
    win_d = nc.dram_tensor("w_in", [D, 2 * DI], F32, kind="ExternalInput").ap()
    convw_d = nc.dram_tensor("conv_w", [DI, 1, 4], F32, kind="ExternalInput").ap()
    convb_d = nc.dram_tensor("conv_b", [DI], F32, kind="ExternalInput").ap()
    wB_d = nc.dram_tensor("w_B", [DI, 16], F32, kind="ExternalInput").ap()
    wC_d = nc.dram_tensor("w_C", [16, DI], F32, kind="ExternalInput").ap()
    wout_d = nc.dram_tensor("w_out", [DI, D], F32, kind="ExternalInput").ap()
    x3t = nc.dram_tensor("x3t", [T, D], F32, kind="ExternalOutput").ap()
    cst = stage_consts(nc, k)
    for eng in ENGINES:
        k.wait(eng, "ld", k.cnt["ld"])
    k.freeze_persistent()
    k.set_phase("pc")
    emit_phase_c(nc, k, cst, xr2, win_d, convw_d, convb_d, wB_d, wC_d, wout_d,
                 x3t)
    k.build()
    return nc



FCH = 22


def emit_quant(k, x_ap, out_ap, W, scr, scalar_dep=None):
    nb = W // 16
    mr_m = scr[0][:, 0:nb]
    mr_r = scr[0][:, nb:2 * nb]
    ab = scr[1][:, 0:W]
    qa = scr[2][:, 0:W]
    qb = scr[3][:, 0:W]
    qc = scr[4][:, 0:W]
    x3 = x_ap.rearrange("p (b s) -> p b s", s=16)
    k.vop_s(lambda e, o=mr_m, i_=x3:
            e.reduce_max(o, i_, AX.X, apply_absolute_value=True))
    k.vop_s(lambda e, o=mr_m, i_=mr_m:
            e.tensor_scalar(o, i_, 1.0 / 6.0, None, ALU.mult))
    k.vop_s(lambda e, o=mr_m, i_=mr_m:
            e.tensor_scalar(o, i_, 1e-12, None, ALU.max))
    k.vop_s(lambda e, o=mr_r, i_=mr_m: e.reciprocal(o, i_))
    act = k.op("scalar", lambda e, o=ab, i_=x_ap:
               e.activation(o, i_, AF.Abs, bias=0.0, scale=1.0), inc="ac")
    k.wait("vector", "ac", act)
    rb = mr_r.rearrange("p (b one) -> p b one", one=1).to_broadcast((128, nb, 16))
    k.op("vector", lambda e, o=ab.rearrange("p (b s) -> p b s", s=16),
         a=ab.rearrange("p (b s) -> p b s", s=16), b_=rb:
         e.tensor_tensor(o, a, b_, ALU.mult))
    k.op("vector", lambda e, o=qa, i_=ab:
         e.tensor_scalar(o, i_, 0.25, None, ALU.is_gt))
    k.op("vector", lambda e, o=qb, i_=ab, a=qa:
         e.scalar_tensor_tensor(o, i_, 0.75, a, ALU.is_gt, ALU.add))
    k.op("vector", lambda e, o=qa, i_=ab, a=qb:
         e.scalar_tensor_tensor(o, i_, 1.25, a, ALU.is_gt, ALU.add))
    k.op("vector", lambda e, o=qb, i_=ab, a=qa:
         e.scalar_tensor_tensor(o, i_, 1.75, a, ALU.is_gt, ALU.add))
    k.op("vector", lambda e, o=qa, i_=ab:
         e.tensor_scalar(o, i_, 2.5, None, ALU.is_gt))
    k.op("vector", lambda e, o=qc, i_=ab, a=qa:
         e.scalar_tensor_tensor(o, i_, 3.5, a, ALU.is_gt, ALU.add))
    k.op("vector", lambda e, o=qa, i_=ab:
         e.tensor_scalar(o, i_, 5.0, None, ALU.is_gt))
    k.op("vector", lambda e, o=qb, i_=qb, a=qc:
         e.scalar_tensor_tensor(o, i_, 0.5, a, ALU.mult, ALU.add))
    k.op("vector", lambda e, o=qa, i_=qa, a=qb:
         e.scalar_tensor_tensor(o, i_, 2.0, a, ALU.mult, ALU.add), inc="dv")
    k.wait("scalar", "dv", k.cnt["dv"])
    act = k.op("scalar", lambda e, o=qc, i_=x_ap:
               e.activation(o, i_, AF.Sign, bias=0.0, scale=1.0), inc="ac")
    k.wait("vector", "ac", act)
    k.op("vector", lambda e, o=qa, a=qa, b_=qc: e.tensor_mul(o, a, b_))
    mb = mr_m.rearrange("p (b one) -> p b one", one=1).to_broadcast((128, nb, 16))
    dvt = k.op("vector", lambda e, o=out_ap.rearrange("p (b s) -> p b s", s=16),
               a=qa.rearrange("p (b s) -> p b s", s=16), b_=mb:
               e.tensor_tensor(o, a, b_, ALU.mult), inc="dv")
    return dvt


def emit_phase_d(nc, k, cst, x3t, wg_d, wu_d, wd_d, out_d, wgq_d, wuq_d, wdq_d):
    k.sreset()
    xq = k.salloc([128, DCH * T], BF16)
    mr = k.salloc([128, 128], F32)
    ab = k.salloc([128, 512], F32)
    qa = k.salloc([128, 512], F32)
    qb = k.salloc([128, 512], F32)
    qc = k.salloc([128, 512], F32)
    scr = (mr, ab, qa, qb, qc)

    def quant_sliced(x_ap, out_ap, W):
        for cs in range(0, W, 512):
            cw = min(512, W - cs)
            emit_quant(k, x_ap[:, cs:cs + cw], out_ap[:, cs:cs + cw], cw, scr)
    xt = k.salloc([128, D], F32)
    lnt = k.salloc([128, D], F32)
    stat = k.salloc([128, 8], F32)
    qt = k.salloc([128, D], BF16)
    wgq = k.salloc([128, 2 * 1024], BF16)
    wuq = k.salloc([128, 2 * 1024], BF16)
    wdj = k.salloc([128, 2 * 2816], BF16)
    sg = k.salloc([128, 1024], F32)
    hf = k.salloc([128, 1024], F32)
    htok = k.salloc([128, 1024], F32)
    hqt = k.salloc([128, 1024], BF16)
    outfm = k.salloc([128, DCH * 1024], BF16)
    outtok = k.salloc([128, D], F32)
    xold = k.salloc([128, D], F32)
    xnew = k.salloc([128, D], F32)
    m_pre = k.smark()
    wtile = k.salloc([128, 2816], F32)
    wqt = k.salloc([128, 2816], BF16)
    wT = k.salloc([128, 1024], BF16)

    psT = k.pb2k[0][:]
    psU = k.pb2k[1][:]
    psW = k.pb2k[2][:]
    psD = k.pb2k[3][:]

    ntile = T // 128
    prev_pe = 0
    for i in range(ntile):
        xt_i = xt[:]
        k.wait("sync", "dv", k.cnt["dv"])
        ldt = k.dma("sync", xt_i, x3t[i * 128:(i + 1) * 128, :], "ld")
        k.wait("vector", "ld", ldt)
        k.wait("scalar", "ld", ldt)
        k.vop_s(lambda e, o=stat[:, 0:1], i_=xt_i: e.reduce_sum(o, i_, AX.X))
        k.vop_s(lambda e, o=hf[:], a=xt_i, b_=xt_i: e.tensor_mul(o, a, b_))
        k.vop_s(lambda e, o=stat[:, 1:2], i_=hf[:]:
                e.reduce_sum(o, i_, AX.X))
        k.vop_s(lambda e, o=stat[:, 2:4], i_=stat[:, 0:2]:
                e.tensor_scalar(o, i_, 1.0 / D, None, ALU.mult))
        k.vop_s(lambda e, o=stat[:, 4:5], a=stat[:, 2:3], b_=stat[:, 2:3]:
                e.tensor_mul(o, a, b_))
        k.vop_s(lambda e, o=stat[:, 5:6], a=stat[:, 3:4], b_=stat[:, 4:5]:
                e.tensor_sub(o, a, b_))
        k.vop_s(lambda e, o=stat[:, 6:7], i_=stat[:, 5:6]:
                e.tensor_scalar(o, i_, 1e-5, None, ALU.add))
        k.wait("scalar", "dv", k.cnt["dv"])
        sct = k.op("scalar", lambda e, o=stat[:, 5:6], i_=stat[:, 6:7]:
                   e.activation(o, i_, AF.Sqrt, bias=0.0, scale=1.0), inc="ac")
        k.wait("vector", "ac", sct)
        k.vop_s(lambda e, o=stat[:, 6:7], i_=stat[:, 5:6]: e.reciprocal(o, i_))
        k.op("vector", lambda e, o=lnt[:], i_=xt_i, m=stat[:, 2:3], r=stat[:, 6:7]:
             e.tensor_scalar(o, i_, m, r, ALU.subtract, ALU.mult), inc="dv")
        k.wait("scalar", "dv", k.cnt["dv"])
        quant_sliced(lnt[:], qt[:], D)
        k.wait("tensor", "dv", k.cnt["dv"])
        k.wait("tensor", "pe", prev_pe)
        for a in range(DCH):
            bank = psT[:, (a % 2) * 512:(a % 2) * 512 + 128].bitcast(BF16)
            pet = k.op("tensor", lambda e, o=bank[:, 0:128],
                       i_=qt[:, a * 128:(a + 1) * 128], idn=cst["ident_bf"][:]:
                       e.transpose(o, i_, idn), inc="pe", mmb=0)
            k.wait("vector", "pe", pet)
            dvt = k.op("vector", lambda e, o=xq[:, a * T + i * 128:
                                             a * T + (i + 1) * 128],
                       i_=bank[:, 0:128]: e.tensor_copy(o, i_), inc="dv")
            k.wait("tensor", "dv", dvt)
        prev_pe = k.cnt["pe"]

    def wprep(src_d, dst_d, nrow_chunks, wcols):
        for f in range(nrow_chunks):
            k.wait("sync", "dv", k.cnt["dv"])
            k.wait("sync", "wld", k.cnt["wld"])
            ldt = k.dma("sync", wtile[:, 0:wcols],
                        src_d[f * 128:(f + 1) * 128, :], "ld")
            k.wait("vector", "ld", ldt)
            k.wait("scalar", "ld", ldt)
            quant_sliced(wtile[:, 0:wcols], wqt[:, 0:wcols], wcols)
            k.wait("tensor", "dv", k.cnt["dv"])
            nblk = wcols // 128
            for a in range(nblk):
                bank = psT[:, (a % 2) * 512:(a % 2) * 512 + 128].bitcast(BF16)
                pet = k.op("tensor", lambda e, o=bank[:, 0:128],
                           i_=wqt[:, a * 128:(a + 1) * 128],
                           idn=cst["ident_bf"][:]:
                           e.transpose(o, i_, idn), inc="pe", mmb=0)
                k.wait("vector", "pe", pet)
                if a % 8 == 0:
                    k.wait("vector", "wld", k.cnt["wld"])
                col = (a % 8) * 128
                dvt = k.op("vector", lambda e, o=wT[:, col:col + 128],
                           i_=bank[:, 0:128]: e.tensor_copy(o, i_), inc="dv")
                k.wait("tensor", "dv", dvt)
                if a % 8 == 7 or a == nblk - 1:
                    k.wait("gpsimd", "dv", dvt)
                    k.dma("gpsimd", dst_d[f, :, (a // 8) * 1024:
                                          (a // 8) * 1024 + col + 128],
                          wT[:, 0:col + 128], "wld")
    wprep(wg_d, wgq_d, FCH, D)
    wprep(wu_d, wuq_d, FCH, D)
    wprep(wd_d, wdq_d, DCH, 2816)
    wp_done = k.fence("vector", "dv")
    k.wait("sync", "dv", wp_done)
    k.wait("sync", "wld", k.cnt["wld"])
    k.srewind(m_pre)
    hq = k.salloc([128, FCH * 1024], BF16)

    wld_guard = [0, 0]
    for ns in range(4):
        nsl = slice(ns * 1024, (ns + 1) * 1024)
        for f in range(FCH):
            buf = f % 2
            k.wait("sync", "pe", wld_guard[buf])
            lg = k.dma("sync", wgq[:, buf * 1024:(buf + 1) * 1024],
                       wgq_d[f, :, :], "ld")
            lu = k.dma("sync", wuq[:, buf * 1024:(buf + 1) * 1024],
                       wuq_d[f, :, :], "ld")
            k.wait("tensor", "ld", lu)
            for half in range(2):
                hsl = slice(half * 512, (half + 1) * 512)
                for a in range(DCH):
                    k.op("tensor", lambda e, o=psU[:, hsl],
                         l=wgq[:, buf * 1024 + a * 128:buf * 1024 + (a + 1) * 128],
                         r=xq[:, a * T + ns * 1024 + half * 512:
                               a * T + ns * 1024 + half * 512 + 512],
                         s=(a == 0), q=(a == DCH - 1):
                         e.matmul(o, l, r, start=s, stop=q), mmb=0)
                for a in range(DCH):
                    pet = k.op("tensor", lambda e, o=psW[:, hsl],
                               l=wuq[:, buf * 1024 + a * 128:
                                     buf * 1024 + (a + 1) * 128],
                               r=xq[:, a * T + ns * 1024 + half * 512:
                                     a * T + ns * 1024 + half * 512 + 512],
                               s=(a == 0), q=(a == DCH - 1):
                               e.matmul(o, l, r, start=s, stop=q), inc="pe",
                               mmb=0)
            wld_guard[buf] = pet
            k.wait("scalar", "pe", pet)
            act = k.op("scalar", lambda e, o=sg[:], i_=psU[:]:
                       e.activation(o, i_, AF.Silu, bias=0.0, scale=1.0),
                       inc="ac")
            k.wait("vector", "ac", act)
            dvt = k.op("vector", lambda e, o=hf[:], a=sg[:], b_=psW[:]:
                       e.tensor_mul(o, a, b_), inc="dv")
            k.wait("tensor", "dv", dvt)
            for a in range(DCH):
                bank = psT[:, (a % 2) * 512:(a % 2) * 512 + 128]
                pet = k.op("tensor", lambda e, o=bank,
                           i_=hf[:, a * 128:(a + 1) * 128],
                           idn=cst["ident_f32"][:]:
                           e.transpose(o, i_, idn), inc="pe", mmb=0)
                k.wait("vector", "pe", pet)
                dvt = k.op("vector", lambda e, o=htok[:, a * 128:(a + 1) * 128],
                           i_=bank: e.tensor_copy(o, i_), inc="dv")
                k.wait("tensor", "dv", dvt)
            k.wait("scalar", "dv", k.cnt["dv"])
            quant_sliced(htok[:], hqt[:], D)
            k.wait("tensor", "dv", k.cnt["dv"])
            for a in range(DCH):
                bank = psT[:, (a % 2) * 512:(a % 2) * 512 + 128].bitcast(BF16)
                pet = k.op("tensor", lambda e, o=bank[:, 0:128],
                           i_=hqt[:, a * 128:(a + 1) * 128],
                           idn=cst["ident_bf"][:]:
                           e.transpose(o, i_, idn), inc="pe", mmb=0)
                k.wait("vector", "pe", pet)
                dvt = k.op("vector", lambda e,
                           o=hq[:, f * 1024 + a * 128:f * 1024 + (a + 1) * 128],
                           i_=bank[:, 0:128]: e.tensor_copy(o, i_), inc="dv")
                k.wait("tensor", "dv", dvt)
        for j in range(DCH):
            buf = j % 2
            k.wait("sync", "pe", wld_guard[buf])
            ld = k.dma("sync", wdj[:, buf * 2816:(buf + 1) * 2816],
                       wdq_d[j, :, :], "ld")
            k.wait("tensor", "ld", ld)
            for half in range(2):
                bank = psD[:, half * 512:(half + 1) * 512]
                for f in range(FCH):
                    pet = k.op("tensor", lambda e, o=bank,
                               l=wdj[:, buf * 2816 + f * 128:
                                     buf * 2816 + (f + 1) * 128],
                               r=hq[:, f * 1024 + half * 512:
                                     f * 1024 + half * 512 + 512],
                               s=(f == 0), q=(f == FCH - 1):
                               e.matmul(o, l, r, start=s, stop=q), inc="pe",
                               mmb=0)
            wld_guard[buf] = pet
            k.wait("vector", "pe", pet)
            dvt = k.op("vector", lambda e, o=outfm[:, j * 1024:(j + 1) * 1024],
                       i_=psD[:]: e.tensor_copy(o, i_), inc="dv")
            k.wait("tensor", "dv", dvt)
        for tt in range(8):
            t0 = ns * 1024 + tt * 128
            for j in range(DCH):
                bank = psT[:, (j % 2) * 512:(j % 2) * 512 + 128].bitcast(BF16)
                pet = k.op("tensor", lambda e, o=bank[:, 0:128],
                           i_=outfm[:, j * 1024 + tt * 128:
                                    j * 1024 + (tt + 1) * 128],
                           idn=cst["ident_bf"][:]:
                           e.transpose(o, i_, idn), inc="pe", mmb=0)
                k.wait("vector", "pe", pet)
                dvt = k.op("vector", lambda e, o=outtok[:, j * 128:(j + 1) * 128],
                           i_=bank[:, 0:128]: e.tensor_copy(o, i_), inc="dv")
                k.wait("tensor", "dv", dvt)
            k.wait("sync", "dv", k.cnt["dv"])
            ldt = k.dma("sync", xold[:], x3t[t0:t0 + 128, :], "ld")
            k.wait("vector", "ld", ldt)
            dvt = k.op("vector", lambda e, o=xnew[:], a=outtok[:], b_=xold[:]:
                       e.tensor_add(o, a, b_), inc="dv")
            k.wait("sync", "dv", dvt)
            stt = k.dma("sync", out_d[t0:t0 + 128, :], xnew[:], "st")
            k.wait("vector", "st", stt)
    fin = k.fence("vector", "dv")
    k.wait("sync", "dv", fin)
    k.wait("sync", "st", k.cnt["st"])


def build_phase_d_program():
    nc, k = new_program()
    x3t = nc.dram_tensor("x3t", [T, D], F32, kind="ExternalInput").ap()
    wg_d = nc.dram_tensor("w_gate", [2816, D], F32, kind="ExternalInput").ap()
    wu_d = nc.dram_tensor("w_up", [2816, D], F32, kind="ExternalInput").ap()
    wd_d = nc.dram_tensor("w_down", [D, 2816], F32, kind="ExternalInput").ap()
    out_d = nc.dram_tensor("out", [T, D], F32, kind="ExternalOutput").ap()
    wgq_d = nc.dram_tensor("wgq", [FCH, 128, D], BF16, kind="Internal").ap()
    wuq_d = nc.dram_tensor("wuq", [FCH, 128, D], BF16, kind="Internal").ap()
    wdq_d = nc.dram_tensor("wdq", [DCH, 128, 2816], BF16, kind="Internal").ap()
    cst = stage_consts(nc, k)
    for eng in ENGINES:
        k.wait(eng, "ld", k.cnt["ld"])
    k.freeze_persistent()
    k.set_phase("pd")
    emit_phase_d(nc, k, cst, x3t, wg_d, wu_d, wd_d, out_d, wgq_d, wuq_d, wdq_d)
    k.build()
    return nc



def build_fused_program():
    nc, k = new_program()
    x_in = nc.dram_tensor("x", [T, D], F32, kind="ExternalInput").ap()
    wq_d = nc.dram_tensor("Wq", [D, D], F32, kind="ExternalInput").ap()
    wk_d = nc.dram_tensor("Wk", [D, D], F32, kind="ExternalInput").ap()
    wv_d = nc.dram_tensor("Wv", [D, D], F32, kind="ExternalInput").ap()
    wo_d = nc.dram_tensor("Wo", [D, D], F32, kind="ExternalInput").ap()
    wg1_d = nc.dram_tensor("Wg1", [D, 16], F32, kind="ExternalInput").ap()
    wg2_d = nc.dram_tensor("Wg2", [16, D], F32, kind="ExternalInput").ap()
    normw_d = nc.dram_tensor("norm_w", [D], F32, kind="ExternalInput").ap()
    router_d = nc.dram_tensor("router", [D, 8], F32, kind="ExternalInput").ap()
    w1_d = nc.dram_tensor("w1", [8, D, 2816], F32, kind="ExternalInput").ap()
    w3_d = nc.dram_tensor("w3", [8, D, 2816], F32, kind="ExternalInput").ap()
    w2_d = nc.dram_tensor("w2", [8, 2816, D], F32, kind="ExternalInput").ap()
    ws1_d = nc.dram_tensor("ws1", [D, 2816], F32, kind="ExternalInput").ap()
    ws3_d = nc.dram_tensor("ws3", [D, 2816], F32, kind="ExternalInput").ap()
    ws2_d = nc.dram_tensor("ws2", [2816, D], F32, kind="ExternalInput").ap()
    win_d = nc.dram_tensor("w_in", [D, 2 * DI], F32, kind="ExternalInput").ap()
    convw_d = nc.dram_tensor("conv_w", [DI, 1, 4], F32, kind="ExternalInput").ap()
    convb_d = nc.dram_tensor("conv_b", [DI], F32, kind="ExternalInput").ap()
    wB_d = nc.dram_tensor("w_B", [DI, 16], F32, kind="ExternalInput").ap()
    wC_d = nc.dram_tensor("w_C", [16, DI], F32, kind="ExternalInput").ap()
    wout_d = nc.dram_tensor("w_out", [DI, D], F32, kind="ExternalInput").ap()
    wg_d = nc.dram_tensor("w_gate", [2816, D], F32, kind="ExternalInput").ap()
    wu_d = nc.dram_tensor("w_up", [2816, D], F32, kind="ExternalInput").ap()
    wd_d = nc.dram_tensor("w_down", [D, 2816], F32, kind="ExternalInput").ap()
    out_d = nc.dram_tensor("out", [T, D], F32, kind="ExternalOutput").ap()
    import os as _os
    _ko = "ExternalOutput" if _os.environ.get("DBG_FX") else "Internal"
    xr0 = nc.dram_tensor("xr0", [D, T], F32, kind="Internal").ap()
    xr1 = nc.dram_tensor("xr1", [D, T], F32, kind=_ko).ap()
    xr2 = nc.dram_tensor("xr2", [D, T], F32, kind=_ko).ap()
    x3t = nc.dram_tensor("x3t", [T, D], F32, kind=_ko).ap()
    o_s = nc.dram_tensor("o_s", [D, T], BF16, kind="Internal").ap()
    lnS = nc.dram_tensor("lnS", [D, T], BF16, kind="Internal").ap()
    alpha_d = nc.dram_tensor("alpha_d", [8, T], BF16, kind="Internal").ap()
    wgq_d = nc.dram_tensor("wgq", [FCH, 128, D], BF16, kind="Internal").ap()
    wuq_d = nc.dram_tensor("wuq", [FCH, 128, D], BF16, kind="Internal").ap()
    wdq_d = nc.dram_tensor("wdq", [DCH, 128, 2816], BF16, kind="Internal").ap()

    cst = stage_consts(nc, k)
    for eng in ENGINES:
        k.wait(eng, "ld", k.cnt["ld"])
    k.freeze_persistent()
    k.set_phase("pa")
    emit_phase_a(nc, k, cst, x_in, wq_d, wk_d, wv_d, wo_d, wg1_d, wg2_d,
                 normw_d, xr0, xr1, o_s)
    k.barrier_all()
    k.set_phase("pb")
    emit_phase_b(nc, k, cst, xr1, router_d, w1_d, w3_d, w2_d, ws1_d, ws3_d,
                 ws2_d, xr2, lnS, alpha_d)
    k.barrier_all()
    k.set_phase("pc")
    emit_phase_c(nc, k, cst, xr2, win_d, convw_d, convb_d, wB_d, wC_d, wout_d,
                 x3t)
    k.barrier_all()
    k.set_phase("pd")
    emit_phase_d(nc, k, cst, x3t, wg_d, wu_d, wd_d, out_d, wgq_d, wuq_d, wdq_d)
    k.build()
    return nc


def kernel_fused(**inputs):
    consts = host_constants()
    NCORES = 8
    x_full = inputs["L0_attn_x"]
    ncF = _prog("fused", build_fused_program)
    maps = []
    for c in range(NCORES):
        m = dict(consts)
        m["x"] = np.ascontiguousarray(
            x_full[c * B_CORE:(c + 1) * B_CORE].reshape(T, D))
        for nm, src in (
                ("Wq", "L0_attn_Wq"), ("Wk", "L0_attn_Wk"), ("Wv", "L0_attn_Wv"),
                ("Wo", "L0_attn_Wo"), ("Wg1", "L0_attn_Wg1"),
                ("Wg2", "L0_attn_Wg2"), ("norm_w", "L0_attn_norm_w"),
                ("router", "L0_ffn_router_DE"), ("w1", "L0_ffn_w1_eDF"),
                ("w3", "L0_ffn_w3_eDF"), ("w2", "L0_ffn_w2_eFD"),
                ("ws1", "L0_ffn_ws1_DF"), ("ws3", "L0_ffn_ws3_DF"),
                ("ws2", "L0_ffn_ws2_FD"), ("w_in", "L1_attn_w_in"),
                ("conv_w", "L1_attn_conv_w"), ("conv_b", "L1_attn_conv_b"),
                ("w_B", "L1_attn_w_B"), ("w_C", "L1_attn_w_C"),
                ("w_out", "L1_attn_w_out"), ("w_gate", "L1_ffn_w_gate"),
                ("w_up", "L1_ffn_w_up"), ("w_down", "L1_ffn_w_down")):
            m[nm] = inputs[src]
        maps.append(m)
    res = run_bass_kernel_spmd(ncF, maps, core_ids=list(range(NCORES)))
    outs = [np.asarray(r["out"]) for r in res.results]
    return np.concatenate(outs, axis=0).reshape(16, N, D).astype(np.float32)


# version 07
# speedup vs baseline: 1.1304x; 1.1298x over previous
import contextlib
import numpy as np
import ml_dtypes
import concourse.bass as bass
import concourse.mybir as mybir
from concourse.bass_utils import run_bass_kernel_spmd

F32 = mybir.dt.float32
BF16 = mybir.dt.bfloat16
AF = mybir.ActivationFunctionType
ALU = mybir.AluOpType
AX = mybir.AxisListType

B_CORE = 2
N = 2048
D = 1024
T = B_CORE * N
H = 16
HD = 64
CHUNK = 64
NCH = N // CHUNK
NPAIR = 8
DCH = D // 128

ENGINES = ("sync", "tensor", "vector", "scalar", "gpsimd")


ARENA_BASE = 20 * 1024
SBUF_BYTES = 229312


BASE_SEMS = ("ld", "st", "wld", "pe", "dv", "ac", "gp", "sy")


class _CntView:

    def __init__(self, k):
        self._k = k

    def __getitem__(self, base):
        return self._k._cnt[self._k._n(base)]

    def __setitem__(self, base, val):
        self._k._cnt[self._k._n(base)] = val


class K:

    def __init__(self, nc):
        self.nc = nc
        self.q = {e: [] for e in ENGINES}
        self.sem_names = []
        self._cnt = {}
        self.sfx = ""
        self.cnt = _CntView(self)
        self._uid = 0
        self.mm_base = None
        self.p_off = ARENA_BASE
        self.s_base = None
        self.s_off = None
        self.s_peak = 0

    def _n(self, base):
        return base + self.sfx

    def set_phase(self, tag):
        self.sfx = "_" + tag
        for b in BASE_SEMS:
            name = b + self.sfx
            if name not in self._cnt:
                self.sem_names.append(name)
                self._cnt[name] = 0


    def _alloc_at(self, shape, dtype, off):
        self._uid += 1
        nbytes = int(np.prod(shape[1:])) * (2 if dtype == BF16 else 4)
        t = self.nc.alloc_sbuf_tensor_at(f"sb{self._uid}", list(shape), dtype,
                                         offset=off)
        return t, off + ((nbytes + 31) // 32) * 32

    def palloc(self, shape, dtype):
        t, self.p_off = self._alloc_at(shape, dtype, self.p_off)
        assert self.s_base is None, "persistent alloc after freeze"
        return t

    def freeze_persistent(self):
        self.s_base = self.p_off
        self.s_off = self.p_off

    def sreset(self):
        self.s_peak = max(self.s_peak, self.s_off)
        self.s_off = self.s_base

    def salloc(self, shape, dtype):
        t, self.s_off = self._alloc_at(shape, dtype, self.s_off)
        assert self.s_off <= SBUF_BYTES, f"sbuf over budget: {self.s_off}"
        return t

    def smark(self):
        return self.s_off

    def srewind(self, mark):
        self.s_peak = max(self.s_peak, self.s_off)
        self.s_off = mark

    def sem(self, name):
        self.sem_names.append(name)
        self.cnt[name] = 0
        return name

    def vop_s(self, fn):
        dv = self._n("dv")
        self._cnt[dv] += 1
        self.q["vector"].append(("op", fn, dv, 1))
        self.q["vector"].append(("wait", dv, self._cnt[dv]))
        return self._cnt[dv]

    def op(self, eng, fn, inc=None, amt=1, mmb=None):
        if mmb is not None and eng == "tensor":
            if self.mm_base is not None and self.mm_base != mmb:
                self.q[eng].append(("op", lambda e: e.nop(), None, 1))
                self.q[eng].append(("op", lambda e: e.nop(), None, 1))
            self.mm_base = mmb
        if inc is not None:
            inc = self._n(inc)
            self._cnt[inc] += amt
        self.q[eng].append(("op", fn, inc, amt))
        return self._cnt[inc] if inc is not None else None

    def dma(self, eng, out, in_, sem="ld", **kw):
        sem = self._n(sem)
        self._cnt[sem] += 16
        tgt = self._cnt[sem]
        self.q[eng].append(
            ("op", lambda e, o=out, i=in_, kw=kw: e.dma_start(out=o, in_=i, **kw),
             sem, 16))
        return tgt

    def wait(self, eng, sem, val):
        self.q[eng].append(("wait", self._n(sem), val))

    def fence(self, eng, sem):
        sem = self._n(sem)
        self._cnt[sem] += 1
        if eng in ("vector", "gpsimd"):
            self.q[eng].append(("op", lambda e: e.engine_nop(), sem, 1))
        else:
            self.q[eng].append(("op", lambda e: e.nop(), sem, 1))
        return self._cnt[sem]

    def barrier_all(self):
        for _round in range(2):
            tpe = self.fence("tensor", "pe")
            tdv = self.fence("vector", "dv")
            self._cnt[self._n("ac")] += 1
            tac = self._cnt[self._n("ac")]
            bar = self.bar_tile
            self.q["scalar"].append(
                ("op", lambda e, o=bar: e.activation(
                    o, o, AF.Identity, bias=0.0, scale=1.0), self._n("ac"), 1))
            tgp = self.fence("gpsimd", "gp")
            tsy = self.fence("sync", "sy")
            for eng in ENGINES:
                for sem, tgt in (("pe", tpe), ("dv", tdv), ("ac", tac),
                                 ("gp", tgp), ("sy", tsy)):
                    self.wait(eng, sem, tgt)

    def build(self):
        nc = self.nc
        with contextlib.ExitStack() as stack:
            sems = {n: stack.enter_context(nc.semaphore(n)) for n in self.sem_names}
            with nc.Block() as block:
                def mk(engname):
                    def emit(eng):
                        for item in self.q[engname]:
                            if item[0] == "wait":
                                eng.wait_ge(sems[item[1]], item[2])
                            else:
                                _, fn, inc, amt = item
                                ins = fn(eng)
                                if inc is not None:
                                    ins.then_inc(sems[inc], amt)
                    return emit
                block.sync(mk("sync"))
                block.tensor(mk("tensor"))
                block.vector(mk("vector"))
                block.scalar(mk("scalar"))
                block.gpsimd(mk("gpsimd"))
        return nc



def host_constants():
    c = {}
    t = np.arange(N, dtype=np.float64)[None, :]
    j = np.arange(HD, dtype=np.float64)[:, None]
    theta = t * (10000.0 ** (-2.0 * j / HD))
    cos1 = np.cos(theta).astype(np.float32)
    sin1 = np.sin(theta).astype(np.float32)
    c["cstab"] = np.concatenate([cos1, sin1], 0).astype(ml_dtypes.bfloat16)
    for hh in range(2):
        selT = np.zeros((128, 128), np.float32)
        for pcol in range(128):
            selT[64 * hh + (pcol % 64), pcol] = 1.0
        c[f"selT{hh}"] = selT
    c["ident_f32"] = np.eye(128, dtype=np.float32)
    c["ident_bf"] = np.eye(128, dtype=np.float32).astype(ml_dtypes.bfloat16)
    s = np.arange(CHUNK)[:, None]
    cc = np.arange(CHUNK)[None, :]
    c["triu1"] = (cc >= s).astype(np.float32)
    c["ones_col_bf"] = np.ones((128, 1), np.float32).astype(ml_dtypes.bfloat16)
    c["ones_row_f32"] = np.ones((1, 128), np.float32)
    c["ones_row_bf"] = np.ones((1, 128), np.float32).astype(ml_dtypes.bfloat16)
    c["k09c"] = np.full((16, N), 0.9, np.float32)
    c["ones_col_f32"] = np.ones((128, 1), np.float32)
    return c


CONST_SPECS = {
    "cstab": ([128, N], BF16),
    "selT0": ([128, 128], F32), "selT1": ([128, 128], F32),
    "ident_f32": ([128, 128], F32), "ident_bf": ([128, 128], BF16),
    "triu1": ([64, 64], F32),
    "ones_col_bf": ([128, 1], BF16), "ones_row_f32": ([1, 128], F32),
    "ones_row_bf": ([1, 128], BF16), "k09c": ([16, N], F32),
    "ones_col_f32": ([128, 1], F32),
}


def stage_consts(nc, k):
    sb = {}
    for name, (shape, dt) in CONST_SPECS.items():
        dram = nc.dram_tensor(name, shape, dt, kind="ExternalInput").ap()
        if name == "k09c":
            sb["k09c_dram"] = dram
            continue
        sb[name] = k.palloc(shape, dt)
        k.dma("sync", sb[name][:], dram, "ld")
    return sb



def emit_phase_a(nc, k, cst, x_in, wq_d, wk_d, wv_d, wo_d, wg1_d, wg2_d,
                 normw_d, xr0, xr1, o_s, dbg_bp=99, dbg_a3=True):
    k.sreset()
    ln0 = k.salloc([128, DCH * T], BF16)
    xt = k.salloc([128, 2 * D], F32)
    lnt = k.salloc([128, 2 * D], BF16)
    xsq = k.salloc([128, D], F32)
    stat = k.salloc([128, 8], F32)
    xcol = k.salloc([128, 2 * D], F32)
    m_gla = k.smark()

    wk = k.salloc([128, DCH * 128], BF16)
    wqd = [k.salloc([128, DCH * 128], BF16) for _ in range(2)]
    wvh = [k.salloc([128, DCH * 64], BF16) for _ in range(2)]
    kr = k.salloc([128, N], F32)
    qs = k.salloc([128, N], F32)
    KG = k.salloc([128, N], F32)
    CS = k.salloc([128, N], F32)
    EQ = k.salloc([128, N], F32)
    tmpf = k.salloc([128, N], F32)
    sc = k.salloc([128, 512], F32)
    QG = k.salloc([128, N], BF16)
    KGC = k.salloc([128, N], BF16)
    K2 = k.salloc([128, N], BF16)
    EQd = k.salloc([128, 32], F32)
    vfm_h = k.salloc([64, N], BF16)
    v_cm_h = k.salloc([64, NCH * 64], BF16)
    Atile = k.salloc([64, 64], BF16)
    k2tok = k.salloc([64, 128], BF16)
    S_h = k.salloc([128, 64], F32)
    S_hb = k.salloc([128, 64], BF16)
    O_h = k.salloc([64, N], BF16)

    psT = k.pbank[0][:]
    psT2 = k.pbank[1][:]
    psP = k.pbank[2][:]
    psP2 = k.pbank[3][:]
    psA = k.pbank[4][:][:, 0:128]
    psO = k.pbank[5][:][:, 0:64]
    psS = k.pbank[5][:][:, 64:192]
    psK = k.pbank[6][:].bitcast(BF16)[0:64, 0:384]

    ntile = T // 128
    xt_chk = [0] * 2
    lnt_chk = [0] * 2
    xcol_chk = [0] * 2
    for i in range(ntile):
        buf = i % 2
        xt_i = xt[:, buf * D:(buf + 1) * D]
        lnt_i = lnt[:, buf * D:(buf + 1) * D]
        xcol_i = xcol[:, buf * D:(buf + 1) * D]
        if i >= 2:
            k.wait("sync", "pe", xt_chk[buf])
        ldt = k.dma("sync", xt_i, x_in[i * 128:(i + 1) * 128, :], "ld")
        k.wait("vector", "ld", ldt)
        k.op("vector", lambda e, o=stat[:, 0:1], i_=xt_i: e.reduce_sum(o, i_, AX.X))
        k.wait("scalar", "ld", ldt)
        k.op("scalar", lambda e, o=xsq[:], i_=xt_i:
             e.activation(o, i_, AF.Square, bias=0.0, scale=1.0), inc="ac")
        k.wait("vector", "ac", k.cnt["ac"])
        k.op("vector", lambda e, o=stat[:, 1:2], i_=xsq[:]: e.reduce_sum(o, i_, AX.X),
             inc="dv")
        k.wait("vector", "dv", k.cnt["dv"])
        k.op("vector", lambda e, o=stat[:, 2:4], i_=stat[:, 0:2]:
             e.tensor_scalar(o, i_, 1.0 / D, None, ALU.mult), inc="dv")
        k.wait("vector", "dv", k.cnt["dv"])
        k.op("vector", lambda e, o=stat[:, 4:5], a=stat[:, 2:3], b=stat[:, 2:3]:
             e.tensor_mul(o, a, b), inc="dv")
        k.wait("vector", "dv", k.cnt["dv"])
        k.op("vector", lambda e, o=stat[:, 5:6], a=stat[:, 3:4], b=stat[:, 4:5]:
             e.tensor_sub(o, a, b), inc="dv")
        k.wait("vector", "dv", k.cnt["dv"])
        k.op("vector", lambda e, o=stat[:, 7:8], i_=stat[:, 5:6]:
             e.tensor_scalar(o, i_, 1e-5, None, ALU.add), inc="dv")
        k.wait("scalar", "dv", k.cnt["dv"])
        k.op("scalar", lambda e, o=stat[:, 5:6], i_=stat[:, 7:8]:
             e.activation(o, i_, AF.Sqrt, bias=0.0, scale=1.0), inc="ac")
        k.wait("vector", "ac", k.cnt["ac"])
        k.op("vector", lambda e, o=stat[:, 6:7], i_=stat[:, 5:6]:
             e.reciprocal(o, i_), inc="dv")
        k.wait("vector", "dv", k.cnt["dv"])
        if i >= 2:
            k.wait("vector", "pe", lnt_chk[buf])
        k.op("vector", lambda e, o=lnt_i, i_=xt_i, m=stat[:, 2:3], r=stat[:, 6:7]:
             e.tensor_scalar(o, i_, m, r, ALU.subtract, ALU.mult), inc="dv")
        dv_ln = k.cnt["dv"]

        k.wait("tensor", "ld", ldt)
        if i >= 2:
            k.wait("tensor", "st", xcol_chk[buf])
        for j in range(DCH):
            bank = psT if j % 2 == 0 else psT2
            pet = k.op("tensor", lambda e, o=bank[:, 0:128],
                       i_=xt_i[:, j * 128:(j + 1) * 128], idn=cst["ident_f32"][:]:
                       e.transpose(o, i_, idn), inc="pe")
            k.wait("vector", "pe", pet)
            dvt = k.op("vector", lambda e, o=xcol_i[:, j * 128:(j + 1) * 128],
                       i_=bank[:, 0:128]: e.tensor_copy(o, i_), inc="dv")
            k.wait("tensor", "dv", dvt)
        k.wait("sync", "dv", k.cnt["dv"])
        dst = xr0[:, i * 128:(i + 1) * 128].rearrange("(c p) t -> p c t", p=128)
        xcol_chk[buf] = k.dma("sync", dst, xcol_i.rearrange("p (c t) -> p c t", t=128),
                              "st")
        k.wait("tensor", "dv", dv_ln)
        for j in range(DCH):
            bank = (psT if j % 2 == 0 else psT2)[:, 0:128].bitcast(BF16)
            pet = k.op("tensor", lambda e, o=bank[:, 0:128],
                       i_=lnt_i[:, j * 128:(j + 1) * 128], idn=cst["ident_bf"][:]:
                       e.transpose(o, i_, idn), inc="pe")
            k.wait("vector", "pe", pet)
            dvt = k.op("vector", lambda e, o=ln0[:, j * T + i * 128: j * T + (i + 1) * 128],
                       i_=bank[:, 0:128]: e.tensor_copy(o, i_), inc="dv")
            k.wait("tensor", "dv", dvt)
        lnt_chk[buf] = k.cnt["pe"]
        xt_chk[buf] = k.cnt["pe"]
    a1_dv = k.fence("vector", "dv")
    for eng in ("tensor", "sync", "gpsimd"):
        k.wait(eng, "dv", a1_dv)

    NSL = N // 512

    def proj_fm(wsb, out_sb, bslice, out_dtype, act=None):
        for ns in range(NSL):
            bank = psP if ns % 2 == 0 else psP2
            for a in range(DCH):
                pet = k.op("tensor", lambda e, o=bank[:],
                           l=wsb[:, a * 128:(a + 1) * 128],
                           r=ln0[:, a * T + bslice * N + ns * 512:
                                 a * T + bslice * N + ns * 512 + 512],
                           s=(a == 0), p=(a == DCH - 1):
                           e.matmul(o, l, r, start=s, stop=p), inc="pe")
            k.wait("vector" if act is None else "scalar", "pe", pet)
            o_sl = out_sb[:, ns * 512:(ns + 1) * 512]
            if act is None:
                dvt = k.op("vector", lambda e, o=o_sl, i_=bank[:]:
                           e.tensor_copy(o, i_), inc="dv")
                k.wait("tensor", "dv", dvt)
            else:
                sct = k.op("scalar", lambda e, o=o_sl, i_=bank[:], f=act:
                           e.activation(o, i_, f, bias=0.0, scale=1.0), inc="ac")
                k.wait("tensor", "ac", sct)

    import os
    a2stop = int(os.environ.get("DBG_A2STOP", "99"))
    for bp in range(min(B_CORE * NPAIR, dbg_bp)):
        p, b = divmod(bp, B_CORE)
        c0 = p * 128
        if b == 0:
            if p > 0:
                k.wait("gpsimd", "pe", k.cnt["pe"])
            src = wk_d[:, c0:c0 + 128].rearrange("(a p) c -> p a c", p=128)
            k.dma("gpsimd", wk[:].rearrange("p (a c) -> p a c", c=128), src, "wld")
            for hh in range(2):
                hc = c0 + 64 * hh
                qsrc = wq_d[:, hc:hc + 64].rearrange("(a p) c -> p a c", p=128)
                wqv = wqd[hh][:].rearrange("p (a c) -> p a c", c=128)
                k.dma("gpsimd", wqv[:, :, 0:64], qsrc, "wld")
                k.dma("gpsimd", wqv[:, :, 64:128], qsrc, "wld")
                vsrc = wv_d[:, hc:hc + 64].rearrange("(a p) c -> p a c", p=128)
                k.dma("gpsimd", wvh[hh][:].rearrange("p (a c) -> p a c", c=64),
                      vsrc, "wld")
            k.wait("tensor", "wld", k.cnt["wld"])
        proj_fm(wk, kr, b, F32)
        if a2stop <= 1:
            continue

        kr3 = kr[:].rearrange("p (c k) -> p c k", k=CHUNK)
        m32 = sc[:, 0:32]
        T32 = sc[:, 32:64]
        La = sc[:, 96:128]
        Lb = sc[:, 128:160]
        Zc = sc[:, 160:192]
        t32a = sc[:, 192:224]
        t32b = sc[:, 224:256]
        EQend = sc[:, 256:288]
        k.vop_s(lambda e, o=m32, i_=kr3: e.reduce_max(o, i_, AX.X))
        k.op("vector", lambda e, o=tmpf[:].rearrange("p (c k) -> p c k", k=CHUNK),
             a=kr3, m=m32: e.tensor_tensor(
                 o, a, m.unsqueeze(2).to_broadcast((128, NCH, CHUNK)), ALU.subtract),
             inc="dv")
        k.wait("scalar", "dv", k.cnt["dv"])
        k.op("scalar", lambda e, o=tmpf[:], i_=tmpf[:]:
             e.activation(o, i_, AF.Exp, bias=0.0, scale=1.0), inc="ac")
        k.wait("vector", "ac", k.cnt["ac"])
        k.op("vector", lambda e, o=T32,
             i_=tmpf[:].rearrange("p (c k) -> p c k", k=CHUNK):
             e.reduce_sum(o, i_, AX.X), inc="dv")
        k.wait("scalar", "dv", k.cnt["dv"])
        k.op("scalar", lambda e, o=T32, i_=T32:
             e.activation(o, i_, AF.Ln, bias=0.0, scale=1.0), inc="ac")
        k.wait("vector", "ac", k.cnt["ac"])
        k.vop_s(lambda e, o=La, a=m32, b_=T32: e.tensor_add(o, a, b_))
        cur, nxt = La, Lb
        for d in (1, 2, 4, 8, 16):
            k.vop_s(lambda e, o=nxt[:, 0:d], i_=cur[:, 0:d], d=d:
                    e.tensor_copy(o, i_))
            k.vop_s(lambda e, o=nxt[:, d:32], a=cur[:, d:32], b_=cur[:, 0:32 - d],
                    d=d: e.tensor_max(o, a, b_))
            k.vop_s(lambda e, o=t32a[:, 0:32 - d], a=cur[:, d:32],
                    b_=cur[:, 0:32 - d], d=d: e.tensor_tensor(o, a, b_, ALU.min))
            k.vop_s(lambda e, o=t32a[:, 0:32 - d], a=t32a[:, 0:32 - d],
                    b_=nxt[:, d:32], d=d: e.tensor_sub(o, a, b_))
            k.wait("scalar", "dv", k.cnt["dv"])
            sct = k.op("scalar", lambda e, o=t32b[:, 0:32 - d], i_=t32a[:, 0:32 - d],
                       d=d: e.activation(o, i_, AF.Exp, bias=0.0, scale=1.0),
                       inc="ac")
            k.wait("scalar", "ac", sct)
            k.op("scalar", lambda e, o=t32b[:, 0:32 - d], i_=t32b[:, 0:32 - d], d=d:
                 e.activation(o, i_, AF.Ln, bias=1.0, scale=1.0), inc="ac")
            k.wait("vector", "ac", k.cnt["ac"])
            k.vop_s(lambda e, o=nxt[:, d:32], a=nxt[:, d:32],
                    b_=t32b[:, 0:32 - d], d=d: e.tensor_add(o, a, b_))
            cur, nxt = nxt, cur
        k.vop_s(lambda e, o=Zc[:, 0:1]: e.memset(o, 0.0))
        k.wait("scalar", "dv", k.cnt["dv"])
        sct = k.op("scalar", lambda e, o=t32a[:, 0:31], i_=cur[:, 0:31]:
                   e.activation(o, i_, AF.Abs, bias=0.0, scale=1.0), inc="ac")
        k.wait("scalar", "ac", sct)
        sct = k.op("scalar", lambda e, o=t32a[:, 0:31], i_=t32a[:, 0:31]:
                   e.activation(o, i_, AF.Exp, bias=0.0, scale=-1.0), inc="ac")
        k.wait("scalar", "ac", sct)
        k.op("scalar", lambda e, o=t32a[:, 0:31], i_=t32a[:, 0:31]:
             e.activation(o, i_, AF.Ln, bias=1.0, scale=1.0), inc="ac")
        k.wait("vector", "ac", k.cnt["ac"])
        k.vop_s(lambda e, o=t32b[:, 0:31], i_=cur[:, 0:31]:
                e.tensor_scalar(o, i_, 0.0, None, ALU.max))
        k.op("vector", lambda e, o=Zc[:, 1:32], a=t32b[:, 0:31], b_=t32a[:, 0:31]:
             e.tensor_add(o, a, b_), inc="dv")
        k.op("vector", lambda e, o=KG[:].rearrange("p (c k) -> p c k", k=CHUNK),
             a=kr3, z=Zc: e.tensor_tensor(
                 o, a, z.unsqueeze(2).to_broadcast((128, NCH, CHUNK)), ALU.subtract),
             inc="dv")
        k.wait("scalar", "dv", k.cnt["dv"])
        k.op("scalar", lambda e, o=KG[:], i_=KG[:]:
             e.activation(o, i_, AF.Exp, bias=0.0, scale=1.0), inc="ac")
        ac_kg = k.cnt["ac"]
        k.wait("vector", "ac", ac_kg)
        for c in range(NCH):
            k.op("vector", lambda e, o=CS[:, c * 64:(c + 1) * 64],
                 d0=KG[:, c * 64:(c + 1) * 64]:
                 e.tensor_tensor_scan(o, d0, d0, 1.0, ALU.add, ALU.bypass))
        k.op("vector", lambda e: e.engine_nop(), inc="dv")
        k.wait("scalar", "dv", k.cnt["dv"])
        k.op("scalar", lambda e, o=EQ[:], i_=CS[:]:
             e.activation(o, i_, AF.Ln, bias=0.0, scale=1.0))
        k.op("scalar", lambda e, o=EQ[:], i_=EQ[:]:
             e.activation(o, i_, AF.Exp, bias=0.0, scale=-1.0), inc="ac")
        k.wait("vector", "ac", k.cnt["ac"])
        k.vop_s(lambda e, o=EQend.unsqueeze(2),
                i_=EQ[:].rearrange("p (c k) -> p c k", k=CHUNK)[:, :, 63:64]:
                e.tensor_copy(o, i_))
        k.op("vector", lambda e: e.engine_nop(), inc="dv")
        zfull_dv = k.cnt["dv"]
        zfull_ac = k.cnt["ac"]
        if os.environ.get("DBG_DUMP") and bp == 0:
            dmp = nc.dram_tensor("dump", [6, 128, N], F32, kind="ExternalOutput").ap()
            k.wait("sync", "dv", k.cnt["dv"])
            k.wait("sync", "ac", k.cnt["ac"])
            for di, buf in enumerate((kr, KG, CS, EQ)):
                k.dma("sync", dmp[di], buf[:], "st")
            k.dma("gpsimd", dmp[4], ln0[:, 0:N], "wld")
            k.wait("sync", "wld", k.cnt["wld"])
            k.dma("sync", dmp[5, 0:128, 0:64], sc[:, 0:64], "st")
            k.dma("sync", dmp[5, 0:128, 96:192], sc[:, 96:192], "st")
            k.dma("sync", dmp[5, 0:128, 256:288], sc[:, 256:288], "st")
            k.wait("vector", "st", k.cnt["st"])
        if a2stop <= 2:
            continue

        for hh in range(2):
            proj_fm(wqd[hh], qs, b, F32, act=AF.Silu)
            for ns in range(NSL):
                bank = (psP if ns % 2 == 0 else psP2)[0:64, :]
                for a in range(DCH):
                    pet = k.op("tensor", lambda e, o=bank,
                               l=wvh[hh][:, a * 64:(a + 1) * 64],
                               r=ln0[:, a * T + b * N + ns * 512:
                                     a * T + b * N + ns * 512 + 512],
                               s=(a == 0), q=(a == DCH - 1):
                               e.matmul(o, l, r, start=s, stop=q), inc="pe",
                               mmb=0)
                k.wait("vector", "pe", pet)
                dvt = k.op("vector", lambda e, o=vfm_h[:, ns * 512:(ns + 1) * 512],
                           i_=bank: e.tensor_copy(o, i_), inc="dv")
                k.wait("tensor", "dv", dvt)
            k.wait("tensor", "dv", zfull_dv)
            k.wait("tensor", "ac", zfull_ac)
            for ns in range(NSL):
                nsl = slice(ns * 512, (ns + 1) * 512)
                pet = k.op("tensor", lambda e, o=psT[:], l=cst[f"selT{hh}"][:],
                           r=EQ[:, nsl]: e.matmul(o, l, r, start=True, stop=True),
                           mmb=0)
                pet = k.op("tensor", lambda e, o=psT2[:], l=cst[f"selT{hh}"][:],
                           r=KG[:, nsl]: e.matmul(o, l, r, start=True, stop=True),
                           inc="pe", mmb=0)
                k.wait("vector", "pe", pet)
                k.op("vector", lambda e, o=QG[:, nsl], a=qs[:, nsl],
                     b_=cst["cstab"][:, nsl]: e.tensor_mul(o, a, b_))
                k.op("vector", lambda e, o=QG[:, nsl], a=QG[:, nsl], b_=psT[:]:
                     e.tensor_mul(o, a, b_))
                dvt = k.op("vector", lambda e, o=KGC[:, nsl],
                           a=cst["cstab"][:, nsl], b_=psT2[:]:
                           e.tensor_mul(o, a, b_), inc="dv")
                k.wait("tensor", "dv", dvt)
            pet = k.op("tensor", lambda e, o=psT[:, 0:32], l=cst[f"selT{hh}"][:],
                       r=EQend: e.matmul(o, l, r, start=True, stop=True),
                       inc="pe", mmb=0)
            k.wait("vector", "pe", pet)
            k.vop_s(lambda e, o=EQd[:], i_=psT[:, 0:32]: e.tensor_copy(o, i_))
            k.op("vector", lambda e,
                 o=K2[:].rearrange("p (c k) -> p c k", k=CHUNK),
                 a=KGC[:].rearrange("p (c k) -> p c k", k=CHUNK), z=EQd[:]:
                 e.tensor_tensor(o, a, z.unsqueeze(2).to_broadcast(
                     (128, NCH, CHUNK)), ALU.mult))
            dvt = k.op("vector", lambda e: e.engine_nop(), inc="dv")
            k.wait("tensor", "dv", dvt)
            for c in range(NCH):
                bank = psK[:, 256 + (c % 2) * 64: 256 + (c % 2) * 64 + 64]
                pet = k.op("tensor", lambda e, o=bank,
                           i_=vfm_h[:, c * 64:(c + 1) * 64],
                           idn=cst["ident_bf"][0:64, 0:64]:
                           e.transpose(o, i_, idn), inc="pe", mmb=0)
                k.wait("vector", "pe", pet)
                dvt = k.op("vector", lambda e, o=v_cm_h[:, c * 64:(c + 1) * 64],
                           i_=bank: e.tensor_copy(o, i_), inc="dv")
                k.wait("tensor", "dv", dvt)
            k.op("vector", lambda e, o=S_h[:]: e.memset(o, 0.0))
            k.op("vector", lambda e, o=S_hb[:]: e.memset(o, 0.0), inc="dv")
            k.wait("tensor", "dv", k.cnt["dv"])
            if a2stop <= 3:
                continue

            cmask = int(os.environ.get("DBG_CHUNK", "15"))
            for c in range(NCH):
                csl = slice(c * 64, (c + 1) * 64)
                vsl = slice(c * 64, (c + 1) * 64)
                if cmask & 8:
                    k.op("tensor", lambda e, o=psK[:, 0:128], i_=K2[:, csl],
                         idn=cst["ident_bf"][:]: e.transpose(o, i_, idn), mmb=0)
                pet = None
                if cmask & 1:
                    pet = k.op("tensor", lambda e, o=psA[0:64, 0:64], l=KGC[:, csl],
                               r=QG[:, csl]: e.matmul(o, l, r, start=True, stop=True),
                               inc="pe", mmb=0)
                if pet is not None:
                    k.wait("vector", "pe", pet)
                k.op("vector", lambda e, o=k2tok[:], i_=psK[0:64, 0:128]:
                     e.tensor_copy(o, i_))
                dvt = k.op("vector", lambda e, o=Atile[:], a=psA[0:64, 0:64],
                           m=cst["triu1"][:]: e.tensor_mul(o, a, m), inc="dv")
                k.wait("tensor", "dv", dvt)
                pet = None
                if cmask & 2:
                    k.op("tensor", lambda e, o=psO[0:64, 0:64], l=S_hb[:],
                         r=QG[:, csl]: e.matmul(o, l, r, start=True, stop=False),
                         mmb=0)
                    pet = k.op("tensor", lambda e, o=psO[0:64, 0:64], l=v_cm_h[:, vsl],
                               r=Atile[:]: e.matmul(o, l, r, start=False, stop=True),
                               mmb=0, inc="pe")
                if cmask & 4:
                    pet = k.op("tensor", lambda e, o=psS[:, 0:64], l=k2tok[:],
                               r=v_cm_h[:, vsl]:
                               e.matmul(o, l, r, start=True, stop=True),
                               inc="pe", mmb=0)
                if pet is not None:
                    k.wait("vector", "pe", pet)
                k.op("vector", lambda e, o=O_h[:, csl], i_=psO[0:64, 0:64]:
                     e.tensor_copy(o, i_))
                k.op("vector", lambda e, o=S_h[:], i_=S_h[:], s=EQd[:, c:c + 1],
                     p_=psS[:, 0:64]:
                     e.scalar_tensor_tensor(o, i_, s, p_, ALU.mult, ALU.add))
                dvt = k.op("vector", lambda e, o=S_hb[:], i_=S_h[:]:
                           e.tensor_copy(o, i_), inc="dv")
                k.wait("tensor", "dv", dvt)
            k.op("vector", lambda e: e.engine_nop(), inc="dv")
            k.wait("sync", "dv", k.cnt["dv"])
            r0 = (2 * p + hh) * 64
            ot = k.dma("sync", o_s[r0:r0 + 64, b * N:(b + 1) * N], O_h[:], "st")
            k.wait("vector", "st", ot)

    if not dbg_a3:
        fin = k.fence("vector", "dv")
        k.wait("sync", "dv", fin)
        k.wait("sync", "st", k.cnt["st"])
        return
    k.srewind(m_gla)
    wg1 = k.salloc([128, DCH * 16], BF16)
    wg2 = k.salloc([16, D], BF16)
    wo = k.salloc([128, DCH * D], BF16)
    nrm = k.salloc([128, DCH], F32)
    t1 = k.salloc([16, T], BF16)
    gt = k.salloc([128, 512], BF16)
    og = k.salloc([128, DCH * 512], BF16)
    osl = k.salloc([128, 512], BF16)
    sq = k.salloc([128, 512], BF16)
    ssq = k.salloc([1, 512], F32)
    rr = k.salloc([1, 512], F32)
    xold = k.salloc([128, 2 * 512], F32)
    xnew = k.salloc([128, 2 * 512], F32)

    k.wait("gpsimd", "pe", k.cnt["pe"])
    k.dma("gpsimd", wg1[:].rearrange("p (a c) -> p a c", c=16),
          wg1_d.rearrange("(a p) c -> p a c", p=128), "wld")
    k.dma("gpsimd", wg2[:], wg2_d, "wld")
    k.dma("gpsimd", wo[:].rearrange("p (a c) -> p a c", c=D),
          wo_d.rearrange("(a p) c -> p a c", p=128), "wld")
    for j in range(DCH):
        k.dma("gpsimd", nrm[:, j:j + 1],
              normw_d[j * 128:(j + 1) * 128].unsqueeze(1), "wld")
    wld3 = k.cnt["wld"]
    k.wait("tensor", "wld", wld3)
    k.wait("vector", "wld", wld3)

    for ns in range(T // 512):
        bank = psP if ns % 2 == 0 else psP2
        for a in range(DCH):
            pet = k.op("tensor", lambda e, o=bank[0:16, :],
                       l=wg1[:, a * 16:(a + 1) * 16],
                       r=ln0[:, a * T + ns * 512: a * T + ns * 512 + 512],
                       s=(a == 0), p=(a == DCH - 1):
                       e.matmul(o, l, r, start=s, stop=p), inc="pe")
        k.wait("vector", "pe", pet)
        dvt = k.op("vector", lambda e, o=t1[:, ns * 512:(ns + 1) * 512],
                   i_=bank[0:16, :]: e.tensor_copy(o, i_), inc="dv")
        k.wait("tensor", "dv", dvt)

    og_prev = 0
    sqmm_prev = 0
    for ts in range(T // 512):
        tsl = slice(ts * 512, (ts + 1) * 512)
        for j in range(DCH):
            bank = psP
            pet = k.op("tensor", lambda e, o=bank[:], l=wg2[:, j * 128:(j + 1) * 128],
                       r=t1[:, tsl]: e.matmul(o, l, r, start=True, stop=True),
                       inc="pe")
            k.wait("scalar", "pe", pet)
            k.wait("scalar", "dv", og_prev)
            act = k.op("scalar", lambda e, o=gt[:], i_=bank[:]:
                       e.activation(o, i_, AF.Sigmoid, bias=0.0, scale=1.0),
                       inc="ac")
            k.wait("tensor", "ac", act)
            k.wait("sync", "dv", og_prev)
            lo = k.dma("sync", osl[:], o_s[j * 128:(j + 1) * 128, tsl], "ld")
            k.wait("vector", "ld", lo)
            k.wait("vector", "ac", act)
            og_prev = k.op("vector", lambda e, o=og[:, j * 512:(j + 1) * 512],
                           a=osl[:], b_=gt[:]: e.tensor_mul(o, a, b_), inc="dv")
            k.wait("scalar", "dv", og_prev)
            k.wait("scalar", "pe", sqmm_prev)
            act2 = k.op("scalar", lambda e, o=sq[:], i_=og[:, j * 512:(j + 1) * 512]:
                        e.activation(o, i_, AF.Square, bias=0.0, scale=1.0),
                        inc="ac")
            k.wait("tensor", "ac", act2)
            sqmm_prev = k.op("tensor", lambda e, o=psP2[0:1, :],
                             l=cst["ones_col_bf"][:], r=sq[:],
                             s=(j == 0), p=(j == DCH - 1):
                             e.matmul(o, l, r, start=s, stop=p), inc="pe")
        k.wait("vector", "pe", k.cnt["pe"])
        k.op("vector", lambda e, o=ssq[:], i_=psP2[0:1, :]: e.tensor_copy(o, i_))
        k.op("vector", lambda e, o=ssq[:], i_=ssq[:]:
             e.tensor_scalar(o, i_, 1.0 / D, 1e-6, ALU.mult, ALU.add), inc="dv")
        k.wait("scalar", "dv", k.cnt["dv"])
        k.op("scalar", lambda e, o=rr[:], i_=ssq[:]:
             e.activation(o, i_, AF.Sqrt, bias=0.0, scale=1.0), inc="ac")
        k.wait("vector", "ac", k.cnt["ac"])
        k.op("vector", lambda e, o=rr[:], i_=rr[:]: e.reciprocal(o, i_), inc="dv")
        k.wait("tensor", "dv", k.cnt["dv"])
        pet = k.op("tensor", lambda e, o=psP2[:], l=cst["ones_row_f32"][:], r=rr[:]:
                   e.matmul(o, l, r, start=True, stop=True), inc="pe")
        k.wait("vector", "pe", pet)
        for j in range(DCH):
            k.op("vector", lambda e, o=og[:, j * 512:(j + 1) * 512],
                 i_=og[:, j * 512:(j + 1) * 512], s=nrm[:, j:j + 1], p_=psP2[:]:
                 e.scalar_tensor_tensor(o, i_, s, p_, ALU.mult, ALU.mult))
        dvt = k.op("vector", lambda e: e.engine_nop(), inc="dv")
        k.wait("tensor", "dv", dvt)
        for j2 in range(DCH):
            bank = psP
            for a in range(DCH):
                pet = k.op("tensor", lambda e, o=bank[:],
                           l=wo[:, a * D + j2 * 128: a * D + j2 * 128 + 128],
                           r=og[:, a * 512:(a + 1) * 512],
                           s=(a == 0), p=(a == DCH - 1):
                           e.matmul(o, l, r, start=s, stop=p), inc="pe")
            buf = j2 % 2
            lo = k.dma("sync", xold[:, buf * 512:(buf + 1) * 512],
                       xr0[j2 * 128:(j2 + 1) * 128, tsl], "ld")
            k.wait("vector", "pe", pet)
            k.wait("vector", "ld", lo)
            k.op("vector", lambda e, o=xnew[:, buf * 512:(buf + 1) * 512],
                 a=bank[:], b_=xold[:, buf * 512:(buf + 1) * 512]:
                 e.tensor_add(o, a, b_), inc="dv")
            k.wait("tensor", "dv", k.cnt["dv"])
            k.wait("sync", "dv", k.cnt["dv"])
            k.dma("sync", xr1[j2 * 128:(j2 + 1) * 128, tsl],
                  xnew[:, buf * 512:(buf + 1) * 512], "st")
    if os.environ.get("DBG_A3D"):
        dmp3 = nc.dram_tensor("dumpa3", [4, 128, 4096], F32,
                              kind="ExternalOutput").ap()
        f2 = k.fence("vector", "dv")
        k.wait("sync", "dv", f2)
        k.wait("gpsimd", "dv", f2)
        k.dma("gpsimd", dmp3[0], og[:], "wld")
        k.dma("gpsimd", dmp3[1][:, 0:512], gt[:], "wld")
        k.dma("sync", dmp3[2][0:1, 0:512], rr[:], "st")
        k.dma("sync", dmp3[3][:, 0:1024], xold[:], "st")
        k.wait("sync", "wld", k.cnt["wld"])
    fin = k.fence("vector", "dv")
    k.wait("sync", "dv", fin)
    k.wait("sync", "st", k.cnt["st"])



def new_program():
    nc = bass.Bass("TRN2", target_bir_lowering=False,
                   detect_race_conditions=False)
    k = K(nc)
    for s in BASE_SEMS:
        k.sem(s)
    bt = nc.alloc_sbuf_tensor_at("bar_tile", [1, 8], F32, offset=ARENA_BASE - 64)
    k.bar_tile = bt[0:1, 0:1]
    k.q["scalar"].append(("op", lambda e, o=k.bar_tile: e.memzero(o), None, 1))
    k.pb2k = [nc.alloc_psum_tensor(f"pb2k{i}", [128, 1024], F32) for i in range(4)]

    class _V:
        def __init__(self, ap):
            self._ap = ap

        def __getitem__(self, key):
            return self._ap[key]

    k.pbank = [_V(k.pb2k[i // 2][:][:, (i % 2) * 512:(i % 2 + 1) * 512])
               for i in range(8)]
    return nc, k


def build_phase_a_program():
    nc, k = new_program()
    x_in = nc.dram_tensor("x", [T, D], F32, kind="ExternalInput").ap()
    wq_d = nc.dram_tensor("Wq", [D, D], F32, kind="ExternalInput").ap()
    wk_d = nc.dram_tensor("Wk", [D, D], F32, kind="ExternalInput").ap()
    wv_d = nc.dram_tensor("Wv", [D, D], F32, kind="ExternalInput").ap()
    wo_d = nc.dram_tensor("Wo", [D, D], F32, kind="ExternalInput").ap()
    wg1_d = nc.dram_tensor("Wg1", [D, 16], F32, kind="ExternalInput").ap()
    wg2_d = nc.dram_tensor("Wg2", [16, D], F32, kind="ExternalInput").ap()
    normw_d = nc.dram_tensor("norm_w", [D], F32, kind="ExternalInput").ap()
    xr0 = nc.dram_tensor("xr0", [D, T], F32, kind="Internal").ap()
    xr1 = nc.dram_tensor("xr1", [D, T], F32, kind="ExternalOutput").ap()
    import os as _os
    o_s = nc.dram_tensor("o_s", [D, T], BF16,
                         kind="ExternalOutput" if _os.environ.get("DBG_OS")
                         else "Internal").ap()
    cst = stage_consts(nc, k)
    for eng in ENGINES:
        k.wait(eng, "ld", k.cnt["ld"])
    k.freeze_persistent()
    k.set_phase("pa")
    import os
    dbg_bp = int(os.environ.get("DBG_BP", "99"))
    dbg_a3 = os.environ.get("DBG_A3", "1") == "1"
    emit_phase_a(nc, k, cst, x_in, wq_d, wk_d, wv_d, wo_d, wg1_d, wg2_d,
                 normw_d, xr0, xr1, o_s, dbg_bp=dbg_bp, dbg_a3=dbg_a3)
    k.build()
    return nc


_PROGS = {}


def _prog(name, builder):
    if name not in _PROGS:
        _PROGS[name] = builder()
    return _PROGS[name]


def kernel(**inputs):
    import os as _os
    if _os.environ.get("UNFUSED"):
        return kernel_unfused(**inputs)
    return kernel_fused(**inputs)


def kernel_unfused(**inputs):
    consts = host_constants()
    NCORES = 8
    x_full = inputs["L0_attn_x"]
    x_slices = [np.ascontiguousarray(
        x_full[c * B_CORE:(c + 1) * B_CORE].reshape(T, D)) for c in range(NCORES)]

    ncA = _prog("a", build_phase_a_program)
    maps = []
    for c in range(NCORES):
        m = dict(consts)
        m["x"] = x_slices[c]
        for nm, src in (("Wq", "L0_attn_Wq"), ("Wk", "L0_attn_Wk"),
                        ("Wv", "L0_attn_Wv"), ("Wo", "L0_attn_Wo"),
                        ("Wg1", "L0_attn_Wg1"), ("Wg2", "L0_attn_Wg2"),
                        ("norm_w", "L0_attn_norm_w")):
            m[nm] = inputs[src]
        maps.append(m)
    resA = run_bass_kernel_spmd(ncA, maps, core_ids=list(range(NCORES)))
    xr1s = [r["xr1"] for r in resA.results]

    ncB = _prog("b", build_phase_b_program)
    maps = []
    for c in range(NCORES):
        m = dict(consts)
        m["xr1"] = xr1s[c]
        for nm, src in (("router", "L0_ffn_router_DE"), ("w1", "L0_ffn_w1_eDF"),
                        ("w3", "L0_ffn_w3_eDF"), ("w2", "L0_ffn_w2_eFD"),
                        ("ws1", "L0_ffn_ws1_DF"), ("ws3", "L0_ffn_ws3_DF"),
                        ("ws2", "L0_ffn_ws2_FD")):
            m[nm] = inputs[src]
        maps.append(m)
    resB = run_bass_kernel_spmd(ncB, maps, core_ids=list(range(NCORES)))
    xr2s = [r["xr2"] for r in resB.results]

    ncC = _prog("c", build_phase_c_program)
    maps = []
    for c in range(NCORES):
        m = dict(consts)
        m["xr2"] = xr2s[c]
        for nm, src in (("w_in", "L1_attn_w_in"), ("conv_w", "L1_attn_conv_w"),
                        ("conv_b", "L1_attn_conv_b"), ("w_B", "L1_attn_w_B"),
                        ("w_C", "L1_attn_w_C"), ("w_out", "L1_attn_w_out")):
            m[nm] = inputs[src]
        maps.append(m)
    resC = run_bass_kernel_spmd(ncC, maps, core_ids=list(range(NCORES)))
    x3ts = [r["x3t"] for r in resC.results]

    ncD = _prog("d", build_phase_d_program)
    maps = []
    for c in range(NCORES):
        m = dict(consts)
        m["x3t"] = x3ts[c]
        for nm, src in (("w_gate", "L1_ffn_w_gate"), ("w_up", "L1_ffn_w_up"),
                        ("w_down", "L1_ffn_w_down")):
            m[nm] = inputs[src]
        maps.append(m)
    resD = run_bass_kernel_spmd(ncD, maps, core_ids=list(range(NCORES)))
    outs = [np.asarray(r["out"]) for r in resD.results]
    return np.concatenate(outs, axis=0).reshape(16, N, D).astype(np.float32)



F_HALF = 1408
FC_HALF = 11
NS_B = 4


def emit_phase_b(nc, k, cst, xr1, router_d, w1_d, w3_d, w2_d, ws1_d, ws3_d,
                 ws2_d, xr2, lnS, alpha_d):
    k.sreset()
    m_b1 = k.smark()
    xsl = k.salloc([128, 512], F32)
    lnf = [k.salloc([128, 512], F32) for _ in range(DCH)]
    xb = k.salloc([128, 512], BF16)
    xsqb = k.salloc([128, 512], BF16)
    row = k.salloc([1, 512 * 4], F32)
    Rb = k.salloc([128, 512], F32)
    lnbf = k.salloc([128, 512], BF16)
    alpha = k.palloc_dummy if False else None
    alpha = k.salloc([8, T], F32)
    router = k.salloc([128, DCH * 8], F32)
    scst = k.salloc([128, 19], F32)

    psA_ = k.pbank[0][:]
    psB_ = k.pbank[1][:]
    psM = k.pbank[2][:]
    psR = k.pbank[3][:]
    psS_ = k.pbank[4][:]

    k.dma("sync", router[:].rearrange("p (a c) -> p a c", c=8),
          router_d.rearrange("(a p) c -> p a c", p=128), "ld")
    b1_ld = k.cnt["ld"]
    k.wait("tensor", "ld", b1_ld)

    prev_copy_dv = 0
    prev_mm_pe = 0
    for ts in range(T // 512):
        tsl = slice(ts * 512, (ts + 1) * 512)
        for j in range(DCH):
            k.wait("sync", "dv", prev_copy_dv)
            ldt = k.dma("sync", xsl[:], xr1[j * 128:(j + 1) * 128, tsl], "ld")
            k.wait("vector", "ld", ldt)
            k.wait("vector", "pe", prev_mm_pe)
            dvt = k.op("vector", lambda e, o=xb[:], i_=xsl[:]:
                       e.tensor_copy(o, i_), inc="dv")
            k.wait("scalar", "dv", dvt)
            k.wait("scalar", "pe", prev_mm_pe)
            act = k.op("scalar", lambda e, o=xsqb[:], i_=xb[:]:
                       e.activation(o, i_, AF.Square, bias=0.0, scale=1.0),
                       inc="ac")
            k.wait("tensor", "dv", dvt)
            k.wait("tensor", "ac", act)
            k.op("tensor", lambda e, o=psA_[0:1, :], l=cst["ones_col_bf"][:],
                 r=xb[:], s=(j == 0), q=(j == DCH - 1):
                 e.matmul(o, l, r, start=s, stop=q), mmb=0)
            prev_mm_pe = k.op("tensor", lambda e, o=psB_[0:1, :],
                              l=cst["ones_col_bf"][:],
                              r=xsqb[:], s=(j == 0), q=(j == DCH - 1):
                              e.matmul(o, l, r, start=s, stop=q), inc="pe", mmb=0)
            prev_copy_dv = k.op("vector", lambda e, o=lnf[j][:], i_=xsl[:]:
                                e.tensor_copy(o, i_), inc="dv")
        k.wait("vector", "pe", k.cnt["pe"])
        r_m = row[:, 0:512]
        r_msq = row[:, 512:1024]
        r_var = row[:, 1024:1536]
        r_rstd = row[:, 1536:2048]
        k.vop_s(lambda e, o=r_m, i_=psA_[0:1, :]:
                e.tensor_scalar(o, i_, 1.0 / D, None, ALU.mult))
        k.vop_s(lambda e, o=r_msq, i_=psB_[0:1, :]:
                e.tensor_scalar(o, i_, 1.0 / D, None, ALU.mult))
        k.vop_s(lambda e, o=r_var, a=r_m, b_=r_m: e.tensor_mul(o, a, b_))
        k.vop_s(lambda e, o=r_rstd, a=r_msq, b_=r_var: e.tensor_sub(o, a, b_))
        k.vop_s(lambda e, o=r_var, i_=r_rstd:
                e.tensor_scalar(o, i_, 1e-5, None, ALU.add))
        k.wait("scalar", "dv", k.cnt["dv"])
        sct = k.op("scalar", lambda e, o=r_rstd, i_=r_var:
                   e.activation(o, i_, AF.Sqrt, bias=0.0, scale=1.0), inc="ac")
        k.wait("vector", "ac", sct)
        k.vop_s(lambda e, o=r_var, i_=r_rstd: e.reciprocal(o, i_))
        k.wait("tensor", "dv", k.cnt["dv"])
        k.op("tensor", lambda e, o=psM[:], l=cst["ones_row_f32"][:], r=r_m:
             e.matmul(o, l, r, start=True, stop=True), mmb=0)
        pet = k.op("tensor", lambda e, o=psR[:], l=cst["ones_row_f32"][:],
                   r=r_var: e.matmul(o, l, r, start=True, stop=True),
                   inc="pe", mmb=0)
        k.wait("vector", "pe", pet)
        k.op("vector", lambda e, o=Rb[:], i_=psR[:]: e.tensor_copy(o, i_))
        for j in range(DCH):
            k.op("vector", lambda e, o=lnf[j][:], a=lnf[j][:], b_=psM[:]:
                 e.tensor_sub(o, a, b_))
            k.op("vector", lambda e, o=lnf[j][:], a=lnf[j][:], b_=Rb[:]:
                 e.tensor_mul(o, a, b_))
            dvt = k.op("vector", lambda e, o=lnbf[:], i_=lnf[j][:]:
                       e.tensor_copy(o, i_), inc="dv")
            k.wait("sync", "dv", dvt)
            stt = k.dma("sync", lnS[j * 128:(j + 1) * 128, tsl], lnbf[:], "st")
            k.wait("vector", "st", stt)
        k.op("vector", lambda e: e.engine_nop(), inc="dv")
        k.wait("tensor", "dv", k.cnt["dv"])
        for tt in range(4):
            ttl = slice(tt * 128, (tt + 1) * 128)
            for j in range(DCH):
                pet = k.op("tensor", lambda e, o=psS_[:, 0:8], l=lnf[j][:, ttl],
                           r=router[:, j * 8:(j + 1) * 8], s=(j == 0),
                           q=(j == DCH - 1): e.matmul(o, l, r, start=s, stop=q),
                           inc="pe", mmb=0)
            prev_mm_pe = pet
            k.wait("vector", "pe", pet)
            k.vop_s(lambda e, o=scst[:, 0:1], i_=psS_[:, 0:8]:
                    e.reduce_max(o, i_, AX.X))
            k.wait("scalar", "dv", k.cnt["dv"])
            sct = k.op("scalar", lambda e, o=scst[:, 1:2], i_=scst[:, 0:1]:
                       e.activation(o, i_, AF.Sigmoid, bias=0.0, scale=1.0),
                       inc="ac")
            k.vop_s(lambda e, o=scst[:, 2:3], i_=scst[:, 0:1]:
                    e.tensor_copy(o, i_))
            k.op("vector", lambda e, o=scst[:, 3:11], i_=psS_[:, 0:8],
                 m=scst[:, 0:1]:
                 e.tensor_scalar(o, i_, m, None, ALU.is_equal))
            k.wait("vector", "ac", sct)
            k.vop_s(lambda e, o=scst[:, 11:19], i_=scst[:, 3:11], g=scst[:, 1:2]:
                    e.tensor_scalar(o, i_, g, None, ALU.mult))
            k.wait("tensor", "dv", k.cnt["dv"])
            pet = k.op("tensor", lambda e, o=psS_[0:8, 8:136], i_=scst[:, 11:19],
                       idn=cst["ident_f32"][:]: e.transpose(o, i_, idn),
                       inc="pe", mmb=0)
            k.wait("vector", "pe", pet)
            dvt = k.op("vector", lambda e, o=alpha[:, ts * 512 + tt * 128:
                                               ts * 512 + (tt + 1) * 128],
                       i_=psS_[0:8, 8:136]: e.tensor_copy(o, i_), inc="dv")
            k.wait("tensor", "dv", dvt)
            prev_mm_pe = pet
    b1_done = k.fence("vector", "dv")
    k.wait("gpsimd", "dv", b1_done)
    k.wait("sync", "dv", b1_done)
    k.dma("gpsimd", alpha_d, alpha[:], "wld")
    k.wait("sync", "wld", k.cnt["wld"])
    k.wait("sync", "st", k.cnt["st"])

    k.srewind(m_b1)
    alpha2 = alpha
    out_sb = [k.salloc([128, T], BF16) for _ in range(DCH)]
    m_b2 = k.smark()
    w1h = k.salloc([128, DCH * F_HALF], BF16)
    w3h = k.salloc([128, DCH * F_HALF], BF16)
    w2h = k.salloc([128, FC_HALF * 1024], BF16)
    midh = k.salloc([128, FC_HALF * 1024], BF16)
    xe = k.salloc([128, DCH * 1024], BF16)
    arow = k.salloc([1, T], BF16)
    Ae = k.salloc([128, T], BF16)
    lnt2 = k.salloc([128, 2 * 1024], BF16)
    sil = k.salloc([128, 2 * 1024], BF16)

    psU = k.pb2k[1][:]
    psW = k.pb2k[2][:]
    psUb = k.pb2k[0][:]
    psWb = k.pb2k[3][:]
    psD = k.pb2k[3][:]
    psDb = k.pb2k[0][:]

    xe_dv = [0, 0]
    xe_pe = 0
    mid_dv = [0, 0]
    mid_ac = [0, 0]
    down_dv = [0, 0]
    for e_i in range(9):
        shared = (e_i == 8)
        if not shared:
            al = k.dma("sync", arow[:], alpha_d[e_i:e_i + 1, :], "ld")
            k.wait("tensor", "ld", al)
            for ns2 in range(DCH):
                n2 = slice(ns2 * 512, (ns2 + 1) * 512)
                pet = k.op("tensor", lambda e, o=psM[:], l=cst["ones_row_bf"][:],
                           r=arow[0:1, n2]:
                           e.matmul(o, l, r, start=True, stop=True),
                           inc="pe", mmb=0)
                k.wait("vector", "pe", pet)
                dvt = k.op("vector", lambda e, o=Ae[:, n2], i_=psM[:]:
                           e.tensor_copy(o, i_), inc="dv")
                k.wait("tensor", "dv", dvt)
        for fh in range(2):
            f0 = fh * F_HALF
            k.wait("gpsimd", "pe", k.cnt["pe"])
            for kc in range(DCH):
                if shared:
                    srcw1 = ws1_d[kc * 128:(kc + 1) * 128, f0:f0 + F_HALF]
                    srcw3 = ws3_d[kc * 128:(kc + 1) * 128, f0:f0 + F_HALF]
                else:
                    srcw1 = w1_d[e_i, kc * 128:(kc + 1) * 128, f0:f0 + F_HALF]
                    srcw3 = w3_d[e_i, kc * 128:(kc + 1) * 128, f0:f0 + F_HALF]
                k.dma("gpsimd", w1h[:, kc * F_HALF:(kc + 1) * F_HALF], srcw1, "wld")
                k.dma("gpsimd", w3h[:, kc * F_HALF:(kc + 1) * F_HALF], srcw3, "wld")
            for kc in range(FC_HALF):
                if shared:
                    srcw2 = ws2_d[f0 + kc * 128:f0 + (kc + 1) * 128, :]
                else:
                    srcw2 = w2_d[e_i, f0 + kc * 128:f0 + (kc + 1) * 128, :]
                k.dma("gpsimd", w2h[:, kc * 1024:(kc + 1) * 1024], srcw2, "wld")
            wld_t = k.cnt["wld"]
            k.wait("tensor", "wld", wld_t)
            for ns in range(NS_B):
                nsl = slice(ns * 1024, (ns + 1) * 1024)
                for j in range(DCH):
                    buf = j % 2
                    lslice = lnt2[:, buf * 1024:(buf + 1) * 1024]
                    k.wait("sync", "dv", xe_dv[buf])
                    ldt = k.dma("sync", lslice, lnS[j * 128:(j + 1) * 128, nsl],
                                "ld")
                    k.wait("vector", "ld", ldt)
                    k.wait("vector", "pe", xe_pe)
                    if shared:
                        dvt = k.op("vector", lambda e,
                                   o=xe[:, j * 1024:(j + 1) * 1024], i_=lslice:
                                   e.tensor_copy(o, i_), inc="dv")
                    else:
                        dvt = k.op("vector", lambda e,
                                   o=xe[:, j * 1024:(j + 1) * 1024], a=lslice,
                                   b_=Ae[:, nsl]: e.tensor_mul(o, a, b_),
                                   inc="dv")
                    xe_dv[buf] = dvt
                k.wait("tensor", "dv", k.cnt["dv"])
                for f in range(FC_HALF):
                    par = f % 2
                    pU = (psU if par == 0 else psUb)
                    pW = (psW if par == 0 else psWb)
                    k.wait("tensor", "dv", mid_dv[par])
                    k.wait("tensor", "ac", mid_ac[par])
                    for half in range(2):
                        hsl = slice(half * 512, (half + 1) * 512)
                        for kc in range(DCH):
                            k.op("tensor", lambda e, o=pU[:, hsl],
                                 l=w1h[:, kc * F_HALF + f * 128:
                                       kc * F_HALF + (f + 1) * 128],
                                 r=xe[:, kc * 1024 + half * 512:
                                       kc * 1024 + (half + 1) * 512],
                                 s=(kc == 0), q=(kc == DCH - 1):
                                 e.matmul(o, l, r, start=s, stop=q), mmb=0)
                        for kc in range(DCH):
                            pet = k.op("tensor", lambda e, o=pW[:, hsl],
                                       l=w3h[:, kc * F_HALF + f * 128:
                                             kc * F_HALF + (f + 1) * 128],
                                       r=xe[:, kc * 1024 + half * 512:
                                             kc * 1024 + (half + 1) * 512],
                                       s=(kc == 0), q=(kc == DCH - 1):
                                       e.matmul(o, l, r, start=s, stop=q),
                                       inc="pe", mmb=0)
                    xe_pe = pet
                    k.wait("scalar", "pe", pet)
                    k.wait("scalar", "dv", mid_dv[1 - par])
                    mid_ac[par] = k.op("scalar", lambda e, o=sil[:, par * 1024:
                                       (par + 1) * 1024], i_=pU[:]:
                                       e.activation(o, i_, AF.Silu, bias=0.0,
                                                    scale=1.0), inc="ac")
                    k.wait("vector", "ac", mid_ac[par])
                    mid_dv[par] = k.op("vector", lambda e,
                                       o=midh[:, f * 1024:(f + 1) * 1024],
                                       a=sil[:, par * 1024:(par + 1) * 1024],
                                       b_=pW[:]: e.tensor_mul(o, a, b_),
                                       inc="dv")
                k.wait("tensor", "dv", k.cnt["dv"])
                k.wait("tensor", "ac", k.cnt["ac"])
                for j in range(DCH):
                    par = j % 2
                    pD = (psD if par == 0 else psDb)
                    k.wait("tensor", "dv", down_dv[par])
                    for half in range(2):
                        hsl = slice(half * 512, (half + 1) * 512)
                        for kc in range(FC_HALF):
                            pet = k.op("tensor", lambda e, o=pD[:, hsl],
                                       l=w2h[:, kc * 1024 + j * 128:
                                             kc * 1024 + (j + 1) * 128],
                                       r=midh[:, kc * 1024 + half * 512:
                                              kc * 1024 + (half + 1) * 512],
                                       s=(kc == 0), q=(kc == FC_HALF - 1):
                                       e.matmul(o, l, r, start=s, stop=q),
                                       inc="pe", mmb=0)
                    k.wait("vector", "pe", pet)
                    if e_i == 0 and fh == 0:
                        down_dv[par] = k.op("vector", lambda e,
                                            o=out_sb[j][:, nsl],
                                            i_=pD[:]: e.tensor_copy(o, i_),
                                            inc="dv")
                    else:
                        down_dv[par] = k.op("vector", lambda e,
                                            o=out_sb[j][:, nsl],
                                            a=out_sb[j][:, nsl], b_=pD[:]:
                                            e.tensor_add(o, a, b_), inc="dv")

    k.srewind(m_b2)
    xo = k.salloc([128, 2 * 512], F32)
    xn = k.salloc([128, 2 * 512], F32)
    b2_done = k.fence("vector", "dv")
    k.wait("sync", "dv", b2_done)
    for ts in range(T // 512):
        tsl = slice(ts * 512, (ts + 1) * 512)
        for j in range(DCH):
            buf = j % 2
            ldt = k.dma("sync", xo[:, buf * 512:(buf + 1) * 512],
                        xr1[j * 128:(j + 1) * 128, tsl], "ld")
            k.wait("vector", "ld", ldt)
            dvt = k.op("vector", lambda e, o=xn[:, buf * 512:(buf + 1) * 512],
                       a=xo[:, buf * 512:(buf + 1) * 512],
                       b_=out_sb[j][:, tsl]: e.tensor_add(o, a, b_), inc="dv")
            k.wait("sync", "dv", dvt)
            k.dma("sync", xr2[j * 128:(j + 1) * 128, tsl],
                  xn[:, buf * 512:(buf + 1) * 512], "st")
    fin = k.fence("vector", "dv")
    k.wait("sync", "dv", fin)
    k.wait("sync", "st", k.cnt["st"])


def build_phase_b_program():
    nc, k = new_program()
    xr1 = nc.dram_tensor("xr1", [D, T], F32, kind="ExternalInput").ap()
    router_d = nc.dram_tensor("router", [D, 8], F32, kind="ExternalInput").ap()
    w1_d = nc.dram_tensor("w1", [8, D, 2816], F32, kind="ExternalInput").ap()
    w3_d = nc.dram_tensor("w3", [8, D, 2816], F32, kind="ExternalInput").ap()
    w2_d = nc.dram_tensor("w2", [8, 2816, D], F32, kind="ExternalInput").ap()
    ws1_d = nc.dram_tensor("ws1", [D, 2816], F32, kind="ExternalInput").ap()
    ws3_d = nc.dram_tensor("ws3", [D, 2816], F32, kind="ExternalInput").ap()
    ws2_d = nc.dram_tensor("ws2", [2816, D], F32, kind="ExternalInput").ap()
    xr2 = nc.dram_tensor("xr2", [D, T], F32, kind="ExternalOutput").ap()
    lnS = nc.dram_tensor("lnS", [D, T], BF16, kind="Internal").ap()
    alpha_d = nc.dram_tensor("alpha_d", [8, T], BF16, kind="Internal").ap()
    cst = stage_consts(nc, k)
    for eng in ENGINES:
        k.wait(eng, "ld", k.cnt["ld"])
    k.freeze_persistent()
    k.set_phase("pb")
    emit_phase_b(nc, k, cst, xr1, router_d, w1_d, w3_d, w2_d, ws1_d, ws3_d,
                 ws2_d, xr2, lnS, alpha_d)
    k.build()
    return nc



DI = 2048
DIC = DI // 128


def emit_ln_resident(nc, k, cst, src, dst, psA_, psB_, psM, psR, scratch):
    xsl, xb, xsqb, row, Rb = scratch
    prev_copy_dv = 0
    prev_mm_pe = 0
    for ts in range(T // 512):
        tsl = slice(ts * 512, (ts + 1) * 512)
        lnf_dv = 0
        for j in range(DCH):
            k.wait("sync", "dv", prev_copy_dv)
            ldt = k.dma("sync", xsl[:, j * 512:(j + 1) * 512],
                        src[j * 128:(j + 1) * 128, tsl], "ld")
            k.wait("vector", "ld", ldt)
            k.wait("vector", "pe", prev_mm_pe)
            dvt = k.op("vector", lambda e, o=xb[:], i_=xsl[:, j * 512:(j + 1) * 512]:
                       e.tensor_copy(o, i_), inc="dv")
            k.wait("scalar", "dv", dvt)
            k.wait("scalar", "pe", prev_mm_pe)
            act = k.op("scalar", lambda e, o=xsqb[:], i_=xb[:]:
                       e.activation(o, i_, AF.Square, bias=0.0, scale=1.0),
                       inc="ac")
            k.wait("tensor", "dv", dvt)
            k.wait("tensor", "ac", act)
            k.op("tensor", lambda e, o=psA_[0:1, :], l=cst["ones_col_bf"][:],
                 r=xb[:], s=(j == 0), q=(j == DCH - 1):
                 e.matmul(o, l, r, start=s, stop=q), mmb=0)
            prev_mm_pe = k.op("tensor", lambda e, o=psB_[0:1, :],
                              l=cst["ones_col_bf"][:],
                              r=xsqb[:], s=(j == 0), q=(j == DCH - 1):
                              e.matmul(o, l, r, start=s, stop=q), inc="pe",
                              mmb=0)
            prev_copy_dv = k.cnt["dv"]
        k.wait("vector", "pe", k.cnt["pe"])
        r_m = row[:, 0:512]
        r_msq = row[:, 512:1024]
        r_var = row[:, 1024:1536]
        r_rstd = row[:, 1536:2048]
        k.vop_s(lambda e, o=r_m, i_=psA_[0:1, :]:
                e.tensor_scalar(o, i_, 1.0 / D, None, ALU.mult))
        k.vop_s(lambda e, o=r_msq, i_=psB_[0:1, :]:
                e.tensor_scalar(o, i_, 1.0 / D, None, ALU.mult))
        k.vop_s(lambda e, o=r_var, a=r_m, b_=r_m: e.tensor_mul(o, a, b_))
        k.vop_s(lambda e, o=r_rstd, a=r_msq, b_=r_var: e.tensor_sub(o, a, b_))
        k.vop_s(lambda e, o=r_var, i_=r_rstd:
                e.tensor_scalar(o, i_, 1e-5, None, ALU.add))
        k.wait("scalar", "dv", k.cnt["dv"])
        sct = k.op("scalar", lambda e, o=r_rstd, i_=r_var:
                   e.activation(o, i_, AF.Sqrt, bias=0.0, scale=1.0), inc="ac")
        k.wait("vector", "ac", sct)
        k.vop_s(lambda e, o=r_var, i_=r_rstd: e.reciprocal(o, i_))
        k.wait("tensor", "dv", k.cnt["dv"])
        k.op("tensor", lambda e, o=psM[:], l=cst["ones_row_f32"][:], r=r_m:
             e.matmul(o, l, r, start=True, stop=True), mmb=0)
        pet = k.op("tensor", lambda e, o=psR[:], l=cst["ones_row_f32"][:],
                   r=r_var: e.matmul(o, l, r, start=True, stop=True),
                   inc="pe", mmb=0)
        prev_mm_pe = pet
        k.wait("vector", "pe", pet)
        k.op("vector", lambda e, o=Rb[:], i_=psR[:]: e.tensor_copy(o, i_))
        for j in range(DCH):
            k.op("vector", lambda e, o=xsl[:, j * 512:(j + 1) * 512],
                 a=xsl[:, j * 512:(j + 1) * 512], b_=psM[:]:
                 e.tensor_sub(o, a, b_))
            dvt = k.op("vector", lambda e, o=dst[:, j * T + ts * 512:
                                              j * T + (ts + 1) * 512],
                       a=xsl[:, j * 512:(j + 1) * 512], b_=Rb[:]:
                       e.tensor_mul(o, a, b_), inc="dv")
        prev_copy_dv = k.cnt["dv"]
    done = k.fence("vector", "dv")
    for eng in ("tensor", "sync", "gpsimd", "scalar"):
        k.wait(eng, "dv", done)


def emit_phase_c(nc, k, cst, xr2, win_d, convw_d, convb_d, wB_d, wC_d, wout_d,
                 x3t):
    k.sreset()
    ln2 = k.salloc([128, DCH * T], BF16)
    xsl = k.salloc([128, DCH * 512], F32)
    xb = k.salloc([128, 512], BF16)
    xsqb = k.salloc([128, 512], BF16)
    row = k.salloc([1, 2048], F32)
    Rb = k.salloc([128, 512], F32)

    k09c = k.salloc([16, N], F32)
    _k09t = k.dma("sync", k09c[:], cst["k09c_dram"], "ld")
    k.wait("vector", "ld", _k09t)
    cst = dict(cst)
    cst["k09c"] = k09c

    psA_ = k.pbank[0][:]
    psB_ = k.pbank[1][:]
    psM = k.pbank[2][:]
    psR = k.pbank[3][:]
    emit_ln_resident(nc, k, cst, xr2, ln2, psA_, psB_, psM, psR,
                     (xsl, xb, xsqb, row, Rb))

    wsl = k.salloc([128, DCH * 128], BF16)
    convw = k.salloc([128, DIC * 4], F32)
    convb = k.salloc([128, DIC], F32)
    wB = k.salloc([128, DIC * 16], BF16)
    wC = k.salloc([16, DI], BF16)
    woutb = k.salloc([128, 2 * DIC * 128], BF16)
    xpb = k.salloc([128, 4 + N], F32)
    xcc = k.salloc([128, N], BF16)
    Bsb = k.salloc([16, N], F32)
    Hb = k.salloc([16, N], BF16)
    zsil = k.salloc([128, 512], BF16)
    ygns = k.salloc([128, DIC * 512], BF16)
    xns = [k.salloc([128, 512], F32) for _ in range(DCH)]
    xold = k.salloc([128, 2 * 512], F32)
    xtok = k.salloc([128, D], F32)

    for kc in range(DIC):
        k.dma("gpsimd", convw[:, kc * 4:(kc + 1) * 4],
              convw_d[kc * 128:(kc + 1) * 128, 0, :], "wld")
        k.dma("gpsimd", convb[:, kc:kc + 1],
              convb_d[kc * 128:(kc + 1) * 128].unsqueeze(1), "wld")
    k.dma("gpsimd", wB[:].rearrange("p (a c) -> p a c", c=16),
          wB_d.rearrange("(a p) c -> p a c", p=128), "wld")
    k.dma("gpsimd", wC[:], wC_d, "wld")
    wld0 = k.cnt["wld"]
    k.wait("tensor", "wld", wld0)
    k.wait("vector", "wld", wld0)
    k.wait("scalar", "wld", wld0)

    psX = k.pb2k[0][:]
    psY = k.pb2k[1][:]
    psBm = k.pb2k[2][:]
    psBm2 = k.pb2k[3][:]
    psD2 = k.pb2k[1][:]
    psT_ = k.pb2k[2][:]

    import os as _os
    if _os.environ.get("DBG_DUMPC"):
        dmpc = nc.dram_tensor("dumpc", [5, 128, N + 4], F32,
                              kind="ExternalOutput").ap()
    wout_pe = [0, 0]
    xold_dv = [0, 0]
    for b in range(B_CORE):
        bsl = slice(b * N, (b + 1) * N)
        for kc in range(DIC):
            k.wait("gpsimd", "pe", k.cnt["pe"])
            wldt = k.dma("gpsimd", wsl[:].rearrange("p (a c) -> p a c", c=128),
                         win_d[:, kc * 128:(kc + 1) * 128].rearrange(
                             "(a p) c -> p a c", p=128), "wld")
            k.wait("tensor", "wld", wldt)
            k.vop_s(lambda e, o=xpb[:, 0:4]: e.memset(o, 0.0))
            for ns in range(4):
                bank = psX[:, (ns % 2) * 512:(ns % 2) * 512 + 512]
                for a in range(DCH):
                    pet = k.op("tensor", lambda e, o=bank,
                               l=wsl[:, a * 128:(a + 1) * 128],
                               r=ln2[:, a * T + b * N + ns * 512:
                                     a * T + b * N + ns * 512 + 512],
                               s=(a == 0), q=(a == DCH - 1):
                               e.matmul(o, l, r, start=s, stop=q), inc="pe",
                               mmb=0)
                k.wait("vector", "pe", pet)
                dvt = k.op("vector", lambda e, o=xpb[:, 4 + ns * 512:
                                                  4 + (ns + 1) * 512],
                           i_=bank: e.tensor_copy(o, i_), inc="dv")
                k.wait("tensor", "dv", dvt)
            V = lambda d: xpb[:, 4 - d: 4 - d + N]
            acc = [xsl[:, 0:N], xsl[:, N:2 * N]]
            k.op("vector", lambda e, o=acc[0], i_=V(3),
                 w=convw[:, kc * 4:kc * 4 + 1]:
                 e.tensor_scalar(o, i_, w, None, ALU.mult))
            for jj in range(1, 4):
                k.op("vector", lambda e, o=acc[jj % 2], i_=V(3 - jj),
                     w=convw[:, kc * 4 + jj:kc * 4 + jj + 1],
                     a=acc[(jj + 1) % 2]:
                     e.scalar_tensor_tensor(o, i_, w, a, ALU.mult, ALU.add),
                     inc="dv")
            k.wait("scalar", "dv", k.cnt["dv"])
            k.wait("scalar", "pe", k.cnt["pe"])
            act = k.op("scalar", lambda e, o=xcc[:], i_=acc[1],
                       bb=convb[:, kc:kc + 1]:
                       e.activation(o, i_, AF.Silu, bias=bb, scale=1.0),
                       inc="ac")
            k.wait("tensor", "ac", act)
            import os as _os
            if _os.environ.get("DBG_DUMPC") and b == 0 and kc == 0:
                k.wait("sync", "ac", act)
                k.wait("gpsimd", "ac", act)
                k.dma("gpsimd", dmpc[2][:, 0:N], xcc[:], "wld")
                k.dma("sync", dmpc[4][:, 0:N + 4], xpb[:], "st")
            for ns in range(4):
                bank = (psBm if ns < 2 else psBm2)[0:16, (ns % 2) * 512:
                                                   (ns % 2) * 512 + 512]
                pet = k.op("tensor", lambda e, o=bank,
                           l=wB[:, kc * 16:(kc + 1) * 16],
                           r=xcc[:, ns * 512:(ns + 1) * 512],
                           s=(kc == 0), q=(kc == DIC - 1):
                           e.matmul(o, l, r, start=s, stop=q), inc="pe", mmb=0)
        k.wait("vector", "pe", k.cnt["pe"])
        for ns in range(4):
            bank = (psBm if ns < 2 else psBm2)[0:16, (ns % 2) * 512:
                                               (ns % 2) * 512 + 512]
            k.op("vector", lambda e, o=Bsb[:, ns * 512:(ns + 1) * 512], i_=bank:
                 e.tensor_copy(o, i_))
        k.vop_s(lambda e, o=Bsb[:], i_=Bsb[:]:
                e.tensor_scalar(o, i_, 0.1, None, ALU.mult))
        k.vop_s(lambda e, o=Hb[:], d0=cst["k09c"][:], d1=Bsb[:]:
                e.tensor_tensor_scan(o, d0, d1, 0.0, ALU.mult, ALU.add))
        import os as _os
        if _os.environ.get("DBG_DUMPC") and b == 0:
            k.wait("sync", "dv", k.cnt["dv"])
            k.wait("gpsimd", "dv", k.cnt["dv"])
            k.dma("sync", dmpc[0][0:16, 0:N], Bsb[:], "st")
            k.dma("gpsimd", dmpc[1][0:16, 0:N], Hb[:], "wld")
        dvt = k.op("vector", lambda e: e.engine_nop(), inc="dv")
        k.wait("tensor", "dv", dvt)
        for ns in range(4):
            nsl = slice(b * N + ns * 512, b * N + (ns + 1) * 512)
            for kc in range(DIC):
                wldt = k.dma("gpsimd", wsl[:].rearrange("p (a c) -> p a c", c=128),
                             win_d[:, DI + kc * 128: DI + (kc + 1) * 128].rearrange(
                                 "(a p) c -> p a c", p=128), "wld")
                k.wait("tensor", "wld", wldt)
                bankZ = psX[:, (kc % 2) * 512:(kc % 2) * 512 + 512]
                bankY = psY[:, (kc % 2) * 512:(kc % 2) * 512 + 512]
                for a in range(DCH):
                    pet = k.op("tensor", lambda e, o=bankZ,
                               l=wsl[:, a * 128:(a + 1) * 128],
                               r=ln2[:, a * T + b * N + ns * 512:
                                     a * T + b * N + ns * 512 + 512],
                               s=(a == 0), q=(a == DCH - 1):
                               e.matmul(o, l, r, start=s, stop=q), inc="pe",
                               mmb=0)
                pety = k.op("tensor", lambda e, o=bankY,
                            l=wC[:, kc * 128:(kc + 1) * 128],
                            r=Hb[:, ns * 512:(ns + 1) * 512]:
                            e.matmul(o, l, r, start=True, stop=True), inc="pe",
                            mmb=0)
                k.wait("scalar", "pe", pet)
                act = k.op("scalar", lambda e, o=zsil[:], i_=bankZ:
                           e.activation(o, i_, AF.Silu, bias=0.0, scale=1.0),
                           inc="ac")
                k.wait("vector", "ac", act)
                k.wait("vector", "pe", pety)
                dvt = k.op("vector", lambda e,
                           o=ygns[:, kc * 512:(kc + 1) * 512],
                           a=zsil[:], b_=bankY:
                           e.tensor_mul(o, a, b_), inc="dv")
                k.wait("tensor", "dv", dvt)
                k.wait("scalar", "dv", dvt)
                k.wait("gpsimd", "pe", pet)
                import os as _os
                if _os.environ.get("DBG_DUMPC") and b == 0 and ns == 0 and kc == 0:
                    k.wait("gpsimd", "dv", dvt)
                    k.dma("gpsimd", dmpc[3][:, 0:512], ygns[:, 0:512], "wld")
            for j in range(DCH):
                wbuf = woutb[:, (j % 2) * DIC * 128:(j % 2 + 1) * DIC * 128]
                k.wait("gpsimd", "pe", wout_pe[j % 2])
                wldt = k.dma("gpsimd",
                             wbuf.rearrange("p (a c) -> p a c", c=128),
                             wout_d[:, j * 128:(j + 1) * 128].rearrange(
                                 "(a p) c -> p a c", p=128), "wld")
                k.wait("tensor", "wld", wldt)
                bank = psD2[:, (j % 2) * 512:(j % 2) * 512 + 512]
                for kc in range(DIC):
                    pet = k.op("tensor", lambda e, o=bank,
                               l=wbuf[:, kc * 128:(kc + 1) * 128],
                               r=ygns[:, kc * 512:(kc + 1) * 512],
                               s=(kc == 0), q=(kc == DIC - 1):
                               e.matmul(o, l, r, start=s, stop=q), inc="pe",
                               mmb=0)
                wout_pe[j % 2] = pet
                buf = j % 2
                k.wait("sync", "dv", xold_dv[buf])
                ldt = k.dma("sync", xold[:, buf * 512:(buf + 1) * 512],
                            xr2[j * 128:(j + 1) * 128, nsl], "ld")
                k.wait("vector", "pe", pet)
                k.wait("vector", "ld", ldt)
                dvt = k.op("vector", lambda e, o=xns[j][:], a=bank,
                           b_=xold[:, buf * 512:(buf + 1) * 512]:
                           e.tensor_add(o, a, b_), inc="dv")
                xold_dv[buf] = dvt
                k.wait("tensor", "dv", dvt)
            for tt in range(4):
                for j in range(DCH):
                    bankT = psT_[:, (j % 2) * 512:(j % 2) * 512 + 128]
                    pet = k.op("tensor", lambda e, o=bankT,
                               i_=xns[j][:, tt * 128:(tt + 1) * 128],
                               idn=cst["ident_f32"][:]:
                               e.transpose(o, i_, idn), inc="pe", mmb=0)
                    k.wait("vector", "pe", pet)
                    dvt = k.op("vector", lambda e, o=xtok[:, j * 128:(j + 1) * 128],
                               i_=bankT: e.tensor_copy(o, i_), inc="dv")
                    k.wait("tensor", "dv", dvt)
                k.wait("sync", "dv", k.cnt["dv"])
                t0 = b * N + ns * 512 + tt * 128
                stt = k.dma("sync", x3t[t0:t0 + 128, :], xtok[:], "st")
                k.wait("vector", "st", stt)
    fin = k.fence("vector", "dv")
    k.wait("sync", "dv", fin)
    k.wait("sync", "st", k.cnt["st"])


def build_phase_c_program():
    nc, k = new_program()
    xr2 = nc.dram_tensor("xr2", [D, T], F32, kind="ExternalInput").ap()
    win_d = nc.dram_tensor("w_in", [D, 2 * DI], F32, kind="ExternalInput").ap()
    convw_d = nc.dram_tensor("conv_w", [DI, 1, 4], F32, kind="ExternalInput").ap()
    convb_d = nc.dram_tensor("conv_b", [DI], F32, kind="ExternalInput").ap()
    wB_d = nc.dram_tensor("w_B", [DI, 16], F32, kind="ExternalInput").ap()
    wC_d = nc.dram_tensor("w_C", [16, DI], F32, kind="ExternalInput").ap()
    wout_d = nc.dram_tensor("w_out", [DI, D], F32, kind="ExternalInput").ap()
    x3t = nc.dram_tensor("x3t", [T, D], F32, kind="ExternalOutput").ap()
    cst = stage_consts(nc, k)
    for eng in ENGINES:
        k.wait(eng, "ld", k.cnt["ld"])
    k.freeze_persistent()
    k.set_phase("pc")
    emit_phase_c(nc, k, cst, xr2, win_d, convw_d, convb_d, wB_d, wC_d, wout_d,
                 x3t)
    k.build()
    return nc



FCH = 22


def emit_quant(k, x_ap, out_ap, W, scr, scalar_dep=None):
    nb = W // 16
    mr_m = scr[0][:, 0:nb]
    mr_r = scr[0][:, nb:2 * nb]
    ab = scr[1][:, 0:W]
    qa = scr[2][:, 0:W]
    qb = scr[3][:, 0:W]
    qc = scr[4][:, 0:W]
    x3 = x_ap.rearrange("p (b s) -> p b s", s=16)
    k.vop_s(lambda e, o=mr_m, i_=x3:
            e.reduce_max(o, i_, AX.X, apply_absolute_value=True))
    k.vop_s(lambda e, o=mr_m, i_=mr_m:
            e.tensor_scalar(o, i_, 1.0 / 6.0, None, ALU.mult))
    k.vop_s(lambda e, o=mr_m, i_=mr_m:
            e.tensor_scalar(o, i_, 1e-12, None, ALU.max))
    k.vop_s(lambda e, o=mr_r, i_=mr_m: e.reciprocal(o, i_))
    act = k.op("scalar", lambda e, o=ab, i_=x_ap:
               e.activation(o, i_, AF.Abs, bias=0.0, scale=1.0), inc="ac")
    k.wait("vector", "ac", act)
    rb = mr_r.rearrange("p (b one) -> p b one", one=1).to_broadcast((128, nb, 16))
    k.op("vector", lambda e, o=ab.rearrange("p (b s) -> p b s", s=16),
         a=ab.rearrange("p (b s) -> p b s", s=16), b_=rb:
         e.tensor_tensor(o, a, b_, ALU.mult))
    k.op("vector", lambda e, o=qa, i_=ab:
         e.tensor_scalar(o, i_, 0.25, None, ALU.is_gt))
    k.op("vector", lambda e, o=qb, i_=ab, a=qa:
         e.scalar_tensor_tensor(o, i_, 0.75, a, ALU.is_gt, ALU.add))
    k.op("vector", lambda e, o=qa, i_=ab, a=qb:
         e.scalar_tensor_tensor(o, i_, 1.25, a, ALU.is_gt, ALU.add))
    k.op("vector", lambda e, o=qb, i_=ab, a=qa:
         e.scalar_tensor_tensor(o, i_, 1.75, a, ALU.is_gt, ALU.add))
    k.op("vector", lambda e, o=qa, i_=ab:
         e.tensor_scalar(o, i_, 2.5, None, ALU.is_gt))
    k.op("vector", lambda e, o=qc, i_=ab, a=qa:
         e.scalar_tensor_tensor(o, i_, 3.5, a, ALU.is_gt, ALU.add))
    k.op("vector", lambda e, o=qa, i_=ab:
         e.tensor_scalar(o, i_, 5.0, None, ALU.is_gt))
    k.op("vector", lambda e, o=qb, i_=qb, a=qc:
         e.scalar_tensor_tensor(o, i_, 0.5, a, ALU.mult, ALU.add))
    k.op("vector", lambda e, o=qa, i_=qa, a=qb:
         e.scalar_tensor_tensor(o, i_, 2.0, a, ALU.mult, ALU.add), inc="dv")
    k.wait("scalar", "dv", k.cnt["dv"])
    act = k.op("scalar", lambda e, o=qc, i_=x_ap:
               e.activation(o, i_, AF.Sign, bias=0.0, scale=1.0), inc="ac")
    k.wait("vector", "ac", act)
    k.op("vector", lambda e, o=qa, a=qa, b_=qc: e.tensor_mul(o, a, b_))
    mb = mr_m.rearrange("p (b one) -> p b one", one=1).to_broadcast((128, nb, 16))
    dvt = k.op("vector", lambda e, o=out_ap.rearrange("p (b s) -> p b s", s=16),
               a=qa.rearrange("p (b s) -> p b s", s=16), b_=mb:
               e.tensor_tensor(o, a, b_, ALU.mult), inc="dv")
    return dvt


def emit_phase_d(nc, k, cst, x3t, wg_d, wu_d, wd_d, out_d, wgq_d, wuq_d, wdq_d):
    k.sreset()
    xq = k.salloc([128, DCH * T], BF16)
    mr = k.salloc([128, 128], F32)
    ab = k.salloc([128, 512], F32)
    qa = k.salloc([128, 512], F32)
    qb = k.salloc([128, 512], F32)
    qc = k.salloc([128, 512], F32)
    scr = (mr, ab, qa, qb, qc)

    def quant_sliced(x_ap, out_ap, W):
        for cs in range(0, W, 512):
            cw = min(512, W - cs)
            emit_quant(k, x_ap[:, cs:cs + cw], out_ap[:, cs:cs + cw], cw, scr)
    xt = k.salloc([128, D], F32)
    lnt = k.salloc([128, D], F32)
    stat = k.salloc([128, 8], F32)
    qt = k.salloc([128, D], BF16)
    wgq = k.salloc([128, 2 * 1024], BF16)
    wuq = k.salloc([128, 2 * 1024], BF16)
    wdj = k.salloc([128, 2 * 2816], BF16)
    sg = k.salloc([128, 2 * 1024], F32)
    hf = k.salloc([128, 2 * 1024], F32)
    htok = k.salloc([128, 1024], F32)
    hqt = k.salloc([128, 2 * 1024], BF16)
    outfm = k.salloc([128, DCH * 1024], BF16)
    outtok = k.salloc([128, D], F32)
    xold = k.salloc([128, D], F32)
    m_pre = k.smark()
    wtile = k.salloc([128, 2816], F32)
    wqt = k.salloc([128, 2816], BF16)
    wT = k.salloc([128, 1024], BF16)

    psT = k.pb2k[0][:]
    psU = k.pb2k[1][:]
    psW = k.pb2k[2][:]
    psD = k.pb2k[3][:]

    ntile = T // 128
    prev_pe = 0
    for i in range(ntile):
        xt_i = xt[:]
        k.wait("sync", "dv", k.cnt["dv"])
        ldt = k.dma("sync", xt_i, x3t[i * 128:(i + 1) * 128, :], "ld")
        k.wait("vector", "ld", ldt)
        k.wait("scalar", "ld", ldt)
        k.vop_s(lambda e, o=stat[:, 0:1], i_=xt_i: e.reduce_sum(o, i_, AX.X))
        k.vop_s(lambda e, o=hf[:, 0:D], a=xt_i, b_=xt_i: e.tensor_mul(o, a, b_))
        k.vop_s(lambda e, o=stat[:, 1:2], i_=hf[:, 0:D]:
                e.reduce_sum(o, i_, AX.X))
        k.vop_s(lambda e, o=stat[:, 2:4], i_=stat[:, 0:2]:
                e.tensor_scalar(o, i_, 1.0 / D, None, ALU.mult))
        k.vop_s(lambda e, o=stat[:, 4:5], a=stat[:, 2:3], b_=stat[:, 2:3]:
                e.tensor_mul(o, a, b_))
        k.vop_s(lambda e, o=stat[:, 5:6], a=stat[:, 3:4], b_=stat[:, 4:5]:
                e.tensor_sub(o, a, b_))
        k.vop_s(lambda e, o=stat[:, 6:7], i_=stat[:, 5:6]:
                e.tensor_scalar(o, i_, 1e-5, None, ALU.add))
        k.wait("scalar", "dv", k.cnt["dv"])
        sct = k.op("scalar", lambda e, o=stat[:, 5:6], i_=stat[:, 6:7]:
                   e.activation(o, i_, AF.Sqrt, bias=0.0, scale=1.0), inc="ac")
        k.wait("vector", "ac", sct)
        k.vop_s(lambda e, o=stat[:, 6:7], i_=stat[:, 5:6]: e.reciprocal(o, i_))
        k.op("vector", lambda e, o=lnt[:], i_=xt_i, m=stat[:, 2:3], r=stat[:, 6:7]:
             e.tensor_scalar(o, i_, m, r, ALU.subtract, ALU.mult), inc="dv")
        k.wait("scalar", "dv", k.cnt["dv"])
        quant_sliced(lnt[:], qt[:], D)
        k.wait("tensor", "dv", k.cnt["dv"])
        k.wait("tensor", "pe", prev_pe)
        for a in range(DCH):
            bank = psT[:, (a % 2) * 512:(a % 2) * 512 + 128].bitcast(BF16)
            pet = k.op("tensor", lambda e, o=bank[:, 0:128],
                       i_=qt[:, a * 128:(a + 1) * 128], idn=cst["ident_bf"][:]:
                       e.transpose(o, i_, idn), inc="pe", mmb=0)
            k.wait("vector", "pe", pet)
            dvt = k.op("vector", lambda e, o=xq[:, a * T + i * 128:
                                             a * T + (i + 1) * 128],
                       i_=bank[:, 0:128]: e.tensor_copy(o, i_), inc="dv")
            k.wait("tensor", "dv", dvt)
        prev_pe = k.cnt["pe"]

    def wprep(src_d, dst_d, nrow_chunks, wcols):
        for f in range(nrow_chunks):
            k.wait("sync", "dv", k.cnt["dv"])
            k.wait("sync", "wld", k.cnt["wld"])
            ldt = k.dma("sync", wtile[:, 0:wcols],
                        src_d[f * 128:(f + 1) * 128, :], "ld")
            k.wait("vector", "ld", ldt)
            k.wait("scalar", "ld", ldt)
            quant_sliced(wtile[:, 0:wcols], wqt[:, 0:wcols], wcols)
            k.wait("tensor", "dv", k.cnt["dv"])
            nblk = wcols // 128
            for a in range(nblk):
                bank = psT[:, (a % 2) * 512:(a % 2) * 512 + 128].bitcast(BF16)
                pet = k.op("tensor", lambda e, o=bank[:, 0:128],
                           i_=wqt[:, a * 128:(a + 1) * 128],
                           idn=cst["ident_bf"][:]:
                           e.transpose(o, i_, idn), inc="pe", mmb=0)
                k.wait("vector", "pe", pet)
                if a % 8 == 0:
                    k.wait("vector", "wld", k.cnt["wld"])
                col = (a % 8) * 128
                dvt = k.op("vector", lambda e, o=wT[:, col:col + 128],
                           i_=bank[:, 0:128]: e.tensor_copy(o, i_), inc="dv")
                k.wait("tensor", "dv", dvt)
                if a % 8 == 7 or a == nblk - 1:
                    k.wait("gpsimd", "dv", dvt)
                    k.dma("gpsimd", dst_d[f, :, (a // 8) * 1024:
                                          (a // 8) * 1024 + col + 128],
                          wT[:, 0:col + 128], "wld")
    wprep(wg_d, wgq_d, FCH, D)
    wprep(wu_d, wuq_d, FCH, D)
    wprep(wd_d, wdq_d, DCH, 2816)
    wp_done = k.fence("vector", "dv")
    k.wait("sync", "dv", wp_done)
    k.wait("sync", "wld", k.cnt["wld"])
    k.srewind(m_pre)
    hq = k.salloc([128, FCH * 1024], BF16)

    wld_guard = [0, 0]
    hq_dv = 0
    hf_ac = [0, 0]
    hf_dv = [0, 0]
    for ns in range(4):
        nsl = slice(ns * 1024, (ns + 1) * 1024)

        def emit_htrans(f):
            par = f % 2
            ev = [k.cnt["dv"], k.cnt["dv"]]
            for a in range(DCH):
                bank = psT[:, (a % 2) * 512:(a % 2) * 512 + 128]
                k.wait("tensor", "dv", ev[a % 2])
                pet = k.op("tensor", lambda e, o=bank,
                           i_=hf[:, par * 1024 + a * 128:par * 1024 + (a + 1) * 128],
                           idn=cst["ident_f32"][:]:
                           e.transpose(o, i_, idn), inc="pe", mmb=0)
                k.wait("vector", "pe", pet)
                ev[a % 2] = k.op("vector",
                                 lambda e, o=htok[:, a * 128:(a + 1) * 128],
                                 i_=bank: e.tensor_copy(o, i_), inc="dv")

        def emit_hquant(f):
            par = f % 2
            k.wait("scalar", "dv", k.cnt["dv"])
            quant_sliced(htok[:], hqt[:, par * 1024:(par + 1) * 1024], D)

        def emit_backtrans(f):
            par = f % 2
            ev = [k.cnt["dv"], k.cnt["dv"]]
            for a in range(DCH):
                bank = psT[:, (a % 2) * 512:(a % 2) * 512 + 128].bitcast(BF16)
                k.wait("tensor", "dv", ev[a % 2])
                pet = k.op("tensor", lambda e, o=bank[:, 0:128],
                           i_=hqt[:, par * 1024 + a * 128:
                                 par * 1024 + (a + 1) * 128],
                           idn=cst["ident_bf"][:]:
                           e.transpose(o, i_, idn), inc="pe", mmb=0)
                k.wait("vector", "pe", pet)
                ev[a % 2] = k.op("vector", lambda e,
                                 o=hq[:, f * 1024 + a * 128:f * 1024 + (a + 1) * 128],
                                 i_=bank[:, 0:128]: e.tensor_copy(o, i_),
                                 inc="dv")

        for f in range(FCH):
            buf = f % 2
            k.wait("sync", "pe", wld_guard[buf])
            lg = k.dma("sync", wgq[:, buf * 1024:(buf + 1) * 1024],
                       wgq_d[f, :, :], "ld")
            lu = k.dma("sync", wuq[:, buf * 1024:(buf + 1) * 1024],
                       wuq_d[f, :, :], "ld")
            k.wait("tensor", "ld", lu)
            k.wait("tensor", "ac", max(hf_ac))
            k.wait("tensor", "dv", max(hf_dv))
            for half in range(2):
                hsl = slice(half * 512, (half + 1) * 512)
                for a in range(DCH):
                    k.op("tensor", lambda e, o=psU[:, hsl],
                         l=wgq[:, buf * 1024 + a * 128:buf * 1024 + (a + 1) * 128],
                         r=xq[:, a * T + ns * 1024 + half * 512:
                               a * T + ns * 1024 + half * 512 + 512],
                         s=(a == 0), q=(a == DCH - 1):
                         e.matmul(o, l, r, start=s, stop=q), mmb=0)
                for a in range(DCH):
                    pet = k.op("tensor", lambda e, o=psW[:, hsl],
                               l=wuq[:, buf * 1024 + a * 128:
                                     buf * 1024 + (a + 1) * 128],
                               r=xq[:, a * T + ns * 1024 + half * 512:
                                     a * T + ns * 1024 + half * 512 + 512],
                               s=(a == 0), q=(a == DCH - 1):
                               e.matmul(o, l, r, start=s, stop=q), inc="pe",
                               mmb=0)
            wld_guard[buf] = pet
            k.wait("scalar", "pe", pet)
            k.wait("scalar", "dv", hf_dv[buf])
            hf_ac[buf] = k.op("scalar", lambda e,
                              o=sg[:, buf * 1024:(buf + 1) * 1024], i_=psU[:]:
                              e.activation(o, i_, AF.Silu, bias=0.0, scale=1.0),
                              inc="ac")
            k.wait("vector", "ac", hf_ac[buf])
            hf_dv[buf] = k.op("vector", lambda e,
                              o=hf[:, buf * 1024:(buf + 1) * 1024],
                              a=sg[:, buf * 1024:(buf + 1) * 1024], b_=psW[:]:
                              e.tensor_mul(o, a, b_), inc="dv")
            if f >= 1:
                emit_backtrans(f - 1) if False else None
            if f >= 1:
                pass
            if f >= 1:
                emit_htrans(f - 1)
                emit_hquant(f - 1)
            if f >= 2:
                emit_backtrans(f - 2)
        emit_htrans(FCH - 1)
        emit_hquant(FCH - 1)
        emit_backtrans(FCH - 2)
        emit_backtrans(FCH - 1)
        hq_dv = k.cnt["dv"]
        k.wait("tensor", "dv", hq_dv)
        for j in range(DCH):
            buf = j % 2
            k.wait("sync", "pe", wld_guard[buf])
            ld = k.dma("sync", wdj[:, buf * 2816:(buf + 1) * 2816],
                       wdq_d[j, :, :], "ld")
            k.wait("tensor", "ld", ld)
            for half in range(2):
                bank = psD[:, half * 512:(half + 1) * 512]
                for f in range(FCH):
                    pet = k.op("tensor", lambda e, o=bank,
                               l=wdj[:, buf * 2816 + f * 128:
                                     buf * 2816 + (f + 1) * 128],
                               r=hq[:, f * 1024 + half * 512:
                                     f * 1024 + half * 512 + 512],
                               s=(f == 0), q=(f == FCH - 1):
                               e.matmul(o, l, r, start=s, stop=q), inc="pe",
                               mmb=0)
            wld_guard[buf] = pet
            k.wait("vector", "pe", pet)
            dvt = k.op("vector", lambda e, o=outfm[:, j * 1024:(j + 1) * 1024],
                       i_=psD[:]: e.tensor_copy(o, i_), inc="dv")
            k.wait("tensor", "dv", dvt)
        for tt in range(8):
            t0 = ns * 1024 + tt * 128
            for j in range(DCH):
                bank = psT[:, (j % 2) * 512:(j % 2) * 512 + 128].bitcast(BF16)
                pet = k.op("tensor", lambda e, o=bank[:, 0:128],
                           i_=outfm[:, j * 1024 + tt * 128:
                                    j * 1024 + (tt + 1) * 128],
                           idn=cst["ident_bf"][:]:
                           e.transpose(o, i_, idn), inc="pe", mmb=0)
                k.wait("vector", "pe", pet)
                dvt = k.op("vector", lambda e, o=outtok[:, j * 128:(j + 1) * 128],
                           i_=bank[:, 0:128]: e.tensor_copy(o, i_), inc="dv")
                k.wait("tensor", "dv", dvt)
            k.wait("sync", "dv", k.cnt["dv"])
            ldt = k.dma("sync", xold[:], x3t[t0:t0 + 128, :], "ld")
            k.wait("vector", "ld", ldt)
            dvt = k.op("vector", lambda e, o=outtok[:], a=outtok[:], b_=xold[:]:
                       e.tensor_add(o, a, b_), inc="dv")
            k.wait("sync", "dv", dvt)
            stt = k.dma("sync", out_d[t0:t0 + 128, :], outtok[:], "st")
            k.wait("vector", "st", stt)
    fin = k.fence("vector", "dv")
    k.wait("sync", "dv", fin)
    k.wait("sync", "st", k.cnt["st"])


def build_phase_d_program():
    nc, k = new_program()
    x3t = nc.dram_tensor("x3t", [T, D], F32, kind="ExternalInput").ap()
    wg_d = nc.dram_tensor("w_gate", [2816, D], F32, kind="ExternalInput").ap()
    wu_d = nc.dram_tensor("w_up", [2816, D], F32, kind="ExternalInput").ap()
    wd_d = nc.dram_tensor("w_down", [D, 2816], F32, kind="ExternalInput").ap()
    out_d = nc.dram_tensor("out", [T, D], F32, kind="ExternalOutput").ap()
    wgq_d = nc.dram_tensor("wgq", [FCH, 128, D], BF16, kind="Internal").ap()
    wuq_d = nc.dram_tensor("wuq", [FCH, 128, D], BF16, kind="Internal").ap()
    wdq_d = nc.dram_tensor("wdq", [DCH, 128, 2816], BF16, kind="Internal").ap()
    cst = stage_consts(nc, k)
    for eng in ENGINES:
        k.wait(eng, "ld", k.cnt["ld"])
    k.freeze_persistent()
    k.set_phase("pd")
    emit_phase_d(nc, k, cst, x3t, wg_d, wu_d, wd_d, out_d, wgq_d, wuq_d, wdq_d)
    k.build()
    return nc



def build_fused_program():
    nc, k = new_program()
    x_in = nc.dram_tensor("x", [T, D], F32, kind="ExternalInput").ap()
    wq_d = nc.dram_tensor("Wq", [D, D], F32, kind="ExternalInput").ap()
    wk_d = nc.dram_tensor("Wk", [D, D], F32, kind="ExternalInput").ap()
    wv_d = nc.dram_tensor("Wv", [D, D], F32, kind="ExternalInput").ap()
    wo_d = nc.dram_tensor("Wo", [D, D], F32, kind="ExternalInput").ap()
    wg1_d = nc.dram_tensor("Wg1", [D, 16], F32, kind="ExternalInput").ap()
    wg2_d = nc.dram_tensor("Wg2", [16, D], F32, kind="ExternalInput").ap()
    normw_d = nc.dram_tensor("norm_w", [D], F32, kind="ExternalInput").ap()
    router_d = nc.dram_tensor("router", [D, 8], F32, kind="ExternalInput").ap()
    w1_d = nc.dram_tensor("w1", [8, D, 2816], F32, kind="ExternalInput").ap()
    w3_d = nc.dram_tensor("w3", [8, D, 2816], F32, kind="ExternalInput").ap()
    w2_d = nc.dram_tensor("w2", [8, 2816, D], F32, kind="ExternalInput").ap()
    ws1_d = nc.dram_tensor("ws1", [D, 2816], F32, kind="ExternalInput").ap()
    ws3_d = nc.dram_tensor("ws3", [D, 2816], F32, kind="ExternalInput").ap()
    ws2_d = nc.dram_tensor("ws2", [2816, D], F32, kind="ExternalInput").ap()
    win_d = nc.dram_tensor("w_in", [D, 2 * DI], F32, kind="ExternalInput").ap()
    convw_d = nc.dram_tensor("conv_w", [DI, 1, 4], F32, kind="ExternalInput").ap()
    convb_d = nc.dram_tensor("conv_b", [DI], F32, kind="ExternalInput").ap()
    wB_d = nc.dram_tensor("w_B", [DI, 16], F32, kind="ExternalInput").ap()
    wC_d = nc.dram_tensor("w_C", [16, DI], F32, kind="ExternalInput").ap()
    wout_d = nc.dram_tensor("w_out", [DI, D], F32, kind="ExternalInput").ap()
    wg_d = nc.dram_tensor("w_gate", [2816, D], F32, kind="ExternalInput").ap()
    wu_d = nc.dram_tensor("w_up", [2816, D], F32, kind="ExternalInput").ap()
    wd_d = nc.dram_tensor("w_down", [D, 2816], F32, kind="ExternalInput").ap()
    out_d = nc.dram_tensor("out", [T, D], F32, kind="ExternalOutput").ap()
    import os as _os
    _ko = "ExternalOutput" if _os.environ.get("DBG_FX") else "Internal"
    xr0 = nc.dram_tensor("xr0", [D, T], F32, kind="Internal").ap()
    xr1 = nc.dram_tensor("xr1", [D, T], F32, kind=_ko).ap()
    xr2 = nc.dram_tensor("xr2", [D, T], F32, kind=_ko).ap()
    x3t = nc.dram_tensor("x3t", [T, D], F32, kind=_ko).ap()
    o_s = nc.dram_tensor("o_s", [D, T], BF16, kind="Internal").ap()
    lnS = nc.dram_tensor("lnS", [D, T], BF16, kind="Internal").ap()
    alpha_d = nc.dram_tensor("alpha_d", [8, T], BF16, kind="Internal").ap()
    wgq_d = nc.dram_tensor("wgq", [FCH, 128, D], BF16, kind="Internal").ap()
    wuq_d = nc.dram_tensor("wuq", [FCH, 128, D], BF16, kind="Internal").ap()
    wdq_d = nc.dram_tensor("wdq", [DCH, 128, 2816], BF16, kind="Internal").ap()

    cst = stage_consts(nc, k)
    for eng in ENGINES:
        k.wait(eng, "ld", k.cnt["ld"])
    k.freeze_persistent()
    k.set_phase("pa")
    emit_phase_a(nc, k, cst, x_in, wq_d, wk_d, wv_d, wo_d, wg1_d, wg2_d,
                 normw_d, xr0, xr1, o_s)
    k.barrier_all()
    k.set_phase("pb")
    emit_phase_b(nc, k, cst, xr1, router_d, w1_d, w3_d, w2_d, ws1_d, ws3_d,
                 ws2_d, xr2, lnS, alpha_d)
    k.barrier_all()
    k.set_phase("pc")
    emit_phase_c(nc, k, cst, xr2, win_d, convw_d, convb_d, wB_d, wC_d, wout_d,
                 x3t)
    k.barrier_all()
    k.set_phase("pd")
    emit_phase_d(nc, k, cst, x3t, wg_d, wu_d, wd_d, out_d, wgq_d, wuq_d, wdq_d)
    k.build()
    return nc


def kernel_fused(**inputs):
    consts = host_constants()
    NCORES = 8
    x_full = inputs["L0_attn_x"]
    ncF = _prog("fused", build_fused_program)
    maps = []
    for c in range(NCORES):
        m = dict(consts)
        m["x"] = np.ascontiguousarray(
            x_full[c * B_CORE:(c + 1) * B_CORE].reshape(T, D))
        for nm, src in (
                ("Wq", "L0_attn_Wq"), ("Wk", "L0_attn_Wk"), ("Wv", "L0_attn_Wv"),
                ("Wo", "L0_attn_Wo"), ("Wg1", "L0_attn_Wg1"),
                ("Wg2", "L0_attn_Wg2"), ("norm_w", "L0_attn_norm_w"),
                ("router", "L0_ffn_router_DE"), ("w1", "L0_ffn_w1_eDF"),
                ("w3", "L0_ffn_w3_eDF"), ("w2", "L0_ffn_w2_eFD"),
                ("ws1", "L0_ffn_ws1_DF"), ("ws3", "L0_ffn_ws3_DF"),
                ("ws2", "L0_ffn_ws2_FD"), ("w_in", "L1_attn_w_in"),
                ("conv_w", "L1_attn_conv_w"), ("conv_b", "L1_attn_conv_b"),
                ("w_B", "L1_attn_w_B"), ("w_C", "L1_attn_w_C"),
                ("w_out", "L1_attn_w_out"), ("w_gate", "L1_ffn_w_gate"),
                ("w_up", "L1_ffn_w_up"), ("w_down", "L1_ffn_w_down")):
            m[nm] = inputs[src]
        maps.append(m)
    res = run_bass_kernel_spmd(ncF, maps, core_ids=list(range(NCORES)))
    outs = [np.asarray(r["out"]) for r in res.results]
    return np.concatenate(outs, axis=0).reshape(16, N, D).astype(np.float32)


# version 09
# speedup vs baseline: 1.1612x; 1.0272x over previous
import contextlib
import numpy as np
import ml_dtypes
import concourse.bass as bass
import concourse.mybir as mybir
from concourse.bass_utils import run_bass_kernel_spmd

F32 = mybir.dt.float32
BF16 = mybir.dt.bfloat16
AF = mybir.ActivationFunctionType
ALU = mybir.AluOpType
AX = mybir.AxisListType

B_CORE = 2
N = 2048
D = 1024
T = B_CORE * N
H = 16
HD = 64
CHUNK = 64
NCH = N // CHUNK
NPAIR = 8
DCH = D // 128

ENGINES = ("sync", "tensor", "vector", "scalar", "gpsimd")


ARENA_BASE = 20 * 1024
SBUF_BYTES = 229312


BASE_SEMS = ("ld", "st", "wld", "pe", "dv", "ac", "gp", "sy")


class _CntView:

    def __init__(self, k):
        self._k = k

    def __getitem__(self, base):
        return self._k._cnt[self._k._n(base)]

    def __setitem__(self, base, val):
        self._k._cnt[self._k._n(base)] = val


class K:

    def __init__(self, nc):
        self.nc = nc
        self.q = {e: [] for e in ENGINES}
        self.sem_names = []
        self._cnt = {}
        self.sfx = ""
        self.cnt = _CntView(self)
        self._uid = 0
        self.mm_base = None
        self.p_off = ARENA_BASE
        self.s_base = None
        self.s_off = None
        self.s_peak = 0

    def _n(self, base):
        return base + self.sfx

    def set_phase(self, tag):
        self.sfx = "_" + tag
        for b in BASE_SEMS:
            name = b + self.sfx
            if name not in self._cnt:
                self.sem_names.append(name)
                self._cnt[name] = 0


    def _alloc_at(self, shape, dtype, off):
        self._uid += 1
        nbytes = int(np.prod(shape[1:])) * (2 if dtype == BF16 else 4)
        t = self.nc.alloc_sbuf_tensor_at(f"sb{self._uid}", list(shape), dtype,
                                         offset=off)
        return t, off + ((nbytes + 31) // 32) * 32

    def palloc(self, shape, dtype):
        t, self.p_off = self._alloc_at(shape, dtype, self.p_off)
        assert self.s_base is None, "persistent alloc after freeze"
        return t

    def freeze_persistent(self):
        self.s_base = self.p_off
        self.s_off = self.p_off

    def sreset(self):
        self.s_peak = max(self.s_peak, self.s_off)
        self.s_off = self.s_base

    def salloc(self, shape, dtype):
        t, self.s_off = self._alloc_at(shape, dtype, self.s_off)
        assert self.s_off <= SBUF_BYTES, f"sbuf over budget: {self.s_off}"
        return t

    def smark(self):
        return self.s_off

    def srewind(self, mark):
        self.s_peak = max(self.s_peak, self.s_off)
        self.s_off = mark

    def sem(self, name):
        self.sem_names.append(name)
        self.cnt[name] = 0
        return name

    def vop_s(self, fn):
        dv = self._n("dv")
        self._cnt[dv] += 1
        self.q["vector"].append(("op", fn, dv, 1))
        self.q["vector"].append(("wait", dv, self._cnt[dv]))
        return self._cnt[dv]

    def op(self, eng, fn, inc=None, amt=1, mmb=None):
        if mmb is not None and eng == "tensor":
            if self.mm_base is not None and self.mm_base != mmb:
                self.q[eng].append(("op", lambda e: e.nop(), None, 1))
                self.q[eng].append(("op", lambda e: e.nop(), None, 1))
            self.mm_base = mmb
        if inc is not None:
            inc = self._n(inc)
            self._cnt[inc] += amt
        self.q[eng].append(("op", fn, inc, amt))
        return self._cnt[inc] if inc is not None else None

    def dma(self, eng, out, in_, sem="ld", **kw):
        sem = self._n(sem)
        self._cnt[sem] += 16
        tgt = self._cnt[sem]
        self.q[eng].append(
            ("op", lambda e, o=out, i=in_, kw=kw: e.dma_start(out=o, in_=i, **kw),
             sem, 16))
        return tgt

    def wait(self, eng, sem, val):
        self.q[eng].append(("wait", self._n(sem), val))

    def fence(self, eng, sem):
        sem = self._n(sem)
        self._cnt[sem] += 1
        if eng in ("vector", "gpsimd"):
            self.q[eng].append(("op", lambda e: e.engine_nop(), sem, 1))
        else:
            self.q[eng].append(("op", lambda e: e.nop(), sem, 1))
        return self._cnt[sem]

    def barrier_all(self):
        for _round in range(2):
            tpe = self.fence("tensor", "pe")
            tdv = self.fence("vector", "dv")
            self._cnt[self._n("ac")] += 1
            tac = self._cnt[self._n("ac")]
            bar = self.bar_tile
            self.q["scalar"].append(
                ("op", lambda e, o=bar: e.activation(
                    o, o, AF.Identity, bias=0.0, scale=1.0), self._n("ac"), 1))
            tgp = self.fence("gpsimd", "gp")
            tsy = self.fence("sync", "sy")
            for eng in ENGINES:
                for sem, tgt in (("pe", tpe), ("dv", tdv), ("ac", tac),
                                 ("gp", tgp), ("sy", tsy)):
                    self.wait(eng, sem, tgt)

    def build(self):
        nc = self.nc
        with contextlib.ExitStack() as stack:
            sems = {n: stack.enter_context(nc.semaphore(n)) for n in self.sem_names}
            with nc.Block() as block:
                def mk(engname):
                    def emit(eng):
                        for item in self.q[engname]:
                            if item[0] == "wait":
                                eng.wait_ge(sems[item[1]], item[2])
                            else:
                                _, fn, inc, amt = item
                                ins = fn(eng)
                                if inc is not None:
                                    ins.then_inc(sems[inc], amt)
                    return emit
                block.sync(mk("sync"))
                block.tensor(mk("tensor"))
                block.vector(mk("vector"))
                block.scalar(mk("scalar"))
                block.gpsimd(mk("gpsimd"))
        return nc



def host_constants():
    c = {}
    t = np.arange(N, dtype=np.float64)[None, :]
    j = np.arange(HD, dtype=np.float64)[:, None]
    theta = t * (10000.0 ** (-2.0 * j / HD))
    cos1 = np.cos(theta).astype(np.float32)
    sin1 = np.sin(theta).astype(np.float32)
    c["cstab"] = np.concatenate([cos1, sin1], 0).astype(ml_dtypes.bfloat16)
    for hh in range(2):
        selT = np.zeros((128, 128), np.float32)
        for pcol in range(128):
            selT[64 * hh + (pcol % 64), pcol] = 1.0
        c[f"selT{hh}"] = selT
    c["ident_f32"] = np.eye(128, dtype=np.float32)
    c["ident_bf"] = np.eye(128, dtype=np.float32).astype(ml_dtypes.bfloat16)
    s = np.arange(CHUNK)[:, None]
    cc = np.arange(CHUNK)[None, :]
    c["triu1"] = (cc >= s).astype(np.float32)
    c["ones_col_bf"] = np.ones((128, 1), np.float32).astype(ml_dtypes.bfloat16)
    c["ones_row_f32"] = np.ones((1, 128), np.float32)
    c["ones_row_bf"] = np.ones((1, 128), np.float32).astype(ml_dtypes.bfloat16)
    c["k09c"] = np.full((16, N), 0.9, np.float32)
    c["ones_col_f32"] = np.ones((128, 1), np.float32)
    return c


CONST_SPECS = {
    "cstab": ([128, N], BF16),
    "selT0": ([128, 128], F32), "selT1": ([128, 128], F32),
    "ident_f32": ([128, 128], F32), "ident_bf": ([128, 128], BF16),
    "triu1": ([64, 64], F32),
    "ones_col_bf": ([128, 1], BF16), "ones_row_f32": ([1, 128], F32),
    "ones_row_bf": ([1, 128], BF16), "k09c": ([16, N], F32),
    "ones_col_f32": ([128, 1], F32),
}


def stage_consts(nc, k):
    sb = {}
    for name, (shape, dt) in CONST_SPECS.items():
        dram = nc.dram_tensor(name, shape, dt, kind="ExternalInput").ap()
        if name == "k09c":
            sb["k09c_dram"] = dram
            continue
        sb[name] = k.palloc(shape, dt)
        k.dma("sync", sb[name][:], dram, "ld")
    return sb



def emit_phase_a(nc, k, cst, x_in, wq_d, wk_d, wv_d, wo_d, wg1_d, wg2_d,
                 normw_d, xr0, xr1, o_s, dbg_bp=99, dbg_a3=True):
    k.sreset()
    ln0 = k.salloc([128, DCH * T], BF16)
    xt = k.salloc([128, 2 * D], F32)
    lnt = k.salloc([128, 2 * D], BF16)
    xsq = k.salloc([128, D], F32)
    stat = k.salloc([128, 8], F32)
    xcol = k.salloc([128, 2 * D], F32)
    m_gla = k.smark()

    wk = k.salloc([128, DCH * 128], BF16)
    wqd = [k.salloc([128, DCH * 128], BF16) for _ in range(2)]
    wvh = [k.salloc([128, DCH * 64], BF16) for _ in range(2)]
    kr = k.salloc([128, N], F32)
    qs = k.salloc([128, N], F32)
    KG = k.salloc([128, N], F32)
    CS = k.salloc([128, N], F32)
    EQ = k.salloc([128, N], F32)
    tmpf = k.salloc([128, N], F32)
    sc = k.salloc([128, 512], F32)
    QG = k.salloc([128, N], BF16)
    KGC = k.salloc([128, N], BF16)
    K2 = k.salloc([128, N], BF16)
    EQd = k.salloc([128, 32], F32)
    vfm_h = k.salloc([64, N], BF16)
    v_cm_h = k.salloc([64, NCH * 64], BF16)
    Atile = k.salloc([64, 128], BF16)
    k2tok = k.salloc([64, 256], BF16)
    S_h = k.salloc([128, 64], F32)
    S_hb = k.salloc([128, 64], BF16)
    O_h = k.salloc([64, N], BF16)

    psT = k.pbank[0][:]
    psT2 = k.pbank[1][:]
    psP = k.pbank[2][:]
    psP2 = k.pbank[3][:]
    psA = k.pbank[4][:][:, 0:128]
    psO = k.pbank[5][:][:, 0:64]
    psS = k.pbank[5][:][:, 64:192]
    psK = k.pbank[6][:].bitcast(BF16)[0:64, 0:384]

    ntile = T // 128
    xt_chk = [0] * 2
    lnt_chk = [0] * 2
    xcol_chk = [0] * 2
    for i in range(ntile):
        buf = i % 2
        xt_i = xt[:, buf * D:(buf + 1) * D]
        lnt_i = lnt[:, buf * D:(buf + 1) * D]
        xcol_i = xcol[:, buf * D:(buf + 1) * D]
        if i >= 2:
            k.wait("sync", "pe", xt_chk[buf])
        ldt = k.dma("sync", xt_i, x_in[i * 128:(i + 1) * 128, :], "ld")
        k.wait("vector", "ld", ldt)
        k.op("vector", lambda e, o=stat[:, 0:1], i_=xt_i: e.reduce_sum(o, i_, AX.X))
        k.wait("scalar", "ld", ldt)
        k.op("scalar", lambda e, o=xsq[:], i_=xt_i:
             e.activation(o, i_, AF.Square, bias=0.0, scale=1.0), inc="ac")
        k.wait("vector", "ac", k.cnt["ac"])
        k.op("vector", lambda e, o=stat[:, 1:2], i_=xsq[:]: e.reduce_sum(o, i_, AX.X),
             inc="dv")
        k.wait("vector", "dv", k.cnt["dv"])
        k.op("vector", lambda e, o=stat[:, 2:4], i_=stat[:, 0:2]:
             e.tensor_scalar(o, i_, 1.0 / D, None, ALU.mult), inc="dv")
        k.wait("vector", "dv", k.cnt["dv"])
        k.op("vector", lambda e, o=stat[:, 4:5], a=stat[:, 2:3], b=stat[:, 2:3]:
             e.tensor_mul(o, a, b), inc="dv")
        k.wait("vector", "dv", k.cnt["dv"])
        k.op("vector", lambda e, o=stat[:, 5:6], a=stat[:, 3:4], b=stat[:, 4:5]:
             e.tensor_sub(o, a, b), inc="dv")
        k.wait("vector", "dv", k.cnt["dv"])
        k.op("vector", lambda e, o=stat[:, 7:8], i_=stat[:, 5:6]:
             e.tensor_scalar(o, i_, 1e-5, None, ALU.add), inc="dv")
        k.wait("scalar", "dv", k.cnt["dv"])
        k.op("scalar", lambda e, o=stat[:, 5:6], i_=stat[:, 7:8]:
             e.activation(o, i_, AF.Sqrt, bias=0.0, scale=1.0), inc="ac")
        k.wait("vector", "ac", k.cnt["ac"])
        k.op("vector", lambda e, o=stat[:, 6:7], i_=stat[:, 5:6]:
             e.reciprocal(o, i_), inc="dv")
        k.wait("vector", "dv", k.cnt["dv"])
        if i >= 2:
            k.wait("vector", "pe", lnt_chk[buf])
        k.op("vector", lambda e, o=lnt_i, i_=xt_i, m=stat[:, 2:3], r=stat[:, 6:7]:
             e.tensor_scalar(o, i_, m, r, ALU.subtract, ALU.mult), inc="dv")
        dv_ln = k.cnt["dv"]

        k.wait("tensor", "ld", ldt)
        if i >= 2:
            k.wait("tensor", "st", xcol_chk[buf])
        for j in range(DCH):
            bank = psT if j % 2 == 0 else psT2
            pet = k.op("tensor", lambda e, o=bank[:, 0:128],
                       i_=xt_i[:, j * 128:(j + 1) * 128], idn=cst["ident_f32"][:]:
                       e.transpose(o, i_, idn), inc="pe")
            k.wait("vector", "pe", pet)
            dvt = k.op("vector", lambda e, o=xcol_i[:, j * 128:(j + 1) * 128],
                       i_=bank[:, 0:128]: e.tensor_copy(o, i_), inc="dv")
            k.wait("tensor", "dv", dvt)
        k.wait("sync", "dv", k.cnt["dv"])
        dst = xr0[:, i * 128:(i + 1) * 128].rearrange("(c p) t -> p c t", p=128)
        xcol_chk[buf] = k.dma("sync", dst, xcol_i.rearrange("p (c t) -> p c t", t=128),
                              "st")
        k.wait("tensor", "dv", dv_ln)
        for j in range(DCH):
            bank = (psT if j % 2 == 0 else psT2)[:, 0:128].bitcast(BF16)
            pet = k.op("tensor", lambda e, o=bank[:, 0:128],
                       i_=lnt_i[:, j * 128:(j + 1) * 128], idn=cst["ident_bf"][:]:
                       e.transpose(o, i_, idn), inc="pe")
            k.wait("vector", "pe", pet)
            dvt = k.op("vector", lambda e, o=ln0[:, j * T + i * 128: j * T + (i + 1) * 128],
                       i_=bank[:, 0:128]: e.tensor_copy(o, i_), inc="dv")
            k.wait("tensor", "dv", dvt)
        lnt_chk[buf] = k.cnt["pe"]
        xt_chk[buf] = k.cnt["pe"]
    a1_dv = k.fence("vector", "dv")
    for eng in ("tensor", "sync", "gpsimd"):
        k.wait(eng, "dv", a1_dv)

    NSL = N // 512

    def proj_fm(wsb, out_sb, bslice, out_dtype, act=None):
        for ns in range(NSL):
            bank = psP if ns % 2 == 0 else psP2
            for a in range(DCH):
                pet = k.op("tensor", lambda e, o=bank[:],
                           l=wsb[:, a * 128:(a + 1) * 128],
                           r=ln0[:, a * T + bslice * N + ns * 512:
                                 a * T + bslice * N + ns * 512 + 512],
                           s=(a == 0), p=(a == DCH - 1):
                           e.matmul(o, l, r, start=s, stop=p), inc="pe")
            k.wait("vector" if act is None else "scalar", "pe", pet)
            o_sl = out_sb[:, ns * 512:(ns + 1) * 512]
            if act is None:
                dvt = k.op("vector", lambda e, o=o_sl, i_=bank[:]:
                           e.tensor_copy(o, i_), inc="dv")
                k.wait("tensor", "dv", dvt)
            else:
                sct = k.op("scalar", lambda e, o=o_sl, i_=bank[:], f=act:
                           e.activation(o, i_, f, bias=0.0, scale=1.0), inc="ac")
                k.wait("tensor", "ac", sct)

    import os
    a2stop = int(os.environ.get("DBG_A2STOP", "99"))
    for bp in range(min(B_CORE * NPAIR, dbg_bp)):
        p, b = divmod(bp, B_CORE)
        c0 = p * 128
        if b == 0:
            if p > 0:
                k.wait("gpsimd", "pe", k.cnt["pe"])
            src = wk_d[:, c0:c0 + 128].rearrange("(a p) c -> p a c", p=128)
            k.dma("gpsimd", wk[:].rearrange("p (a c) -> p a c", c=128), src, "wld")
            for hh in range(2):
                hc = c0 + 64 * hh
                qsrc = wq_d[:, hc:hc + 64].rearrange("(a p) c -> p a c", p=128)
                wqv = wqd[hh][:].rearrange("p (a c) -> p a c", c=128)
                k.dma("gpsimd", wqv[:, :, 0:64], qsrc, "wld")
                k.dma("gpsimd", wqv[:, :, 64:128], qsrc, "wld")
                vsrc = wv_d[:, hc:hc + 64].rearrange("(a p) c -> p a c", p=128)
                k.dma("gpsimd", wvh[hh][:].rearrange("p (a c) -> p a c", c=64),
                      vsrc, "wld")
            k.wait("tensor", "wld", k.cnt["wld"])
        proj_fm(wk, kr, b, F32)
        if a2stop <= 1:
            continue

        kr3 = kr[:].rearrange("p (c k) -> p c k", k=CHUNK)
        m32 = sc[:, 0:32]
        T32 = sc[:, 32:64]
        La = sc[:, 96:128]
        Lb = sc[:, 128:160]
        Zc = sc[:, 160:192]
        t32a = sc[:, 192:224]
        t32b = sc[:, 224:256]
        EQend = sc[:, 256:288]
        k.vop_s(lambda e, o=m32, i_=kr3: e.reduce_max(o, i_, AX.X))
        k.op("vector", lambda e, o=tmpf[:].rearrange("p (c k) -> p c k", k=CHUNK),
             a=kr3, m=m32: e.tensor_tensor(
                 o, a, m.unsqueeze(2).to_broadcast((128, NCH, CHUNK)), ALU.subtract),
             inc="dv")
        k.wait("scalar", "dv", k.cnt["dv"])
        k.op("scalar", lambda e, o=tmpf[:], i_=tmpf[:]:
             e.activation(o, i_, AF.Exp, bias=0.0, scale=1.0), inc="ac")
        k.wait("vector", "ac", k.cnt["ac"])
        k.op("vector", lambda e, o=T32,
             i_=tmpf[:].rearrange("p (c k) -> p c k", k=CHUNK):
             e.reduce_sum(o, i_, AX.X), inc="dv")
        k.wait("scalar", "dv", k.cnt["dv"])
        k.op("scalar", lambda e, o=T32, i_=T32:
             e.activation(o, i_, AF.Ln, bias=0.0, scale=1.0), inc="ac")
        k.wait("vector", "ac", k.cnt["ac"])
        k.vop_s(lambda e, o=La, a=m32, b_=T32: e.tensor_add(o, a, b_))
        cur, nxt = La, Lb
        for d in (1, 2, 4, 8, 16):
            k.vop_s(lambda e, o=nxt[:, 0:d], i_=cur[:, 0:d], d=d:
                    e.tensor_copy(o, i_))
            k.vop_s(lambda e, o=nxt[:, d:32], a=cur[:, d:32], b_=cur[:, 0:32 - d],
                    d=d: e.tensor_max(o, a, b_))
            k.vop_s(lambda e, o=t32a[:, 0:32 - d], a=cur[:, d:32],
                    b_=cur[:, 0:32 - d], d=d: e.tensor_tensor(o, a, b_, ALU.min))
            k.vop_s(lambda e, o=t32a[:, 0:32 - d], a=t32a[:, 0:32 - d],
                    b_=nxt[:, d:32], d=d: e.tensor_sub(o, a, b_))
            k.wait("scalar", "dv", k.cnt["dv"])
            sct = k.op("scalar", lambda e, o=t32b[:, 0:32 - d], i_=t32a[:, 0:32 - d],
                       d=d: e.activation(o, i_, AF.Exp, bias=0.0, scale=1.0),
                       inc="ac")
            k.wait("scalar", "ac", sct)
            k.op("scalar", lambda e, o=t32b[:, 0:32 - d], i_=t32b[:, 0:32 - d], d=d:
                 e.activation(o, i_, AF.Ln, bias=1.0, scale=1.0), inc="ac")
            k.wait("vector", "ac", k.cnt["ac"])
            k.vop_s(lambda e, o=nxt[:, d:32], a=nxt[:, d:32],
                    b_=t32b[:, 0:32 - d], d=d: e.tensor_add(o, a, b_))
            cur, nxt = nxt, cur
        k.vop_s(lambda e, o=Zc[:, 0:1]: e.memset(o, 0.0))
        k.wait("scalar", "dv", k.cnt["dv"])
        sct = k.op("scalar", lambda e, o=t32a[:, 0:31], i_=cur[:, 0:31]:
                   e.activation(o, i_, AF.Abs, bias=0.0, scale=1.0), inc="ac")
        k.wait("scalar", "ac", sct)
        sct = k.op("scalar", lambda e, o=t32a[:, 0:31], i_=t32a[:, 0:31]:
                   e.activation(o, i_, AF.Exp, bias=0.0, scale=-1.0), inc="ac")
        k.wait("scalar", "ac", sct)
        k.op("scalar", lambda e, o=t32a[:, 0:31], i_=t32a[:, 0:31]:
             e.activation(o, i_, AF.Ln, bias=1.0, scale=1.0), inc="ac")
        k.wait("vector", "ac", k.cnt["ac"])
        k.vop_s(lambda e, o=t32b[:, 0:31], i_=cur[:, 0:31]:
                e.tensor_scalar(o, i_, 0.0, None, ALU.max))
        k.op("vector", lambda e, o=Zc[:, 1:32], a=t32b[:, 0:31], b_=t32a[:, 0:31]:
             e.tensor_add(o, a, b_), inc="dv")
        k.op("vector", lambda e, o=KG[:].rearrange("p (c k) -> p c k", k=CHUNK),
             a=kr3, z=Zc: e.tensor_tensor(
                 o, a, z.unsqueeze(2).to_broadcast((128, NCH, CHUNK)), ALU.subtract),
             inc="dv")
        k.wait("scalar", "dv", k.cnt["dv"])
        k.op("scalar", lambda e, o=KG[:], i_=KG[:]:
             e.activation(o, i_, AF.Exp, bias=0.0, scale=1.0), inc="ac")
        ac_kg = k.cnt["ac"]
        k.wait("vector", "ac", ac_kg)
        for c in range(NCH):
            k.op("vector", lambda e, o=CS[:, c * 64:(c + 1) * 64],
                 d0=KG[:, c * 64:(c + 1) * 64]:
                 e.tensor_tensor_scan(o, d0, d0, 1.0, ALU.add, ALU.bypass))
        k.op("vector", lambda e: e.engine_nop(), inc="dv")
        k.wait("scalar", "dv", k.cnt["dv"])
        k.op("scalar", lambda e, o=EQ[:], i_=CS[:]:
             e.activation(o, i_, AF.Ln, bias=0.0, scale=1.0))
        k.op("scalar", lambda e, o=EQ[:], i_=EQ[:]:
             e.activation(o, i_, AF.Exp, bias=0.0, scale=-1.0), inc="ac")
        k.wait("vector", "ac", k.cnt["ac"])
        k.vop_s(lambda e, o=EQend.unsqueeze(2),
                i_=EQ[:].rearrange("p (c k) -> p c k", k=CHUNK)[:, :, 63:64]:
                e.tensor_copy(o, i_))
        k.op("vector", lambda e: e.engine_nop(), inc="dv")
        zfull_dv = k.cnt["dv"]
        zfull_ac = k.cnt["ac"]
        if os.environ.get("DBG_DUMP") and bp == 0:
            dmp = nc.dram_tensor("dump", [6, 128, N], F32, kind="ExternalOutput").ap()
            k.wait("sync", "dv", k.cnt["dv"])
            k.wait("sync", "ac", k.cnt["ac"])
            for di, buf in enumerate((kr, KG, CS, EQ)):
                k.dma("sync", dmp[di], buf[:], "st")
            k.dma("gpsimd", dmp[4], ln0[:, 0:N], "wld")
            k.wait("sync", "wld", k.cnt["wld"])
            k.dma("sync", dmp[5, 0:128, 0:64], sc[:, 0:64], "st")
            k.dma("sync", dmp[5, 0:128, 96:192], sc[:, 96:192], "st")
            k.dma("sync", dmp[5, 0:128, 256:288], sc[:, 256:288], "st")
            k.wait("vector", "st", k.cnt["st"])
        if a2stop <= 2:
            continue

        for hh in range(2):
            proj_fm(wqd[hh], qs, b, F32, act=AF.Silu)
            for ns in range(NSL):
                bank = (psP if ns % 2 == 0 else psP2)[0:64, :]
                for a in range(DCH):
                    pet = k.op("tensor", lambda e, o=bank,
                               l=wvh[hh][:, a * 64:(a + 1) * 64],
                               r=ln0[:, a * T + b * N + ns * 512:
                                     a * T + b * N + ns * 512 + 512],
                               s=(a == 0), q=(a == DCH - 1):
                               e.matmul(o, l, r, start=s, stop=q), inc="pe",
                               mmb=0)
                k.wait("vector", "pe", pet)
                dvt = k.op("vector", lambda e, o=vfm_h[:, ns * 512:(ns + 1) * 512],
                           i_=bank: e.tensor_copy(o, i_), inc="dv")
                k.wait("tensor", "dv", dvt)
            k.wait("tensor", "dv", zfull_dv)
            k.wait("tensor", "ac", zfull_ac)
            for ns in range(NSL):
                nsl = slice(ns * 512, (ns + 1) * 512)
                pet = k.op("tensor", lambda e, o=psT[:], l=cst[f"selT{hh}"][:],
                           r=EQ[:, nsl]: e.matmul(o, l, r, start=True, stop=True),
                           mmb=0)
                pet = k.op("tensor", lambda e, o=psT2[:], l=cst[f"selT{hh}"][:],
                           r=KG[:, nsl]: e.matmul(o, l, r, start=True, stop=True),
                           inc="pe", mmb=0)
                k.wait("vector", "pe", pet)
                k.op("vector", lambda e, o=QG[:, nsl], a=qs[:, nsl],
                     b_=cst["cstab"][:, nsl]: e.tensor_mul(o, a, b_))
                k.op("vector", lambda e, o=QG[:, nsl], a=QG[:, nsl], b_=psT[:]:
                     e.tensor_mul(o, a, b_))
                dvt = k.op("vector", lambda e, o=KGC[:, nsl],
                           a=cst["cstab"][:, nsl], b_=psT2[:]:
                           e.tensor_mul(o, a, b_), inc="dv")
                k.wait("tensor", "dv", dvt)
            pet = k.op("tensor", lambda e, o=psT[:, 0:32], l=cst[f"selT{hh}"][:],
                       r=EQend: e.matmul(o, l, r, start=True, stop=True),
                       inc="pe", mmb=0)
            k.wait("vector", "pe", pet)
            k.vop_s(lambda e, o=EQd[:], i_=psT[:, 0:32]: e.tensor_copy(o, i_))
            k.op("vector", lambda e,
                 o=K2[:].rearrange("p (c k) -> p c k", k=CHUNK),
                 a=KGC[:].rearrange("p (c k) -> p c k", k=CHUNK), z=EQd[:]:
                 e.tensor_tensor(o, a, z.unsqueeze(2).to_broadcast(
                     (128, NCH, CHUNK)), ALU.mult))
            dvt = k.op("vector", lambda e: e.engine_nop(), inc="dv")
            k.wait("tensor", "dv", dvt)
            for c in range(NCH):
                bank = psK[:, 256 + (c % 2) * 64: 256 + (c % 2) * 64 + 64]
                pet = k.op("tensor", lambda e, o=bank,
                           i_=vfm_h[:, c * 64:(c + 1) * 64],
                           idn=cst["ident_bf"][0:64, 0:64]:
                           e.transpose(o, i_, idn), inc="pe", mmb=0)
                k.wait("vector", "pe", pet)
                dvt = k.op("vector", lambda e, o=v_cm_h[:, c * 64:(c + 1) * 64],
                           i_=bank: e.tensor_copy(o, i_), inc="dv")
                k.wait("tensor", "dv", dvt)
            k.op("vector", lambda e, o=S_h[:]: e.memset(o, 0.0))
            k.op("vector", lambda e, o=S_hb[:]: e.memset(o, 0.0), inc="dv")
            k.wait("tensor", "dv", k.cnt["dv"])
            if a2stop <= 3:
                continue

            cmask = int(os.environ.get("DBG_CHUNK", "15"))
            psA2 = k.pb2k[3][:][:, 512:576]
            psK2 = k.pb2k[3][:][:, 576:704].bitcast(BF16)[0:64, 0:256]

            def emit_front(c, ev_guard):
                par = c % 2
                pa = psA[0:64, 0:64] if par == 0 else psA2[0:64, 0:64]
                pk = psK[:, 0:128] if par == 0 else psK2[:, 0:128]
                csl = slice(c * 64, (c + 1) * 64)
                k.wait("tensor", "dv", ev_guard)
                k.op("tensor", lambda e, o=pk, i_=K2[:, csl],
                     idn=cst["ident_bf"][:]: e.transpose(o, i_, idn), mmb=0)
                return k.op("tensor", lambda e, o=pa, l=KGC[:, csl],
                            r=QG[:, csl]: e.matmul(o, l, r, start=True,
                                                   stop=True),
                            inc="pe", mmb=0)

            front_pe = [0, 0]
            front_guard = [0, 0]
            front_pe[0] = emit_front(0, 0)
            for c in range(NCH):
                par = c % 2
                pa = psA[0:64, 0:64] if par == 0 else psA2[0:64, 0:64]
                pk = psK[:, 0:128] if par == 0 else psK2[:, 0:128]
                csl = slice(c * 64, (c + 1) * 64)
                vsl = csl
                k.wait("vector", "pe", front_pe[par])
                k.op("vector", lambda e, o=k2tok[:, par * 128:(par + 1) * 128],
                     i_=pk: e.tensor_copy(o, i_))
                dvt = k.op("vector", lambda e,
                           o=Atile[:, par * 64:(par + 1) * 64], a=pa,
                           m=cst["triu1"][:]: e.tensor_mul(o, a, m), inc="dv")
                front_guard[par] = dvt
                k.wait("tensor", "dv", dvt)
                if c + 1 < NCH:
                    front_pe[(c + 1) % 2] = emit_front(c + 1,
                                                       front_guard[(c + 1) % 2])
                k.op("tensor", lambda e, o=psO[0:64, 0:64], l=S_hb[:],
                     r=QG[:, csl]: e.matmul(o, l, r, start=True, stop=False),
                     mmb=0)
                k.op("tensor", lambda e, o=psO[0:64, 0:64], l=v_cm_h[:, vsl],
                     r=Atile[:, par * 64:(par + 1) * 64]:
                     e.matmul(o, l, r, start=False, stop=True), mmb=0)
                pet = k.op("tensor", lambda e, o=psS[:, 0:64],
                           l=k2tok[:, par * 128:(par + 1) * 128],
                           r=v_cm_h[:, vsl]:
                           e.matmul(o, l, r, start=True, stop=True),
                           inc="pe", mmb=0)
                k.wait("vector", "pe", pet)
                k.op("vector", lambda e, o=O_h[:, csl], i_=psO[0:64, 0:64]:
                     e.tensor_copy(o, i_))
                k.op("vector", lambda e, o=S_h[:], i_=S_h[:], s=EQd[:, c:c + 1],
                     p_=psS[:, 0:64]:
                     e.scalar_tensor_tensor(o, i_, s, p_, ALU.mult, ALU.add))
                dvt = k.op("vector", lambda e, o=S_hb[:], i_=S_h[:]:
                           e.tensor_copy(o, i_), inc="dv")
                k.wait("tensor", "dv", dvt)
            k.op("vector", lambda e: e.engine_nop(), inc="dv")
            k.wait("sync", "dv", k.cnt["dv"])
            r0 = (2 * p + hh) * 64
            ot = k.dma("sync", o_s[r0:r0 + 64, b * N:(b + 1) * N], O_h[:], "st")
            k.wait("vector", "st", ot)

    if not dbg_a3:
        fin = k.fence("vector", "dv")
        k.wait("sync", "dv", fin)
        k.wait("sync", "st", k.cnt["st"])
        return
    k.srewind(m_gla)
    wg1 = k.salloc([128, DCH * 16], BF16)
    wg2 = k.salloc([16, D], BF16)
    wo = k.salloc([128, DCH * D], BF16)
    nrm = k.salloc([128, DCH], F32)
    t1 = k.salloc([16, T], BF16)
    gt = k.salloc([128, 512], BF16)
    og = k.salloc([128, DCH * 512], BF16)
    osl = k.salloc([128, 512], BF16)
    sq = k.salloc([128, 512], BF16)
    ssq = k.salloc([1, 512], F32)
    rr = k.salloc([1, 512], F32)
    xold = k.salloc([128, 2 * 512], F32)
    xnew = k.salloc([128, 2 * 512], F32)

    k.wait("gpsimd", "pe", k.cnt["pe"])
    k.dma("gpsimd", wg1[:].rearrange("p (a c) -> p a c", c=16),
          wg1_d.rearrange("(a p) c -> p a c", p=128), "wld")
    k.dma("gpsimd", wg2[:], wg2_d, "wld")
    k.dma("gpsimd", wo[:].rearrange("p (a c) -> p a c", c=D),
          wo_d.rearrange("(a p) c -> p a c", p=128), "wld")
    for j in range(DCH):
        k.dma("gpsimd", nrm[:, j:j + 1],
              normw_d[j * 128:(j + 1) * 128].unsqueeze(1), "wld")
    wld3 = k.cnt["wld"]
    k.wait("tensor", "wld", wld3)
    k.wait("vector", "wld", wld3)

    for ns in range(T // 512):
        bank = psP if ns % 2 == 0 else psP2
        for a in range(DCH):
            pet = k.op("tensor", lambda e, o=bank[0:16, :],
                       l=wg1[:, a * 16:(a + 1) * 16],
                       r=ln0[:, a * T + ns * 512: a * T + ns * 512 + 512],
                       s=(a == 0), p=(a == DCH - 1):
                       e.matmul(o, l, r, start=s, stop=p), inc="pe")
        k.wait("vector", "pe", pet)
        dvt = k.op("vector", lambda e, o=t1[:, ns * 512:(ns + 1) * 512],
                   i_=bank[0:16, :]: e.tensor_copy(o, i_), inc="dv")
        k.wait("tensor", "dv", dvt)

    og_prev = 0
    sqmm_prev = 0
    for ts in range(T // 512):
        tsl = slice(ts * 512, (ts + 1) * 512)
        for j in range(DCH):
            bank = psP
            pet = k.op("tensor", lambda e, o=bank[:], l=wg2[:, j * 128:(j + 1) * 128],
                       r=t1[:, tsl]: e.matmul(o, l, r, start=True, stop=True),
                       inc="pe")
            k.wait("scalar", "pe", pet)
            k.wait("scalar", "dv", og_prev)
            act = k.op("scalar", lambda e, o=gt[:], i_=bank[:]:
                       e.activation(o, i_, AF.Sigmoid, bias=0.0, scale=1.0),
                       inc="ac")
            k.wait("tensor", "ac", act)
            k.wait("sync", "dv", og_prev)
            lo = k.dma("sync", osl[:], o_s[j * 128:(j + 1) * 128, tsl], "ld")
            k.wait("vector", "ld", lo)
            k.wait("vector", "ac", act)
            og_prev = k.op("vector", lambda e, o=og[:, j * 512:(j + 1) * 512],
                           a=osl[:], b_=gt[:]: e.tensor_mul(o, a, b_), inc="dv")
            k.wait("scalar", "dv", og_prev)
            k.wait("scalar", "pe", sqmm_prev)
            act2 = k.op("scalar", lambda e, o=sq[:], i_=og[:, j * 512:(j + 1) * 512]:
                        e.activation(o, i_, AF.Square, bias=0.0, scale=1.0),
                        inc="ac")
            k.wait("tensor", "ac", act2)
            sqmm_prev = k.op("tensor", lambda e, o=psP2[0:1, :],
                             l=cst["ones_col_bf"][:], r=sq[:],
                             s=(j == 0), p=(j == DCH - 1):
                             e.matmul(o, l, r, start=s, stop=p), inc="pe")
        k.wait("vector", "pe", k.cnt["pe"])
        k.op("vector", lambda e, o=ssq[:], i_=psP2[0:1, :]: e.tensor_copy(o, i_))
        k.op("vector", lambda e, o=ssq[:], i_=ssq[:]:
             e.tensor_scalar(o, i_, 1.0 / D, 1e-6, ALU.mult, ALU.add), inc="dv")
        k.wait("scalar", "dv", k.cnt["dv"])
        k.op("scalar", lambda e, o=rr[:], i_=ssq[:]:
             e.activation(o, i_, AF.Sqrt, bias=0.0, scale=1.0), inc="ac")
        k.wait("vector", "ac", k.cnt["ac"])
        k.op("vector", lambda e, o=rr[:], i_=rr[:]: e.reciprocal(o, i_), inc="dv")
        k.wait("tensor", "dv", k.cnt["dv"])
        pet = k.op("tensor", lambda e, o=psP2[:], l=cst["ones_row_f32"][:], r=rr[:]:
                   e.matmul(o, l, r, start=True, stop=True), inc="pe")
        k.wait("vector", "pe", pet)
        for j in range(DCH):
            k.op("vector", lambda e, o=og[:, j * 512:(j + 1) * 512],
                 i_=og[:, j * 512:(j + 1) * 512], s=nrm[:, j:j + 1], p_=psP2[:]:
                 e.scalar_tensor_tensor(o, i_, s, p_, ALU.mult, ALU.mult))
        dvt = k.op("vector", lambda e: e.engine_nop(), inc="dv")
        k.wait("tensor", "dv", dvt)
        for j2 in range(DCH):
            bank = psP
            for a in range(DCH):
                pet = k.op("tensor", lambda e, o=bank[:],
                           l=wo[:, a * D + j2 * 128: a * D + j2 * 128 + 128],
                           r=og[:, a * 512:(a + 1) * 512],
                           s=(a == 0), p=(a == DCH - 1):
                           e.matmul(o, l, r, start=s, stop=p), inc="pe")
            buf = j2 % 2
            lo = k.dma("sync", xold[:, buf * 512:(buf + 1) * 512],
                       xr0[j2 * 128:(j2 + 1) * 128, tsl], "ld")
            k.wait("vector", "pe", pet)
            k.wait("vector", "ld", lo)
            k.op("vector", lambda e, o=xnew[:, buf * 512:(buf + 1) * 512],
                 a=bank[:], b_=xold[:, buf * 512:(buf + 1) * 512]:
                 e.tensor_add(o, a, b_), inc="dv")
            k.wait("tensor", "dv", k.cnt["dv"])
            k.wait("sync", "dv", k.cnt["dv"])
            k.dma("sync", xr1[j2 * 128:(j2 + 1) * 128, tsl],
                  xnew[:, buf * 512:(buf + 1) * 512], "st")
    if os.environ.get("DBG_A3D"):
        dmp3 = nc.dram_tensor("dumpa3", [4, 128, 4096], F32,
                              kind="ExternalOutput").ap()
        f2 = k.fence("vector", "dv")
        k.wait("sync", "dv", f2)
        k.wait("gpsimd", "dv", f2)
        k.dma("gpsimd", dmp3[0], og[:], "wld")
        k.dma("gpsimd", dmp3[1][:, 0:512], gt[:], "wld")
        k.dma("sync", dmp3[2][0:1, 0:512], rr[:], "st")
        k.dma("sync", dmp3[3][:, 0:1024], xold[:], "st")
        k.wait("sync", "wld", k.cnt["wld"])
    fin = k.fence("vector", "dv")
    k.wait("sync", "dv", fin)
    k.wait("sync", "st", k.cnt["st"])



def new_program():
    nc = bass.Bass("TRN2", target_bir_lowering=False,
                   detect_race_conditions=False)
    k = K(nc)
    for s in BASE_SEMS:
        k.sem(s)
    bt = nc.alloc_sbuf_tensor_at("bar_tile", [1, 8], F32, offset=ARENA_BASE - 64)
    k.bar_tile = bt[0:1, 0:1]
    k.q["scalar"].append(("op", lambda e, o=k.bar_tile: e.memzero(o), None, 1))
    k.pb2k = [nc.alloc_psum_tensor(f"pb2k{i}", [128, 1024], F32) for i in range(4)]

    class _V:
        def __init__(self, ap):
            self._ap = ap

        def __getitem__(self, key):
            return self._ap[key]

    k.pbank = [_V(k.pb2k[i // 2][:][:, (i % 2) * 512:(i % 2 + 1) * 512])
               for i in range(8)]
    return nc, k


def build_phase_a_program():
    nc, k = new_program()
    x_in = nc.dram_tensor("x", [T, D], F32, kind="ExternalInput").ap()
    wq_d = nc.dram_tensor("Wq", [D, D], F32, kind="ExternalInput").ap()
    wk_d = nc.dram_tensor("Wk", [D, D], F32, kind="ExternalInput").ap()
    wv_d = nc.dram_tensor("Wv", [D, D], F32, kind="ExternalInput").ap()
    wo_d = nc.dram_tensor("Wo", [D, D], F32, kind="ExternalInput").ap()
    wg1_d = nc.dram_tensor("Wg1", [D, 16], F32, kind="ExternalInput").ap()
    wg2_d = nc.dram_tensor("Wg2", [16, D], F32, kind="ExternalInput").ap()
    normw_d = nc.dram_tensor("norm_w", [D], F32, kind="ExternalInput").ap()
    xr0 = nc.dram_tensor("xr0", [D, T], F32, kind="Internal").ap()
    xr1 = nc.dram_tensor("xr1", [D, T], F32, kind="ExternalOutput").ap()
    import os as _os
    o_s = nc.dram_tensor("o_s", [D, T], BF16,
                         kind="ExternalOutput" if _os.environ.get("DBG_OS")
                         else "Internal").ap()
    cst = stage_consts(nc, k)
    for eng in ENGINES:
        k.wait(eng, "ld", k.cnt["ld"])
    k.freeze_persistent()
    k.set_phase("pa")
    import os
    dbg_bp = int(os.environ.get("DBG_BP", "99"))
    dbg_a3 = os.environ.get("DBG_A3", "1") == "1"
    emit_phase_a(nc, k, cst, x_in, wq_d, wk_d, wv_d, wo_d, wg1_d, wg2_d,
                 normw_d, xr0, xr1, o_s, dbg_bp=dbg_bp, dbg_a3=dbg_a3)
    k.build()
    return nc


_PROGS = {}


def _prog(name, builder):
    if name not in _PROGS:
        _PROGS[name] = builder()
    return _PROGS[name]


def kernel(**inputs):
    import os as _os
    if _os.environ.get("UNFUSED"):
        return kernel_unfused(**inputs)
    return kernel_fused(**inputs)


def kernel_unfused(**inputs):
    consts = host_constants()
    NCORES = 8
    x_full = inputs["L0_attn_x"]
    x_slices = [np.ascontiguousarray(
        x_full[c * B_CORE:(c + 1) * B_CORE].reshape(T, D)) for c in range(NCORES)]

    ncA = _prog("a", build_phase_a_program)
    maps = []
    for c in range(NCORES):
        m = dict(consts)
        m["x"] = x_slices[c]
        for nm, src in (("Wq", "L0_attn_Wq"), ("Wk", "L0_attn_Wk"),
                        ("Wv", "L0_attn_Wv"), ("Wo", "L0_attn_Wo"),
                        ("Wg1", "L0_attn_Wg1"), ("Wg2", "L0_attn_Wg2"),
                        ("norm_w", "L0_attn_norm_w")):
            m[nm] = inputs[src]
        maps.append(m)
    resA = run_bass_kernel_spmd(ncA, maps, core_ids=list(range(NCORES)))
    xr1s = [r["xr1"] for r in resA.results]

    ncB = _prog("b", build_phase_b_program)
    maps = []
    for c in range(NCORES):
        m = dict(consts)
        m["xr1"] = xr1s[c]
        for nm, src in (("router", "L0_ffn_router_DE"), ("w1", "L0_ffn_w1_eDF"),
                        ("w3", "L0_ffn_w3_eDF"), ("w2", "L0_ffn_w2_eFD"),
                        ("ws1", "L0_ffn_ws1_DF"), ("ws3", "L0_ffn_ws3_DF"),
                        ("ws2", "L0_ffn_ws2_FD")):
            m[nm] = inputs[src]
        maps.append(m)
    resB = run_bass_kernel_spmd(ncB, maps, core_ids=list(range(NCORES)))
    xr2s = [r["xr2"] for r in resB.results]

    ncC = _prog("c", build_phase_c_program)
    maps = []
    for c in range(NCORES):
        m = dict(consts)
        m["xr2"] = xr2s[c]
        for nm, src in (("w_in", "L1_attn_w_in"), ("conv_w", "L1_attn_conv_w"),
                        ("conv_b", "L1_attn_conv_b"), ("w_B", "L1_attn_w_B"),
                        ("w_C", "L1_attn_w_C"), ("w_out", "L1_attn_w_out")):
            m[nm] = inputs[src]
        maps.append(m)
    resC = run_bass_kernel_spmd(ncC, maps, core_ids=list(range(NCORES)))
    x3ts = [r["x3t"] for r in resC.results]

    ncD = _prog("d", build_phase_d_program)
    maps = []
    for c in range(NCORES):
        m = dict(consts)
        m["x3t"] = x3ts[c]
        for nm, src in (("w_gate", "L1_ffn_w_gate"), ("w_up", "L1_ffn_w_up"),
                        ("w_down", "L1_ffn_w_down")):
            m[nm] = inputs[src]
        maps.append(m)
    resD = run_bass_kernel_spmd(ncD, maps, core_ids=list(range(NCORES)))
    outs = [np.asarray(r["out"]) for r in resD.results]
    return np.concatenate(outs, axis=0).reshape(16, N, D).astype(np.float32)



F_HALF = 1408
FC_HALF = 11
NS_B = 4


def emit_phase_b(nc, k, cst, xr1, router_d, w1_d, w3_d, w2_d, ws1_d, ws3_d,
                 ws2_d, xr2, lnS, alpha_d):
    k.sreset()
    m_b1 = k.smark()
    xsl = k.salloc([128, 512], F32)
    lnf = [k.salloc([128, 512], F32) for _ in range(DCH)]
    xb = k.salloc([128, 512], BF16)
    xsqb = k.salloc([128, 512], BF16)
    row = k.salloc([1, 512 * 4], F32)
    Rb = k.salloc([128, 512], F32)
    lnbf = k.salloc([128, 512], BF16)
    alpha = k.palloc_dummy if False else None
    alpha = k.salloc([8, T], F32)
    router = k.salloc([128, DCH * 8], F32)
    scst = k.salloc([128, 19], F32)

    psA_ = k.pbank[0][:]
    psB_ = k.pbank[1][:]
    psM = k.pbank[2][:]
    psR = k.pbank[3][:]
    psS_ = k.pbank[4][:]

    k.dma("sync", router[:].rearrange("p (a c) -> p a c", c=8),
          router_d.rearrange("(a p) c -> p a c", p=128), "ld")
    b1_ld = k.cnt["ld"]
    k.wait("tensor", "ld", b1_ld)

    prev_copy_dv = 0
    prev_mm_pe = 0
    for ts in range(T // 512):
        tsl = slice(ts * 512, (ts + 1) * 512)
        for j in range(DCH):
            k.wait("sync", "dv", prev_copy_dv)
            ldt = k.dma("sync", xsl[:], xr1[j * 128:(j + 1) * 128, tsl], "ld")
            k.wait("vector", "ld", ldt)
            k.wait("vector", "pe", prev_mm_pe)
            dvt = k.op("vector", lambda e, o=xb[:], i_=xsl[:]:
                       e.tensor_copy(o, i_), inc="dv")
            k.wait("scalar", "dv", dvt)
            k.wait("scalar", "pe", prev_mm_pe)
            act = k.op("scalar", lambda e, o=xsqb[:], i_=xb[:]:
                       e.activation(o, i_, AF.Square, bias=0.0, scale=1.0),
                       inc="ac")
            k.wait("tensor", "dv", dvt)
            k.wait("tensor", "ac", act)
            k.op("tensor", lambda e, o=psA_[0:1, :], l=cst["ones_col_bf"][:],
                 r=xb[:], s=(j == 0), q=(j == DCH - 1):
                 e.matmul(o, l, r, start=s, stop=q), mmb=0)
            prev_mm_pe = k.op("tensor", lambda e, o=psB_[0:1, :],
                              l=cst["ones_col_bf"][:],
                              r=xsqb[:], s=(j == 0), q=(j == DCH - 1):
                              e.matmul(o, l, r, start=s, stop=q), inc="pe", mmb=0)
            prev_copy_dv = k.op("vector", lambda e, o=lnf[j][:], i_=xsl[:]:
                                e.tensor_copy(o, i_), inc="dv")
        k.wait("vector", "pe", k.cnt["pe"])
        r_m = row[:, 0:512]
        r_msq = row[:, 512:1024]
        r_var = row[:, 1024:1536]
        r_rstd = row[:, 1536:2048]
        k.vop_s(lambda e, o=r_m, i_=psA_[0:1, :]:
                e.tensor_scalar(o, i_, 1.0 / D, None, ALU.mult))
        k.vop_s(lambda e, o=r_msq, i_=psB_[0:1, :]:
                e.tensor_scalar(o, i_, 1.0 / D, None, ALU.mult))
        k.vop_s(lambda e, o=r_var, a=r_m, b_=r_m: e.tensor_mul(o, a, b_))
        k.vop_s(lambda e, o=r_rstd, a=r_msq, b_=r_var: e.tensor_sub(o, a, b_))
        k.vop_s(lambda e, o=r_var, i_=r_rstd:
                e.tensor_scalar(o, i_, 1e-5, None, ALU.add))
        k.wait("scalar", "dv", k.cnt["dv"])
        sct = k.op("scalar", lambda e, o=r_rstd, i_=r_var:
                   e.activation(o, i_, AF.Sqrt, bias=0.0, scale=1.0), inc="ac")
        k.wait("vector", "ac", sct)
        k.vop_s(lambda e, o=r_var, i_=r_rstd: e.reciprocal(o, i_))
        k.wait("tensor", "dv", k.cnt["dv"])
        k.op("tensor", lambda e, o=psM[:], l=cst["ones_row_f32"][:], r=r_m:
             e.matmul(o, l, r, start=True, stop=True), mmb=0)
        pet = k.op("tensor", lambda e, o=psR[:], l=cst["ones_row_f32"][:],
                   r=r_var: e.matmul(o, l, r, start=True, stop=True),
                   inc="pe", mmb=0)
        k.wait("vector", "pe", pet)
        k.op("vector", lambda e, o=Rb[:], i_=psR[:]: e.tensor_copy(o, i_))
        for j in range(DCH):
            k.op("vector", lambda e, o=lnf[j][:], a=lnf[j][:], b_=psM[:]:
                 e.tensor_sub(o, a, b_))
            k.op("vector", lambda e, o=lnf[j][:], a=lnf[j][:], b_=Rb[:]:
                 e.tensor_mul(o, a, b_))
            dvt = k.op("vector", lambda e, o=lnbf[:], i_=lnf[j][:]:
                       e.tensor_copy(o, i_), inc="dv")
            k.wait("sync", "dv", dvt)
            stt = k.dma("sync", lnS[j * 128:(j + 1) * 128, tsl], lnbf[:], "st")
            k.wait("vector", "st", stt)
        k.op("vector", lambda e: e.engine_nop(), inc="dv")
        k.wait("tensor", "dv", k.cnt["dv"])
        for tt in range(4):
            ttl = slice(tt * 128, (tt + 1) * 128)
            for j in range(DCH):
                pet = k.op("tensor", lambda e, o=psS_[:, 0:8], l=lnf[j][:, ttl],
                           r=router[:, j * 8:(j + 1) * 8], s=(j == 0),
                           q=(j == DCH - 1): e.matmul(o, l, r, start=s, stop=q),
                           inc="pe", mmb=0)
            prev_mm_pe = pet
            k.wait("vector", "pe", pet)
            k.vop_s(lambda e, o=scst[:, 0:1], i_=psS_[:, 0:8]:
                    e.reduce_max(o, i_, AX.X))
            k.wait("scalar", "dv", k.cnt["dv"])
            sct = k.op("scalar", lambda e, o=scst[:, 1:2], i_=scst[:, 0:1]:
                       e.activation(o, i_, AF.Sigmoid, bias=0.0, scale=1.0),
                       inc="ac")
            k.vop_s(lambda e, o=scst[:, 2:3], i_=scst[:, 0:1]:
                    e.tensor_copy(o, i_))
            k.op("vector", lambda e, o=scst[:, 3:11], i_=psS_[:, 0:8],
                 m=scst[:, 0:1]:
                 e.tensor_scalar(o, i_, m, None, ALU.is_equal))
            k.wait("vector", "ac", sct)
            k.vop_s(lambda e, o=scst[:, 11:19], i_=scst[:, 3:11], g=scst[:, 1:2]:
                    e.tensor_scalar(o, i_, g, None, ALU.mult))
            k.wait("tensor", "dv", k.cnt["dv"])
            pet = k.op("tensor", lambda e, o=psS_[0:8, 8:136], i_=scst[:, 11:19],
                       idn=cst["ident_f32"][:]: e.transpose(o, i_, idn),
                       inc="pe", mmb=0)
            k.wait("vector", "pe", pet)
            dvt = k.op("vector", lambda e, o=alpha[:, ts * 512 + tt * 128:
                                               ts * 512 + (tt + 1) * 128],
                       i_=psS_[0:8, 8:136]: e.tensor_copy(o, i_), inc="dv")
            k.wait("tensor", "dv", dvt)
            prev_mm_pe = pet
    b1_done = k.fence("vector", "dv")
    k.wait("gpsimd", "dv", b1_done)
    k.wait("sync", "dv", b1_done)
    k.dma("gpsimd", alpha_d, alpha[:], "wld")
    k.wait("sync", "wld", k.cnt["wld"])
    k.wait("sync", "st", k.cnt["st"])

    k.srewind(m_b1)
    alpha2 = alpha
    out_sb = [k.salloc([128, T], BF16) for _ in range(DCH)]
    m_b2 = k.smark()
    w1h = k.salloc([128, DCH * F_HALF], BF16)
    w3h = k.salloc([128, DCH * F_HALF], BF16)
    w2h = k.salloc([128, FC_HALF * 1024], BF16)
    midh = k.salloc([128, FC_HALF * 1024], BF16)
    xe = k.salloc([128, DCH * 1024], BF16)
    arow = k.salloc([1, 512], BF16)
    Ae = k.salloc([128, T], BF16)
    lnt8 = k.salloc([128, DCH * 1024], BF16)
    sil = k.salloc([128, 2 * 1024], BF16)

    psU = k.pb2k[1][:]
    psW = k.pb2k[2][:]
    psUb = k.pb2k[0][:]
    psWb = k.pb2k[3][:]
    psD = k.pb2k[3][:]
    psDb = k.pb2k[0][:]

    xe_dv = [0, 0]
    xe_pe = 0
    mid_dv = [0, 0]
    mid_ac = [0, 0]
    down_dv = [0, 0]
    for e_i in range(9):
        shared = (e_i == 8)
        if not shared:
            for ns2 in range(DCH):
                n2 = slice(ns2 * 512, (ns2 + 1) * 512)
                al = k.dma("sync", arow[0:1, 0:512],
                           alpha_d[e_i:e_i + 1, n2], "ld")
                k.wait("tensor", "ld", al)
                pet = k.op("tensor", lambda e, o=psM[:], l=cst["ones_row_bf"][:],
                           r=arow[0:1, 0:512]:
                           e.matmul(o, l, r, start=True, stop=True),
                           inc="pe", mmb=0)
                k.wait("vector", "pe", pet)
                dvt = k.op("vector", lambda e, o=Ae[:, n2], i_=psM[:]:
                           e.tensor_copy(o, i_), inc="dv")
                k.wait("tensor", "dv", dvt)
                k.wait("sync", "dv", dvt)
        for fh in range(2):
            f0 = fh * F_HALF
            gi = e_i * 2 + fh

            def issue_w13(e2, fh2):
                f02 = fh2 * F_HALF
                k.wait("gpsimd", "pe", k.cnt["pe"])
                for kc in range(DCH):
                    if e2 == 8:
                        srcw1 = ws1_d[kc * 128:(kc + 1) * 128, f02:f02 + F_HALF]
                        srcw3 = ws3_d[kc * 128:(kc + 1) * 128, f02:f02 + F_HALF]
                    else:
                        srcw1 = w1_d[e2, kc * 128:(kc + 1) * 128, f02:f02 + F_HALF]
                        srcw3 = w3_d[e2, kc * 128:(kc + 1) * 128, f02:f02 + F_HALF]
                    k.dma("gpsimd", w1h[:, kc * F_HALF:(kc + 1) * F_HALF], srcw1,
                          "wld")
                    k.dma("gpsimd", w3h[:, kc * F_HALF:(kc + 1) * F_HALF], srcw3,
                          "wld")
                return k.cnt["wld"]

            def issue_w2(e2, fh2):
                f02 = fh2 * F_HALF
                k.wait("gpsimd", "pe", k.cnt["pe"])
                for kc in range(FC_HALF):
                    if e2 == 8:
                        srcw2 = ws2_d[f02 + kc * 128:f02 + (kc + 1) * 128, :]
                    else:
                        srcw2 = w2_d[e2, f02 + kc * 128:f02 + (kc + 1) * 128, :]
                    k.dma("gpsimd", w2h[:, kc * 1024:(kc + 1) * 1024], srcw2,
                          "wld")
                return k.cnt["wld"]

            if gi == 0:
                w13_tgt = issue_w13(0, 0)
            k.wait("tensor", "wld", w13_tgt)
            w2_issued = False
            for ns in range(NS_B):
                nsl = slice(ns * 1024, (ns + 1) * 1024)
                if ns == 0:
                    for j in range(DCH):
                        k.wait("sync", "dv", xe_dv[0])
                        ln_tgt = k.dma("sync", lnt8[:, j * 1024:(j + 1) * 1024],
                                       lnS[j * 128:(j + 1) * 128, nsl], "ld")
                k.wait("vector", "ld", ln_tgt)
                k.wait("vector", "pe", xe_pe)
                for j in range(DCH):
                    if shared:
                        k.op("vector", lambda e,
                             o=xe[:, j * 1024:(j + 1) * 1024],
                             i_=lnt8[:, j * 1024:(j + 1) * 1024]:
                             e.tensor_copy(o, i_),
                             inc="dv" if j == DCH - 1 else None)
                    else:
                        k.op("vector", lambda e,
                             o=xe[:, j * 1024:(j + 1) * 1024],
                             a=lnt8[:, j * 1024:(j + 1) * 1024],
                             b_=Ae[:, nsl]: e.tensor_mul(o, a, b_),
                             inc="dv" if j == DCH - 1 else None)
                xe_dv[0] = k.cnt["dv"]
                if ns + 1 < NS_B:
                    n2 = slice((ns + 1) * 1024, (ns + 2) * 1024)
                    k.wait("sync", "dv", xe_dv[0])
                    for j in range(DCH):
                        ln_tgt = k.dma("sync", lnt8[:, j * 1024:(j + 1) * 1024],
                                       lnS[j * 128:(j + 1) * 128, n2], "ld")
                k.wait("tensor", "dv", k.cnt["dv"])
                for f in range(FC_HALF):
                    par = f % 2
                    pU = (psU if par == 0 else psUb)
                    pW = (psW if par == 0 else psWb)
                    k.wait("tensor", "dv", mid_dv[par])
                    k.wait("tensor", "ac", mid_ac[par])
                    for half in range(2):
                        hsl = slice(half * 512, (half + 1) * 512)
                        for kc in range(DCH):
                            k.op("tensor", lambda e, o=pU[:, hsl],
                                 l=w1h[:, kc * F_HALF + f * 128:
                                       kc * F_HALF + (f + 1) * 128],
                                 r=xe[:, kc * 1024 + half * 512:
                                       kc * 1024 + (half + 1) * 512],
                                 s=(kc == 0), q=(kc == DCH - 1):
                                 e.matmul(o, l, r, start=s, stop=q), mmb=0)
                        for kc in range(DCH):
                            pet = k.op("tensor", lambda e, o=pW[:, hsl],
                                       l=w3h[:, kc * F_HALF + f * 128:
                                             kc * F_HALF + (f + 1) * 128],
                                       r=xe[:, kc * 1024 + half * 512:
                                             kc * 1024 + (half + 1) * 512],
                                       s=(kc == 0), q=(kc == DCH - 1):
                                       e.matmul(o, l, r, start=s, stop=q),
                                       inc="pe", mmb=0)
                    xe_pe = pet
                    k.wait("scalar", "pe", pet)
                    k.wait("scalar", "dv", mid_dv[1 - par])
                    mid_ac[par] = k.op("scalar", lambda e, o=sil[:, par * 1024:
                                       (par + 1) * 1024], i_=pU[:]:
                                       e.activation(o, i_, AF.Silu, bias=0.0,
                                                    scale=1.0), inc="ac")
                    k.wait("vector", "ac", mid_ac[par])
                    mid_dv[par] = k.op("vector", lambda e,
                                       o=midh[:, f * 1024:(f + 1) * 1024],
                                       a=sil[:, par * 1024:(par + 1) * 1024],
                                       b_=pW[:]: e.tensor_mul(o, a, b_),
                                       inc="dv")
                if not w2_issued:
                    w2_tgt = issue_w2(e_i, fh)
                    w2_issued = True
                k.wait("tensor", "dv", k.cnt["dv"])
                k.wait("tensor", "ac", k.cnt["ac"])
                k.wait("tensor", "wld", w2_tgt)
                for j in range(DCH):
                    par = j % 2
                    pD = (psD if par == 0 else psDb)
                    k.wait("tensor", "dv", down_dv[par])
                    for half in range(2):
                        hsl = slice(half * 512, (half + 1) * 512)
                        for kc in range(FC_HALF):
                            pet = k.op("tensor", lambda e, o=pD[:, hsl],
                                       l=w2h[:, kc * 1024 + j * 128:
                                             kc * 1024 + (j + 1) * 128],
                                       r=midh[:, kc * 1024 + half * 512:
                                              kc * 1024 + (half + 1) * 512],
                                       s=(kc == 0), q=(kc == FC_HALF - 1):
                                       e.matmul(o, l, r, start=s, stop=q),
                                       inc="pe", mmb=0)
                    k.wait("vector", "pe", pet)
                    if e_i == 0 and fh == 0:
                        down_dv[par] = k.op("vector", lambda e,
                                            o=out_sb[j][:, nsl],
                                            i_=pD[:]: e.tensor_copy(o, i_),
                                            inc="dv")
                    else:
                        down_dv[par] = k.op("vector", lambda e,
                                            o=out_sb[j][:, nsl],
                                            a=out_sb[j][:, nsl], b_=pD[:]:
                                            e.tensor_add(o, a, b_), inc="dv")
                if ns == NS_B - 1 and gi + 1 < 18:
                    e2, fh2 = divmod(gi + 1, 2)
                    w13_tgt = issue_w13(e2, fh2)
    k.srewind(m_b2)
    xo = k.salloc([128, 2 * 512], F32)
    xn = k.salloc([128, 2 * 512], F32)
    b2_done = k.fence("vector", "dv")
    k.wait("sync", "dv", b2_done)
    for ts in range(T // 512):
        tsl = slice(ts * 512, (ts + 1) * 512)
        for j in range(DCH):
            buf = j % 2
            ldt = k.dma("sync", xo[:, buf * 512:(buf + 1) * 512],
                        xr1[j * 128:(j + 1) * 128, tsl], "ld")
            k.wait("vector", "ld", ldt)
            dvt = k.op("vector", lambda e, o=xn[:, buf * 512:(buf + 1) * 512],
                       a=xo[:, buf * 512:(buf + 1) * 512],
                       b_=out_sb[j][:, tsl]: e.tensor_add(o, a, b_), inc="dv")
            k.wait("sync", "dv", dvt)
            k.dma("sync", xr2[j * 128:(j + 1) * 128, tsl],
                  xn[:, buf * 512:(buf + 1) * 512], "st")
    fin = k.fence("vector", "dv")
    k.wait("sync", "dv", fin)
    k.wait("sync", "st", k.cnt["st"])


def build_phase_b_program():
    nc, k = new_program()
    xr1 = nc.dram_tensor("xr1", [D, T], F32, kind="ExternalInput").ap()
    router_d = nc.dram_tensor("router", [D, 8], F32, kind="ExternalInput").ap()
    w1_d = nc.dram_tensor("w1", [8, D, 2816], F32, kind="ExternalInput").ap()
    w3_d = nc.dram_tensor("w3", [8, D, 2816], F32, kind="ExternalInput").ap()
    w2_d = nc.dram_tensor("w2", [8, 2816, D], F32, kind="ExternalInput").ap()
    ws1_d = nc.dram_tensor("ws1", [D, 2816], F32, kind="ExternalInput").ap()
    ws3_d = nc.dram_tensor("ws3", [D, 2816], F32, kind="ExternalInput").ap()
    ws2_d = nc.dram_tensor("ws2", [2816, D], F32, kind="ExternalInput").ap()
    xr2 = nc.dram_tensor("xr2", [D, T], F32, kind="ExternalOutput").ap()
    lnS = nc.dram_tensor("lnS", [D, T], BF16, kind="Internal").ap()
    alpha_d = nc.dram_tensor("alpha_d", [8, T], BF16, kind="Internal").ap()
    cst = stage_consts(nc, k)
    for eng in ENGINES:
        k.wait(eng, "ld", k.cnt["ld"])
    k.freeze_persistent()
    k.set_phase("pb")
    emit_phase_b(nc, k, cst, xr1, router_d, w1_d, w3_d, w2_d, ws1_d, ws3_d,
                 ws2_d, xr2, lnS, alpha_d)
    k.build()
    return nc



DI = 2048
DIC = DI // 128


def emit_ln_resident(nc, k, cst, src, dst, psA_, psB_, psM, psR, scratch):
    xsl, xb, xsqb, row, Rb = scratch
    prev_copy_dv = 0
    prev_mm_pe = 0
    for ts in range(T // 512):
        tsl = slice(ts * 512, (ts + 1) * 512)
        lnf_dv = 0
        for j in range(DCH):
            k.wait("sync", "dv", prev_copy_dv)
            ldt = k.dma("sync", xsl[:, j * 512:(j + 1) * 512],
                        src[j * 128:(j + 1) * 128, tsl], "ld")
            k.wait("vector", "ld", ldt)
            k.wait("vector", "pe", prev_mm_pe)
            dvt = k.op("vector", lambda e, o=xb[:], i_=xsl[:, j * 512:(j + 1) * 512]:
                       e.tensor_copy(o, i_), inc="dv")
            k.wait("scalar", "dv", dvt)
            k.wait("scalar", "pe", prev_mm_pe)
            act = k.op("scalar", lambda e, o=xsqb[:], i_=xb[:]:
                       e.activation(o, i_, AF.Square, bias=0.0, scale=1.0),
                       inc="ac")
            k.wait("tensor", "dv", dvt)
            k.wait("tensor", "ac", act)
            k.op("tensor", lambda e, o=psA_[0:1, :], l=cst["ones_col_bf"][:],
                 r=xb[:], s=(j == 0), q=(j == DCH - 1):
                 e.matmul(o, l, r, start=s, stop=q), mmb=0)
            prev_mm_pe = k.op("tensor", lambda e, o=psB_[0:1, :],
                              l=cst["ones_col_bf"][:],
                              r=xsqb[:], s=(j == 0), q=(j == DCH - 1):
                              e.matmul(o, l, r, start=s, stop=q), inc="pe",
                              mmb=0)
            prev_copy_dv = k.cnt["dv"]
        k.wait("vector", "pe", k.cnt["pe"])
        r_m = row[:, 0:512]
        r_msq = row[:, 512:1024]
        r_var = row[:, 1024:1536]
        r_rstd = row[:, 1536:2048]
        k.vop_s(lambda e, o=r_m, i_=psA_[0:1, :]:
                e.tensor_scalar(o, i_, 1.0 / D, None, ALU.mult))
        k.vop_s(lambda e, o=r_msq, i_=psB_[0:1, :]:
                e.tensor_scalar(o, i_, 1.0 / D, None, ALU.mult))
        k.vop_s(lambda e, o=r_var, a=r_m, b_=r_m: e.tensor_mul(o, a, b_))
        k.vop_s(lambda e, o=r_rstd, a=r_msq, b_=r_var: e.tensor_sub(o, a, b_))
        k.vop_s(lambda e, o=r_var, i_=r_rstd:
                e.tensor_scalar(o, i_, 1e-5, None, ALU.add))
        k.wait("scalar", "dv", k.cnt["dv"])
        sct = k.op("scalar", lambda e, o=r_rstd, i_=r_var:
                   e.activation(o, i_, AF.Sqrt, bias=0.0, scale=1.0), inc="ac")
        k.wait("vector", "ac", sct)
        k.vop_s(lambda e, o=r_var, i_=r_rstd: e.reciprocal(o, i_))
        k.wait("tensor", "dv", k.cnt["dv"])
        k.op("tensor", lambda e, o=psM[:], l=cst["ones_row_f32"][:], r=r_m:
             e.matmul(o, l, r, start=True, stop=True), mmb=0)
        pet = k.op("tensor", lambda e, o=psR[:], l=cst["ones_row_f32"][:],
                   r=r_var: e.matmul(o, l, r, start=True, stop=True),
                   inc="pe", mmb=0)
        prev_mm_pe = pet
        k.wait("vector", "pe", pet)
        k.op("vector", lambda e, o=Rb[:], i_=psR[:]: e.tensor_copy(o, i_))
        for j in range(DCH):
            k.op("vector", lambda e, o=xsl[:, j * 512:(j + 1) * 512],
                 a=xsl[:, j * 512:(j + 1) * 512], b_=psM[:]:
                 e.tensor_sub(o, a, b_))
            dvt = k.op("vector", lambda e, o=dst[:, j * T + ts * 512:
                                              j * T + (ts + 1) * 512],
                       a=xsl[:, j * 512:(j + 1) * 512], b_=Rb[:]:
                       e.tensor_mul(o, a, b_), inc="dv")
        prev_copy_dv = k.cnt["dv"]
    done = k.fence("vector", "dv")
    for eng in ("tensor", "sync", "gpsimd", "scalar"):
        k.wait(eng, "dv", done)


def emit_phase_c(nc, k, cst, xr2, win_d, convw_d, convb_d, wB_d, wC_d, wout_d,
                 x3t):
    k.sreset()
    ln2 = k.salloc([128, DCH * T], BF16)
    xsl = k.salloc([128, DCH * 512], F32)
    xb = k.salloc([128, 512], BF16)
    xsqb = k.salloc([128, 512], BF16)
    row = k.salloc([1, 2048], F32)
    Rb = k.salloc([128, 512], F32)

    k09c = k.salloc([16, N], F32)
    _k09t = k.dma("sync", k09c[:], cst["k09c_dram"], "ld")
    k.wait("vector", "ld", _k09t)
    cst = dict(cst)
    cst["k09c"] = k09c

    psA_ = k.pbank[0][:]
    psB_ = k.pbank[1][:]
    psM = k.pbank[2][:]
    psR = k.pbank[3][:]
    emit_ln_resident(nc, k, cst, xr2, ln2, psA_, psB_, psM, psR,
                     (xsl, xb, xsqb, row, Rb))

    wsl = k.salloc([128, DCH * 128], BF16)
    convw = k.salloc([128, DIC * 4], F32)
    convb = k.salloc([128, DIC], F32)
    wB = k.salloc([128, DIC * 16], BF16)
    wC = k.salloc([16, DI], BF16)
    woutb = k.salloc([128, 2 * DIC * 128], BF16)
    xpb = k.salloc([128, 4 + N], F32)
    xcc = k.salloc([128, N], BF16)
    Bsb = k.salloc([16, N], F32)
    Hb = k.salloc([16, N], BF16)
    zsil = k.salloc([128, 512], BF16)
    ygns = k.salloc([128, DIC * 512], BF16)
    xns = [k.salloc([128, 512], F32) for _ in range(DCH)]
    xold = k.salloc([128, 2 * 512], F32)
    xtok = k.salloc([128, D], F32)

    for kc in range(DIC):
        k.dma("gpsimd", convw[:, kc * 4:(kc + 1) * 4],
              convw_d[kc * 128:(kc + 1) * 128, 0, :], "wld")
        k.dma("gpsimd", convb[:, kc:kc + 1],
              convb_d[kc * 128:(kc + 1) * 128].unsqueeze(1), "wld")
    k.dma("gpsimd", wB[:].rearrange("p (a c) -> p a c", c=16),
          wB_d.rearrange("(a p) c -> p a c", p=128), "wld")
    k.dma("gpsimd", wC[:], wC_d, "wld")
    wld0 = k.cnt["wld"]
    k.wait("tensor", "wld", wld0)
    k.wait("vector", "wld", wld0)
    k.wait("scalar", "wld", wld0)

    psX = k.pb2k[0][:]
    psY = k.pb2k[1][:]
    psBm = k.pb2k[2][:]
    psBm2 = k.pb2k[3][:]
    psD2 = k.pb2k[1][:]
    psT_ = k.pb2k[2][:]

    import os as _os
    if _os.environ.get("DBG_DUMPC"):
        dmpc = nc.dram_tensor("dumpc", [5, 128, N + 4], F32,
                              kind="ExternalOutput").ap()
    wout_pe = [0, 0]
    xold_dv = [0, 0]
    for b in range(B_CORE):
        bsl = slice(b * N, (b + 1) * N)
        for kc in range(DIC):
            k.wait("gpsimd", "pe", k.cnt["pe"])
            wldt = k.dma("gpsimd", wsl[:].rearrange("p (a c) -> p a c", c=128),
                         win_d[:, kc * 128:(kc + 1) * 128].rearrange(
                             "(a p) c -> p a c", p=128), "wld")
            k.wait("tensor", "wld", wldt)
            k.vop_s(lambda e, o=xpb[:, 0:4]: e.memset(o, 0.0))
            for ns in range(4):
                bank = psX[:, (ns % 2) * 512:(ns % 2) * 512 + 512]
                for a in range(DCH):
                    pet = k.op("tensor", lambda e, o=bank,
                               l=wsl[:, a * 128:(a + 1) * 128],
                               r=ln2[:, a * T + b * N + ns * 512:
                                     a * T + b * N + ns * 512 + 512],
                               s=(a == 0), q=(a == DCH - 1):
                               e.matmul(o, l, r, start=s, stop=q), inc="pe",
                               mmb=0)
                k.wait("vector", "pe", pet)
                dvt = k.op("vector", lambda e, o=xpb[:, 4 + ns * 512:
                                                  4 + (ns + 1) * 512],
                           i_=bank: e.tensor_copy(o, i_), inc="dv")
                k.wait("tensor", "dv", dvt)
            V = lambda d: xpb[:, 4 - d: 4 - d + N]
            acc = [xsl[:, 0:N], xsl[:, N:2 * N]]
            k.op("vector", lambda e, o=acc[0], i_=V(3),
                 w=convw[:, kc * 4:kc * 4 + 1]:
                 e.tensor_scalar(o, i_, w, None, ALU.mult))
            for jj in range(1, 4):
                k.op("vector", lambda e, o=acc[jj % 2], i_=V(3 - jj),
                     w=convw[:, kc * 4 + jj:kc * 4 + jj + 1],
                     a=acc[(jj + 1) % 2]:
                     e.scalar_tensor_tensor(o, i_, w, a, ALU.mult, ALU.add),
                     inc="dv")
            k.wait("scalar", "dv", k.cnt["dv"])
            k.wait("scalar", "pe", k.cnt["pe"])
            act = k.op("scalar", lambda e, o=xcc[:], i_=acc[1],
                       bb=convb[:, kc:kc + 1]:
                       e.activation(o, i_, AF.Silu, bias=bb, scale=1.0),
                       inc="ac")
            k.wait("tensor", "ac", act)
            import os as _os
            if _os.environ.get("DBG_DUMPC") and b == 0 and kc == 0:
                k.wait("sync", "ac", act)
                k.wait("gpsimd", "ac", act)
                k.dma("gpsimd", dmpc[2][:, 0:N], xcc[:], "wld")
                k.dma("sync", dmpc[4][:, 0:N + 4], xpb[:], "st")
            for ns in range(4):
                bank = (psBm if ns < 2 else psBm2)[0:16, (ns % 2) * 512:
                                                   (ns % 2) * 512 + 512]
                pet = k.op("tensor", lambda e, o=bank,
                           l=wB[:, kc * 16:(kc + 1) * 16],
                           r=xcc[:, ns * 512:(ns + 1) * 512],
                           s=(kc == 0), q=(kc == DIC - 1):
                           e.matmul(o, l, r, start=s, stop=q), inc="pe", mmb=0)
        k.wait("vector", "pe", k.cnt["pe"])
        for ns in range(4):
            bank = (psBm if ns < 2 else psBm2)[0:16, (ns % 2) * 512:
                                               (ns % 2) * 512 + 512]
            k.op("vector", lambda e, o=Bsb[:, ns * 512:(ns + 1) * 512], i_=bank:
                 e.tensor_copy(o, i_))
        k.vop_s(lambda e, o=Bsb[:], i_=Bsb[:]:
                e.tensor_scalar(o, i_, 0.1, None, ALU.mult))
        k.vop_s(lambda e, o=Hb[:], d0=cst["k09c"][:], d1=Bsb[:]:
                e.tensor_tensor_scan(o, d0, d1, 0.0, ALU.mult, ALU.add))
        import os as _os
        if _os.environ.get("DBG_DUMPC") and b == 0:
            k.wait("sync", "dv", k.cnt["dv"])
            k.wait("gpsimd", "dv", k.cnt["dv"])
            k.dma("sync", dmpc[0][0:16, 0:N], Bsb[:], "st")
            k.dma("gpsimd", dmpc[1][0:16, 0:N], Hb[:], "wld")
        dvt = k.op("vector", lambda e: e.engine_nop(), inc="dv")
        k.wait("tensor", "dv", dvt)
        for ns in range(4):
            nsl = slice(b * N + ns * 512, b * N + (ns + 1) * 512)
            for kc in range(DIC):
                wldt = k.dma("gpsimd", wsl[:].rearrange("p (a c) -> p a c", c=128),
                             win_d[:, DI + kc * 128: DI + (kc + 1) * 128].rearrange(
                                 "(a p) c -> p a c", p=128), "wld")
                k.wait("tensor", "wld", wldt)
                bankZ = psX[:, (kc % 2) * 512:(kc % 2) * 512 + 512]
                bankY = psY[:, (kc % 2) * 512:(kc % 2) * 512 + 512]
                for a in range(DCH):
                    pet = k.op("tensor", lambda e, o=bankZ,
                               l=wsl[:, a * 128:(a + 1) * 128],
                               r=ln2[:, a * T + b * N + ns * 512:
                                     a * T + b * N + ns * 512 + 512],
                               s=(a == 0), q=(a == DCH - 1):
                               e.matmul(o, l, r, start=s, stop=q), inc="pe",
                               mmb=0)
                pety = k.op("tensor", lambda e, o=bankY,
                            l=wC[:, kc * 128:(kc + 1) * 128],
                            r=Hb[:, ns * 512:(ns + 1) * 512]:
                            e.matmul(o, l, r, start=True, stop=True), inc="pe",
                            mmb=0)
                k.wait("scalar", "pe", pet)
                act = k.op("scalar", lambda e, o=zsil[:], i_=bankZ:
                           e.activation(o, i_, AF.Silu, bias=0.0, scale=1.0),
                           inc="ac")
                k.wait("vector", "ac", act)
                k.wait("vector", "pe", pety)
                dvt = k.op("vector", lambda e,
                           o=ygns[:, kc * 512:(kc + 1) * 512],
                           a=zsil[:], b_=bankY:
                           e.tensor_mul(o, a, b_), inc="dv")
                k.wait("tensor", "dv", dvt)
                k.wait("scalar", "dv", dvt)
                k.wait("gpsimd", "pe", pet)
                import os as _os
                if _os.environ.get("DBG_DUMPC") and b == 0 and ns == 0 and kc == 0:
                    k.wait("gpsimd", "dv", dvt)
                    k.dma("gpsimd", dmpc[3][:, 0:512], ygns[:, 0:512], "wld")
            for j in range(DCH):
                wbuf = woutb[:, (j % 2) * DIC * 128:(j % 2 + 1) * DIC * 128]
                k.wait("gpsimd", "pe", wout_pe[j % 2])
                wldt = k.dma("gpsimd",
                             wbuf.rearrange("p (a c) -> p a c", c=128),
                             wout_d[:, j * 128:(j + 1) * 128].rearrange(
                                 "(a p) c -> p a c", p=128), "wld")
                k.wait("tensor", "wld", wldt)
                bank = psD2[:, (j % 2) * 512:(j % 2) * 512 + 512]
                for kc in range(DIC):
                    pet = k.op("tensor", lambda e, o=bank,
                               l=wbuf[:, kc * 128:(kc + 1) * 128],
                               r=ygns[:, kc * 512:(kc + 1) * 512],
                               s=(kc == 0), q=(kc == DIC - 1):
                               e.matmul(o, l, r, start=s, stop=q), inc="pe",
                               mmb=0)
                wout_pe[j % 2] = pet
                buf = j % 2
                k.wait("sync", "dv", xold_dv[buf])
                ldt = k.dma("sync", xold[:, buf * 512:(buf + 1) * 512],
                            xr2[j * 128:(j + 1) * 128, nsl], "ld")
                k.wait("vector", "pe", pet)
                k.wait("vector", "ld", ldt)
                dvt = k.op("vector", lambda e, o=xns[j][:], a=bank,
                           b_=xold[:, buf * 512:(buf + 1) * 512]:
                           e.tensor_add(o, a, b_), inc="dv")
                xold_dv[buf] = dvt
                k.wait("tensor", "dv", dvt)
            for tt in range(4):
                for j in range(DCH):
                    bankT = psT_[:, (j % 2) * 512:(j % 2) * 512 + 128]
                    pet = k.op("tensor", lambda e, o=bankT,
                               i_=xns[j][:, tt * 128:(tt + 1) * 128],
                               idn=cst["ident_f32"][:]:
                               e.transpose(o, i_, idn), inc="pe", mmb=0)
                    k.wait("vector", "pe", pet)
                    dvt = k.op("vector", lambda e, o=xtok[:, j * 128:(j + 1) * 128],
                               i_=bankT: e.tensor_copy(o, i_), inc="dv")
                    k.wait("tensor", "dv", dvt)
                k.wait("sync", "dv", k.cnt["dv"])
                t0 = b * N + ns * 512 + tt * 128
                stt = k.dma("sync", x3t[t0:t0 + 128, :], xtok[:], "st")
                k.wait("vector", "st", stt)
    fin = k.fence("vector", "dv")
    k.wait("sync", "dv", fin)
    k.wait("sync", "st", k.cnt["st"])


def build_phase_c_program():
    nc, k = new_program()
    xr2 = nc.dram_tensor("xr2", [D, T], F32, kind="ExternalInput").ap()
    win_d = nc.dram_tensor("w_in", [D, 2 * DI], F32, kind="ExternalInput").ap()
    convw_d = nc.dram_tensor("conv_w", [DI, 1, 4], F32, kind="ExternalInput").ap()
    convb_d = nc.dram_tensor("conv_b", [DI], F32, kind="ExternalInput").ap()
    wB_d = nc.dram_tensor("w_B", [DI, 16], F32, kind="ExternalInput").ap()
    wC_d = nc.dram_tensor("w_C", [16, DI], F32, kind="ExternalInput").ap()
    wout_d = nc.dram_tensor("w_out", [DI, D], F32, kind="ExternalInput").ap()
    x3t = nc.dram_tensor("x3t", [T, D], F32, kind="ExternalOutput").ap()
    cst = stage_consts(nc, k)
    for eng in ENGINES:
        k.wait(eng, "ld", k.cnt["ld"])
    k.freeze_persistent()
    k.set_phase("pc")
    emit_phase_c(nc, k, cst, xr2, win_d, convw_d, convb_d, wB_d, wC_d, wout_d,
                 x3t)
    k.build()
    return nc



FCH = 22


def emit_quant(k, x_ap, out_ap, W, scr, scalar_dep=None):
    nb = W // 16
    mr_m = scr[0][:, 0:nb]
    mr_r = scr[0][:, nb:2 * nb]
    ab = scr[1][:, 0:W]
    qa = scr[2][:, 0:W]
    qb = scr[3][:, 0:W]
    qc = scr[4][:, 0:W]
    x3 = x_ap.rearrange("p (b s) -> p b s", s=16)
    k.vop_s(lambda e, o=mr_m, i_=x3:
            e.reduce_max(o, i_, AX.X, apply_absolute_value=True))
    k.vop_s(lambda e, o=mr_m, i_=mr_m:
            e.tensor_scalar(o, i_, 1.0 / 6.0, None, ALU.mult))
    k.vop_s(lambda e, o=mr_m, i_=mr_m:
            e.tensor_scalar(o, i_, 1e-12, None, ALU.max))
    k.vop_s(lambda e, o=mr_r, i_=mr_m: e.reciprocal(o, i_))
    act = k.op("scalar", lambda e, o=ab, i_=x_ap:
               e.activation(o, i_, AF.Abs, bias=0.0, scale=1.0), inc="ac")
    k.wait("vector", "ac", act)
    rb = mr_r.rearrange("p (b one) -> p b one", one=1).to_broadcast((128, nb, 16))
    k.op("vector", lambda e, o=ab.rearrange("p (b s) -> p b s", s=16),
         a=ab.rearrange("p (b s) -> p b s", s=16), b_=rb:
         e.tensor_tensor(o, a, b_, ALU.mult))
    k.op("vector", lambda e, o=qa, i_=ab:
         e.tensor_scalar(o, i_, 0.25, None, ALU.is_gt))
    k.op("vector", lambda e, o=qb, i_=ab, a=qa:
         e.scalar_tensor_tensor(o, i_, 0.75, a, ALU.is_gt, ALU.add))
    k.op("vector", lambda e, o=qa, i_=ab, a=qb:
         e.scalar_tensor_tensor(o, i_, 1.25, a, ALU.is_gt, ALU.add))
    k.op("vector", lambda e, o=qb, i_=ab, a=qa:
         e.scalar_tensor_tensor(o, i_, 1.75, a, ALU.is_gt, ALU.add))
    k.op("vector", lambda e, o=qa, i_=ab:
         e.tensor_scalar(o, i_, 2.5, None, ALU.is_gt))
    k.op("vector", lambda e, o=qc, i_=ab, a=qa:
         e.scalar_tensor_tensor(o, i_, 3.5, a, ALU.is_gt, ALU.add))
    k.op("vector", lambda e, o=qa, i_=ab:
         e.tensor_scalar(o, i_, 5.0, None, ALU.is_gt))
    k.op("vector", lambda e, o=qb, i_=qb, a=qc:
         e.scalar_tensor_tensor(o, i_, 0.5, a, ALU.mult, ALU.add))
    k.op("vector", lambda e, o=qa, i_=qa, a=qb:
         e.scalar_tensor_tensor(o, i_, 2.0, a, ALU.mult, ALU.add), inc="dv")
    k.wait("scalar", "dv", k.cnt["dv"])
    act = k.op("scalar", lambda e, o=qc, i_=x_ap:
               e.activation(o, i_, AF.Sign, bias=0.0, scale=1.0), inc="ac")
    k.wait("vector", "ac", act)
    k.op("vector", lambda e, o=qa, a=qa, b_=qc: e.tensor_mul(o, a, b_))
    mb = mr_m.rearrange("p (b one) -> p b one", one=1).to_broadcast((128, nb, 16))
    dvt = k.op("vector", lambda e, o=out_ap.rearrange("p (b s) -> p b s", s=16),
               a=qa.rearrange("p (b s) -> p b s", s=16), b_=mb:
               e.tensor_tensor(o, a, b_, ALU.mult), inc="dv")
    return dvt


def emit_phase_d(nc, k, cst, x3t, wg_d, wu_d, wd_d, out_d, wgq_d, wuq_d, wdq_d):
    k.sreset()
    xq = k.salloc([128, DCH * T], BF16)
    mr = k.salloc([128, 128], F32)
    ab = k.salloc([128, 512], F32)
    qa = k.salloc([128, 512], F32)
    qb = k.salloc([128, 512], F32)
    qc = k.salloc([128, 512], F32)
    scr = (mr, ab, qa, qb, qc)

    def quant_sliced(x_ap, out_ap, W):
        for cs in range(0, W, 512):
            cw = min(512, W - cs)
            emit_quant(k, x_ap[:, cs:cs + cw], out_ap[:, cs:cs + cw], cw, scr)
    xt = k.salloc([128, D], F32)
    lnt = k.salloc([128, D], F32)
    stat = k.salloc([128, 8], F32)
    qt = k.salloc([128, D], BF16)
    wgq = k.salloc([128, 2 * 1024], BF16)
    wuq = k.salloc([128, 2 * 1024], BF16)
    wdj = k.salloc([128, 2 * 2816], BF16)
    sg = k.salloc([128, 2 * 1024], F32)
    hf = k.salloc([128, 2 * 1024], F32)
    htok = k.salloc([128, 1024], F32)
    hqt = k.salloc([128, 2 * 1024], BF16)
    outfm = k.salloc([128, DCH * 1024], BF16)
    outtok = k.salloc([128, D], F32)
    xold = k.salloc([128, D], F32)
    m_pre = k.smark()
    wtile = k.salloc([128, 2816], F32)
    wqt = k.salloc([128, 2816], BF16)
    wT = k.salloc([128, 1024], BF16)

    psT = k.pb2k[0][:]
    psU = k.pb2k[1][:]
    psW = k.pb2k[2][:]
    psD = k.pb2k[3][:]

    ntile = T // 128
    prev_pe = 0
    for i in range(ntile):
        xt_i = xt[:]
        k.wait("sync", "dv", k.cnt["dv"])
        ldt = k.dma("sync", xt_i, x3t[i * 128:(i + 1) * 128, :], "ld")
        k.wait("vector", "ld", ldt)
        k.wait("scalar", "ld", ldt)
        k.vop_s(lambda e, o=stat[:, 0:1], i_=xt_i: e.reduce_sum(o, i_, AX.X))
        k.vop_s(lambda e, o=hf[:, 0:D], a=xt_i, b_=xt_i: e.tensor_mul(o, a, b_))
        k.vop_s(lambda e, o=stat[:, 1:2], i_=hf[:, 0:D]:
                e.reduce_sum(o, i_, AX.X))
        k.vop_s(lambda e, o=stat[:, 2:4], i_=stat[:, 0:2]:
                e.tensor_scalar(o, i_, 1.0 / D, None, ALU.mult))
        k.vop_s(lambda e, o=stat[:, 4:5], a=stat[:, 2:3], b_=stat[:, 2:3]:
                e.tensor_mul(o, a, b_))
        k.vop_s(lambda e, o=stat[:, 5:6], a=stat[:, 3:4], b_=stat[:, 4:5]:
                e.tensor_sub(o, a, b_))
        k.vop_s(lambda e, o=stat[:, 6:7], i_=stat[:, 5:6]:
                e.tensor_scalar(o, i_, 1e-5, None, ALU.add))
        k.wait("scalar", "dv", k.cnt["dv"])
        sct = k.op("scalar", lambda e, o=stat[:, 5:6], i_=stat[:, 6:7]:
                   e.activation(o, i_, AF.Sqrt, bias=0.0, scale=1.0), inc="ac")
        k.wait("vector", "ac", sct)
        k.vop_s(lambda e, o=stat[:, 6:7], i_=stat[:, 5:6]: e.reciprocal(o, i_))
        k.op("vector", lambda e, o=lnt[:], i_=xt_i, m=stat[:, 2:3], r=stat[:, 6:7]:
             e.tensor_scalar(o, i_, m, r, ALU.subtract, ALU.mult), inc="dv")
        k.wait("scalar", "dv", k.cnt["dv"])
        quant_sliced(lnt[:], qt[:], D)
        k.wait("tensor", "dv", k.cnt["dv"])
        k.wait("tensor", "pe", prev_pe)
        for a in range(DCH):
            bank = psT[:, (a % 2) * 512:(a % 2) * 512 + 128].bitcast(BF16)
            pet = k.op("tensor", lambda e, o=bank[:, 0:128],
                       i_=qt[:, a * 128:(a + 1) * 128], idn=cst["ident_bf"][:]:
                       e.transpose(o, i_, idn), inc="pe", mmb=0)
            k.wait("vector", "pe", pet)
            dvt = k.op("vector", lambda e, o=xq[:, a * T + i * 128:
                                             a * T + (i + 1) * 128],
                       i_=bank[:, 0:128]: e.tensor_copy(o, i_), inc="dv")
            k.wait("tensor", "dv", dvt)
        prev_pe = k.cnt["pe"]

    def wprep(src_d, dst_d, nrow_chunks, wcols):
        for f in range(nrow_chunks):
            k.wait("sync", "dv", k.cnt["dv"])
            k.wait("sync", "wld", k.cnt["wld"])
            ldt = k.dma("sync", wtile[:, 0:wcols],
                        src_d[f * 128:(f + 1) * 128, :], "ld")
            k.wait("vector", "ld", ldt)
            k.wait("scalar", "ld", ldt)
            quant_sliced(wtile[:, 0:wcols], wqt[:, 0:wcols], wcols)
            k.wait("tensor", "dv", k.cnt["dv"])
            nblk = wcols // 128
            for a in range(nblk):
                bank = psT[:, (a % 2) * 512:(a % 2) * 512 + 128].bitcast(BF16)
                pet = k.op("tensor", lambda e, o=bank[:, 0:128],
                           i_=wqt[:, a * 128:(a + 1) * 128],
                           idn=cst["ident_bf"][:]:
                           e.transpose(o, i_, idn), inc="pe", mmb=0)
                k.wait("vector", "pe", pet)
                if a % 8 == 0:
                    k.wait("vector", "wld", k.cnt["wld"])
                col = (a % 8) * 128
                dvt = k.op("vector", lambda e, o=wT[:, col:col + 128],
                           i_=bank[:, 0:128]: e.tensor_copy(o, i_), inc="dv")
                k.wait("tensor", "dv", dvt)
                if a % 8 == 7 or a == nblk - 1:
                    k.wait("gpsimd", "dv", dvt)
                    k.dma("gpsimd", dst_d[f, :, (a // 8) * 1024:
                                          (a // 8) * 1024 + col + 128],
                          wT[:, 0:col + 128], "wld")
    wprep(wg_d, wgq_d, FCH, D)
    wprep(wu_d, wuq_d, FCH, D)
    wprep(wd_d, wdq_d, DCH, 2816)
    wp_done = k.fence("vector", "dv")
    k.wait("sync", "dv", wp_done)
    k.wait("sync", "wld", k.cnt["wld"])
    k.srewind(m_pre)
    hq = k.salloc([128, FCH * 1024], BF16)

    wld_guard = [0, 0]
    hq_dv = 0
    hf_ac = [0, 0]
    hf_dv = [0, 0]
    for ns in range(4):
        nsl = slice(ns * 1024, (ns + 1) * 1024)

        def emit_htrans(f):
            par = f % 2
            ev = [k.cnt["dv"], k.cnt["dv"]]
            for a in range(DCH):
                bank = psT[:, (a % 2) * 512:(a % 2) * 512 + 128]
                k.wait("tensor", "dv", ev[a % 2])
                pet = k.op("tensor", lambda e, o=bank,
                           i_=hf[:, par * 1024 + a * 128:par * 1024 + (a + 1) * 128],
                           idn=cst["ident_f32"][:]:
                           e.transpose(o, i_, idn), inc="pe", mmb=0)
                k.wait("vector", "pe", pet)
                ev[a % 2] = k.op("vector",
                                 lambda e, o=htok[:, a * 128:(a + 1) * 128],
                                 i_=bank: e.tensor_copy(o, i_), inc="dv")

        def emit_hquant(f):
            par = f % 2
            k.wait("scalar", "dv", k.cnt["dv"])
            quant_sliced(htok[:], hqt[:, par * 1024:(par + 1) * 1024], D)

        def emit_backtrans(f):
            par = f % 2
            ev = [k.cnt["dv"], k.cnt["dv"]]
            for a in range(DCH):
                bank = psT[:, (a % 2) * 512:(a % 2) * 512 + 128].bitcast(BF16)
                k.wait("tensor", "dv", ev[a % 2])
                pet = k.op("tensor", lambda e, o=bank[:, 0:128],
                           i_=hqt[:, par * 1024 + a * 128:
                                 par * 1024 + (a + 1) * 128],
                           idn=cst["ident_bf"][:]:
                           e.transpose(o, i_, idn), inc="pe", mmb=0)
                k.wait("vector", "pe", pet)
                ev[a % 2] = k.op("vector", lambda e,
                                 o=hq[:, f * 1024 + a * 128:f * 1024 + (a + 1) * 128],
                                 i_=bank[:, 0:128]: e.tensor_copy(o, i_),
                                 inc="dv")

        def issue_wload(f):
            buf = f % 2
            k.wait("sync", "pe", wld_guard[buf])
            k.dma("sync", wgq[:, buf * 1024:(buf + 1) * 1024],
                  wgq_d[f, :, :], "ld")
            return k.dma("sync", wuq[:, buf * 1024:(buf + 1) * 1024],
                         wuq_d[f, :, :], "ld")

        wtgt = [0, 0]
        wtgt[0] = issue_wload(0)
        for f in range(FCH):
            buf = f % 2
            if f + 1 < FCH:
                wtgt[(f + 1) % 2] = issue_wload(f + 1)
            k.wait("tensor", "ld", wtgt[buf])
            k.wait("tensor", "ac", max(hf_ac))
            k.wait("tensor", "dv", max(hf_dv))
            for half in range(2):
                hsl = slice(half * 512, (half + 1) * 512)
                for a in range(DCH):
                    k.op("tensor", lambda e, o=psU[:, hsl],
                         l=wgq[:, buf * 1024 + a * 128:buf * 1024 + (a + 1) * 128],
                         r=xq[:, a * T + ns * 1024 + half * 512:
                               a * T + ns * 1024 + half * 512 + 512],
                         s=(a == 0), q=(a == DCH - 1):
                         e.matmul(o, l, r, start=s, stop=q), mmb=0)
                for a in range(DCH):
                    pet = k.op("tensor", lambda e, o=psW[:, hsl],
                               l=wuq[:, buf * 1024 + a * 128:
                                     buf * 1024 + (a + 1) * 128],
                               r=xq[:, a * T + ns * 1024 + half * 512:
                                     a * T + ns * 1024 + half * 512 + 512],
                               s=(a == 0), q=(a == DCH - 1):
                               e.matmul(o, l, r, start=s, stop=q), inc="pe",
                               mmb=0)
            wld_guard[buf] = pet
            k.wait("scalar", "pe", pet)
            k.wait("scalar", "dv", hf_dv[buf])
            hf_ac[buf] = k.op("scalar", lambda e,
                              o=sg[:, buf * 1024:(buf + 1) * 1024], i_=psU[:]:
                              e.activation(o, i_, AF.Silu, bias=0.0, scale=1.0),
                              inc="ac")
            k.wait("vector", "ac", hf_ac[buf])
            hf_dv[buf] = k.op("vector", lambda e,
                              o=hf[:, buf * 1024:(buf + 1) * 1024],
                              a=sg[:, buf * 1024:(buf + 1) * 1024], b_=psW[:]:
                              e.tensor_mul(o, a, b_), inc="dv")
            if f >= 1:
                emit_backtrans(f - 1) if False else None
            if f >= 1:
                pass
            if f >= 1:
                emit_htrans(f - 1)
                emit_hquant(f - 1)
            if f >= 2:
                emit_backtrans(f - 2)
        emit_htrans(FCH - 1)
        emit_hquant(FCH - 1)
        emit_backtrans(FCH - 2)
        emit_backtrans(FCH - 1)
        hq_dv = k.cnt["dv"]
        k.wait("tensor", "dv", hq_dv)
        def issue_dload(j):
            buf = j % 2
            k.wait("sync", "pe", wld_guard[buf])
            return k.dma("sync", wdj[:, buf * 2816:(buf + 1) * 2816],
                         wdq_d[j, :, :], "ld")

        dtgt = [0, 0]
        dtgt[0] = issue_dload(0)
        for j in range(DCH):
            buf = j % 2
            if j + 1 < DCH:
                dtgt[(j + 1) % 2] = issue_dload(j + 1)
            k.wait("tensor", "ld", dtgt[buf])
            for half in range(2):
                bank = psD[:, half * 512:(half + 1) * 512]
                for f in range(FCH):
                    pet = k.op("tensor", lambda e, o=bank,
                               l=wdj[:, buf * 2816 + f * 128:
                                     buf * 2816 + (f + 1) * 128],
                               r=hq[:, f * 1024 + half * 512:
                                     f * 1024 + half * 512 + 512],
                               s=(f == 0), q=(f == FCH - 1):
                               e.matmul(o, l, r, start=s, stop=q), inc="pe",
                               mmb=0)
            wld_guard[buf] = pet
            k.wait("vector", "pe", pet)
            dvt = k.op("vector", lambda e, o=outfm[:, j * 1024:(j + 1) * 1024],
                       i_=psD[:]: e.tensor_copy(o, i_), inc="dv")
            k.wait("tensor", "dv", dvt)
        for tt in range(8):
            t0 = ns * 1024 + tt * 128
            for j in range(DCH):
                bank = psT[:, (j % 2) * 512:(j % 2) * 512 + 128].bitcast(BF16)
                pet = k.op("tensor", lambda e, o=bank[:, 0:128],
                           i_=outfm[:, j * 1024 + tt * 128:
                                    j * 1024 + (tt + 1) * 128],
                           idn=cst["ident_bf"][:]:
                           e.transpose(o, i_, idn), inc="pe", mmb=0)
                k.wait("vector", "pe", pet)
                dvt = k.op("vector", lambda e, o=outtok[:, j * 128:(j + 1) * 128],
                           i_=bank[:, 0:128]: e.tensor_copy(o, i_), inc="dv")
                k.wait("tensor", "dv", dvt)
            k.wait("sync", "dv", k.cnt["dv"])
            ldt = k.dma("sync", xold[:], x3t[t0:t0 + 128, :], "ld")
            k.wait("vector", "ld", ldt)
            dvt = k.op("vector", lambda e, o=outtok[:], a=outtok[:], b_=xold[:]:
                       e.tensor_add(o, a, b_), inc="dv")
            k.wait("sync", "dv", dvt)
            stt = k.dma("sync", out_d[t0:t0 + 128, :], outtok[:], "st")
            k.wait("vector", "st", stt)
    fin = k.fence("vector", "dv")
    k.wait("sync", "dv", fin)
    k.wait("sync", "st", k.cnt["st"])


def build_phase_d_program():
    nc, k = new_program()
    x3t = nc.dram_tensor("x3t", [T, D], F32, kind="ExternalInput").ap()
    wg_d = nc.dram_tensor("w_gate", [2816, D], F32, kind="ExternalInput").ap()
    wu_d = nc.dram_tensor("w_up", [2816, D], F32, kind="ExternalInput").ap()
    wd_d = nc.dram_tensor("w_down", [D, 2816], F32, kind="ExternalInput").ap()
    out_d = nc.dram_tensor("out", [T, D], F32, kind="ExternalOutput").ap()
    wgq_d = nc.dram_tensor("wgq", [FCH, 128, D], BF16, kind="Internal").ap()
    wuq_d = nc.dram_tensor("wuq", [FCH, 128, D], BF16, kind="Internal").ap()
    wdq_d = nc.dram_tensor("wdq", [DCH, 128, 2816], BF16, kind="Internal").ap()
    cst = stage_consts(nc, k)
    for eng in ENGINES:
        k.wait(eng, "ld", k.cnt["ld"])
    k.freeze_persistent()
    k.set_phase("pd")
    emit_phase_d(nc, k, cst, x3t, wg_d, wu_d, wd_d, out_d, wgq_d, wuq_d, wdq_d)
    k.build()
    return nc



def build_fused_program():
    nc, k = new_program()
    x_in = nc.dram_tensor("x", [T, D], F32, kind="ExternalInput").ap()
    wq_d = nc.dram_tensor("Wq", [D, D], F32, kind="ExternalInput").ap()
    wk_d = nc.dram_tensor("Wk", [D, D], F32, kind="ExternalInput").ap()
    wv_d = nc.dram_tensor("Wv", [D, D], F32, kind="ExternalInput").ap()
    wo_d = nc.dram_tensor("Wo", [D, D], F32, kind="ExternalInput").ap()
    wg1_d = nc.dram_tensor("Wg1", [D, 16], F32, kind="ExternalInput").ap()
    wg2_d = nc.dram_tensor("Wg2", [16, D], F32, kind="ExternalInput").ap()
    normw_d = nc.dram_tensor("norm_w", [D], F32, kind="ExternalInput").ap()
    router_d = nc.dram_tensor("router", [D, 8], F32, kind="ExternalInput").ap()
    w1_d = nc.dram_tensor("w1", [8, D, 2816], F32, kind="ExternalInput").ap()
    w3_d = nc.dram_tensor("w3", [8, D, 2816], F32, kind="ExternalInput").ap()
    w2_d = nc.dram_tensor("w2", [8, 2816, D], F32, kind="ExternalInput").ap()
    ws1_d = nc.dram_tensor("ws1", [D, 2816], F32, kind="ExternalInput").ap()
    ws3_d = nc.dram_tensor("ws3", [D, 2816], F32, kind="ExternalInput").ap()
    ws2_d = nc.dram_tensor("ws2", [2816, D], F32, kind="ExternalInput").ap()
    win_d = nc.dram_tensor("w_in", [D, 2 * DI], F32, kind="ExternalInput").ap()
    convw_d = nc.dram_tensor("conv_w", [DI, 1, 4], F32, kind="ExternalInput").ap()
    convb_d = nc.dram_tensor("conv_b", [DI], F32, kind="ExternalInput").ap()
    wB_d = nc.dram_tensor("w_B", [DI, 16], F32, kind="ExternalInput").ap()
    wC_d = nc.dram_tensor("w_C", [16, DI], F32, kind="ExternalInput").ap()
    wout_d = nc.dram_tensor("w_out", [DI, D], F32, kind="ExternalInput").ap()
    wg_d = nc.dram_tensor("w_gate", [2816, D], F32, kind="ExternalInput").ap()
    wu_d = nc.dram_tensor("w_up", [2816, D], F32, kind="ExternalInput").ap()
    wd_d = nc.dram_tensor("w_down", [D, 2816], F32, kind="ExternalInput").ap()
    out_d = nc.dram_tensor("out", [T, D], F32, kind="ExternalOutput").ap()
    import os as _os
    _ko = "ExternalOutput" if _os.environ.get("DBG_FX") else "Internal"
    xr0 = nc.dram_tensor("xr0", [D, T], F32, kind="Internal").ap()
    xr1 = nc.dram_tensor("xr1", [D, T], F32, kind=_ko).ap()
    xr2 = nc.dram_tensor("xr2", [D, T], F32, kind=_ko).ap()
    x3t = nc.dram_tensor("x3t", [T, D], F32, kind=_ko).ap()
    o_s = nc.dram_tensor("o_s", [D, T], BF16, kind="Internal").ap()
    lnS = nc.dram_tensor("lnS", [D, T], BF16, kind="Internal").ap()
    alpha_d = nc.dram_tensor("alpha_d", [8, T], BF16, kind="Internal").ap()
    wgq_d = nc.dram_tensor("wgq", [FCH, 128, D], BF16, kind="Internal").ap()
    wuq_d = nc.dram_tensor("wuq", [FCH, 128, D], BF16, kind="Internal").ap()
    wdq_d = nc.dram_tensor("wdq", [DCH, 128, 2816], BF16, kind="Internal").ap()

    cst = stage_consts(nc, k)
    for eng in ENGINES:
        k.wait(eng, "ld", k.cnt["ld"])
    k.freeze_persistent()
    k.set_phase("pa")
    emit_phase_a(nc, k, cst, x_in, wq_d, wk_d, wv_d, wo_d, wg1_d, wg2_d,
                 normw_d, xr0, xr1, o_s)
    k.barrier_all()
    k.set_phase("pb")
    emit_phase_b(nc, k, cst, xr1, router_d, w1_d, w3_d, w2_d, ws1_d, ws3_d,
                 ws2_d, xr2, lnS, alpha_d)
    k.barrier_all()
    k.set_phase("pc")
    emit_phase_c(nc, k, cst, xr2, win_d, convw_d, convb_d, wB_d, wC_d, wout_d,
                 x3t)
    k.barrier_all()
    k.set_phase("pd")
    emit_phase_d(nc, k, cst, x3t, wg_d, wu_d, wd_d, out_d, wgq_d, wuq_d, wdq_d)
    k.build()
    return nc


def kernel_fused(**inputs):
    consts = host_constants()
    NCORES = 8
    x_full = inputs["L0_attn_x"]
    ncF = _prog("fused", build_fused_program)
    maps = []
    for c in range(NCORES):
        m = dict(consts)
        m["x"] = np.ascontiguousarray(
            x_full[c * B_CORE:(c + 1) * B_CORE].reshape(T, D))
        for nm, src in (
                ("Wq", "L0_attn_Wq"), ("Wk", "L0_attn_Wk"), ("Wv", "L0_attn_Wv"),
                ("Wo", "L0_attn_Wo"), ("Wg1", "L0_attn_Wg1"),
                ("Wg2", "L0_attn_Wg2"), ("norm_w", "L0_attn_norm_w"),
                ("router", "L0_ffn_router_DE"), ("w1", "L0_ffn_w1_eDF"),
                ("w3", "L0_ffn_w3_eDF"), ("w2", "L0_ffn_w2_eFD"),
                ("ws1", "L0_ffn_ws1_DF"), ("ws3", "L0_ffn_ws3_DF"),
                ("ws2", "L0_ffn_ws2_FD"), ("w_in", "L1_attn_w_in"),
                ("conv_w", "L1_attn_conv_w"), ("conv_b", "L1_attn_conv_b"),
                ("w_B", "L1_attn_w_B"), ("w_C", "L1_attn_w_C"),
                ("w_out", "L1_attn_w_out"), ("w_gate", "L1_ffn_w_gate"),
                ("w_up", "L1_ffn_w_up"), ("w_down", "L1_ffn_w_down")):
            m[nm] = inputs[src]
        maps.append(m)
    res = run_bass_kernel_spmd(ncF, maps, core_ids=list(range(NCORES)))
    outs = [np.asarray(r["out"]) for r in res.results]
    return np.concatenate(outs, axis=0).reshape(16, N, D).astype(np.float32)
